# version 07 (final)
# baseline (speedup 1.0000x reference)
import numpy as np
import ml_dtypes
import concourse.bass as bass
import concourse.mybir as mybir
from concourse.bass_utils import run_bass_kernel_spmd
from contextlib import ExitStack

F32 = mybir.dt.float32
BF16 = mybir.dt.bfloat16
AF = mybir.ActivationFunctionType
ALU = mybir.AluOpType
AX = mybir.AxisListType

B, T, D = 4, 4096, 1024
NCORES = 8
OWN = 2048
HALO = 384
NT0 = HALO + OWN
C8 = 8
TB = 512
NTB = 5
NEG = -1.0e30
EPS = 1e-5

A0_HEADS, A0_DH, A0_WIN = 16, 64, 32
IN_DEPTH, IN_HEADS, IN_DH, IN_WIN = 6, 16, 64, 256
DIM_INNER = 2730
NPAIR = (DIM_INNER + 127) // 128
QK_SCALE = 8.0
L1_HEADS, L1_DH, L1_HID, L1_WIN = 8, 64, 512, 128
NQ, NKV = 1024, 2048
KQ, KKV = 1152.0, 2304.0
N_ITERS = 50


def _rope_tables_a0(pos):
    inv_freq = 1.0 / 10000.0 ** (np.arange(0, A0_DH, 2, dtype=np.float64) / A0_DH)
    p = np.maximum(pos, 0).astype(np.float64)
    fr = p[:, None] * inv_freq[None, :]
    fr = np.concatenate([fr, fr], -1)
    cos = np.cos(fr).astype(np.float32)
    sin = np.sin(fr).astype(np.float32)
    return cos.T.copy(), sin.T.copy()


def _xpos_tables(pos, n):
    half = np.arange(0, IN_DH, 2, dtype=np.float32)
    inv_freq = (1.0 / 10000.0 ** (half / IN_DH)).astype(np.float32)
    p = np.maximum(pos, 0).astype(np.float32)
    fr = p[:, None] * inv_freq[None, :]
    fr = np.concatenate([fr, fr], -1)
    cos, sin = np.cos(fr), np.sin(fr)
    base = ((half + 0.4 * IN_DH) / (1.4 * IN_DH)).astype(np.float32)
    power = ((p - n // 2) / (IN_WIN // 2)).astype(np.float32)
    scale = base[None, :] ** power[:, None]
    scale = np.concatenate([scale, scale], -1).astype(np.float32)
    inv = (1.0 / scale).astype(np.float32)
    return ((QK_SCALE * cos * scale).T.copy(),
            (QK_SCALE * sin * scale).T.copy(),
            (cos * inv).T.copy(), (sin * inv).T.copy())


def _rot_sign_matrix(dh=64):
    P = np.zeros((128, 128), dtype=np.float32)
    for dp in range(128):
        base = (dp // dh) * dh
        o = dp % dh
        if o < dh // 2:
            P[base + o + dh // 2, dp] = -1.0
        else:
            P[base + o - dh // 2, dp] = 1.0
    return P


def _masks():
    k = np.arange(128)[:, None]
    q = np.arange(128)[None, :]
    m0 = (k <= q).astype(np.float32)
    m2 = (k >= q).astype(np.float32)
    md = (((q - k) >= 0) & ((q - k) < A0_WIN)).astype(np.float32)
    mp = ((k - q) > (128 - A0_WIN)).astype(np.float32)
    return np.stack([m0, m2, md, mp])


def _bf(x):
    return np.ascontiguousarray(np.asarray(x).astype(ml_dtypes.bfloat16))


def _f32(x):
    return np.ascontiguousarray(np.asarray(x, dtype=np.float32))


def make_core_inputs(ins, c):
    b, h = c // 2, c % 2
    own0 = h * OWN
    g = {}
    x = np.asarray(ins["L0_attn_x"][b])
    lo = own0 - HALO
    sh = np.zeros((NT0, D), dtype=np.float32)
    s0 = max(lo, 0)
    sh[s0 - lo:, :] = x[s0:own0 + OWN]
    g["xT"] = _f32(sh.T)

    pos = np.arange(lo, own0 + OWN)
    g["padrowbf"] = _bf(np.where(pos < 0, np.float32(NEG), 0.0)[None, :])
    g["coremask"] = np.full((128, 1), 1.0 if h == 1 else 0.0, np.float32)

    cosA, sinA = _rope_tables_a0(pos)
    dup = lambda t: np.concatenate([t, t], axis=0)
    g["TcA"], g["TsA"] = _bf(dup(cosA)), _bf(dup(sinA))
    Tc, Ts, Tci, Tsi = _xpos_tables(pos, T)
    g["TcB"], g["TsB"] = _bf(dup(Tc)), _bf(dup(Ts))
    g["TciB"], g["TsiB"] = _bf(dup(Tci)), _bf(dup(Tsi))
    g["MSK"] = _bf(_masks().transpose(1, 0, 2).reshape(128, 4 * 128))
    g["onesbf"] = _bf(np.ones((1, 1408)))
    g["PROTF"] = _f32(_rot_sign_matrix())
    bd2 = np.zeros((128, 2), np.float32)
    bd2[0:64, 0] = 1.0
    bd2[64:128, 1] = 1.0
    g["BD2"] = bd2
    g["BD2T"] = _f32(bd2.T)
    g["IDF"] = _f32(np.eye(128))

    g["w_qkvT"] = _f32(np.asarray(ins["L0_attn_w_qkv"]).T)
    g["w_outT"] = _f32(np.asarray(ins["L0_attn_w_out"]).T)
    g["a0_lnw"] = _f32(np.asarray(ins["L0_attn_ln_w"]).reshape(C8, 128).T)
    g["a0_lnb"] = _f32(np.asarray(ins["L0_attn_ln_b"]).reshape(C8, 128).T)

    g["Wqkv"] = _f32(ins["L0_ffn_Wqkv"])
    g["Wo"] = _f32(ins["L0_ffn_Wo"])
    qs, ks = np.asarray(ins["L0_ffn_q_scale"]), np.asarray(ins["L0_ffn_k_scale"])
    g["qs_col"] = _f32(np.tile(qs, (1, 2)).T)
    g["ks_col"] = _f32(np.tile(ks, (1, 2)).T)
    g["ln_g"] = _f32(np.asarray(ins["L0_ffn_ln_g"]).reshape(IN_DEPTH, C8, 128).transpose(0, 2, 1))
    g["ln_b"] = _f32(np.asarray(ins["L0_ffn_ln_b"]).reshape(IN_DEPTH, C8, 128).transpose(0, 2, 1))
    W1 = np.asarray(ins["L0_ffn_W1"])
    b1 = np.asarray(ins["L0_ffn_b1"])
    W1p = np.zeros((IN_DEPTH, D, NPAIR * 256), np.float32)
    for p in range(NPAIR):
        a0, a1 = p * 128, min((p + 1) * 128, DIM_INNER)
        W1p[:, :, p * 256:p * 256 + (a1 - a0)] = W1[:, :, a0:a1]
        W1p[:, :, p * 256 + 128:p * 256 + 128 + (a1 - a0)] = \
            W1[:, :, DIM_INNER + a0:DIM_INNER + a1]
    g["W1p"] = W1p
    b1p = np.zeros((IN_DEPTH, 2 * NPAIR, 128), np.float32)
    for p in range(NPAIR):
        a0, a1 = p * 128, min((p + 1) * 128, DIM_INNER)
        b1p[:, 2 * p, :a1 - a0] = b1[:, a0:a1]
        b1p[:, 2 * p + 1, :a1 - a0] = b1[:, DIM_INNER + a0:DIM_INNER + a1]
    g["b1p"] = _f32(b1p.transpose(0, 2, 1))
    g["W2"] = _f32(ins["L0_ffn_W2"])
    g["b2c"] = _f32(np.asarray(ins["L0_ffn_b2"]).reshape(IN_DEPTH, C8, 128).transpose(0, 2, 1))

    g["wq1"] = _f32(ins["L1_attn_wq"])
    g["wk1"] = _f32(ins["L1_attn_wk"])
    g["wv1T"] = _f32(np.asarray(ins["L1_attn_wv"]).T)

    g["IDB"] = _bf(np.eye(128))
    g["onesb1"] = _bf(np.ones((1, 4096)))
    wkvl = np.asarray(ins["L1_ffn_wkv_light"])
    cols = []
    for hh in range(8):
        cols.append(wkvl[:, 64 * hh:64 * hh + 64])
        cols.append(wkvl[:, 512 + 64 * hh:512 + 64 * hh + 64])
    g["wkvl_r"] = _f32(np.concatenate(cols, axis=1))
    g["gl_col"] = _f32(np.asarray(ins["L1_ffn_gamma_light"]).reshape(C8, 128).T)
    g["gh_col"] = _f32(np.asarray(ins["L1_ffn_gamma_heavy"]).reshape(C8, 128).T)
    g["wq_l"] = _f32(ins["L1_ffn_wq_light"])
    g["wkv_l"] = _f32(ins["L1_ffn_wkv_light"])
    g["wo_l"] = _f32(ins["L1_ffn_wo_light"])
    g["rt_q"] = _f32(np.asarray(ins["L1_ffn_q_routing_token"]).reshape(C8, 128).T)
    g["rt_kv"] = _f32(np.asarray(ins["L1_ffn_kv_routing_token"]).reshape(C8, 128).T)
    g["wq_h"] = _f32(ins["L1_ffn_wq_heavy"])
    g["wkv_hT"] = _f32(np.asarray(ins["L1_ffn_wkv_heavy"]).T)
    g["wo_h"] = _f32(ins["L1_ffn_wo_heavy"])
    return g


class Chain:

    def __init__(self, nc, sem, auto=True, wcache=None):
        self.nc = nc
        self.sem = sem
        self.auto = auto
        self.val = 0
        self.prev = None
        self.wcache = wcache if wcache is not None else {}

    def _wait(self, eng, t):
        key = (eng.engine, id(t[0]))
        if self.wcache.get(key, -1) >= t[1]:
            return
        self.wcache[key] = t[1]
        eng.wait_ge(t[0], t[1])

    def do(self, eng, build, waits=(), mark=None):
        for t in waits:
            if t is not None:
                self._wait(eng, t)
        if self.auto and self.prev is not None:
            self._wait(eng, self.prev)
        ins = build(eng)
        if self.auto or mark:
            ins.then_inc(self.sem, 1)
            self.val += 1
            self.prev = (self.sem, self.val)
        return self.prev


class KB:
    def __init__(self, dev_taps=()):
        self.nc = bass.Bass("TRN2", target_bir_lowering=False)
        self.dev_taps = set(dev_taps)
        self.ctx = ExitStack()
        self.inp = {}
        self.outp = {}
        self.sb = {}
        self.sems = {}
        self.V = None
        self.S = None
        self.P = None
        self.G = None

    def din(self, name, shape, dtype=F32):
        ap = self.nc.dram_tensor(name, list(shape), dtype, kind="ExternalInput").ap()
        self.inp[name] = ap
        return ap

    def dout(self, name, shape, dtype=F32):
        ap = self.nc.dram_tensor(name, list(shape), dtype, kind="ExternalOutput").ap()
        self.outp[name] = ap
        return ap

    def dint(self, name, shape, dtype=F32, addr_space="Local"):
        return self.nc.dram_tensor(name, list(shape), dtype, kind="Internal",
                                   addr_space=addr_space).ap()

    def sbuf(self, name, shape, dtype=F32):
        t = self.ctx.enter_context(self.nc.sbuf_tensor(name, list(shape), dtype))
        self.sb[name] = t
        return t

    def mkslab(self, name, nbytes):
        slab = self.ctx.enter_context(
            self.nc.sbuf_tensor(name, [128, nbytes], mybir.dt.uint8))
        return self.nc.lookup_mloc(slab).addr

    def sbuf_at(self, name, shape, dtype, base, off):
        t = self.nc.alloc_sbuf_tensor_at(name, list(shape), dtype,
                                         offset=base + off)
        self.sb[name] = t
        return t

    def psum(self, name, shape, dtype=F32):
        return self.ctx.enter_context(self.nc.psum_tensor(name, list(shape), dtype))

    def sem(self, name):
        s = self.ctx.enter_context(self.nc.semaphore(name))
        self.sems[name] = s
        return s

    def open(self):
        self.wcache = {}
        self.V = Chain(self.nc, self.sem("cv"), wcache=self.wcache)
        self.S = Chain(self.nc, self.sem("cs"), wcache=self.wcache)
        self.P = Chain(self.nc, self.sem("cp"), auto=False,
                       wcache=self.wcache)
        self.G = Chain(self.nc, self.sem("cg"), wcache=self.wcache)
        self.dsem = [self.sem(f"d{i}") for i in range(16)]
        self.dcnt = [0] * 16

    def dma(self, qeng, out, in_, slot, waits=(), **kw):
        for t in waits:
            if t is not None:
                key = (qeng.engine, id(t[0]))
                if self.wcache.get(key, -1) >= t[1]:
                    continue
                self.wcache[key] = t[1]
                qeng.wait_ge(t[0], t[1])
        qeng.dma_start(out=out, in_=in_, **kw).then_inc(self.dsem[slot], 16)
        self.dcnt[slot] += 16
        return (self.dsem[slot], self.dcnt[slot])

    def close(self):
        self.ctx.close()


def emit_load_x(kb, blk, resid, xT, nt=NT0):
    toks = []

    @blk.sync
    def _(q):
        for j in range(C8):
            toks.append(kb.dma(q, resid[:, j * nt:(j + 1) * nt],
                               xT[j * 128:(j + 1) * 128, :], slot=8))
    return toks[-1]


def emit_ln(kb, blk, src, dst, nt, ranges, scr,
            gamma=None, beta=None, shift=False, extra_wait=None,
            dst_bf16=None, nt_dst=None, dst_off=0):
    V, S, P = kb.V, kb.S, kb.P
    pA, pB = scr["pA"], scr["pB"]
    bc = [scr["bc0"], scr["bc1"]]
    sq, rows, aff = scr["sq"], scr["rows"], scr["aff"]
    ndst = nt if nt_dst is None else nt_dst
    pe_stats_tok = [extra_wait]
    vec_rows_tok = [None]
    aff_tok = [None, None]

    def col(j, t0, t1, buf=src, n=None, noshift=False, off=0):
        n = nt if n is None else n
        sh = 1 if (shift and j >= C8 // 2 and not noshift) else 0
        return buf[:, j * n + t0 - off - sh: j * n + t1 - off - sh]

    for (t0, t1) in ranges:
        W = t1 - t0

        stage_tok = [None] * 4
        for stage in range(4):
            @blk.scalar
            def _(s, t0=t0, t1=t1, stage=stage):
                for j in range(2 * stage, 2 * stage + 2):
                    S.do(s, lambda e, j=j: e.activation(
                        sq[:, (j % 2) * TB:(j % 2) * TB + W], col(j, t0, t1),
                        AF.Square),
                        waits=[pe_stats_tok[0] if stage == 0 and j == 0
                               else None,
                               stage_tok[stage - 1] if stage > 0 and
                               j == 2 * stage else None])
            sq_done = S.prev

            @blk.tensor
            def _(p, t0=t0, t1=t1, stage=stage):
                if stage == 0:
                    for j in range(C8):
                        P.do(p, lambda e, j=j: e.matmul(
                            pA[0:1, 0:W], kb.oc[:, 0:1], col(j, t0, t1),
                            start=(j == 0), stop=(j == C8 - 1)),
                            waits=[vec_rows_tok[0], pe_stats_tok[0]]
                            if j == 0 else [])
                for j in range(2 * stage, 2 * stage + 2):
                    P.do(p, lambda e, j=j: e.matmul(
                        pB[0:1, 0:W], kb.oc[:, 0:1],
                        sq[:, (j % 2) * TB:(j % 2) * TB + W],
                        start=(j == 0), stop=(j == C8 - 1)),
                        waits=[sq_done if j == 2 * stage else None],
                        mark=(j % 2 == 1))
                stage_tok[stage] = P.prev
        stats_done = P.prev

        @blk.vector
        def _(v):
            V.do(v, lambda e: e.tensor_scalar(
                rows[0:1, 0*TB:0*TB+W], pA[0:1, 0:W], 1.0 / D, None, op0=ALU.mult),
                waits=[stats_done])
            V.do(v, lambda e: e.tensor_mul(rows[0:1, 4*TB:4*TB+W], rows[0:1, 0*TB:0*TB+W],
                                           rows[0:1, 0*TB:0*TB+W]))
            V.do(v, lambda e: e.scalar_tensor_tensor(
                rows[0:1, 4*TB:4*TB+W], pB[0:1, 0:W], 1.0 / D, rows[0:1, 4*TB:4*TB+W],
                op0=ALU.mult, op1=ALU.subtract))

        @blk.scalar
        def _(s):
            S.do(s, lambda e: e.activation(rows[0:1, 3*TB:3*TB+W], rows[0:1, 4*TB:4*TB+W],
                                           AF.Sqrt, bias=kb.c_eps[0:1, 0:1]),
                 waits=[V.prev])

        @blk.vector
        def _(v):
            V.do(v, lambda e: e.reciprocal(rows[0:1, 4*TB:4*TB+W], rows[0:1, 3*TB:3*TB+W]),
                 waits=[S.prev])
            tok = V.do(v, lambda e: e.tensor_mul(rows[0:1, 5*TB:5*TB+W], rows[0:1, 0*TB:0*TB+W],
                                                 rows[0:1, 4*TB:4*TB+W]))
            vec_rows_tok[0] = tok
            pe_stats_tok[0] = tok

        @blk.tensor
        def _(p):
            P.do(p, lambda e: e.matmul(bc[0][:, 0:W], kb.orow[:, :], rows[0:1, 4*TB:4*TB+W],
                                       start=True, stop=True),
                 waits=[V.prev, aff_tok[0]])
            P.do(p, lambda e: e.matmul(bc[1][:, 0:W], kb.orow[:, :], rows[0:1, 5*TB:5*TB+W],
                                       start=True, stop=True),
                 waits=[aff_tok[1]], mark=True)
        bc_done = P.prev

        @blk.vector
        def _(v, t0=t0, t1=t1):
            for j in range(C8):
                V.do(v, lambda e, j=j: e.tensor_mul(
                    aff[:, 0:W], col(j, t0, t1), bc[0][:, 0:W]), waits=[bc_done])
                if gamma is None:
                    tok = V.do(v, lambda e, j=j: e.tensor_sub(
                        col(j, t0, t1, buf=dst, n=ndst, noshift=True,
                            off=dst_off),
                        aff[:, 0:W], bc[1][:, 0:W]))
                else:
                    V.do(v, lambda e: e.tensor_sub(aff[:, 0:W], aff[:, 0:W], bc[1][:, 0:W]))
                    tok = V.do(v, lambda e, j=j: e.tensor_scalar(
                        col(j, t0, t1, buf=dst, n=ndst, noshift=True,
                            off=dst_off), aff[:, 0:W],
                        gamma[:, j:j + 1], beta[:, j:j + 1],
                        op0=ALU.mult, op1=ALU.add))
                if dst_bf16 is not None:
                    tok = V.do(v, lambda e, j=j: e.tensor_copy(
                        col(j, t0, t1, buf=dst_bf16, noshift=True),
                        col(j, t0, t1, buf=dst, n=ndst, noshift=True,
                            off=dst_off)))
                if j == 0:
                    aff_tok[0] = tok
            aff_tok[1] = tok
    return V.prev


def alloc_ln_scratch(kb):
    return {
        "pA": kb.PS[4],
        "pB": kb.PS[5],
        "bc0": kb.PS[6],
        "bc1": kb.PS[7],
        "sq": kb.sbuf_at("sq", [128, 2 * TB], F32, kb.ovl_base, 31584),
        "rows": kb.sbuf_at("rows", [1, 6 * TB], F32, kb.ovl_base,
                           31584 + 4096),
        "aff": kb.sbuf("aff", [128, TB], F32),
    }


def emit_consts(kb, blk):
    if not hasattr(kb, "oc"):
        alloc_consts(kb)

    @blk.gpsimd
    def _(g):
        kb.G.do(g, lambda e: e.memset(kb.oc[:], 1.0))
        kb.G.do(g, lambda e: e.memset(kb.orow[:], 1.0))
        kb.G.do(g, lambda e: e.memset(kb.c_eps[:], float(EPS)))
        kb.G.do(g, lambda e: e.memset(kb.c_zero[:], 0.0))
    return kb.G.prev


def alloc_consts(kb):
    kb.oc = kb.sbuf("ones_c", [128, 1], F32)
    kb.orow = kb.sbuf("ones_r", [1, 128], F32)
    kb.c_eps = kb.sbuf("c_eps", [128, 1], F32)
    kb.c_zero = kb.sbuf("c_zero", [128, 1], F32)


def tblocks(t0, t1, step=TB):
    return [(a, min(a + step, t1)) for a in range(t0, t1, step)]


QMAX = 1152
KMAX = 1408
KTILES = 11

HALVES_IN = [(2, 10, 0), (11, 18, 9)]
HALVES_A0 = [(3, 10, 2), (11, 18, 10)]


def alloc_attn_state(kb, base):
    st = {}
    o = [0]

    def at(name, shape, dtype):
        nb = int(np.prod(shape[1:])) if len(shape) > 2 else shape[1]
        nb = nb * (2 if dtype == BF16 else 4)
        t = kb.sbuf_at(name, shape, dtype, base, o[0])
        o[0] += (nb + 31) // 32 * 32
        return t

    st["wqkvW"] = at("wqkvW", [128, 3 * 1024], BF16)
    st["woW"] = at("woW", [128, 4 * 1024], BF16)
    st["qst"] = [at(f"qst{h}", [66, QMAX], BF16) for h in range(2)]
    st["kst"] = [at(f"kst{h}", [66, KMAX], BF16) for h in range(2)]
    st["vst"] = at("vst", [128, KTILES * 130], BF16)
    st["Pt"] = at("Pt", [128, 4 * 384], BF16)
    st["nmst"] = at("nmst", [128, 128], BF16)
    st["nmtr"] = at("nmtr", [128, 128], BF16)
    st["osb"] = at("osb", [128, 128], F32)
    st["rc"] = at("rc", [128, 2], F32)
    st["accst"] = [kb.sbuf_at(f"accst{i}", [128, TB], F32, base,
                              16384 + 2048 * i) for i in range(2)]
    st["otst"] = at("otst", [128, 8 * QMAX], BF16)
    assert o[0] <= OVLB, o[0]
    return st


OVLB = 50176


def emit_stage_consts(kb, blk, st, padrow_sb):
    V, S = kb.V, kb.S

    @blk.gpsimd
    def _(g):
        kb.G.do(g, lambda e: e.memset(st["vst"][:], 1.0))
        kb.G.do(g, lambda e: e.memset(st["nmst"][:], 0.0))
        for h in range(2):
            kb.G.do(g, lambda e, h=h: e.memset(st["qst"][h][64:65, :], 1.0))

    @blk.sync
    def _(q):
        for h in range(2):
            kb.dma(q, st["kst"][h][65:66, :], kb.inp["onesbf"], slot=0)
    return kb.G.prev


def qkv_production(kb, blk, st, cfg, kind, trange, stage_cols0):
    V, S, P = kb.V, kb.S, kb.P
    xs = cfg["xs"]
    dst = cfg["dst"]
    PS = kb.PS
    ptk = st.setdefault("ptk", {"ev": [None, None], "rot": [None],
                                "p1": [None, None], "fb": [None],
                                "fr": [None], "sq": [None],
                                "rotrd": [None], "fbrd": [None]})
    if "rotrd" not in ptk:
        ptk.update({"rotrd": [None], "fbrd": [None]})
    evtok = ptk["ev"]
    rottok = ptk["rot"]
    p1tok = ptk["p1"]
    fbtok = ptk["fb"]
    frtok = ptk["fr"]
    sqtok = ptk["sq"]
    first = True
    for (t0, t1) in tblocks(*trange):
        Wd = t1 - t0
        par = (t0 // TB) % 2

        @blk.tensor
        def _(p, t0=t0, t1=t1, Wd=Wd, par=par, first=first):
            for c in range(C8):
                P.do(p, lambda e, c=c: e.matmul(
                    PS[par][:, 0:Wd],
                    st["wqkvW"][:, cfg["woff"] + c * 128:cfg["woff"] + c * 128 + 128],
                    xs[:, c * NT0 + t0:c * NT0 + t1],
                    start=(c == 0), stop=(c == C8 - 1)),
                    waits=([cfg["wtok"], cfg["war"], cfg.get("war_v"),
                            cfg["xstok"], evtok[par]] if c == 0 else []),
                    mark=(c == C8 - 1))
        mmtok = P.prev

        @blk.scalar
        def _(s, Wd=Wd, par=par):
            S.do(s, lambda e: e.copy(st["rawst"][par][:, 0:Wd], PS[par][:, 0:Wd]),
                 waits=[mmtok, rottok[0], p1tok[par]])
            evtok[par] = S.prev
            if cfg["l2n"]:
                S.do(s, lambda e: e.activation(kb.ln_scr["aff"][:, 0:Wd],
                                               st["rawst"][par][:, 0:Wd],
                                               AF.Square))
                sqtok[0] = S.prev

        @blk.tensor
        def _(p, Wd=Wd, par=par):
            P.do(p, lambda e: e.matmul(PS[2][:, 0:Wd], cfg["prot"],
                                       st["rawst"][par][:, 0:Wd],
                                       start=True, stop=True),
                 waits=[evtok[par], ptk["rotrd"][0], cfg.get("prottok")],
                 mark=True)
            rmtok = P.prev
            rottok[0] = rmtok
            if cfg["l2n"]:
                P.do(p, lambda e: e.matmul(PS[3][0:2, 0:Wd], kb.bd2[:, :],
                                           kb.ln_scr["aff"][:, 0:Wd],
                                           start=True, stop=True),
                     waits=[sqtok[0], frtok[0]], mark=True)
                segtok = P.prev
            else:
                segtok = None
            cfg["_rm"] = rmtok
            cfg["_seg"] = segtok

        if cfg["l2n"]:
            @blk.scalar
            def _(s, Wd=Wd):
                S.do(s, lambda e: e.activation(st["fr"][0:2, 0:Wd],
                                               PS[3][0:2, 0:Wd], AF.Sqrt,
                                               bias=kb.c_zero[0:2, 0:1]),
                     waits=[cfg["_seg"]])

            @blk.vector
            def _(v, Wd=Wd):
                V.do(v, lambda e: e.tensor_scalar(
                    st["fr"][0:2, 0:Wd], st["fr"][0:2, 0:Wd],
                    1e-12, None, op0=ALU.max), waits=[S.prev])
                V.do(v, lambda e: e.reciprocal(st["fr"][0:2, 0:Wd],
                                               st["fr"][0:2, 0:Wd]))
                frtok[0] = V.prev

            @blk.tensor
            def _(p, Wd=Wd):
                P.do(p, lambda e: e.matmul(PS[6][:, 0:Wd], kb.bd2t[0:2, :],
                                           st["fr"][0:2, 0:Wd],
                                           start=True, stop=True),
                     waits=[frtok[0], ptk["fbrd"][0]], mark=True)
                fbtok[0] = P.prev

        @blk.vector
        def _(v, t0=t0, t1=t1, Wd=Wd, par=par):
            tc, ts = cfg["tcb"], cfg["tsb"]
            qs = cfg["qs_ap"]
            if qs is not None:
                V.do(v, lambda e: e.scalar_tensor_tensor(
                    st["t1st"][:, 0:Wd], st["rawst"][par][:, 0:Wd],
                    qs[:, :], tc[:, t0:t1], op0=ALU.mult, op1=ALU.mult),
                    waits=[evtok[par], cfg["war"]])
            else:
                V.do(v, lambda e: e.scalar_tensor_tensor(
                    st["t1st"][:, 0:Wd], st["rawst"][par][:, 0:Wd],
                    cfg["scl"], tc[:, t0:t1], op0=ALU.mult, op1=ALU.mult),
                    waits=[evtok[par], cfg["war"]])
            V.do(v, lambda e: e.scalar_tensor_tensor(
                st["t3st"][:, 0:Wd], PS[2][:, 0:Wd],
                cfg["scl"], ts[:, t0:t1], op0=ALU.mult, op1=ALU.mult),
                waits=[cfg["_rm"]])
            ptk["rotrd"][0] = V.prev
            V.do(v, lambda e: e.tensor_add(st["t1st"][:, 0:Wd],
                                           st["t1st"][:, 0:Wd],
                                           st["t3st"][:, 0:Wd]))
            p1tok[par] = V.prev
            lc = t0 - stage_cols0
            if cfg["l2n"]:
                for hh in range(2):
                    V.do(v, lambda e, hh=hh, lc=lc: e.tensor_mul(
                        dst[hh][0:64, lc:lc + Wd],
                        st["t1st"][hh * 64:hh * 64 + 64, 0:Wd],
                        PS[6][hh * 64:hh * 64 + 64, 0:Wd]),
                        waits=[fbtok[0]] if hh == 0 else [])
                ptk["fbrd"][0] = V.prev
            else:
                for hh in range(2):
                    V.do(v, lambda e, hh=hh, lc=lc: e.tensor_copy(
                        dst[hh][0:64, lc:lc + Wd],
                        st["t1st"][hh * 64:hh * 64 + 64, 0:Wd]))
            p1tok[par] = V.prev
        first = False
    return P.prev, V.prev


def v_production(kb, blk, st, cfg, kts):
    V, S, P = kb.V, kb.S, kb.P
    PS = kb.PS
    xs = cfg["xs"]
    ptk = st.setdefault("ptk", {})
    evtok = ptk.setdefault("vev", [None, None])
    for i, kt in enumerate(kts):
        par = i % 2

        @blk.tensor
        def _(p, kt=kt, par=par):
            for c in range(C8):
                P.do(p, lambda e, c=c: e.matmul(
                    PS[par][:, 0:128],
                    xs[:, c * NT0 + kt * 128:c * NT0 + kt * 128 + 128],
                    st["wqkvW"][:, cfg["woff"] + c * 128:cfg["woff"] + c * 128 + 128],
                    start=(c == 0), stop=(c == C8 - 1)),
                    waits=[cfg["wtok"], cfg["war"], evtok[par]] if c == 0 else [],
                    mark=(c == C8 - 1))
        mmtok = P.prev

        @blk.scalar
        def _(s, i=i, par=par):
            for hh in range(2):
                S.do(s, lambda e, hh=hh: e.copy(
                    st["vst"][:, i * 130 + hh * 65:i * 130 + hh * 65 + 64],
                    PS[par][:, hh * 64:hh * 64 + 64]),
                    waits=[mmtok, cfg["war"], cfg.get("vinit")]
                    if hh == 0 else [])
            evtok[par] = S.prev
    return S.prev


def attn_scores_pv(kb, blk, st, cfg):
    V, S, P = kb.V, kb.S, kb.P
    PS = kb.PS
    qt0, qtn, kt0 = cfg["qt0"], cfg["qtn"], cfg["kt0"]
    lb = cfg["lookback"]
    nq = qtn - qt0 + 1
    ktn = qtn
    nmcols = []
    SB3 = (4, 5, 7)
    s2tok = [None] * 3
    for hh in range(2):
        for qt in range(qt0, qtn + 1):
            par = (qt - qt0) % 3
            ka = max(qt - lb, kt0)
            nk = (qt - ka + 1) * 128
            qlc = (qt - qt0) * 128
            klc = (ka - kt0) * 128

            @blk.tensor
            def _(p, hh=hh, qt=qt, par=par, nk=nk, qlc=qlc, klc=klc):
                P.do(p, lambda e: e.matmul(
                    PS[SB3[par]][:, 0:nk],
                    st["qst"][hh][0:65, qlc:qlc + 128],
                    st["kst"][hh][0:65, klc:klc + nk],
                    start=True, stop=True),
                    waits=[cfg["qtok"], cfg["ktok"], cfg.get("qones_tok"),
                           s2tok[par]], mark=True)
            mmtok = P.prev

            @blk.vector
            def _(v, hh=hh, qt=qt, par=par, nk=nk):
                col = hh * nq + (qt - qt0)
                V.do(v, lambda e: e.reduce_max(
                    st["nmst"][:, col:col + 1], PS[SB3[par]][:, 0:nk],
                    axis=AX.X, negate=True),
                    waits=[mmtok, cfg.get("nm_war")
                           if (hh == 0 and qt == qt0) else None])
                s2tok[par] = V.prev
    @blk.sync
    def _(q):
        t1 = kb.dma(q, st["nmtr"][:], st["nmst"][:], slot=7,
                    waits=[kb.V.prev], transpose=True)
        for hh in range(2):
            kb.dma(q, st["qst"][hh][65:66, 0:nq * 128],
                   st["nmtr"][hh * nq:hh * nq + nq, 0:128], slot=7,
                   waits=[t1] if hh == 0 else [])
    bias_tok = (kb.dsem[7], kb.dcnt[7])
    sttok = [None] * 3
    pvtok = [None, None]
    exptok = {}
    masktok = {}
    ottok = [None]
    osbtok = [None, None]
    trtok = [None, None]
    for hh in range(2):
        for kt in range(kt0, ktn + 1):
            par = kt % 3
            qa = max(kt, qt0)
            qb = min(kt + lb, qtn)
            nqq = (qb - qa + 1) * 128
            klc = (kt - kt0) * 128
            qlc = (qa - qt0) * 128

            @blk.tensor
            def _(p, hh=hh, kt=kt, par=par, nqq=nqq, klc=klc, qlc=qlc):
                P.do(p, lambda e: e.matmul(
                    PS[SB3[par]][:, 0:nqq],
                    st["kst"][hh][0:66, klc:klc + 128],
                    st["qst"][hh][0:66, qlc:qlc + nqq],
                    start=True, stop=True),
                    waits=[bias_tok, cfg.get("aug_tok"), sttok[par]],
                    mark=True)
            mmtok = P.prev

            @blk.scalar
            def _(s, hh=hh, kt=kt, par=par, nqq=nqq):
                slot = kt % 4
                S.do(s, lambda e: e.activation(
                    st["Pt"][:, slot * 384:slot * 384 + nqq],
                    PS[SB3[par]][:, 0:nqq], AF.Exp,
                    bias=kb.c_zero[:, 0:1]),
                    waits=[mmtok, pvtok[0], pvtok[1]])
                exptok[(hh, kt)] = S.prev
                sttok[par] = S.prev

            @blk.vector
            def _(v, hh=hh, kt=kt, qa=qa, qb=qb):
                slot = kt % 4
                tok = exptok[(hh, kt)]
                for qt in range(qa, qb + 1):
                    o = qt - kt
                    mo = cfg["mask_of"](o)
                    if mo is not None:
                        V.do(v, lambda e, qt=qt, mo=mo: e.tensor_mul(
                            st["Pt"][:, slot * 384 + (qt - qa) * 128:
                                     slot * 384 + (qt - qa) * 128 + 128],
                            st["Pt"][:, slot * 384 + (qt - qa) * 128:
                                     slot * 384 + (qt - qa) * 128 + 128],
                            kb.msk[:, mo * 128:mo * 128 + 128]),
                            waits=[tok])
                masktok[(hh, kt)] = V.prev

            qt = kt
            if qt >= qt0:
                @blk.tensor
                def _(p, hh=hh, qt=qt):
                    ks = range(max(qt - lb, kt0), qt + 1)
                    nk = len(list(ks))
                    for r, kk in enumerate(ks):
                        slot = kk % 4
                        qa2 = max(kk, qt0)
                        off = slot * 384 + (qt - qa2) * 128
                        P.do(p, lambda e, kk=kk, off=off, r=r, nk=nk: e.matmul(
                            kb.PS[2 + hh][:, 0:65],
                            st["Pt"][:, off:off + 128],
                            st["vst"][:, (kk - kt0) * 130 + hh * 65:
                                      (kk - kt0) * 130 + hh * 65 + 65],
                            start=(r == 0), stop=(r == nk - 1)),
                            waits=([masktok[(hh, kk)] for kk in ks] +
                                   [cfg["vtok"], osbtok[hh]]) if r == 0 else [],
                            mark=(r == nk - 1))
                    pvtok[hh] = kb.P.prev

                @blk.vector
                def _(v, hh=hh, qt=qt):
                    V.do(v, lambda e: e.reciprocal(
                        st["rc"][:, hh:hh + 1],
                        kb.PS[2 + hh][:, 64:65]), waits=[pvtok[hh]])
                    V.do(v, lambda e: e.tensor_scalar(
                        st["osb"][:, hh * 64:hh * 64 + 64],
                        kb.PS[2 + hh][:, 0:64],
                        st["rc"][:, hh:hh + 1], None,
                        op0=ALU.mult), waits=[trtok[hh]])
                    osbtok[hh] = V.prev

                @blk.tensor
                def _(p, hh=hh, qt=qt):
                    P.do(p, lambda e: e.transpose(
                        kb.PS[6][0:64, 0:128],
                        st["osb"][:, hh * 64:hh * 64 + 64], kb.idf[:]),
                        waits=[osbtok[hh], ottok[0]], mark=True)
                    trtok[hh] = kb.P.prev
                    cfg["_ott"] = kb.P.prev

                @blk.scalar
                def _(s, hh=hh, qt=qt):
                    qlc = (qt - qt0) * 128
                    S.do(s, lambda e: e.copy(
                        st["otst"][hh * 64:hh * 64 + 64,
                                   cfg["hb"] * QMAX + qlc:
                                   cfg["hb"] * QMAX + qlc + 128],
                        kb.PS[6][0:64, 0:128]), waits=[cfg["_ott"]])
                    ottok[0] = S.prev
    return P.prev, V.prev, S.prev


def attn_wo(kb, blk, st, cfg):
    V, S, P = kb.V, kb.S, kb.P
    PS = kb.PS
    qt0, qtn = cfg["qt0"], cfg["qtn"]
    q0, q1 = qt0 * 128, (qtn + 1) * 128
    wo = cfg["wo_ap"]
    wov = wo.rearrange("(g p) m -> g p m", p=128)
    addtok = [None]
    ldwait = [cfg["wo_war"]]
    for half in range(2):
        ldtoks = []

        @blk.gpsimd
        def _(g, half=half):
            for gi in range(4):
                hb = half * 4 + gi
                ldtoks.append(kb.dma(g, st["woW"][:, gi * 1024:(gi + 1) * 1024],
                                     wov[hb], slot=3,
                                     waits=[ldwait[0]] if gi == 0 else []))
        wotok = ldtoks[-1]
        for dm in range(C8):
            for (t0, t1) in tblocks(q0, q1):
                Wd = t1 - t0
                par = (dm + t0 // TB) % 2

                @blk.tensor
                def _(p, half=half, dm=dm, t0=t0, t1=t1, Wd=Wd, par=par):
                    for gi in range(4):
                        hb = half * 4 + gi
                        P.do(p, lambda e, gi=gi, hb=hb: e.matmul(
                            PS[par][:, 0:Wd],
                            st["woW"][:, gi * 1024 + dm * 128:
                                      gi * 1024 + dm * 128 + 128],
                            st["otst"][:, hb * QMAX + (t0 - q0):
                                       hb * QMAX + (t1 - q0)],
                            start=(gi == 0), stop=(gi == 3)),
                            waits=[wotok, cfg["ottok"], addtok[0]]
                            if gi == 0 else [],
                            mark=(gi == 3))
                mmtok = P.prev

                ldwait[0] = P.prev
                if cfg["dest"] == "resid":
                    @blk.vector
                    def _(v, dm=dm, t0=t0, t1=t1, Wd=Wd, par=par):
                        r = cfg["resid"]
                        if cfg.get("gate_first") and t0 == q0:
                            V.do(v, lambda e: e.scalar_tensor_tensor(
                                r[:, dm * NT0 + t0:dm * NT0 + t0 + 128],
                                PS[par][:, 0:128], kb.cmask[:, 0:1],
                                r[:, dm * NT0 + t0:dm * NT0 + t0 + 128],
                                op0=ALU.mult, op1=ALU.add), waits=[mmtok])
                            V.do(v, lambda e: e.tensor_add(
                                r[:, dm * NT0 + t0 + 128:dm * NT0 + t1],
                                r[:, dm * NT0 + t0 + 128:dm * NT0 + t1],
                                PS[par][:, 128:Wd]))
                        else:
                            V.do(v, lambda e: e.tensor_add(
                                r[:, dm * NT0 + t0:dm * NT0 + t1],
                                r[:, dm * NT0 + t0:dm * NT0 + t1],
                                PS[par][:, 0:Wd]), waits=[mmtok])
                        addtok[0] = V.prev
                else:
                    @blk.vector
                    def _(v, half=half, dm=dm, t0=t0, t1=t1, Wd=Wd, par=par):
                        r = cfg["resid"]
                        if half == 0:
                            V.do(v, lambda e: e.tensor_add(
                                st["accst"][par][:, 0:Wd],
                                r[:, dm * NT0 + t0:dm * NT0 + t1],
                                PS[par][:, 0:Wd]),
                                waits=[mmtok, cfg["acc_war"][par]])
                        else:
                            V.do(v, lambda e: e.tensor_copy(
                                st["accst"][par][:, 0:Wd],
                                PS[par][:, 0:Wd]),
                                waits=[mmtok, cfg["acc_war"][par]])
                        addtok[0] = V.prev
                        cfg["_acc_v"] = V.prev

                    @blk.sync
                    def _(q, half=half, dm=dm, t0=t0, t1=t1, Wd=Wd, par=par):
                        tok = kb.dma(q, cfg["acc_out"][half * D + dm * 128:
                                                       half * D + dm * 128 + 128,
                                                       t0 - HALO:t1 - HALO],
                                     st["accst"][par][:, 0:Wd], slot=1 + 8 * par,
                                     waits=[cfg["_acc_v"]])
                        cfg["acc_war"][par] = tok
    return P.prev, V.prev


def emit_cast_shift(kb, blk, resid, xs, waits=()):
    V = kb.V

    @blk.vector
    def _(v):
        first = [True]
        for j in range(C8):
            for (t0, t1) in tblocks(0, NT0):
                w = []
                if first[0]:
                    w = list(waits)
                    first[0] = False
                if j < C8 // 2:
                    V.do(v, lambda e, j=j, t0=t0, t1=t1: e.tensor_copy(
                        xs[:, j * NT0 + t0:j * NT0 + t1],
                        resid[:, j * NT0 + t0:j * NT0 + t1]), waits=w)
                else:
                    a0 = max(t0, 1)
                    V.do(v, lambda e, j=j, a0=a0, t1=t1: e.tensor_copy(
                        xs[:, j * NT0 + a0:j * NT0 + t1],
                        resid[:, j * NT0 + a0 - 1:j * NT0 + t1 - 1]), waits=w)
                    if t0 == 0:
                        V.do(v, lambda e, j=j: e.tensor_copy(
                            xs[:, j * NT0:j * NT0 + 1],
                            resid[:, j * NT0:j * NT0 + 1]))
    return V.prev


def emit_inner_attn(kb, blk, st, l, waps, tok):
    V, S, P = kb.V, kb.S, kb.P
    dws = tok.pop("acc_dma", None) or [None, None]

    @blk.vector
    def _(v):
        kb.V.do(v, lambda e: e.nop(), waits=dws)

    @blk.scalar
    def _(s):
        kb.S.do(s, lambda e: e.nop(), waits=dws)
    xstok = emit_cast_shift(kb, blk, st["resid"], st["xs"],
                            waits=[tok.get("resid_v")])

    @blk.vector
    def _(v):
        V.do(v, lambda e: e.tensor_scalar(
            kb.protq[:], kb.protF[:], kb.qsW[:, l:l + 1], None, op0=ALU.mult))
        V.do(v, lambda e: e.tensor_scalar(
            kb.protk[:], kb.protF[:], kb.ksW[:, l:l + 1], None, op0=ALU.mult))
    prottok = V.prev

    @blk.gpsimd
    def _(g):
        kb.G.do(g, lambda e: e.memset(st["vst"][:], 1.0),
                waits=[tok.get("pe_free"), tok.get("resid_v"),
                       dws[0], dws[1]])
        for h in range(2):
            kb.G.do(g, lambda e, h=h: e.memset(st["qst"][h][64:65, :], 1.0))
    vinit_tok = kb.G.prev

    @blk.sync
    def _(q):
        for h in range(2):
            kb.dma(q, st["kst"][h][65:66, :], kb.inp["onesbf"], slot=10,
                   waits=[tok.get("pe_free"), dws[0], dws[1]]
                   if h == 0 else [])
    aug_tok = (kb.dsem[10], kb.dcnt[10])

    wv = waps["Wqkv"][l].rearrange("(c p) m -> p c m", p=128)
    stage_war = [tok.get("pe_free")]
    war_v = [tok.get("resid_v")]
    wo_war = [None]
    wtokL = [None]
    wv2 = None

    def load_wqkv(hb, wait):
        toks = []

        @blk.gpsimd
        def _(g):
            for i in range(3):
                toks.append(kb.dma(
                    g, st["wqkvW"][:, i * 1024:(i + 1) * 1024].rearrange(
                        "p (c m) -> p c m", c=C8),
                    wv[:, :, i * D + hb * 128:i * D + (hb + 1) * 128],
                    slot=2, waits=[wait] if i == 0 else []))
        return toks[-1]

    for half in range(2):
        qt0, qtn, kt0 = HALVES_IN[half]
        qrange = (qt0 * 128, (qtn + 1) * 128)
        krange = (kt0 * 128, (qtn + 1) * 128)
        kts = list(range(kt0, qtn + 1))

        @blk.scalar
        def _(s, kt0=kt0, qtn=qtn):
            nk = (qtn - kt0 + 1) * 128
            for hh in range(2):
                S.do(s, lambda e, hh=hh, nk=nk: e.copy(
                    st["kst"][hh][64:65, 0:nk],
                    kb.padrow_sb[0:1, kt0 * 128:kt0 * 128 + nk]),
                    waits=[stage_war[0]] if hh == 0 else [])

        if half == 0:
            wtokL[0] = load_wqkv(0, stage_war[0])
        stok2 = None
        for hb in range(8):
            wtok = wtokL[0]
            cfgq = dict(xs=st["xs"], woff=0, wtok=wtok, war=stage_war[0],
                        prottok=prottok,
                        war_v=war_v[0],
                        xstok=xstok, dst=st["qst"], l2n=True,
                        prot=kb.protq[:], qs_ap=kb.qsW[:, l:l + 1], scl=1.0,
                        tcb=kb.tcB, tsb=kb.tsB)
            qpe, qv = qkv_production(kb, blk, st, cfgq, "q", qrange, qt0 * 128)
            cfgk = dict(xs=st["xs"], woff=1024, wtok=wtok, war=stage_war[0],
                        prottok=prottok,
                        war_v=war_v[0],
                        xstok=xstok, dst=st["kst"], l2n=True,
                        prot=kb.protk[:], qs_ap=kb.ksW[:, l:l + 1], scl=1.0,
                        tcb=kb.tciB, tsb=kb.tsiB)
            kpe, kv = qkv_production(kb, blk, st, cfgk, "k", krange, kt0 * 128)
            cfgv = dict(xs=st["xs"], woff=2048, wtok=wtok, war=stage_war[0],
                        vinit=vinit_tok)
            vtok = v_production(kb, blk, st, cfgv, kts)
            if hb < 7:
                wtokL[0] = load_wqkv(hb + 1, kb.P.prev)
            elif half == 0:
                wtokL[0] = load_wqkv(0, kb.P.prev)
            scfg = dict(qt0=qt0, qtn=qtn, kt0=kt0, lookback=2,
                        mask_of=lambda o: {0: 0, 2: 1}.get(o),
                        hb=hb, qtok=qv, ktok=kv, vtok=vtok,
                        aug_tok=aug_tok, qones_tok=vinit_tok,
                        nm_war=tok.get("nm_war"))
            ptok, vtok2, stok2 = attn_scores_pv(kb, blk, st, scfg)
            stage_war[0] = ptok
            war_v[0] = vtok2
            tok["nm_war"] = (kb.dsem[7], kb.dcnt[7])
        wcfg = dict(qt0=qt0, qtn=qtn, wo_ap=waps["Wo"][l], dest="resid",
                    resid=st["resid"], gate_first=(half == 0),
                    wo_war=wo_war[0], ottok=stok2, pe_tok=stage_war[0])
        wpe, wv2 = attn_wo(kb, blk, st, wcfg)
        wo_war[0] = wpe
        stage_war[0] = wpe
        war_v[0] = wv2
    tok["resid_v"] = wv2
    tok["pe_free"] = stage_war[0]
    return tok


def emit_inner_ffn(kb, blk, st, l, waps, tok):
    V, S, P = kb.V, kb.S, kb.P
    PS = kb.PS
    own = tblocks(HALO, NT0)
    lng = kb.lngW[:, l * C8:(l + 1) * C8]
    lnb = kb.lnbW[:, l * C8:(l + 1) * C8]
    lntok = emit_ln(kb, blk, st["resid"], st["xs"], NT0, own, kb.ln_scr,
                    gamma=lng, beta=lnb, shift=True,
                    extra_wait=tok.get("resid_v"))
    w1v = waps["W1p"][l].rearrange("(c p) m -> p c m", p=128)
    w2v = waps["W2"][l]
    groups = [list(range(g, min(g + 4, NPAIR))) for g in range(0, NPAIR, 4)]

    def load_group(gi, wait):
        prs = groups[gi]
        c0, c1 = prs[0] * 256, (prs[-1] + 1) * 256
        r0, r1 = prs[0] * 128, min((prs[-1] + 1) * 128, DIM_INNER)
        toks = []

        @blk.gpsimd
        def _(g):
            toks.append(kb.dma(
                g, st["w1W"][:, 0:8 * (c1 - c0)].rearrange(
                    "p (c m) -> p c m", c=C8),
                w1v[:, :, c0:c1], slot=4, waits=[wait]))
            t2 = None
            for pi, pr in enumerate(prs):
                rows = min(128, DIM_INNER - pr * 128)
                t2 = kb.dma(g, st["w2W"][0:rows, pi * 1024:(pi + 1) * 1024],
                            w2v[pr * 128:pr * 128 + rows, :], slot=5)
            toks.append(t2)
        return toks

    gwar = [tok.get("pe_free")]
    pa_war = [None]
    pg_war = [None]
    addtok = [None, None]
    fusedwar = [None]
    ldt = load_group(0, gwar[0])
    for gi, prs in enumerate(groups):
        w1t, w2t = ldt
        npr = len(prs)
        ncol = npr * 256
        for (t0, t1) in own:
            Wd = t1 - t0
            fusedtok = {}
            for pi, pr in enumerate(prs):
                rows = min(128, DIM_INNER - pr * 128)

                @blk.tensor
                def _(p, pr=pr, pi=pi, t0=t0, t1=t1, Wd=Wd, rows=rows):
                    for c in range(C8):
                        P.do(p, lambda e, c=c: e.matmul(
                            PS[0][0:rows, 0:Wd],
                            st["w1W"][:, c * ncol + pi * 256:
                                      c * ncol + pi * 256 + rows],
                            st["xs"][:, c * NT0 + t0:c * NT0 + t1],
                            start=(c == 0), stop=(c == C8 - 1)),
                            waits=[w1t, lntok, pa_war[0]] if c == 0 else [],
                            mark=(c == C8 - 1))
                    for c in range(C8):
                        P.do(p, lambda e, c=c: e.matmul(
                            PS[1][0:rows, 0:Wd],
                            st["w1W"][:, c * ncol + pi * 256 + 128:
                                      c * ncol + pi * 256 + 128 + rows],
                            st["xs"][:, c * NT0 + t0:c * NT0 + t1],
                            start=(c == 0), stop=(c == C8 - 1)),
                            waits=[pg_war[0]] if c == 0 else [],
                            mark=(c == C8 - 1))
                mmtok = P.prev

                @blk.scalar
                def _(s, pr=pr, Wd=Wd, rows=rows):
                    S.do(s, lambda e: e.activation(
                        kb.ln_scr["aff"][0:rows, 0:Wd], PS[1][0:rows, 0:Wd],
                        AF.Gelu, bias=kb.b1W[0:rows, l * 44 + 2 * pr + 1:
                                             l * 44 + 2 * pr + 2]),
                        waits=[mmtok])
                    pg_war[0] = S.prev

                @blk.vector
                def _(v, pr=pr, pi=pi, t0=t0, Wd=Wd, rows=rows):
                    V.do(v, lambda e: e.scalar_tensor_tensor(
                        st["fusedst"][:, pi * TB:pi * TB + Wd][0:rows, :],
                        PS[0][0:rows, 0:Wd],
                        kb.b1W[0:rows, l * 44 + 2 * pr:l * 44 + 2 * pr + 1],
                        kb.ln_scr["aff"][0:rows, 0:Wd],
                        op0=ALU.add, op1=ALU.mult),
                        waits=[kb.S.prev, fusedwar[0]])
                    fusedtok[pi] = V.prev
                    pa_war[0] = V.prev
            for dm in range(C8):
                ppar = dm % 2

                @blk.tensor
                def _(p, dm=dm, t0=t0, t1=t1, Wd=Wd, ppar=ppar):
                    for pi, pr in enumerate(prs):
                        rows = min(128, DIM_INNER - pr * 128)
                        P.do(p, lambda e, pi=pi, rows=rows: e.matmul(
                            PS[2 + ppar][:, 0:Wd],
                            st["w2W"][0:rows, pi * 1024 + dm * 128:
                                      pi * 1024 + dm * 128 + 128],
                            st["fusedst"][:, pi * TB:pi * TB + Wd][0:rows, :],
                            start=(pi == 0), stop=(pi == npr - 1)),
                            waits=[w2t, fusedtok[npr - 1], addtok[ppar]]
                            if pi == 0 else [],
                            mark=(pi == npr - 1))
                    if dm == C8 - 1:
                        fusedwar[0] = kb.P.prev
                mmtok2 = P.prev

                @blk.vector
                def _(v, dm=dm, t0=t0, t1=t1, Wd=Wd, ppar=ppar, gi=gi):
                    r = st["resid"]
                    if gi == len(groups) - 1:
                        V.do(v, lambda e: e.scalar_tensor_tensor(
                            r[:, dm * NT0 + t0:dm * NT0 + t1],
                            PS[2 + ppar][:, 0:Wd],
                            kb.b2W[:, l * C8 + dm:l * C8 + dm + 1],
                            r[:, dm * NT0 + t0:dm * NT0 + t1],
                            op0=ALU.add, op1=ALU.add), waits=[mmtok2])
                    else:
                        V.do(v, lambda e: e.tensor_add(
                            r[:, dm * NT0 + t0:dm * NT0 + t1],
                            r[:, dm * NT0 + t0:dm * NT0 + t1],
                            PS[2 + ppar][:, 0:Wd]), waits=[mmtok2])
                    addtok[ppar] = V.prev
        if gi + 1 < len(groups):
            ldt = load_group(gi + 1, fusedwar[0])
    tok["resid_v"] = V.prev
    tok["pe_free"] = P.prev
    return tok


def emit_halo_exchange(kb, blk, st, idx, tok):
    V = kb.V
    cin, cout = kb.cinH, kb.coutH

    @blk.sync
    def _(q):
        for j in range(C8):
            kb.dma(q, cin[j * 128:(j + 1) * 128, :],
                   st["resid"][:, j * NT0 + OWN:j * NT0 + NT0], slot=6,
                   waits=[tok["resid_v"]] if j == 0 else [])
    dtok = (kb.dsem[6], kb.dcnt[6])

    @blk.gpsimd
    def _(g):
        g.wait_ge(*dtok)
        kb.G.do(g, lambda e: e.collective_compute(
            "AllGather", mybir.AluOpType.bypass,
            replica_groups=[[0, 1], [2, 3], [4, 5], [6, 7]],
            ins=[cin.opt()], outs=[cout.opt()]))
    ctok = kb.G.prev

    @blk.sync
    def _(q):
        for j in range(C8):
            kb.dma(q, st["resid"][:, j * NT0:j * NT0 + HALO],
                   cout[j * 128:(j + 1) * 128, :], slot=6,
                   waits=[ctok] if j == 0 else [])
    htok = (kb.dsem[6], kb.dcnt[6])

    @blk.vector
    def _(v):
        V.do(v, lambda e: e.tensor_scalar(
            st["resid"].rearrange("p (c t) -> p c t", c=C8)[:, :, 0:HALO],
            st["resid"].rearrange("p (c t) -> p c t", c=C8)[:, :, 0:HALO],
            kb.cmask[:, 0:1], None, op0=ALU.mult), waits=[htok])
    tok["resid_v"] = V.prev
    return tok


def alloc_block0(kb):
    st = {}
    st["resid"] = kb.sbuf("resid", [128, C8 * NT0], F32)
    st["xs"] = kb.sbuf("xs", [128, C8 * NT0], BF16)
    base = kb.mkslab("ovl", OVLB)
    kb.ovl_base = base
    st.update(alloc_attn_state(kb, base))
    st["rawst"] = [kb.sbuf(f"raw{i}", [128, TB], F32) for i in range(2)]
    st["t1st"] = kb.sbuf("t1st", [128, TB], F32)
    st["t3st"] = kb.sbuf("t3st", [128, TB], F32)
    st["fr"] = kb.sbuf("fr", [2, TB], F32)
    st["w1W"] = kb.sbuf_at("w1W", [128, 8 * 1024], BF16, base, 0)
    st["w2W"] = kb.sbuf_at("w2W", [128, 4 * 1024], BF16, base, 16384)
    st["fusedst"] = kb.sbuf_at("fusedst", [128, 4 * TB], BF16, base, 24576)
    kb.PS = [kb.psum(f"PSB{i}", [128, 512], F32) for i in range(8)]
    kb.ln_scr = alloc_ln_scratch(kb)
    kb.tcB = kb.sbuf("tcB", [128, NT0], BF16)
    kb.tsB = kb.sbuf("tsB", [128, NT0], BF16)
    kb.tciB = kb.sbuf("tciB", [128, NT0], BF16)
    kb.tsiB = kb.sbuf("tsiB", [128, NT0], BF16)
    kb.tcA = kb.sbuf_at("tcA", [128, NT0], BF16,
                        kb.nc.lookup_mloc(kb.tcB).addr, 0)
    kb.tsA = kb.sbuf_at("tsA", [128, NT0], BF16,
                        kb.nc.lookup_mloc(kb.tsB).addr, 0)
    kb.msk = kb.sbuf("mskS", [128, 4 * 128], BF16)
    kb.protF = kb.sbuf("protF", [128, 128], F32)
    kb.protq = kb.sbuf("protq", [128, 128], F32)
    kb.protk = kb.sbuf("protk", [128, 128], F32)
    kb.bd2 = kb.sbuf("bd2S", [128, 2], F32)
    kb.bd2t = kb.sbuf("bd2tS", [2, 128], F32)
    kb.idf = kb.sbuf("idfS", [128, 128], F32)
    kb.padrow_sb = kb.sbuf("padrowS", [1, NT0], BF16)
    kb.cmask = kb.sbuf("cmaskS", [128, 1], F32)
    kb.qsW = kb.sbuf("qsWS", [128, IN_DEPTH], F32)
    kb.ksW = kb.sbuf("ksWS", [128, IN_DEPTH], F32)
    kb.lngW = kb.sbuf("lngWS", [128, IN_DEPTH * C8], F32)
    kb.lnbW = kb.sbuf("lnbWS", [128, IN_DEPTH * C8], F32)
    kb.b1W = kb.sbuf("b1WS", [128, IN_DEPTH * 44], F32)
    kb.b2W = kb.sbuf("b2WS", [128, IN_DEPTH * C8], F32)
    return st


def emit_prelude(kb, blk, st, ins):
    ctok = emit_consts(kb, blk)
    sctok = emit_stage_consts(kb, blk, st, kb.padrow_sb)

    @blk.sync
    def _(q):
        pairs = [
            (kb.tciB, "TciB"),
            (kb.tsiB, "TsiB"), (kb.tcA, "TcA"), (kb.tsA, "TsA"),
            (kb.msk, "MSK"), (kb.protF, "PROTF"), (kb.bd2, "BD2"),
            (kb.bd2t, "BD2T"), (kb.idf, "IDF"), (kb.padrow_sb, "padrowbf"),
            (kb.cmask, "coremask"),
        ]
        if hasattr(kb, "a0w"):
            pairs += [(kb.a0w, "a0_lnw"), (kb.a0b, "a0_lnb")]
        for sb, name in pairs:
            if name in ins:
                kb.dma(q, sb[:], ins[name], slot=0)
        if "qs_col" in ins:
            kb.dma(q, kb.qsW[:], ins["qs_col"], slot=0)
            kb.dma(q, kb.ksW[:], ins["ks_col"], slot=0)
        for sb, name, w in [(kb.lngW, "ln_g", C8), (kb.lnbW, "ln_b", C8),
                            (kb.b1W, "b1p", 44), (kb.b2W, "b2c", C8)]:
            if name in ins:
                kb.dma(q, sb[:].rearrange("p (l c) -> p l c", l=IN_DEPTH),
                       ins[name].rearrange("l p c -> p l c"), slot=0)
    ldtok = (kb.dsem[0], kb.dcnt[0])
    @blk.vector
    def _(v):
        kb.V.do(v, lambda e: e.nop(), waits=[ldtok, ctok, sctok])

    @blk.scalar
    def _(s):
        kb.S.do(s, lambda e: e.nop(), waits=[ldtok, ctok, sctok])

    @blk.tensor
    def _(p):
        kb.P.do(p, lambda e: e.nop(), waits=[ldtok, ctok, sctok], mark=True)

    @blk.gpsimd
    def _(g):
        g.wait_ge(*ldtok)
    return {"resid_v": kb.V.prev, "pe_free": kb.P.prev}


def build_piece_inner(l0=0, nl=1):
    kb = KB()
    nc = kb.nc
    rin = kb.din("residT", [D, NT0])
    waps = {
        "Wqkv": kb.din("Wqkv", [IN_DEPTH, D, 3 * 1024]),
        "Wo": kb.din("Wo", [IN_DEPTH, D, D]),
        "W1p": kb.din("W1p", [IN_DEPTH, D, NPAIR * 256]),
        "W2": kb.din("W2", [IN_DEPTH, DIM_INNER, D]),
    }
    ins = {}
    for name, shape in [("TcB", [128, NT0]), ("TsB", [128, NT0]),
                        ("TciB", [128, NT0]), ("TsiB", [128, NT0]),
                        ("MSK", [128, 4 * 128]), ("padrowbf", [1, NT0]),
                        ("onesbf", [1, KMAX])]:
        ins[name] = kb.din(name, shape, BF16)
    for name, shape in [("PROTF", [128, 128]), ("BD2", [128, 2]),
                        ("BD2T", [2, 128]), ("IDF", [128, 128]),
                        ("coremask", [128, 1]),
                        ("qs_col", [128, IN_DEPTH]),
                        ("ks_col", [128, IN_DEPTH]),
                        ("ln_g", [IN_DEPTH, 128, C8]),
                        ("ln_b", [IN_DEPTH, 128, C8]),
                        ("b1p", [IN_DEPTH, 128, 44]),
                        ("b2c", [IN_DEPTH, 128, C8])]:
        ins[name] = kb.din(name, shape, F32)
    rout = kb.dout("residO", [D, NT0])
    st = alloc_block0(kb)
    kb.open()
    with nc.Block() as blk:
        tok = emit_prelude(kb, blk, st, ins)
        ldtok = emit_load_x(kb, blk, st["resid"], rin)

        @blk.tensor
        def _(p):
            p.wait_ge(*ldtok)

        @blk.vector
        def _(v):
            v.wait_ge(*ldtok)

        @blk.scalar
        def _(s):
            s.wait_ge(*ldtok)
        for l in range(l0, l0 + nl):
            tok = emit_inner_attn(kb, blk, st, l, waps, tok)
            tok = emit_inner_ffn(kb, blk, st, l, waps, tok)

        @blk.sync
        def _(q):
            t = None
            for j in range(C8):
                t = kb.dma(q, rout[j * 128:(j + 1) * 128, :],
                           st["resid"][:, j * NT0:(j + 1) * NT0], slot=1,
                           waits=[tok["resid_v"], tok["pe_free"]]
                           if j == 0 else [])
            q.wait_ge(*t)
    kb.close()
    return kb


def emit_load_tablesB(kb, blk, ins, tok):
    @blk.sync
    def _(q):
        kb.dma(q, kb.tcB[:], ins["TcB"], slot=11,
               waits=[tok["pe_free"], tok["resid_v"]])
        kb.dma(q, kb.tsB[:], ins["TsB"], slot=11)
    t = (kb.dsem[11], kb.dcnt[11])

    @blk.vector
    def _(v):
        kb.V.do(v, lambda e: e.nop(), waits=[t])
    return tok


def emit_a0(kb, blk, st, waps, tok, acc_out):
    V, S, P = kb.V, kb.S, kb.P
    lntok = emit_ln(kb, blk, st["resid"], st["xs"], NT0, tblocks(256, NT0),
                    kb.ln_scr, gamma=kb.a0w, beta=kb.a0b, shift=False,
                    extra_wait=tok.get("resid_v"))
    wv = waps["w_qkvT"].rearrange("(c p) m -> p c m", p=128)
    stage_war = [tok.get("pe_free")]
    war_v = [lntok]
    wo_war = [None]
    wtokL = [None]
    wv2 = None

    def load_wqkv(hb, wait):
        toks = []

        @blk.gpsimd
        def _(g):
            for i in range(3):
                toks.append(kb.dma(
                    g, st["wqkvW"][:, i * 1024:(i + 1) * 1024].rearrange(
                        "p (c m) -> p c m", c=C8),
                    wv[:, :, i * D + hb * 128:i * D + (hb + 1) * 128],
                    slot=2, waits=[wait] if i == 0 else []))
        return toks[-1]

    acc_war = [None, None]
    dma_war = [None, None]
    for half in range(2):
        qt0, qtn, kt0 = HALVES_A0[half]
        qrange = (qt0 * 128, (qtn + 1) * 128)
        krange = (kt0 * 128, (qtn + 1) * 128)
        kts = list(range(kt0, qtn + 1))

        @blk.scalar
        def _(s, kt0=kt0, qtn=qtn):
            nk = (qtn - kt0 + 1) * 128
            for hh in range(2):
                S.do(s, lambda e, hh=hh, nk=nk: e.copy(
                    st["kst"][hh][64:65, 0:nk],
                    kb.padrow_sb[0:1, kt0 * 128:kt0 * 128 + nk]),
                    waits=[stage_war[0], dma_war[0], dma_war[1]]
                    if hh == 0 else [])

        @blk.vector
        def _(v):
            if dma_war[0] is not None:
                kb.V.do(v, lambda e: e.nop(), waits=[dma_war[0], dma_war[1]])

        @blk.gpsimd
        def _(g):
            for h in range(2):
                kb.G.do(g, lambda e, h=h: e.memset(st["qst"][h][64:65, :], 1.0),
                        waits=[stage_war[0], dma_war[0], dma_war[1]]
                        if h == 0 else [])

        @blk.sync
        def _(q):
            for h in range(2):
                kb.dma(q, st["kst"][h][65:66, :], kb.inp["onesbf"], slot=10,
                       waits=[stage_war[0], dma_war[0], dma_war[1]]
                       if h == 0 else [])
        a0_aug_tok = (kb.dsem[10], kb.dcnt[10])
        a0_qones = [kb.G.prev]

        if half == 0:
            wtokL[0] = load_wqkv(0, stage_war[0])
        stok2 = None
        for hb in range(8):
            wtok = wtokL[0]
            cfgq = dict(xs=st["xs"], woff=0, wtok=wtok, war=stage_war[0],
                        war_v=war_v[0], prottok=None, xstok=lntok,
                        dst=st["qst"], l2n=False, prot=kb.protF[:],
                        qs_ap=None, scl=0.125, tcb=kb.tcA, tsb=kb.tsA)
            qpe, qv = qkv_production(kb, blk, st, cfgq, "q", qrange, qt0 * 128)
            cfgk = dict(xs=st["xs"], woff=1024, wtok=wtok, war=stage_war[0],
                        war_v=war_v[0], prottok=None, xstok=lntok,
                        dst=st["kst"], l2n=False, prot=kb.protF[:],
                        qs_ap=None, scl=1.0, tcb=kb.tcA, tsb=kb.tsA)
            kpe, kv = qkv_production(kb, blk, st, cfgk, "k", krange, kt0 * 128)
            cfgv = dict(xs=st["xs"], woff=2048, wtok=wtok, war=stage_war[0],
                        vinit=None)
            vtok = v_production(kb, blk, st, cfgv, kts)
            if hb < 7:
                wtokL[0] = load_wqkv(hb + 1, kb.P.prev)
            elif half == 0:
                wtokL[0] = load_wqkv(0, kb.P.prev)
            scfg = dict(qt0=qt0, qtn=qtn, kt0=kt0, lookback=1,
                        mask_of=lambda o: {0: 2, 1: 3}.get(o),
                        hb=hb, qtok=qv, ktok=kv, vtok=vtok,
                        aug_tok=a0_aug_tok, qones_tok=a0_qones[0],
                        nm_war=tok.get("nm_war"))
            ptok, vtok2, stok2 = attn_scores_pv(kb, blk, st, scfg)
            stage_war[0] = ptok
            war_v[0] = vtok2
            tok["nm_war"] = (kb.dsem[7], kb.dcnt[7])
        wcfg = dict(qt0=qt0, qtn=qtn, wo_ap=waps["w_outT"], dest="acc",
                    resid=st["resid"], acc_out=acc_out, acc_war=acc_war,
                    wo_war=wo_war[0], ottok=stok2, pe_tok=stage_war[0])
        wpe, wv2 = attn_wo(kb, blk, st, wcfg)
        wo_war[0] = wpe
        stage_war[0] = wpe
        war_v[0] = wv2
        dma_war[0] = (kb.dsem[1], kb.dcnt[1])
        dma_war[1] = (kb.dsem[9], kb.dcnt[9])
    tok["resid_v"] = wv2
    tok["pe_free"] = stage_war[0]
    tok["acc_dma"] = [(kb.dsem[1], kb.dcnt[1]), (kb.dsem[9], kb.dcnt[9])]
    return tok


def emit_combine_block0(kb, blk, st, xT, acc_in, x1_out, tok):
    V = kb.V
    ldtoks = {}
    addv = [None, None]
    for j in range(C8):
        for bi, (t0, t1) in enumerate(tblocks(HALO, NT0)):
            Wd = t1 - t0
            par = (j * 4 + bi) % 2

            @blk.sync
            def _(q, j=j, t0=t0, t1=t1, par=par):
                adm = tok.get("acc_dma") or [None, None]
                first = (j == 0 and t0 == HALO)
                xw = [addv[par]]
                if first:
                    xw += [adm[0], adm[1], (kb.dsem[7], kb.dcnt[7]),
                           (kb.dsem[10], kb.dcnt[10]),
                           tok.get("pe_free"), tok.get("resid_v")]
                t1a = kb.dma(q, st["rawst"][par][:, 0:t1 - t0],
                             xT[j * 128:(j + 1) * 128, t0:t1], slot=12 + par,
                             waits=xw)
                t1b = kb.dma(q, st["accst"][par][:, 0:t1 - t0],
                             acc_in[j * 128:(j + 1) * 128, t0 - HALO:t1 - HALO],
                             slot=12 + par)
                t1c = kb.dma(q, (st["t1st"] if par == 0 else st["t3st"])[:, 0:t1 - t0],
                             acc_in[D + j * 128:D + (j + 1) * 128,
                                    t0 - HALO:t1 - HALO],
                             slot=12 + par)
                ldtoks[(j, t0)] = t1c

            @blk.vector
            def _(v, j=j, t0=t0, t1=t1, Wd=Wd, par=par):
                V.do(v, lambda e: e.tensor_add(
                    st["resid"][:, j * NT0 + t0:j * NT0 + t1],
                    st["resid"][:, j * NT0 + t0:j * NT0 + t1],
                    st["rawst"][par][:, 0:Wd]),
                    waits=[ldtoks[(j, t0)], tok.get("resid_v")])
                V.do(v, lambda e: e.tensor_add(
                    st["resid"][:, j * NT0 + t0:j * NT0 + t1],
                    st["resid"][:, j * NT0 + t0:j * NT0 + t1],
                    st["accst"][par][:, 0:Wd]))
                V.do(v, lambda e: e.tensor_add(
                    st["resid"][:, j * NT0 + t0:j * NT0 + t1],
                    st["resid"][:, j * NT0 + t0:j * NT0 + t1],
                    (st["t1st"] if par == 0 else st["t3st"])[:, 0:Wd]))
                addv[par] = V.prev

            @blk.sync
            def _(q, j=j, t0=t0, t1=t1, par=par):
                kb.dma(q, x1_out[j * 128:(j + 1) * 128, t0 - HALO:t1 - HALO],
                       st["resid"][:, j * NT0 + t0:j * NT0 + t1], slot=1,
                       waits=[addv[par]])
    tok["x1_dma"] = (kb.dsem[1], kb.dcnt[1])
    return tok


def build_block0(with_exchange=True, dev=True):
    kb = K_B0 = KB()
    nc = kb.nc
    xT = kb.din("xT", [D, NT0])
    waps = {
        "Wqkv": kb.din("Wqkv", [IN_DEPTH, D, 3 * 1024]),
        "Wo": kb.din("Wo", [IN_DEPTH, D, D]),
        "W1p": kb.din("W1p", [IN_DEPTH, D, NPAIR * 256]),
        "W2": kb.din("W2", [IN_DEPTH, DIM_INNER, D]),
        "w_qkvT": kb.din("w_qkvT", [D, 3 * 1024]),
        "w_outT": kb.din("w_outT", [D, D]),
    }
    ins = {}
    for name, shape in [("TcB", [128, NT0]), ("TsB", [128, NT0]),
                        ("TciB", [128, NT0]), ("TsiB", [128, NT0]),
                        ("TcA", [128, NT0]), ("TsA", [128, NT0]),
                        ("MSK", [128, 4 * 128]), ("padrowbf", [1, NT0]),
                        ("onesbf", [1, KMAX])]:
        ins[name] = kb.din(name, shape, BF16)
    for name, shape in [("PROTF", [128, 128]), ("BD2", [128, 2]),
                        ("BD2T", [2, 128]), ("IDF", [128, 128]),
                        ("coremask", [128, 1]),
                        ("qs_col", [128, IN_DEPTH]),
                        ("ks_col", [128, IN_DEPTH]),
                        ("ln_g", [IN_DEPTH, 128, C8]),
                        ("ln_b", [IN_DEPTH, 128, C8]),
                        ("b1p", [IN_DEPTH, 128, 44]),
                        ("b2c", [IN_DEPTH, 128, C8]),
                        ("a0_lnw", [128, C8]), ("a0_lnb", [128, C8])]:
        ins[name] = kb.din(name, shape, F32)
    x1o = kb.dout("x1T", [D, OWN])
    spill_acc = kb.dint("spill_acc", [2 * D, OWN])
    st = alloc_block0(kb)
    kb.a0w = kb.sbuf("a0w", [128, C8], F32)
    kb.a0b = kb.sbuf("a0b", [128, C8], F32)
    kb.cinH = kb.dint("cinH", [D, HALO])
    kb.coutH = kb.dint("coutH", [2 * D, HALO])
    kb.open()
    with nc.Block() as blk:
        tok = emit_prelude(kb, blk, st, ins)
        ldtok = emit_load_x(kb, blk, st["resid"], xT)

        @blk.tensor
        def _(p):
            p.wait_ge(*ldtok)

        @blk.vector
        def _(v):
            v.wait_ge(*ldtok)

        @blk.scalar
        def _(s):
            s.wait_ge(*ldtok)
        lt = emit_ln(kb, blk, st["resid"], st["resid"], NT0, tblocks(0, NT0),
                     kb.ln_scr)
        tok["resid_v"] = lt
        tok = emit_a0(kb, blk, st, waps, tok, spill_acc)
        tok = emit_load_tablesB(kb, blk, ins, tok)
        for l in range(IN_DEPTH):
            tok = emit_inner_attn(kb, blk, st, l, waps, tok)
            tok = emit_inner_ffn(kb, blk, st, l, waps, tok)
            if with_exchange and l < IN_DEPTH - 1:
                tok = emit_halo_exchange(kb, blk, st, l, tok)
        tok = emit_combine_block0(kb, blk, st, xT, spill_acc, x1o, tok)

        @blk.sync
        def _(q):
            q.wait_ge(*tok["x1_dma"])
    kb.close()
    return kb


def build_piece_a0():
    kb = KB()
    nc = kb.nc
    rin = kb.din("residT", [D, NT0])
    waps = {
        "w_qkvT": kb.din("w_qkvT", [D, 3 * 1024]),
        "w_outT": kb.din("w_outT", [D, D]),
    }
    ins = {}
    for name, shape in [("TcA", [128, NT0]), ("TsA", [128, NT0]),
                        ("MSK", [128, 4 * 128]), ("padrowbf", [1, NT0]),
                        ("onesbf", [1, KMAX])]:
        ins[name] = kb.din(name, shape, BF16)
    for name, shape in [("PROTF", [128, 128]), ("BD2", [128, 2]),
                        ("BD2T", [2, 128]), ("IDF", [128, 128]),
                        ("coremask", [128, 1]),
                        ("a0_lnw", [128, C8]), ("a0_lnb", [128, C8])]:
        ins[name] = kb.din(name, shape, F32)
    acc = kb.dout("acc", [2 * D, OWN])
    st = alloc_block0(kb)
    kb.a0w = kb.sbuf("a0w", [128, C8], F32)
    kb.a0b = kb.sbuf("a0b", [128, C8], F32)
    kb.open()
    with nc.Block() as blk:
        tok = emit_prelude(kb, blk, st, ins)
        ldtok = emit_load_x(kb, blk, st["resid"], rin)

        @blk.tensor
        def _(p):
            p.wait_ge(*ldtok)

        @blk.vector
        def _(v):
            v.wait_ge(*ldtok)

        @blk.scalar
        def _(s):
            s.wait_ge(*ldtok)
        tok = emit_a0(kb, blk, st, waps, tok, acc)

        @blk.sync
        def _(q):
            q.wait_ge(*tok["acc_dma"][0])
            q.wait_ge(*tok["acc_dma"][1])
    kb.close()
    return kb


NB1 = OWN
FT = 4096
OVL1 = 75840
RG1 = [[0, 1], [2, 3], [4, 5], [6, 7]]


def alloc_block1(kb, fused_base=None):
    st = {}
    if fused_base is None:
        st["x1"] = kb.sbuf("x1", [128, C8 * NB1], F32)
        st["xob"] = kb.sbuf("xob", [128, C8 * NB1], BF16)
        st["xpb"] = kb.sbuf("xpb", [128, C8 * NB1], BF16)
        st["cstat"] = kb.sbuf("cstat", [128, 128], F32)
        st["fcol"] = kb.sbuf("fcol", [128, 32], F32)
        st["qmcol"] = kb.sbuf("qmcol", [128, 16], F32)
        kb.gcl = kb.sbuf("gclS", [128, C8], F32)
        kb.gch = kb.sbuf("gchS", [128, C8], F32)
        kb.rtq = kb.sbuf("rtqS", [128, C8], F32)
        kb.rtkv = kb.sbuf("rtkvS", [128, C8], F32)
        kb.idf = kb.sbuf("idfS", [128, 128], F32)
        kb.idb = kb.sbuf("idbS", [128, 128], BF16)
        kb.cmask = kb.sbuf("cmaskS", [128, 1], F32)
        kb.ocb = kb.sbuf("ocbS", [128, 1], BF16)
        alloc_consts(kb)
        base = kb.mkslab("ovl1", OVL1)
    else:
        fb = fused_base
        st["x1"] = kb.sbuf_at("x1", [128, C8 * NB1], F32, fb, 0)
        st["xob"] = kb.sbuf_at("xob", [128, C8 * NB1], BF16, fb, 65536)
        st["xpb"] = kb.sbuf_at("xpb", [128, C8 * NB1], BF16, fb, 98304)
        base = fb + 131072
        so = [131072 + OVL1]

        def sm(name, shape, dtype):
            nb = int(np.prod(shape[1:])) * (2 if dtype == BF16 else 4)
            t = kb.sbuf_at(name, shape, dtype, fb, so[0])
            so[0] += (nb + 31) // 32 * 32
            return t
        st["cstat"] = sm("cstat", [128, 128], F32)
        st["fcol"] = sm("fcol", [128, 32], F32)
        st["qmcol"] = sm("qmcol", [128, 16], F32)
        kb.gcl = sm("gclS", [128, C8], F32)
        kb.gch = sm("gchS", [128, C8], F32)
        kb.rtq = sm("rtqS", [128, C8], F32)
        kb.rtkv = sm("rtkvS", [128, C8], F32)
        kb.idb = sm("idbS", [128, 128], BF16)
        kb.ocb = sm("ocbS", [128, 1], BF16)
        st["b1_consts"] = {
            "idf": sm("idfB1", [128, 128], F32),
            "cmask": sm("cmaskB1", [128, 1], F32),
            "oc": sm("ocB1", [128, 1], F32),
            "orow": sm("orowB1", [1, 128], F32),
            "c_eps": sm("cepsB1", [128, 1], F32),
            "c_zero": sm("czeroB1", [128, 1], F32),
        }
        assert fb + so[0] <= 229344, so[0]
    kb.ovl_base = base

    def at(name, shape, dtype, off):
        return kb.sbuf_at(name, shape, dtype, base, off)

    st["xnr"] = at("xnr", [128, C8 * TB], F32, 0)
    st["m1row"] = at("m1row", [1, 2 * FT], F32, 32768)
    st["ln_scr"] = {
        "pA": None, "pB": None, "bc0": None, "bc1": None,
        "sq": at("sq1", [128, 2 * TB], F32, 65536),
        "aff": at("aff1", [128, TB], F32, 69632),
        "rows": at("rows1", [1, 6 * TB], F32, 16384),
    }
    st["gtmp"] = at("gtmp", [128, C8 * NB1], BF16, 0)
    st["diff32"] = at("diff32", [128, NB1], F32, 65536)
    st["srowA"] = at("srowA", [1, NB1], F32, 65536)
    st["srowB"] = at("srowB", [1, NB1], F32, 16384)
    st["c1row"] = at("c1row", [1, FT], F32, 10016)
    st["acol"] = at("acol", [128, 2], F32, 26400)
    st["rit"] = at("rit", [128, 2 * 128], F32, 0)
    st["cells"] = at("cells", [1, 64], F32, 1024)
    st["ldq"] = at("ldq", [16, 128], F32, 1280)
    st["raccum"] = at("raccum", [128, 2], F32, 1792)
    st["kvml"] = at("kvml", [1, FT], BF16, 1824)
    st["nrow"] = at("nrow", [1, FT], F32, 0)
    st["ldn"] = at("ldn", [32, 128], F32, 16384)
    st["nsq"] = at("nsq", [128, 2 * 128], BF16, 26432)
    st["qt1"] = at("qt1", [128, C8 * NB1], BF16, 0)
    st["wvs"] = at("wvs", [128, C8 * 1024], BF16, 32768)
    st["wkb"] = at("wkb", [128, C8 * 1024], BF16, 49152)
    st["ktm"] = at("ktm", [128, 2 * 1024], BF16, 65536)
    st["wqst"] = at("wqst", [128, 2 * 1024], BF16, 65536)
    st["vtm"] = at("vtm", [128, 2 * 512], BF16, 69632)
    st["pst"] = at("pst", [128, NB1], BF16, 71680)
    st["wvstage"] = at("wvstage", [128, 2 * 512], F32, 71680)
    st["rc1"] = at("rc1", [128, 16], F32, 75776)
    st["tt1"] = at("tt1", [128, 256], F32, 53248)
    st["qlh"] = [at(f"qlh{i}", [66, NB1], BF16, i * 4096) for i in range(2)]
    st["klh"] = [at(f"klh{i}", [66, FT], BF16, 8192 * (i + 1))
                 for i in range(2)]
    st["vlh"] = at("vlh", [128, 32 * 130], BF16, 24576)
    st["olh"] = at("olh", [128, NB1], BF16, 32928)
    st["wfst"] = at("wfst", [128, C8 * 128], F32, 37024)
    st["wst"] = at("wst", [128, C8 * 128], BF16, 41120)
    st["won"] = at("won", [128, 1024], BF16, 43168)
    st["pstg"] = at("pstg", [128, 2 * 256], BF16, 45216)
    st["nm1"] = at("nm1", [128, 128], BF16, 46240)
    st["nm1t"] = at("nm1t", [128, 128], BF16, 46496)
    st["osb1"] = at("osb1", [128, 64], F32, 46752)
    st["Pq"] = at("Pq", [128, 32 * 256], BF16, 53248)
    st["prow"] = [at(f"prow{i}", [128, 512], BF16, i * 1024)
                  for i in range(2)]
    st["vres"] = kb.sbuf_at("vres", [128, 32 * 1024], BF16,
                            kb.nc.lookup_mloc(st["xob"]).addr, 0)
    if fused_base is None:
        kb.PS = [kb.psum(f"PSB{i}", [128, 512], F32) for i in range(8)]
    st["ln_scr"]["pA"] = kb.PS[4]
    st["ln_scr"]["pB"] = kb.PS[5]
    st["ln_scr"]["bc0"] = kb.PS[6]
    st["ln_scr"]["bc1"] = kb.PS[7]
    if fused_base is None:
        kb.ln_scr = st["ln_scr"]
    return st


def emit_barrier(kb, blk, toks):
    toks = [t for t in toks if t is not None]

    @blk.sync
    def _(q):
        for t in toks:
            q.wait_ge(t[0], t[1])

    @blk.vector
    def _(v):
        kb.V.do(v, lambda e: e.nop(), waits=toks)

    @blk.scalar
    def _(s):
        kb.S.do(s, lambda e: e.nop(), waits=toks)

    @blk.tensor
    def _(p):
        kb.P.do(p, lambda e: e.nop(), waits=toks, mark=True)

    @blk.gpsimd
    def _(g):
        kb.G.do(g, lambda e: e.nop(), waits=toks)
    return {"v": kb.V.prev, "s": kb.S.prev, "p": kb.P.prev, "g": kb.G.prev}


def bar_toks(bar):
    return [bar["v"], bar["s"], bar["p"], bar["g"]]


def emit_b1_prelude(kb, blk, ins):
    ctok = emit_consts(kb, blk)

    @blk.gpsimd
    def _(g):
        kb.G.do(g, lambda e: e.memset(kb.ocb[:], 1.0))
    mtok = kb.G.prev

    @blk.sync
    def _(q):
        for sb, name in [(kb.idf, "IDF"), (kb.idb, "IDB"),
                         (kb.cmask, "coremask"), (kb.gcl, "gl_col"),
                         (kb.gch, "gh_col"), (kb.rtq, "rt_q"),
                         (kb.rtkv, "rt_kv")]:
            kb.dma(q, sb[:], ins[name], slot=0)
    ldtok = (kb.dsem[0], kb.dcnt[0])
    return emit_barrier(kb, blk, [ctok, mtok, ldtok])


def emit_b1_ln_s(kb, blk, st, x1T, tok):
    V, S, P = kb.V, kb.S, kb.P
    PS = kb.PS
    ldtok = emit_load_x(kb, blk, st["x1"], x1T, nt=NB1)

    @blk.tensor
    def _(p):
        p.wait_ge(*ldtok)

    @blk.vector
    def _(v):
        v.wait_ge(*ldtok)

    @blk.scalar
    def _(s):
        s.wait_ge(*ldtok)
    rr = [None]
    srow = [None]
    for bi, (t0, t1) in enumerate(tblocks(0, NB1)):
        W = t1 - t0
        lt = emit_ln(kb, blk, st["x1"], st["xnr"], NB1, [(t0, t1)],
                     kb.ln_scr, dst_bf16=st["xob"], nt_dst=TB, dst_off=t0,
                     extra_wait=rr[0])

        @blk.tensor
        def _(p, bi=bi, W=W, lt=lt):
            for r in range(2):
                for j in range(C8):
                    P.do(p, lambda e, r=r, j=j: e.matmul(
                        PS[r][0:1, 0:W],
                        (kb.rtq if r == 0 else kb.rtkv)[:, j:j + 1],
                        st["xnr"][:, j * TB:j * TB + W],
                        start=(j == 0), stop=(j == C8 - 1)),
                        waits=[lt, srow[0]] if (r == 0 and j == 0) else [],
                        mark=(r == 1 and j == C8 - 1))
            rr[0] = P.prev
        mmt = P.prev

        @blk.vector
        def _(v, t0=t0, W=W, mmt=mmt):
            for r in range(2):
                V.do(v, lambda e, r=r: e.tensor_copy(
                    st["m1row"][0:1, r * FT + t0:r * FT + t0 + W],
                    PS[r][0:1, 0:W]), waits=[mmt] if r == 0 else [])
            srow[0] = V.prev

        @blk.sync
        def _(q, t0=t0, t1=t1, lt=lt):
            for j in range(C8):
                kb.dma(q, kb.cin1[j * 128:(j + 1) * 128, t0:t1],
                       st["xob"][:, j * NB1 + t0:j * NB1 + t1], slot=6,
                       waits=[lt] if j == 0 else [])

    @blk.sync
    def _(q):
        kb.dma(q, kb.cins[0:1, :], st["m1row"][0:1, 0:NB1], slot=6,
               waits=[kb.V.prev])
        kb.dma(q, kb.cins[1:2, :], st["m1row"][0:1, FT:FT + NB1], slot=6)
    tok["gin"] = (kb.dsem[6], kb.dcnt[6])
    return tok


def emit_b1_gather(kb, blk, st, tok):
    V = kb.V

    @blk.gpsimd
    def _(g):
        g.wait_ge(*tok["gin"])
        kb.G.do(g, lambda e: e.collective_compute(
            "AllGather", mybir.AluOpType.bypass, replica_groups=RG1,
            ins=[kb.cin1.opt()], outs=[kb.cout1.opt()]))
        kb.G.do(g, lambda e: e.collective_compute(
            "AllGather", mybir.AluOpType.bypass, replica_groups=RG1,
            ins=[kb.cins.opt()], outs=[kb.couts.opt()]))
    ctok = kb.G.prev

    @blk.sync
    def _(q):
        for j in range(C8):
            kb.dma(q, st["xpb"][:, j * NB1:(j + 1) * NB1],
                   kb.cout1[D + j * 128:D + (j + 1) * 128, :], slot=6,
                   waits=[ctok] if j == 0 else [])
            kb.dma(q, st["gtmp"][:, j * NB1:(j + 1) * NB1],
                   kb.cout1[j * 128:(j + 1) * 128, :], slot=6)
    xg = (kb.dsem[6], kb.dcnt[6])
    @blk.vector
    def _(v):
        for j in range(C8):
            V.do(v, lambda e, j=j: e.tensor_sub(
                st["diff32"][:, 0:NB1],
                st["gtmp"][:, j * NB1:(j + 1) * NB1],
                st["xpb"][:, j * NB1:(j + 1) * NB1]),
                waits=[xg] if j == 0 else [])
            V.do(v, lambda e, j=j: e.scalar_tensor_tensor(
                st["xpb"][:, j * NB1:(j + 1) * NB1],
                st["diff32"][:, 0:NB1], kb.cmask[:, 0:1],
                st["xpb"][:, j * NB1:(j + 1) * NB1],
                op0=ALU.mult, op1=ALU.add))
    xv = V.prev

    bv = [xv]
    for r in range(2):
        @blk.sync
        def _(q, r=r, bv=bv):
            kb.dma(q, st["srowA"][0:1, :], kb.couts[r:r + 1, :], slot=6,
                   waits=[bv[0]])
            kb.dma(q, st["srowB"][0:1, :], kb.couts[2 + r:3 + r, :], slot=6)
        sg = (kb.dsem[6], kb.dcnt[6])

        @blk.vector
        def _(v, r=r, sg=sg, bv=bv):
            V.do(v, lambda e: e.tensor_sub(
                st["srowA"][0:1, :], st["srowA"][0:1, :],
                st["srowB"][0:1, :]), waits=[sg])
            V.do(v, lambda e: e.scalar_tensor_tensor(
                st["m1row"][0:1, r * FT + NB1:r * FT + FT],
                st["srowA"][0:1, :], kb.cmask[0:1, 0:1],
                st["srowB"][0:1, :], op0=ALU.mult, op1=ALU.add))
            bv[0] = kb.V.prev
    tok["xnv"] = kb.V.prev
    return tok


def emit_b1_routing(kb, blk, st, tok):
    V, S, P = kb.V, kb.S, kb.P
    PS = kb.PS

    @blk.sync
    def _(q):
        kb.dma(q, kb.rscr[0:1, :], st["m1row"][0:1, :], slot=11,
               waits=[tok["xnv"]])
    t0 = (kb.dsem[11], kb.dcnt[11])

    @blk.sync
    def _(q):
        for r in range(2):
            kb.dma(q, st["rit"][:, r * 128:r * 128 + 32],
                   kb.rscr[0:1, r * FT:(r + 1) * FT].rearrange(
                       "a (p c) -> (a p) c", p=128),
                   slot=11, waits=[t0] if r == 0 else [])
    t1 = (kb.dsem[11], kb.dcnt[11])
    logk = [float(np.log(KQ)), float(np.log(KKV))]
    acell = [st["cells"][0:1, r:r + 1] for r in range(2)]
    lncell = [st["cells"][0:1, 2 + r:3 + r] for r in range(2)]
    apcell = [st["cells"][0:1, 4 + r:5 + r] for r in range(2)]

    @blk.gpsimd
    def _(g):
        kb.G.do(g, lambda e: e.memset(st["cells"][:], 0.0), waits=[t1])
    cmt = kb.G.prev

    @blk.vector
    def _(v):
        for r in range(2):
            V.do(v, lambda e, r=r: e.tensor_copy(
                st["rit"][:, r * 128 + 64:r * 128 + 96],
                st["rit"][:, r * 128:r * 128 + 32]),
                waits=[t1, cmt] if r == 0 else [])
    inittok = V.prev
    abc = [None, None]
    for it in range(N_ITERS):
        for r in range(2):
            @blk.vector
            def _(v, r=r, it=it):
                V.do(v, lambda e: e.tensor_sub(
                    st["rit"][:, r * 128 + 32:r * 128 + 64],
                    st["rit"][:, r * 128:r * 128 + 32],
                    st["rit"][:, r * 128 + 64:r * 128 + 96]),
                    waits=[inittok] if it == 0 and r == 0 else [])
            ttok = V.prev

            @blk.scalar
            def _(s, r=r, it=it, ttok=ttok):
                bias = kb.c_zero[:, 0:1] if it == 0 \
                    else st["acol"][:, r:r + 1]
                S.do(s, lambda e: e.activation(
                    st["rit"][:, r * 128 + 96:r * 128 + 128],
                    st["rit"][:, r * 128 + 32:r * 128 + 64],
                    AF.Exp, bias=bias,
                    accum_out=st["raccum"][:, r:r + 1]), waits=[ttok])
            utok = S.prev

            @blk.tensor
            def _(p, r=r, utok=utok):
                P.do(p, lambda e: e.matmul(
                    PS[6][0:1, r:r + 1], kb.oc[:, 0:1],
                    st["raccum"][:, r:r + 1], start=True, stop=True),
                    waits=[utok], mark=True)
            ttot = P.prev

            @blk.scalar
            def _(s, r=r, ttot=ttot):
                S.do(s, lambda e: e.activation(
                    lncell[r], PS[6][0:1, r:r + 1], AF.Ln,
                    bias=kb.c_zero[0:1, 0:1]), waits=[ttot])
            lnt = S.prev

            @blk.vector
            def _(v, r=r, lnt=lnt):
                V.do(v, lambda e: e.scalar_tensor_tensor(
                    acell[r], lncell[r], -1.0, apcell[r],
                    op0=ALU.mult, op1=ALU.add), waits=[lnt])
                V.do(v, lambda e: e.tensor_scalar(
                    acell[r], acell[r], float(logk[r]), None, op0=ALU.add))
                V.do(v, lambda e: e.tensor_copy(apcell[r], acell[r]))
            atok = V.prev

            @blk.tensor
            def _(p, r=r, atok=atok):
                P.do(p, lambda e: e.matmul(
                    PS[4 + r][:, 0:1], kb.orow[:, :], acell[r],
                    start=True, stop=True),
                    waits=[atok, abc[r]], mark=True)
            btok = P.prev

            @blk.vector
            def _(v, r=r, btok=btok):
                V.do(v, lambda e: e.tensor_copy(
                    st["acol"][:, r:r + 1], PS[4 + r][:, 0:1]),
                    waits=[btok])
            acopy = V.prev

            @blk.scalar
            def _(s, r=r, acopy=acopy):
                S.do(s, lambda e: e.activation(
                    st["rit"][:, r * 128 + 64:r * 128 + 96],
                    st["rit"][:, r * 128:r * 128 + 32],
                    AF.Relu, bias=st["acol"][:, r:r + 1]), waits=[acopy])
                abc[r] = kb.S.prev
    rtok = kb.S.prev
    e0 = st["cells"][0:1, 8:9]
    e1 = st["cells"][0:1, 9:10]
    sp = st["cells"][0:1, 10:11]
    so = st["cells"][0:1, 11:12]
    for r, nsel in ((0, NQ), (1, NKV)):
        row = st["m1row"][0:1, r * FT:(r + 1) * FT]

        @blk.vector
        def _(v, r=r, nsel=nsel, row=row):
            V.do(v, lambda e: e.tensor_scalar(
                row, row, apcell[r], 0.0, op0=ALU.add, op1=ALU.is_ge),
                waits=[rtok] if r == 0 else [])
            V.do(v, lambda e: e.tensor_tensor_scan(
                st["c1row"][0:1, :], row, row, 0.0,
                op0=ALU.add, op1=ALU.bypass))
            V.do(v, lambda e: e.tensor_copy(e0, st["c1row"][0:1, NB1 - 1:NB1]))
            V.do(v, lambda e: e.tensor_copy(e1, st["c1row"][0:1, FT - 1:FT]))
            V.do(v, lambda e: e.tensor_sub(sp, e1, e0))
            V.do(v, lambda e: e.tensor_mul(sp, sp, kb.cmask[0:1, 0:1]))
            V.do(v, lambda e: e.tensor_mul(so, e0, kb.cmask[0:1, 0:1]))
            V.do(v, lambda e: e.tensor_scalar(
                st["c1row"][0:1, 0:NB1], st["c1row"][0:1, 0:NB1], sp,
                float(nsel) + 0.5, op0=ALU.add, op1=ALU.is_le))
            V.do(v, lambda e: e.tensor_scalar(
                st["c1row"][0:1, NB1:FT], st["c1row"][0:1, NB1:FT], so,
                float(nsel) + 0.5, op0=ALU.subtract, op1=ALU.is_le))
            V.do(v, lambda e: e.tensor_mul(row, row, st["c1row"][0:1, :]))

    @blk.vector
    def _(v):
        V.do(v, lambda e: e.tensor_scalar(
            st["kvml"][0:1, :], st["m1row"][0:1, FT:2 * FT], -1.0, 1.0e30,
            op0=ALU.add, op1=ALU.mult))
    mtok = kb.V.prev

    @blk.sync
    def _(q):
        kb.dma(q, kb.kvmd[0:1, :], st["kvml"][0:1, :], slot=11,
               waits=[mtok])
        kb.dma(q, kb.rqm[0:1, :], st["m1row"][0:1, 0:NB1], slot=11)
    d1 = (kb.dsem[11], kb.dcnt[11])

    @blk.sync
    def _(q):
        kb.dma(q, st["ldq"][:], kb.rqm[0:1, :].rearrange(
            "a (p c) -> (a p) c", p=16), slot=11, waits=[d1])
    d2 = (kb.dsem[11], kb.dcnt[11])

    @blk.tensor
    def _(p):
        P.do(p, lambda e: e.matmul(PS[3][0:128, 0:16], st["ldq"][:],
                                   kb.idf[0:16, 0:16], start=True, stop=True),
             waits=[d2], mark=True)
    qmt = P.prev

    @blk.vector
    def _(v):
        V.do(v, lambda e: e.tensor_copy(st["qmcol"][:], PS[3][0:128, 0:16]),
             waits=[qmt])
    tok["maskv"] = kb.V.prev
    return tok


def emit_b1_norms(kb, blk, st, tok):
    V, S, P = kb.V, kb.S, kb.P
    PS = kb.PS
    warP = [None]
    mmtk = {}
    for mt in range(32):
        srcb = st["xob"] if mt < 16 else st["xpb"]
        c0 = (mt % 16) * 128

        @blk.scalar
        def _(s, mt=mt, srcb=srcb, c0=c0):
            for j in range(C8):
                war = (mmtk.get((mt, j - 2)) if j >= 2
                       else mmtk.get((mt - 1, 6 + j)))
                S.do(s, lambda e, j=j: e.activation(
                    st["nsq"][:, (j % 2) * 128:(j % 2) * 128 + 128],
                    srcb[:, j * NB1 + c0:j * NB1 + c0 + 128], AF.Square),
                    waits=[war] +
                    ([tok["maskv"]] if mt == 0 and j == 0 else []))
                sqt = kb.S.prev

                @blk.tensor
                def _(p, j=j, mt=mt, sqt=sqt):
                    kb.P.do(p, lambda e: e.matmul(
                        PS[2][0:1, 0:128], kb.ocb[:, 0:1],
                        st["nsq"][:, (j % 2) * 128:(j % 2) * 128 + 128],
                        start=(j == 0), stop=(j == C8 - 1)),
                        waits=[sqt, warP[0]] if j == 0 else [sqt],
                        mark=True)
                    mmtk[(mt, j)] = kb.P.prev

        @blk.vector
        def _(v, mt=mt):
            V.do(v, lambda e: e.tensor_copy(
                st["nrow"][0:1, mt * 128:mt * 128 + 128], PS[2][0:1, 0:128]),
                waits=[mmtk[(mt, C8 - 1)]])
            warP[0] = kb.V.prev
    rowv = kb.V.prev

    @blk.scalar
    def _(s):
        S.do(s, lambda e: e.activation(st["nrow"][0:1, :], st["nrow"][0:1, :],
                                       AF.Sqrt, bias=kb.c_zero[0:1, 0:1]),
             waits=[rowv])

    @blk.vector
    def _(v):
        V.do(v, lambda e: e.tensor_scalar(
            st["nrow"][0:1, :], st["nrow"][0:1, :], 1e-12, None,
            op0=ALU.max), waits=[kb.S.prev])
        V.do(v, lambda e: e.reciprocal(st["nrow"][0:1, :], st["nrow"][0:1, :]))
        V.do(v, lambda e: e.tensor_scalar(
            st["nrow"][0:1, :], st["nrow"][0:1, :], 32.0, None, op0=ALU.mult))
    ntok = kb.V.prev

    @blk.sync
    def _(q):
        kb.dma(q, kb.rfn[0:1, :], st["nrow"][0:1, :], slot=11, waits=[ntok])
    d1 = (kb.dsem[11], kb.dcnt[11])

    @blk.sync
    def _(q):
        kb.dma(q, st["ldn"][:], kb.rfn[0:1, :].rearrange(
            "a (p c) -> (a p) c", p=32), slot=11, waits=[d1])
    d2 = (kb.dsem[11], kb.dcnt[11])

    @blk.tensor
    def _(p):
        P.do(p, lambda e: e.matmul(PS[3][0:128, 0:32], st["ldn"][:],
                                   kb.idf[0:32, 0:32], start=True, stop=True),
             waits=[d2, tok["maskv"]], mark=True)
    fmt = P.prev

    @blk.vector
    def _(v):
        V.do(v, lambda e: e.tensor_copy(st["fcol"][:], PS[3][0:128, 0:32]),
             waits=[fmt])
    tok["fcolv"] = kb.V.prev
    return tok


def emit_b1_l1a_S(kb, blk, st, waps, tok, bar):
    V, S, P = kb.V, kb.S, kb.P
    PS = kb.PS
    wv_done = [None]
    for j in range(C8):
        dtk = [None]

        @blk.sync
        def _(q, j=j):
            for hf in range(2):
                dtk[0] = kb.dma(q, st["wvstage"][:, hf * 512:(hf + 1) * 512],
                                waps["wv1T"][j * 128:(j + 1) * 128,
                                             hf * 512:(hf + 1) * 512], slot=7,
                                waits=([wv_done[0]] + bar_toks(bar))
                                if hf == 0 else [])

        @blk.vector
        def _(v, j=j):
            V.do(v, lambda e: e.reduce_max(
                st["rc1"][:, 0:1], st["wvstage"][:, 0:512],
                axis=AX.X, negate=True), waits=[dtk[0]])
            V.do(v, lambda e: e.reduce_max(
                st["rc1"][:, 1:2], st["wvstage"][:, 512:1024],
                axis=AX.X, negate=True))
            V.do(v, lambda e: e.tensor_tensor(
                st["rc1"][:, 0:1], st["rc1"][:, 0:1], st["rc1"][:, 1:2],
                op=ALU.min))
        nmt = V.prev

        @blk.scalar
        def _(s, j=j, nmt=nmt):
            for hf in range(2):
                S.do(s, lambda e, hf=hf: e.activation(
                    st["wvstage"][:, hf * 512:(hf + 1) * 512],
                    st["wvstage"][:, hf * 512:(hf + 1) * 512], AF.Exp,
                    bias=st["rc1"][:, 0:1],
                    accum_out=st["rc1"][:, 2 + hf:3 + hf]),
                    waits=[nmt] if hf == 0 else [])
        expt = S.prev

        @blk.vector
        def _(v, j=j, expt=expt):
            V.do(v, lambda e: e.tensor_add(
                st["rc1"][:, 2:3], st["rc1"][:, 2:3], st["rc1"][:, 3:4]),
                waits=[expt])
            V.do(v, lambda e: e.reciprocal(st["rc1"][:, 2:3],
                                           st["rc1"][:, 2:3]))
            for hf in range(2):
                V.do(v, lambda e, hf=hf: e.tensor_scalar(
                    st["wvs"][:, j * 1024 + hf * 512:j * 1024 + hf * 512 + 512],
                    st["wvstage"][:, hf * 512:(hf + 1) * 512],
                    st["rc1"][:, 2:3], 1.0 / 1024.0,
                    op0=ALU.mult, op1=ALU.mult))
            wv_done[0] = V.prev
    wvs_done = kb.V.prev
    @blk.gpsimd
    def _(g):
        kb.dma(g, st["wkb"][:].rearrange("p (c m) -> p c m", c=C8),
               waps["wk1"].rearrange("(c p) m -> p c m", p=128), slot=2,
               waits=bar_toks(bar))
    wktok = (kb.dsem[2], kb.dcnt[2])
    wqv = waps["wq1"].rearrange("(c p) m -> p c m", p=128)
    evq = [None, None]
    for dd in range(C8):
        par = dd % 2
        wslot = 3 if par == 0 else 14

        @blk.gpsimd
        def _(g, dd=dd, par=par, wslot=wslot):
            kb.dma(g, st["wqst"][:, par * 1024:par * 1024 + 1024].rearrange(
                "p (c m) -> p c m", c=C8),
                wqv[:, :, dd * 128:(dd + 1) * 128], slot=wslot,
                waits=[evq[par]] + (bar_toks(bar) if dd == 0 else []))
        wqt = (kb.dsem[wslot], kb.dcnt[wslot])
        for tt in range(16):
            @blk.tensor
            def _(p, dd=dd, tt=tt, par=par, wqt=wqt):
                for c in range(C8):
                    P.do(p, lambda e, c=c: e.matmul(
                        PS[par][:, 0:128],
                        st["wqst"][:, par * 1024 + c * 128:
                                   par * 1024 + c * 128 + 128],
                        st["xob"][:, c * NB1 + tt * 128:c * NB1 + tt * 128 + 128],
                        start=(c == 0), stop=(c == C8 - 1)),
                        waits=[wqt, evq[par]] if c == 0 else [],
                        mark=(c == C8 - 1))
            mmt = P.prev

            @blk.scalar
            def _(s, dd=dd, tt=tt, par=par, mmt=mmt):
                S.do(s, lambda e: e.activation(
                    st["qt1"][:, dd * NB1 + tt * 128:dd * NB1 + tt * 128 + 128],
                    PS[par][:, 0:128], AF.Copy, bias=0.0,
                    scale=1.0 / 64.0), waits=[mmt])
                evq[par] = kb.S.prev
    qt_done = kb.S.prev
    qt_pe = kb.P.prev
    wq_dma_done = (kb.dsem[3], kb.dcnt[3])
    stpe = [None, None]
    stw = [None] * 4
    spb = [None]
    spv = [None]
    pvsum = [None]
    vcp = [None]
    for mt in range(32):
        srcb = st["xob"] if mt < 16 else st["xpb"]
        c0 = (mt % 16) * 128
        slot = mt % 2
        evk = [None]
        for dd in range(C8):
            @blk.tensor
            def _(p, dd=dd, c0=c0, srcb=srcb):
                for c in range(C8):
                    P.do(p, lambda e, c=c: e.matmul(
                        PS[2][:, 0:128],
                        st["wkb"][:, c * 1024 + dd * 128:c * 1024 + dd * 128 + 128],
                        srcb[:, c * NB1 + c0:c * NB1 + c0 + 128],
                        start=(c == 0), stop=(c == C8 - 1)),
                        waits=[wktok, evk[0]] if c == 0 else [],
                        mark=(c == C8 - 1))
            mmt = P.prev

            @blk.scalar
            def _(s, dd=dd, slot=slot, mt=mt, mmt=mmt):
                S.do(s, lambda e: e.copy(
                    st["ktm"][:, slot * 1024 + dd * 128:slot * 1024 + dd * 128 + 128],
                    PS[2][:, 0:128]),
                    waits=[mmt, stpe[slot],
                           wq_dma_done if mt < 2 else None]
                    if dd == 0 else [mmt])
                evk[0] = kb.S.prev
        ktok = kb.S.prev
        for vv in range(2):
            @blk.tensor
            def _(p, vv=vv, c0=c0, srcb=srcb):
                for c in range(C8):
                    P.do(p, lambda e, c=c: e.matmul(
                        PS[3][:, 0:512],
                        srcb[:, c * NB1 + c0:c * NB1 + c0 + 128],
                        st["wvs"][:, c * 1024 + vv * 512:c * 1024 + vv * 512 + 512],
                        start=(c == 0), stop=(c == C8 - 1)),
                        waits=[wvs_done, vcp[0]] if c == 0 else [],
                        mark=(c == C8 - 1))
            mmt = P.prev

            @blk.scalar
            def _(s, vv=vv, mmt=mmt):
                S.do(s, lambda e: e.copy(
                    st["vtm"][:, vv * 512:vv * 512 + 512],
                    PS[3][:, 0:512]),
                    waits=[mmt, spv[0]] if vv == 0 else [mmt])
                vcp[0] = kb.S.prev
        vtok = kb.S.prev

        @blk.sync
        def _(q, mt=mt, vtok=vtok):
            spv[0] = kb.dma(q, kb.spillV[mt * 128:(mt + 1) * 128, :],
                            st["vtm"][:, 0:1024], slot=9, waits=[vtok])
        nmlast = [None]
        for nb in range(4):
            @blk.tensor
            def _(p, nb=nb, slot=slot, ktok=ktok):
                for dd in range(C8):
                    P.do(p, lambda e, dd=dd: e.matmul(
                        PS[4 + nb][:, 0:512],
                        st["ktm"][:, slot * 1024 + dd * 128:
                                  slot * 1024 + dd * 128 + 128],
                        st["qt1"][:, dd * NB1 + nb * 512:dd * NB1 + nb * 512 + 512],
                        start=(dd == 0), stop=(dd == C8 - 1)),
                        waits=[ktok, qt_done, stw[nb]] if dd == 0 else [],
                        mark=(dd == C8 - 1))
                if nb == 3:
                    stpe[slot] = P.prev
            mmt = P.prev

            @blk.vector
            def _(v, nb=nb, mmt=mmt):
                V.do(v, lambda e: e.reduce_max(
                    st["rc1"][:, nb:nb + 1], PS[4 + nb][:, 0:512],
                    axis=AX.X, negate=True), waits=[mmt])
                nmlast[0] = kb.V.prev

        @blk.vector
        def _(v, mt=mt):
            V.do(v, lambda e: e.tensor_tensor(
                st["rc1"][:, 0:1], st["rc1"][:, 0:1], st["rc1"][:, 1:2],
                op=ALU.min), waits=[nmlast[0]])
            V.do(v, lambda e: e.tensor_tensor(
                st["rc1"][:, 2:3], st["rc1"][:, 2:3], st["rc1"][:, 3:4],
                op=ALU.min))
            V.do(v, lambda e: e.tensor_tensor(
                st["cstat"][:, mt:mt + 1], st["rc1"][:, 0:1],
                st["rc1"][:, 2:3], op=ALU.min))
        nmt = kb.V.prev
        for nb in range(4):
            @blk.scalar
            def _(s, nb=nb, mt=mt, nmt=nmt):
                S.do(s, lambda e: e.activation(
                    st["pst"][:, nb * 512:nb * 512 + 512],
                    PS[4 + nb][:, 0:512], AF.Exp,
                    bias=st["cstat"][:, mt:mt + 1],
                    accum_out=st["rc1"][:, 4 + nb:5 + nb]),
                    waits=[nmt, spb[0], pvsum[0]] if nb == 0 else [])
                stw[nb] = kb.S.prev
        expt = kb.S.prev

        @blk.vector
        def _(v, mt=mt, expt=expt):
            V.do(v, lambda e: e.tensor_add(
                st["rc1"][:, 4:5], st["rc1"][:, 4:5], st["rc1"][:, 5:6]),
                waits=[expt])
            V.do(v, lambda e: e.tensor_add(
                st["rc1"][:, 6:7], st["rc1"][:, 6:7], st["rc1"][:, 7:8]))
            V.do(v, lambda e: e.tensor_add(
                st["cstat"][:, 32 + mt:33 + mt], st["rc1"][:, 4:5],
                st["rc1"][:, 6:7]))
            pvsum[0] = kb.V.prev

        @blk.sync
        def _(q, mt=mt, expt=expt):
            spb[0] = kb.dma(q, kb.spillP[mt * 128:(mt + 1) * 128, :],
                            st["pst"][:, 0:NB1], slot=1, waits=[expt])
    tok["spillP"] = (kb.dsem[1], kb.dcnt[1])
    tok["spillV"] = (kb.dsem[9], kb.dcnt[9])
    return tok


def emit_b1_merge(kb, blk, st, tok, bar):
    V, S = kb.V, kb.S

    @blk.sync
    def _(q):
        kb.dma(q, kb.cing[:], st["cstat"][:, 0:64], slot=6,
               waits=bar_toks(bar))

    @blk.gpsimd
    def _(g):
        g.wait_ge(kb.dsem[6], kb.dcnt[6])
        kb.G.do(g, lambda e: e.collective_compute(
            "AllGather", mybir.AluOpType.bypass, replica_groups=RG1,
            ins=[kb.cing.opt()], outs=[kb.coutg.opt()]))
    ctok = kb.G.prev

    @blk.sync
    def _(q):
        kb.dma(q, st["tt1"][:, 0:64], kb.coutg[0:128, :], slot=6,
               waits=[ctok])
        kb.dma(q, st["tt1"][:, 64:128], kb.coutg[128:256, :], slot=6)
    gtok = (kb.dsem[6], kb.dcnt[6])

    @blk.vector
    def _(v):
        V.do(v, lambda e: e.tensor_sub(st["tt1"][:, 0:64], st["tt1"][:, 0:64],
                                       st["tt1"][:, 64:128]), waits=[gtok])
        V.do(v, lambda e: e.scalar_tensor_tensor(
            st["tt1"][:, 128:192], st["tt1"][:, 0:64], kb.cmask[:, 0:1],
            st["tt1"][:, 64:128], op0=ALU.mult, op1=ALU.add))
        for half in range(2):
            for dst in (192, 224):
                V.do(v, lambda e, half=half, dst=dst: e.tensor_copy(
                    st["tt1"][:, dst + half * 16:dst + half * 16 + 16],
                    st["tt1"][:, dst - 64 + (1 - half) * 16:
                              dst - 64 + (1 - half) * 16 + 16]))
        V.do(v, lambda e: e.tensor_tensor(
            st["cstat"][:, 96:128], st["cstat"][:, 0:32],
            st["tt1"][:, 192:224], op=ALU.min))
        V.do(v, lambda e: e.tensor_sub(
            st["tt1"][:, 0:32], st["cstat"][:, 96:128], st["cstat"][:, 0:32]))
        V.do(v, lambda e: e.tensor_sub(
            st["tt1"][:, 32:64], st["cstat"][:, 96:128],
            st["tt1"][:, 192:224]))
    dtok = V.prev

    @blk.scalar
    def _(s):
        S.do(s, lambda e: e.activation(st["tt1"][:, 0:32], st["tt1"][:, 0:32],
                                       AF.Exp, bias=kb.c_zero[:, 0:1]),
             waits=[dtok])
        S.do(s, lambda e: e.activation(st["tt1"][:, 32:64],
                                       st["tt1"][:, 32:64],
                                       AF.Exp, bias=kb.c_zero[:, 0:1]))
    etok = S.prev

    @blk.vector
    def _(v):
        V.do(v, lambda e: e.tensor_mul(st["tt1"][:, 64:96],
                                       st["cstat"][:, 32:64],
                                       st["tt1"][:, 0:32]), waits=[etok])
        V.do(v, lambda e: e.tensor_mul(st["tt1"][:, 96:128],
                                       st["tt1"][:, 224:256],
                                       st["tt1"][:, 32:64]))
        V.do(v, lambda e: e.tensor_add(st["tt1"][:, 64:96],
                                       st["tt1"][:, 64:96],
                                       st["tt1"][:, 96:128]))
        V.do(v, lambda e: e.reciprocal(st["tt1"][:, 64:96],
                                       st["tt1"][:, 64:96]))
        V.do(v, lambda e: e.tensor_mul(st["cstat"][:, 64:96],
                                       st["tt1"][:, 0:32],
                                       st["tt1"][:, 64:96]))
    tok["facv"] = V.prev
    return tok


def emit_b1_lh(kb, blk, st, waps, tok, bar, kind):
    V, S, P = kb.V, kb.S, kb.P
    PS = kb.PS
    heavy = kind == "heavy"
    gcol = kb.gch if heavy else kb.gcl
    wq_ap = waps["wq_h" if heavy else "wq_l"]
    wkv_ap = waps["wkv_hT" if heavy else "wkvl_r"]
    wo_ap = waps["wo_h" if heavy else "wo_l"]
    ktiles = 32 if heavy else 16
    trq = [tok.get("lh_trq")]
    cps = [tok.get("lh_cps")]
    for g_ in range(4):
        firstw = bar_toks(bar) if (g_ == 0 and not heavy) else []
        attn_pe = tok.get("lh_pe")
        bias_prev = tok.get("lh_bias")
        wo_v = tok.get("lh_wov")
        @blk.gpsimd
        def _(gq, attn_pe=attn_pe, firstw=firstw, g_=g_):
            kb.G.do(gq, lambda e: e.memset(st["vlh"][:], 1.0),
                    waits=[attn_pe] + firstw)
            if g_ == 0:
                kb.G.do(gq, lambda e: e.memset(st["nm1"][:], 0.0))
                for hh in range(2):
                    kb.G.do(gq, lambda e, hh=hh: e.memset(
                        st["qlh"][hh][64:65, :], 1.0))
                    if not heavy:
                        kb.G.do(gq, lambda e, hh=hh: e.memset(
                            st["klh"][hh][64:65, :], 0.0))
        mst = kb.G.prev
        if g_ == 0:
            @blk.sync
            def _(q, mst=mst, attn_pe=attn_pe):
                for hh in range(2):
                    kb.dma(q, st["klh"][hh][65:66, :],
                           kb.inp["onesb1"][0:1, :], slot=8,
                           waits=[mst, attn_pe] if hh == 0 else [])
                    if heavy:
                        kb.dma(q, st["klh"][hh][64:65, :], kb.kvmd[0:1, :],
                               slot=8)
            mrow = (kb.dsem[8], kb.dcnt[8])
            tok["lh_mrow"] = mrow
        else:
            mrow = tok["lh_mrow"]
        prod_toks = []
        for ss in range(3):
            isq = ss == 0

            @blk.sync
            def _(q, g_=g_, ss=ss, wo_v=wo_v, firstw=firstw):
                if isq := (ss == 0):
                    src = wq_ap.rearrange("(c p) m -> p c m", p=128)[
                        :, :, g_ * 128:(g_ + 1) * 128]
                else:
                    cb = (2 * g_ + ss - 1) * 128
                    src = wkv_ap.rearrange("(c p) m -> p c m", p=128)[
                        :, :, cb:cb + 128]
                kb.dma(q, st["wfst"][:].rearrange("p (c m) -> p c m", c=C8),
                       src, slot=10,
                       waits=[tok.get("lh_foldpe"), wo_v] + firstw)
            wqt = (kb.dsem[10], kb.dcnt[10])

            @blk.vector
            def _(v, ss=ss, isq=isq, wqt=wqt):
                for c in range(C8):
                    V.do(v, lambda e, c=c: e.tensor_scalar(
                        st["wst"][:, c * 128:c * 128 + 128],
                        st["wfst"][:, c * 128:c * 128 + 128],
                        gcol[:, c:c + 1], 0.125 if isq else 1.0,
                        op0=ALU.mult, op1=ALU.mult),
                        waits=[wqt, tok.get("lh_foldpe")] if c == 0 else [])
            foldv = kb.V.prev
            evp = [None, None]
            ntiles = 16 if isq else ktiles
            for t in range(ntiles):
                par = t % 2
                srcb = st["xob"] if t < 16 else st["xpb"]
                c0 = (t % 16) * 128

                @blk.tensor
                def _(p, t=t, par=par, srcb=srcb, c0=c0, foldv=foldv,
                      mst=mst):
                    for c in range(C8):
                        P.do(p, lambda e, c=c: e.matmul(
                            PS[par][:, 0:128],
                            srcb[:, c * NB1 + c0:c * NB1 + c0 + 128],
                            st["wst"][:, c * 128:c * 128 + 128],
                            start=(c == 0), stop=(c == C8 - 1)),
                            waits=[foldv, evp[par], mst] if c == 0 else [],
                            mark=(c == C8 - 1))
                mmt = P.prev

                @blk.vector
                def _(v, t=t, par=par, mmt=mmt, trq=trq):
                    V.do(v, lambda e: e.tensor_scalar(
                        st["pstg"][:, par * 256:par * 256 + 128],
                        PS[par][:, 0:128], st["fcol"][:, t:t + 1], None,
                        op0=ALU.mult), waits=[mmt, trq[0]])
                    evp[par] = kb.V.prev
                if not isq:
                    @blk.scalar
                    def _(s, t=t, par=par, ss=ss, mst=mst):
                        S.do(s, lambda e: e.copy(
                            st["vlh"][:, t * 130 + (ss - 1) * 65:
                                      t * 130 + (ss - 1) * 65 + 64],
                            st["pstg"][:, par * 256 + 64:par * 256 + 128]),
                            waits=[evp[par], mst])

                @blk.tensor
                def _(p, t=t, par=par, isq=isq, cps=cps, trq=trq):
                    ncols = 128 if isq else 64
                    P.do(p, lambda e: e.matmul(
                        PS[6][0:ncols, 0:128],
                        st["pstg"][:, par * 256:par * 256 + ncols],
                        kb.idb[:], start=True, stop=True),
                        waits=[evp[par], cps[0]], mark=True)
                    trq[0] = kb.P.prev

                @blk.scalar
                def _(s, t=t, isq=isq, ss=ss, trq=trq, cps=cps, mrow=mrow):
                    if isq:
                        for hh in range(2):
                            S.do(s, lambda e, hh=hh: e.copy(
                                st["qlh"][hh][0:64, t * 128:t * 128 + 128],
                                PS[6][hh * 64:hh * 64 + 64, 0:128]),
                                waits=[trq[0]] if hh == 0 else [])
                    else:
                        S.do(s, lambda e: e.copy(
                            st["klh"][ss - 1][0:64, t * 128:t * 128 + 128],
                            PS[6][0:64, 0:128]),
                            waits=[trq[0], mrow] if t == 0 else [trq[0]])
                    cps[0] = kb.S.prev
            prod_toks.append(kb.S.prev)
            tok["lh_foldpe"] = kb.P.prev
        prod_s = kb.S.prev
        @blk.gpsimd
        def _(gq, g_=g_, wo_v=wo_v):
            kb.dma(gq, st["won"][:], wo_ap[g_ * 128:(g_ + 1) * 128, :],
                   slot=14, waits=[wo_v])
        wot = (kb.dsem[14], kb.dcnt[14])
        s2w = [None, None]
        for hh in range(2):
            for qt in range(16):
                jbs = range(8) if heavy else [0]
                for ji in list(jbs):
                    par = ji % 2
                    if heavy:
                        j0, jn = ji * 512, 512
                    else:
                        j0, jn = qt * 128, 128

                    @blk.tensor
                    def _(p, hh=hh, qt=qt, par=par, j0=j0, jn=jn,
                          prod_s=prod_s, mrow=mrow, mst=mst, s2w=s2w):
                        P.do(p, lambda e: e.matmul(
                            PS[4 + par][:, 0:jn],
                            st["qlh"][hh][0:65, qt * 128:qt * 128 + 128],
                            st["klh"][hh][0:65, j0:j0 + jn],
                            start=True, stop=True),
                            waits=[prod_s, mrow, mst, s2w[par]], mark=True)
                    mmt = P.prev

                    @blk.vector
                    def _(v, ji=ji, par=par, jn=jn, mmt=mmt,
                          bias_prev=bias_prev, s2w=s2w):
                        dst = 8 if ji == 0 else 9
                        V.do(v, lambda e: e.reduce_max(
                            st["rc1"][:, dst:dst + 1], PS[4 + par][:, 0:jn],
                            axis=AX.X, negate=True),
                            waits=[mmt, bias_prev])
                        if ji > 0:
                            V.do(v, lambda e: e.tensor_tensor(
                                st["rc1"][:, 8:9], st["rc1"][:, 8:9],
                                st["rc1"][:, 9:10], op=ALU.min))
                        s2w[par] = kb.V.prev

                @blk.vector
                def _(v, hh=hh, qt=qt):
                    V.do(v, lambda e: e.tensor_copy(
                        st["nm1"][:, hh * 16 + qt:hh * 16 + qt + 1],
                        st["rc1"][:, 8:9]))
        nmv = kb.V.prev
        @blk.sync
        def _(q, nmv=nmv, attn_pe=attn_pe):
            t1 = kb.dma(q, st["nm1t"][:], st["nm1"][:], slot=7,
                        waits=[nmv, attn_pe], transpose=True)
            for hh in range(2):
                kb.dma(q, st["qlh"][hh][65:66, 0:NB1],
                       st["nm1t"][hh * 16:hh * 16 + 16, 0:128], slot=7,
                       waits=[t1] if hh == 0 else [])
        bias_tok = (kb.dsem[7], kb.dcnt[7])
        tok["lh_bias"] = bias_tok
        stw2 = [None, None]
        pvv = [None, None]
        trt = [None]
        ocp = [None]
        for hh in range(2):
            qbs = 8 if heavy else 16
            nq = 256 if heavy else 128
            for qb in range(qbs):
                q0 = qb * nq
                kts = list(range(32)) if heavy else [qb]
                expl = None
                etoks = []
                for ki, kt in enumerate(kts):
                    par = ki % 2

                    @blk.tensor
                    def _(p, hh=hh, kt=kt, par=par, q0=q0, nq=nq,
                          bias_tok=bias_tok, stw2=stw2):
                        P.do(p, lambda e: e.matmul(
                            PS[4 + par][:, 0:nq],
                            st["klh"][hh][0:66, kt * 128:kt * 128 + 128],
                            st["qlh"][hh][0:66, q0:q0 + nq],
                            start=True, stop=True),
                            waits=[bias_tok, stw2[par]], mark=True)
                    mmt = P.prev

                    @blk.scalar
                    def _(s, ki=ki, qb=qb, par=par, nq=nq, mmt=mmt,
                          pvv=pvv, stw2=stw2):
                        off = ki * 256 if heavy else (qb % 2) * 128
                        S.do(s, lambda e: e.activation(
                            st["Pq"][:, off:off + nq],
                            PS[4 + par][:, 0:nq], AF.Exp,
                            bias=kb.c_zero[:, 0:1]),
                            waits=[mmt, pvv[0], pvv[1]] if ki == 0
                            else [mmt])
                        stw2[par] = kb.S.prev
                        etoks.append(kb.S.prev)
                    expl = kb.S.prev
                for q2 in range(2 if heavy else 1):
                    pb = (qb * 2 + q2) % 2
                    qt = (q0 + q2 * 128) // 128

                    @blk.tensor
                    def _(p, hh=hh, q2=q2, pb=pb, qb=qb, kts=kts,
                          etoks=etoks, pvv=pvv):
                        for r, kt in enumerate(kts):
                            off = (r * 256 + q2 * 128) if heavy \
                                else (qb % 2) * 128
                            P.do(p, lambda e, kt=kt, off=off, r=r: e.matmul(
                                PS[2 + pb][:, 0:65],
                                st["Pq"][:, off:off + 128],
                                st["vlh"][:, kt * 130 + hh * 65:
                                          kt * 130 + hh * 65 + 65],
                                start=(r == 0), stop=(r == len(kts) - 1)),
                                waits=[etoks[r], pvv[pb]] if r == 0
                                else [etoks[r]],
                                mark=(r == len(kts) - 1))
                    pvt = P.prev

                    @blk.vector
                    def _(v, qt=qt, pb=pb, pvt=pvt, trt=trt, pvv=pvv):
                        V.do(v, lambda e: e.reciprocal(
                            st["rc1"][:, 12:13], PS[2 + pb][:, 64:65]),
                            waits=[pvt])
                        if heavy:
                            V.do(v, lambda e: e.tensor_mul(
                                st["rc1"][:, 12:13], st["rc1"][:, 12:13],
                                st["qmcol"][:, qt:qt + 1]))
                        V.do(v, lambda e: e.tensor_scalar(
                            st["osb1"][:, 0:64], PS[2 + pb][:, 0:64],
                            st["rc1"][:, 12:13], None, op0=ALU.mult),
                            waits=[trt[0]])
                        pvv[pb] = kb.V.prev

                    @blk.tensor
                    def _(p, pb=pb, pvv=pvv, ocp=ocp, trt=trt):
                        P.do(p, lambda e: e.transpose(
                            PS[6][0:64, 0:128], st["osb1"][:, 0:64],
                            kb.idf[:]), waits=[pvv[pb], ocp[0]], mark=True)
                        trt[0] = kb.P.prev

                    @blk.scalar
                    def _(s, hh=hh, qt=qt, trt=trt, ocp=ocp):
                        S.do(s, lambda e: e.copy(
                            st["olh"][hh * 64:hh * 64 + 64,
                                      qt * 128:qt * 128 + 128],
                            PS[6][0:64, 0:128]), waits=[trt[0]])
                        ocp[0] = kb.S.prev
                        cps[0] = kb.S.prev
        tok["lh_pe"] = kb.P.prev
        olh_done = kb.S.prev
        wov = [None, None]
        for dc in range(C8):
            for qb in range(4):
                par = (dc * 4 + qb) % 2

                @blk.tensor
                def _(p, dc=dc, qb=qb, par=par, wot=wot,
                      olh_done=olh_done, wov=wov):
                    P.do(p, lambda e: e.matmul(
                        PS[6 + par][:, 0:512],
                        st["won"][:, dc * 128:dc * 128 + 128],
                        st["olh"][:, qb * 512:qb * 512 + 512],
                        start=True, stop=True),
                        waits=[wot, olh_done, wov[par]]
                        if (dc == 0 and qb == 0) else [wov[par]],
                        mark=True)
                mmt = P.prev

                @blk.vector
                def _(v, dc=dc, qb=qb, par=par, mmt=mmt, wov=wov):
                    V.do(v, lambda e: e.tensor_add(
                        st["x1"][:, dc * NB1 + qb * 512:dc * NB1 + qb * 512 + 512],
                        st["x1"][:, dc * NB1 + qb * 512:dc * NB1 + qb * 512 + 512],
                        PS[6 + par][:, 0:512]), waits=[mmt])
                    wov[par] = kb.V.prev
        tok["lh_wov"] = kb.V.prev
        tok["lh_pe"] = kb.P.prev
    tok["lh_trq"] = trq[0]
    tok["lh_cps"] = cps[0]
    return tok


def emit_b1_pv(kb, blk, st, tok, bar):
    V, S, P = kb.V, kb.S, kb.P
    PS = kb.PS

    @blk.sync
    def _(q):
        for mt in range(32):
            kb.dma(q, st["vres"][:, mt * 1024:(mt + 1) * 1024],
                   kb.spillV[mt * 128:(mt + 1) * 128, :], slot=0,
                   waits=(bar_toks(bar) + [tok["spillV"]])
                   if mt == 0 else [])
    vg = (kb.dsem[0], kb.dcnt[0])
    eyev = [None, None]
    for mt in range(16):
        for dm in range(C8):
            par = (mt * C8 + dm) % 2

            @blk.tensor
            def _(p, mt=mt, dm=dm, par=par, vg=vg, eyev=eyev):
                P.do(p, lambda e: e.matmul(
                    PS[par][0:128, 0:128],
                    st["vres"][:, mt * 1024 + dm * 128:mt * 1024 + dm * 128 + 128],
                    kb.idb[:], start=True, stop=True),
                    waits=[vg, eyev[par]] if (mt == 0 and dm < 2)
                    else [eyev[par]], mark=True)
            mmt = P.prev

            @blk.vector
            def _(v, mt=mt, dm=dm, par=par, mmt=mmt, eyev=eyev):
                V.do(v, lambda e: e.tensor_sub(
                    st["x1"][:, dm * NB1 + mt * 128:dm * NB1 + mt * 128 + 128],
                    st["x1"][:, dm * NB1 + mt * 128:dm * NB1 + mt * 128 + 128],
                    PS[par][0:128, 0:128]), waits=[mmt])
                eyev[par] = kb.V.prev
    eyepe = kb.P.prev

    @blk.vector
    def _(v):
        for mt in range(32):
            V.do(v, lambda e, mt=mt: e.tensor_scalar(
                st["vres"][:, mt * 1024:(mt + 1) * 1024],
                st["vres"][:, mt * 1024:(mt + 1) * 1024],
                st["cstat"][:, 64 + mt:65 + mt], None, op0=ALU.mult),
                waits=[eyepe, vg] if mt == 0 else [])
    fscale = kb.V.prev
    pmark = {}
    addv = [None]
    for nb in range(4):
        for mt in range(32):
            slot = mt % 2

            @blk.sync
            def _(q, nb=nb, mt=mt, slot=slot, addv=addv):
                w = [pmark.get((nb, mt - 2)) if (nb, mt - 2) in pmark
                     else pmark.get((nb - 1, 30 + mt))]
                if nb == 0 and mt < 2:
                    w = [tok["spillP"], fscale]
                kb.dma(q, st["prow"][slot][:],
                       kb.spillP[mt * 128:(mt + 1) * 128,
                                 nb * 512:(nb + 1) * 512],
                       slot=12 + slot, waits=w)
            ptok = (kb.dsem[12 + slot], kb.dcnt[12 + slot])

            @blk.tensor
            def _(p, nb=nb, mt=mt, slot=slot, ptok=ptok, addv=addv):
                for dm in range(C8):
                    P.do(p, lambda e, dm=dm: e.matmul(
                        PS[dm][:, 0:512],
                        st["vres"][:, mt * 1024 + dm * 128:
                                   mt * 1024 + dm * 128 + 128],
                        st["prow"][slot][:],
                        start=(mt == 0), stop=(mt == 31)),
                        waits=[ptok, fscale, addv[0]]
                        if (dm == 0 and mt == 0) else
                        ([ptok] if dm == 0 else []),
                        mark=(dm == C8 - 1))
                pmark[(nb, mt)] = kb.P.prev

        @blk.vector
        def _(v, nb=nb, addv=addv):
            for dm in range(C8):
                V.do(v, lambda e, dm=dm: e.tensor_add(
                    st["x1"][:, dm * NB1 + nb * 512:dm * NB1 + nb * 512 + 512],
                    st["x1"][:, dm * NB1 + nb * 512:dm * NB1 + nb * 512 + 512],
                    PS[dm][:, 0:512]),
                    waits=[pmark[(nb, 31)]] if dm == 0 else [])
            addv[0] = kb.V.prev
    tok["pvv"] = kb.V.prev
    return tok


def emit_b1_out(kb, blk, st, outT, tok):
    @blk.sync
    def _(q):
        last = None
        for j in range(C8):
            last = kb.dma(q, outT[j * 128:(j + 1) * 128, :],
                          st["x1"][:, j * NB1:(j + 1) * NB1], slot=15,
                          waits=[tok["pvv"]] if j == 0 else [])
        q.wait_ge(*last)


B1_PHASES = ["ln_s", "gather", "routing", "norms", "l1aS", "merge",
             "light", "heavy", "pv", "out"]


def build_block1(stop_after=None):
    kb = KB()
    nc = kb.nc
    x1T = kb.din("x1T", [D, OWN])
    waps = {}
    for name, shape in [("wq1", [D, D]), ("wk1", [D, D]), ("wv1T", [D, D]),
                        ("wq_l", [D, 512]), ("wkvl_r", [D, D]),
                        ("wo_l", [512, D]), ("wq_h", [D, 512]),
                        ("wkv_hT", [D, D]), ("wo_h", [512, D])]:
        waps[name] = kb.din(name, shape, F32)
    ins = {}
    for name, shape, dt in [("IDF", [128, 128], F32),
                            ("coremask", [128, 1], F32),
                            ("gl_col", [128, C8], F32),
                            ("gh_col", [128, C8], F32),
                            ("rt_q", [128, C8], F32),
                            ("rt_kv", [128, C8], F32),
                            ("IDB", [128, 128], BF16),
                            ("onesb1", [1, FT], BF16)]:
        ins[name] = kb.din(name, shape, dt)
    outT = kb.dout("outT", [D, OWN])
    kb.spillP = kb.dint("spillP", [FT, NB1], BF16)
    kb.spillV = kb.dint("spillV", [FT, D], BF16)
    kb.cin1 = kb.dint("cin1", [D, NB1], BF16)
    kb.cout1 = kb.dint("cout1", [2 * D, NB1], BF16)
    kb.cins = kb.dint("cins", [2, NB1])
    kb.couts = kb.dint("couts", [4, NB1])
    kb.cing = kb.dint("cing", [128, 64])
    kb.coutg = kb.dint("coutg", [256, 64])
    kb.rscr = kb.dint("rscr", [1, 2 * FT])
    kb.rqm = kb.dint("rqm", [1, NB1])
    kb.kvmd = kb.dint("kvmd", [1, FT], BF16)
    kb.rfn = kb.dint("rfn", [1, FT])
    st = alloc_block1(kb)
    kb.open()
    nphase = B1_PHASES.index(stop_after) + 1 if stop_after else len(B1_PHASES)
    ph = B1_PHASES[:nphase]
    with nc.Block() as blk:
        bar = emit_b1_prelude(kb, blk, ins)
        tok = {}
        tok = emit_b1_ln_s(kb, blk, st, x1T, tok)
        if "gather" in ph:
            tok = emit_b1_gather(kb, blk, st, tok)
        if "routing" in ph:
            tok = emit_b1_routing(kb, blk, st, tok)
        if "norms" in ph:
            tok = emit_b1_norms(kb, blk, st, tok)
        if "l1aS" in ph:
            bar = emit_barrier(kb, blk, [tok.get("maskv"), tok.get("fcolv"),
                                         (kb.dsem[11], kb.dcnt[11]),
                                         (kb.dsem[6], kb.dcnt[6])])
            tok = emit_b1_l1a_S(kb, blk, st, waps, tok, bar)
        if "merge" in ph:
            bar = emit_barrier(kb, blk, [tok["spillP"], tok["spillV"]])
            tok = emit_b1_merge(kb, blk, st, tok, bar)
        if "light" in ph:
            bar = emit_barrier(kb, blk, [tok["facv"]])
            tok = emit_b1_lh(kb, blk, st, waps, tok, bar, "light")
        if "heavy" in ph:
            tok = emit_b1_lh(kb, blk, st, waps, tok, bar, "heavy")
        if "pv" in ph:
            bar = emit_barrier(kb, blk, [tok["lh_wov"], tok["lh_pe"],
                                         tok.get("lh_bias"),
                                         (kb.dsem[8], kb.dcnt[8]),
                                         (kb.dsem[10], kb.dcnt[10]),
                                         (kb.dsem[14], kb.dcnt[14])])
            tok = emit_b1_pv(kb, blk, st, tok, bar)
        if "out" in ph:
            emit_b1_out(kb, blk, st, outT, tok)
        else:
            emit_barrier(kb, blk, [(kb.dsem[i], kb.dcnt[i])
                                   for i in range(16)] +
                         [kb.V.prev, kb.S.prev, kb.G.prev])
    kb.close()
    return kb


def build_fused():
    kb = KB()
    nc = kb.nc
    xT = kb.din("xT", [D, NT0])
    waps = {
        "Wqkv": kb.din("Wqkv", [IN_DEPTH, D, 3 * 1024]),
        "Wo": kb.din("Wo", [IN_DEPTH, D, D]),
        "W1p": kb.din("W1p", [IN_DEPTH, D, NPAIR * 256]),
        "W2": kb.din("W2", [IN_DEPTH, DIM_INNER, D]),
        "w_qkvT": kb.din("w_qkvT", [D, 3 * 1024]),
        "w_outT": kb.din("w_outT", [D, D]),
    }
    ins = {}
    for name, shape in [("TcB", [128, NT0]), ("TsB", [128, NT0]),
                        ("TciB", [128, NT0]), ("TsiB", [128, NT0]),
                        ("TcA", [128, NT0]), ("TsA", [128, NT0]),
                        ("MSK", [128, 4 * 128]), ("padrowbf", [1, NT0]),
                        ("onesbf", [1, KMAX])]:
        ins[name] = kb.din(name, shape, BF16)
    for name, shape in [("PROTF", [128, 128]), ("BD2", [128, 2]),
                        ("BD2T", [2, 128]), ("IDF", [128, 128]),
                        ("coremask", [128, 1]),
                        ("qs_col", [128, IN_DEPTH]),
                        ("ks_col", [128, IN_DEPTH]),
                        ("ln_g", [IN_DEPTH, 128, C8]),
                        ("ln_b", [IN_DEPTH, 128, C8]),
                        ("b1p", [IN_DEPTH, 128, 44]),
                        ("b2c", [IN_DEPTH, 128, C8]),
                        ("a0_lnw", [128, C8]), ("a0_lnb", [128, C8])]:
        ins[name] = kb.din(name, shape, F32)
    for name, shape in [("wq1", [D, D]), ("wk1", [D, D]), ("wv1T", [D, D]),
                        ("wq_l", [D, 512]), ("wkvl_r", [D, D]),
                        ("wo_l", [512, D]), ("wq_h", [D, 512]),
                        ("wkv_hT", [D, D]), ("wo_h", [512, D]),
                        ("gl_col", [128, C8]), ("gh_col", [128, C8]),
                        ("rt_q", [128, C8]), ("rt_kv", [128, C8])]:
        waps[name] = ins[name] = kb.din(name, shape, F32)
    ins["IDB"] = kb.din("IDB", [128, 128], BF16)
    ins["onesb1"] = kb.din("onesb1", [1, FT], BF16)
    outT = kb.dout("outT", [D, OWN])
    x1o = kb.dint("x1T", [D, OWN])
    spill_acc = kb.dint("spill_acc", [2 * D, OWN])
    st = alloc_block0(kb)
    ln_scr0 = kb.ln_scr
    kb.a0w = kb.sbuf("a0w", [128, C8], F32)
    kb.a0b = kb.sbuf("a0b", [128, C8], F32)
    kb.cinH = kb.dint("cinH", [D, HALO])
    kb.coutH = kb.dint("coutH", [2 * D, HALO])
    kb.spillP = kb.dint("spillP", [FT, NB1], BF16)
    kb.spillV = kb.dint("spillV", [FT, D], BF16)
    kb.cin1 = kb.dint("cin1", [D, NB1], BF16)
    kb.cout1 = kb.dint("cout1", [2 * D, NB1], BF16)
    kb.cins = kb.dint("cins", [2, NB1])
    kb.couts = kb.dint("couts", [4, NB1])
    kb.cing = kb.dint("cing", [128, 64])
    kb.coutg = kb.dint("coutg", [256, 64])
    kb.rscr = kb.dint("rscr", [1, 2 * FT])
    kb.rqm = kb.dint("rqm", [1, NB1])
    kb.kvmd = kb.dint("kvmd", [1, FT], BF16)
    kb.rfn = kb.dint("rfn", [1, FT])
    st1 = alloc_block1(kb, fused_base=kb.nc.lookup_mloc(st["resid"]).addr)
    kb.open()
    with nc.Block() as blk:
        tok = emit_prelude(kb, blk, st, ins)
        ldtok = emit_load_x(kb, blk, st["resid"], xT)

        @blk.tensor
        def _(p):
            p.wait_ge(*ldtok)

        @blk.vector
        def _(v):
            v.wait_ge(*ldtok)

        @blk.scalar
        def _(s):
            s.wait_ge(*ldtok)
        lt = emit_ln(kb, blk, st["resid"], st["resid"], NT0, tblocks(0, NT0),
                     kb.ln_scr)
        tok["resid_v"] = lt
        tok = emit_a0(kb, blk, st, waps, tok, spill_acc)
        tok = emit_load_tablesB(kb, blk, ins, tok)
        for l in range(IN_DEPTH):
            tok = emit_inner_attn(kb, blk, st, l, waps, tok)
            tok = emit_inner_ffn(kb, blk, st, l, waps, tok)
            if l < IN_DEPTH - 1:
                tok = emit_halo_exchange(kb, blk, st, l, tok)
        tok = emit_combine_block0(kb, blk, st, xT, spill_acc, x1o, tok)
        kb.ln_scr = st1["ln_scr"]
        for nm, ap in st1["b1_consts"].items():
            setattr(kb, nm, ap)
        bar0 = emit_barrier(kb, blk, [tok["x1_dma"], kb.V.prev, kb.S.prev,
                                      kb.G.prev, kb.P.prev] +
                            [(kb.dsem[i], kb.dcnt[i]) for i in range(16)])
        bar = emit_b1_prelude(kb, blk, ins)
        tok1 = {}
        tok1 = emit_b1_ln_s(kb, blk, st1, x1o, tok1)
        tok1 = emit_b1_gather(kb, blk, st1, tok1)
        tok1 = emit_b1_routing(kb, blk, st1, tok1)
        tok1 = emit_b1_norms(kb, blk, st1, tok1)
        bar = emit_barrier(kb, blk, [tok1.get("maskv"), tok1.get("fcolv"),
                                     (kb.dsem[11], kb.dcnt[11]),
                                     (kb.dsem[6], kb.dcnt[6])])
        tok1 = emit_b1_l1a_S(kb, blk, st1, waps, tok1, bar)
        bar = emit_barrier(kb, blk, [tok1["spillP"], tok1["spillV"]])
        tok1 = emit_b1_merge(kb, blk, st1, tok1, bar)
        bar = emit_barrier(kb, blk, [tok1["facv"]])
        tok1 = emit_b1_lh(kb, blk, st1, waps, tok1, bar, "light")
        tok1 = emit_b1_lh(kb, blk, st1, waps, tok1, bar, "heavy")
        bar = emit_barrier(kb, blk, [tok1["lh_wov"], tok1["lh_pe"],
                                     tok1.get("lh_bias"),
                                     (kb.dsem[8], kb.dcnt[8]),
                                     (kb.dsem[10], kb.dcnt[10]),
                                     (kb.dsem[14], kb.dcnt[14])])
        tok1 = emit_b1_pv(kb, blk, st1, tok1, bar)
        emit_b1_out(kb, blk, st1, outT, tok1)
    kb.close()
    return kb


def kernel(**inputs):
    cores = [make_core_inputs(inputs, c) for c in range(NCORES)]
    kb = build_fused()
    maps = [{k: g[k] for k in kb.inp} for g in cores]
    res = run_bass_kernel_spmd(kb.nc, maps, core_ids=list(range(NCORES)))
    out = np.empty((B, T, D), np.float32)
    for c in range(NCORES):
        b, h = c // 2, c % 2
        out[b, h * OWN:(h + 1) * OWN] = np.asarray(res.results[c]["outT"]).T
    return out


# version 08
# speedup vs baseline: 1.1909x; 1.1909x over previous
import numpy as np
import ml_dtypes
import concourse.bass as bass
import concourse.mybir as mybir
from concourse.bass_utils import run_bass_kernel_spmd
from contextlib import ExitStack

F32 = mybir.dt.float32
BF16 = mybir.dt.bfloat16
AF = mybir.ActivationFunctionType
ALU = mybir.AluOpType
AX = mybir.AxisListType

B, T, D = 4, 4096, 1024
NCORES = 8
OWN = 2048
HALO = 384
NT0 = HALO + OWN
C8 = 8
TB = 512
NTB = 5
NEG = -1.0e30
EPS = 1e-5

A0_HEADS, A0_DH, A0_WIN = 16, 64, 32
IN_DEPTH, IN_HEADS, IN_DH, IN_WIN = 6, 16, 64, 256
DIM_INNER = 2730
NPAIR = (DIM_INNER + 127) // 128
QK_SCALE = 8.0
L1_HEADS, L1_DH, L1_HID, L1_WIN = 8, 64, 512, 128
NQ, NKV = 1024, 2048
KQ, KKV = 1152.0, 2304.0
N_ITERS = 50


def _rope_tables_a0(pos):
    inv_freq = 1.0 / 10000.0 ** (np.arange(0, A0_DH, 2, dtype=np.float64) / A0_DH)
    p = np.maximum(pos, 0).astype(np.float64)
    fr = p[:, None] * inv_freq[None, :]
    fr = np.concatenate([fr, fr], -1)
    cos = np.cos(fr).astype(np.float32)
    sin = np.sin(fr).astype(np.float32)
    return cos.T.copy(), sin.T.copy()


def _xpos_tables(pos, n):
    half = np.arange(0, IN_DH, 2, dtype=np.float32)
    inv_freq = (1.0 / 10000.0 ** (half / IN_DH)).astype(np.float32)
    p = np.maximum(pos, 0).astype(np.float32)
    fr = p[:, None] * inv_freq[None, :]
    fr = np.concatenate([fr, fr], -1)
    cos, sin = np.cos(fr), np.sin(fr)
    base = ((half + 0.4 * IN_DH) / (1.4 * IN_DH)).astype(np.float32)
    power = ((p - n // 2) / (IN_WIN // 2)).astype(np.float32)
    scale = base[None, :] ** power[:, None]
    scale = np.concatenate([scale, scale], -1).astype(np.float32)
    inv = (1.0 / scale).astype(np.float32)
    return ((QK_SCALE * cos * scale).T.copy(),
            (QK_SCALE * sin * scale).T.copy(),
            (cos * inv).T.copy(), (sin * inv).T.copy())


def _rot_sign_matrix(dh=64):
    P = np.zeros((128, 128), dtype=np.float32)
    for dp in range(128):
        base = (dp // dh) * dh
        o = dp % dh
        if o < dh // 2:
            P[base + o + dh // 2, dp] = -1.0
        else:
            P[base + o - dh // 2, dp] = 1.0
    return P


def _masks():
    k = np.arange(128)[:, None]
    q = np.arange(128)[None, :]
    m0 = (k <= q).astype(np.float32)
    m2 = (k >= q).astype(np.float32)
    md = (((q - k) >= 0) & ((q - k) < A0_WIN)).astype(np.float32)
    mp = ((k - q) > (128 - A0_WIN)).astype(np.float32)
    return np.stack([m0, m2, md, mp])


def _bf(x):
    return np.ascontiguousarray(np.asarray(x).astype(ml_dtypes.bfloat16))


def _f32(x):
    return np.ascontiguousarray(np.asarray(x, dtype=np.float32))


def make_core_inputs(ins, c):
    b, h = c // 2, c % 2
    own0 = h * OWN
    g = {}
    x = np.asarray(ins["L0_attn_x"][b])
    lo = own0 - HALO
    sh = np.zeros((NT0, D), dtype=np.float32)
    s0 = max(lo, 0)
    sh[s0 - lo:, :] = x[s0:own0 + OWN]
    g["xT"] = _f32(sh.T)

    pos = np.arange(lo, own0 + OWN)
    g["padrowbf"] = _bf(np.where(pos < 0, np.float32(NEG), 0.0)[None, :])
    g["coremask"] = np.full((128, 1), 1.0 if h == 1 else 0.0, np.float32)

    cosA, sinA = _rope_tables_a0(pos)
    dup = lambda t: np.concatenate([t, t], axis=0)
    g["TcA"], g["TsA"] = _bf(dup(cosA)), _bf(dup(sinA))
    Tc, Ts, Tci, Tsi = _xpos_tables(pos, T)
    g["TcB"], g["TsB"] = _bf(dup(Tc)), _bf(dup(Ts))
    g["TciB"], g["TsiB"] = _bf(dup(Tci)), _bf(dup(Tsi))
    g["MSK"] = _bf(_masks().transpose(1, 0, 2).reshape(128, 4 * 128))
    g["onesbf"] = _bf(np.ones((1, 1408)))
    g["PROTF"] = _f32(_rot_sign_matrix())
    bd2 = np.zeros((128, 2), np.float32)
    bd2[0:64, 0] = 1.0
    bd2[64:128, 1] = 1.0
    g["BD2"] = bd2
    g["BD2T"] = _f32(bd2.T)
    g["IDF"] = _f32(np.eye(128))

    g["w_qkvT"] = _f32(np.asarray(ins["L0_attn_w_qkv"]).T)
    g["w_outT"] = _f32(np.asarray(ins["L0_attn_w_out"]).T)
    g["a0_lnw"] = _f32(np.asarray(ins["L0_attn_ln_w"]).reshape(C8, 128).T)
    g["a0_lnb"] = _f32(np.asarray(ins["L0_attn_ln_b"]).reshape(C8, 128).T)

    g["Wqkv"] = _f32(ins["L0_ffn_Wqkv"])
    g["Wo"] = _f32(ins["L0_ffn_Wo"])
    qs, ks = np.asarray(ins["L0_ffn_q_scale"]), np.asarray(ins["L0_ffn_k_scale"])
    g["qs_col"] = _f32(np.tile(qs, (1, 2)).T)
    g["ks_col"] = _f32(np.tile(ks, (1, 2)).T)
    g["ln_g"] = _f32(np.asarray(ins["L0_ffn_ln_g"]).reshape(IN_DEPTH, C8, 128).transpose(0, 2, 1))
    g["ln_b"] = _f32(np.asarray(ins["L0_ffn_ln_b"]).reshape(IN_DEPTH, C8, 128).transpose(0, 2, 1))
    W1 = np.asarray(ins["L0_ffn_W1"])
    b1 = np.asarray(ins["L0_ffn_b1"])
    W1p = np.zeros((IN_DEPTH, D, NPAIR * 256), np.float32)
    for p in range(NPAIR):
        a0, a1 = p * 128, min((p + 1) * 128, DIM_INNER)
        W1p[:, :, p * 256:p * 256 + (a1 - a0)] = W1[:, :, a0:a1]
        W1p[:, :, p * 256 + 128:p * 256 + 128 + (a1 - a0)] = \
            W1[:, :, DIM_INNER + a0:DIM_INNER + a1]
    g["W1p"] = W1p
    b1p = np.zeros((IN_DEPTH, 2 * NPAIR, 128), np.float32)
    for p in range(NPAIR):
        a0, a1 = p * 128, min((p + 1) * 128, DIM_INNER)
        b1p[:, 2 * p, :a1 - a0] = b1[:, a0:a1]
        b1p[:, 2 * p + 1, :a1 - a0] = b1[:, DIM_INNER + a0:DIM_INNER + a1]
    g["b1p"] = _f32(b1p.transpose(0, 2, 1))
    g["W2"] = _f32(ins["L0_ffn_W2"])
    g["b2c"] = _f32(np.asarray(ins["L0_ffn_b2"]).reshape(IN_DEPTH, C8, 128).transpose(0, 2, 1))

    g["wq1"] = _f32(ins["L1_attn_wq"])
    g["wk1"] = _f32(ins["L1_attn_wk"])
    g["wv1T"] = _f32(np.asarray(ins["L1_attn_wv"]).T)

    g["IDB"] = _bf(np.eye(128))
    g["onesb1"] = _bf(np.ones((1, 4096)))
    wkvl = np.asarray(ins["L1_ffn_wkv_light"])
    cols = []
    for hh in range(8):
        cols.append(wkvl[:, 64 * hh:64 * hh + 64])
        cols.append(wkvl[:, 512 + 64 * hh:512 + 64 * hh + 64])
    g["wkvl_r"] = _f32(np.concatenate(cols, axis=1))
    g["gl_col"] = _f32(np.asarray(ins["L1_ffn_gamma_light"]).reshape(C8, 128).T)
    g["gh_col"] = _f32(np.asarray(ins["L1_ffn_gamma_heavy"]).reshape(C8, 128).T)
    g["wq_l"] = _f32(ins["L1_ffn_wq_light"])
    g["wkv_l"] = _f32(ins["L1_ffn_wkv_light"])
    g["wo_l"] = _f32(ins["L1_ffn_wo_light"])
    g["rt_q"] = _f32(np.asarray(ins["L1_ffn_q_routing_token"]).reshape(C8, 128).T)
    g["rt_kv"] = _f32(np.asarray(ins["L1_ffn_kv_routing_token"]).reshape(C8, 128).T)
    g["wq_h"] = _f32(ins["L1_ffn_wq_heavy"])
    g["wkv_hT"] = _f32(np.asarray(ins["L1_ffn_wkv_heavy"]).T)
    g["wo_h"] = _f32(ins["L1_ffn_wo_heavy"])
    return g


class Chain:

    def __init__(self, nc, sem, auto=True, wcache=None):
        self.nc = nc
        self.sem = sem
        self.auto = auto
        self.val = 0
        self.prev = None
        self.wcache = wcache if wcache is not None else {}

    def _wait(self, eng, t):
        key = (eng.engine, id(t[0]))
        if self.wcache.get(key, -1) >= t[1]:
            return
        self.wcache[key] = t[1]
        eng.wait_ge(t[0], t[1])

    def do(self, eng, build, waits=(), mark=None):
        for t in waits:
            if t is not None:
                self._wait(eng, t)
        if self.auto and self.prev is not None:
            self._wait(eng, self.prev)
        ins = build(eng)
        if self.auto or mark:
            ins.then_inc(self.sem, 1)
            self.val += 1
            self.prev = (self.sem, self.val)
        return self.prev


class KB:
    def __init__(self, dev_taps=()):
        self.nc = bass.Bass("TRN2", target_bir_lowering=False)
        self.dev_taps = set(dev_taps)
        self.ctx = ExitStack()
        self.inp = {}
        self.outp = {}
        self.sb = {}
        self.sems = {}
        self.V = None
        self.S = None
        self.P = None
        self.G = None

    def din(self, name, shape, dtype=F32):
        ap = self.nc.dram_tensor(name, list(shape), dtype, kind="ExternalInput").ap()
        self.inp[name] = ap
        return ap

    def dout(self, name, shape, dtype=F32):
        ap = self.nc.dram_tensor(name, list(shape), dtype, kind="ExternalOutput").ap()
        self.outp[name] = ap
        return ap

    def dint(self, name, shape, dtype=F32, addr_space="Local"):
        return self.nc.dram_tensor(name, list(shape), dtype, kind="Internal",
                                   addr_space=addr_space).ap()

    def sbuf(self, name, shape, dtype=F32):
        t = self.ctx.enter_context(self.nc.sbuf_tensor(name, list(shape), dtype))
        self.sb[name] = t
        return t

    def mkslab(self, name, nbytes):
        slab = self.ctx.enter_context(
            self.nc.sbuf_tensor(name, [128, nbytes], mybir.dt.uint8))
        return self.nc.lookup_mloc(slab).addr

    def sbuf_at(self, name, shape, dtype, base, off):
        t = self.nc.alloc_sbuf_tensor_at(name, list(shape), dtype,
                                         offset=base + off)
        self.sb[name] = t
        return t

    def psum(self, name, shape, dtype=F32):
        return self.ctx.enter_context(self.nc.psum_tensor(name, list(shape), dtype))

    def sem(self, name):
        s = self.ctx.enter_context(self.nc.semaphore(name))
        self.sems[name] = s
        return s

    def open(self):
        self.wcache = {}
        self.V = Chain(self.nc, self.sem("cv"), wcache=self.wcache)
        self.S = Chain(self.nc, self.sem("cs"), wcache=self.wcache)
        self.P = Chain(self.nc, self.sem("cp"), auto=False,
                       wcache=self.wcache)
        self.G = Chain(self.nc, self.sem("cg"), wcache=self.wcache)
        self.dsem = [self.sem(f"d{i}") for i in range(16)]
        self.dcnt = [0] * 16

    def dma(self, qeng, out, in_, slot, waits=(), **kw):
        for t in waits:
            if t is not None:
                key = (qeng.engine, id(t[0]))
                if self.wcache.get(key, -1) >= t[1]:
                    continue
                self.wcache[key] = t[1]
                qeng.wait_ge(t[0], t[1])
        qeng.dma_start(out=out, in_=in_, **kw).then_inc(self.dsem[slot], 16)
        self.dcnt[slot] += 16
        return (self.dsem[slot], self.dcnt[slot])

    def close(self):
        self.ctx.close()


def emit_load_x(kb, blk, resid, xT, nt=NT0):
    toks = []

    @blk.sync
    def _(q):
        for j in range(C8):
            toks.append(kb.dma(q, resid[:, j * nt:(j + 1) * nt],
                               xT[j * 128:(j + 1) * 128, :], slot=8))
    return toks[-1]


def emit_ln(kb, blk, src, dst, nt, ranges, scr,
            gamma=None, beta=None, shift=False, extra_wait=None,
            dst_bf16=None, nt_dst=None, dst_off=0):
    V, S, P = kb.V, kb.S, kb.P
    pA, pB = scr["pA"], scr["pB"]
    bc = [scr["bc0"], scr["bc1"]]
    sq, rows, aff = scr["sq"], scr["rows"], scr["aff"]
    ndst = nt if nt_dst is None else nt_dst
    pe_stats_tok = [extra_wait]
    vec_rows_tok = [None]
    aff_tok = [None, None]

    def col(j, t0, t1, buf=src, n=None, noshift=False, off=0):
        n = nt if n is None else n
        sh = 1 if (shift and j >= C8 // 2 and not noshift) else 0
        return buf[:, j * n + t0 - off - sh: j * n + t1 - off - sh]

    for (t0, t1) in ranges:
        W = t1 - t0

        stage_tok = [None] * 4
        for stage in range(4):
            @blk.scalar
            def _(s, t0=t0, t1=t1, stage=stage):
                for j in range(2 * stage, 2 * stage + 2):
                    S.do(s, lambda e, j=j: e.activation(
                        sq[:, (j % 2) * TB:(j % 2) * TB + W], col(j, t0, t1),
                        AF.Square),
                        waits=[pe_stats_tok[0] if stage == 0 and j == 0
                               else None,
                               stage_tok[stage - 1] if stage > 0 and
                               j == 2 * stage else None])
            sq_done = S.prev

            @blk.tensor
            def _(p, t0=t0, t1=t1, stage=stage):
                if stage == 0:
                    for j in range(C8):
                        P.do(p, lambda e, j=j: e.matmul(
                            pA[0:1, 0:W], kb.oc[:, 0:1], col(j, t0, t1),
                            start=(j == 0), stop=(j == C8 - 1)),
                            waits=[vec_rows_tok[0], pe_stats_tok[0]]
                            if j == 0 else [])
                for j in range(2 * stage, 2 * stage + 2):
                    P.do(p, lambda e, j=j: e.matmul(
                        pB[0:1, 0:W], kb.oc[:, 0:1],
                        sq[:, (j % 2) * TB:(j % 2) * TB + W],
                        start=(j == 0), stop=(j == C8 - 1)),
                        waits=[sq_done if j == 2 * stage else None],
                        mark=(j % 2 == 1))
                stage_tok[stage] = P.prev
        stats_done = P.prev

        @blk.vector
        def _(v):
            V.do(v, lambda e: e.tensor_scalar(
                rows[0:1, 0*TB:0*TB+W], pA[0:1, 0:W], 1.0 / D, None, op0=ALU.mult),
                waits=[stats_done])
            V.do(v, lambda e: e.tensor_mul(rows[0:1, 4*TB:4*TB+W], rows[0:1, 0*TB:0*TB+W],
                                           rows[0:1, 0*TB:0*TB+W]))
            V.do(v, lambda e: e.scalar_tensor_tensor(
                rows[0:1, 4*TB:4*TB+W], pB[0:1, 0:W], 1.0 / D, rows[0:1, 4*TB:4*TB+W],
                op0=ALU.mult, op1=ALU.subtract))

        @blk.scalar
        def _(s):
            S.do(s, lambda e: e.activation(rows[0:1, 3*TB:3*TB+W], rows[0:1, 4*TB:4*TB+W],
                                           AF.Sqrt, bias=kb.c_eps[0:1, 0:1]),
                 waits=[V.prev])

        @blk.vector
        def _(v):
            V.do(v, lambda e: e.reciprocal(rows[0:1, 4*TB:4*TB+W], rows[0:1, 3*TB:3*TB+W]),
                 waits=[S.prev])
            tok = V.do(v, lambda e: e.tensor_mul(rows[0:1, 5*TB:5*TB+W], rows[0:1, 0*TB:0*TB+W],
                                                 rows[0:1, 4*TB:4*TB+W]))
            vec_rows_tok[0] = tok
            pe_stats_tok[0] = tok

        @blk.tensor
        def _(p):
            P.do(p, lambda e: e.matmul(bc[0][:, 0:W], kb.orow[:, :], rows[0:1, 4*TB:4*TB+W],
                                       start=True, stop=True),
                 waits=[V.prev, aff_tok[0]])
            P.do(p, lambda e: e.matmul(bc[1][:, 0:W], kb.orow[:, :], rows[0:1, 5*TB:5*TB+W],
                                       start=True, stop=True),
                 waits=[aff_tok[1]], mark=True)
        bc_done = P.prev

        @blk.vector
        def _(v, t0=t0, t1=t1):
            for j in range(C8):
                V.do(v, lambda e, j=j: e.tensor_mul(
                    aff[:, 0:W], col(j, t0, t1), bc[0][:, 0:W]), waits=[bc_done])
                if gamma is None:
                    tok = V.do(v, lambda e, j=j: e.tensor_sub(
                        col(j, t0, t1, buf=dst, n=ndst, noshift=True,
                            off=dst_off),
                        aff[:, 0:W], bc[1][:, 0:W]))
                else:
                    V.do(v, lambda e: e.tensor_sub(aff[:, 0:W], aff[:, 0:W], bc[1][:, 0:W]))
                    tok = V.do(v, lambda e, j=j: e.tensor_scalar(
                        col(j, t0, t1, buf=dst, n=ndst, noshift=True,
                            off=dst_off), aff[:, 0:W],
                        gamma[:, j:j + 1], beta[:, j:j + 1],
                        op0=ALU.mult, op1=ALU.add))
                if dst_bf16 is not None:
                    tok = V.do(v, lambda e, j=j: e.tensor_copy(
                        col(j, t0, t1, buf=dst_bf16, noshift=True),
                        col(j, t0, t1, buf=dst, n=ndst, noshift=True,
                            off=dst_off)))
                if j == 0:
                    aff_tok[0] = tok
            aff_tok[1] = tok
    return V.prev


def alloc_ln_scratch(kb):
    return {
        "pA": kb.PS[4],
        "pB": kb.PS[5],
        "bc0": kb.PS[6],
        "bc1": kb.PS[7],
        "sq": kb.sbuf_at("sq", [128, 2 * TB], F32, kb.ovl_base, 31584),
        "rows": kb.sbuf_at("rows", [1, 6 * TB], F32, kb.ovl_base,
                           31584 + 4096),
        "aff": kb.sbuf("aff", [128, TB], F32),
    }


def emit_consts(kb, blk):
    if not hasattr(kb, "oc"):
        alloc_consts(kb)

    @blk.gpsimd
    def _(g):
        kb.G.do(g, lambda e: e.memset(kb.oc[:], 1.0))
        kb.G.do(g, lambda e: e.memset(kb.orow[:], 1.0))
        kb.G.do(g, lambda e: e.memset(kb.c_eps[:], float(EPS)))
        kb.G.do(g, lambda e: e.memset(kb.c_zero[:], 0.0))
    return kb.G.prev


def alloc_consts(kb):
    kb.oc = kb.sbuf("ones_c", [128, 1], F32)
    kb.orow = kb.sbuf("ones_r", [1, 128], F32)
    kb.c_eps = kb.sbuf("c_eps", [128, 1], F32)
    kb.c_zero = kb.sbuf("c_zero", [128, 1], F32)


def tblocks(t0, t1, step=TB):
    return [(a, min(a + step, t1)) for a in range(t0, t1, step)]


QMAX = 1152
KMAX = 1408
KTILES = 11

HALVES_IN = [(2, 10, 0), (11, 18, 9)]
HALVES_A0 = [(3, 10, 2), (11, 18, 10)]


def alloc_attn_state(kb, base):
    st = {}
    o = [0]

    def at(name, shape, dtype):
        nb = int(np.prod(shape[1:])) if len(shape) > 2 else shape[1]
        nb = nb * (2 if dtype == BF16 else 4)
        t = kb.sbuf_at(name, shape, dtype, base, o[0])
        o[0] += (nb + 31) // 32 * 32
        return t

    st["wqkvW"] = at("wqkvW", [128, 3 * 1024], BF16)
    st["woW"] = at("woW", [128, 4 * 1024], BF16)
    st["qst"] = [at(f"qst{h}", [66, QMAX], BF16) for h in range(2)]
    st["kst"] = [at(f"kst{h}", [66, KMAX], BF16) for h in range(2)]
    st["vst"] = at("vst", [128, KTILES * 130], BF16)
    st["Pt"] = at("Pt", [128, 4 * 384], BF16)
    st["nmst"] = at("nmst", [128, 128], BF16)
    st["nmtr"] = at("nmtr", [128, 128], BF16)
    st["osb"] = at("osb", [128, 128], F32)
    st["rc"] = at("rc", [128, 2], F32)
    st["accst"] = [kb.sbuf_at(f"accst{i}", [128, TB], F32, base,
                              16384 + 2048 * i) for i in range(2)]
    st["otst"] = at("otst", [128, 8 * QMAX], BF16)
    assert o[0] <= OVLB, o[0]
    return st


OVLB = 50176


def emit_stage_consts(kb, blk, st, padrow_sb):
    V, S = kb.V, kb.S

    @blk.gpsimd
    def _(g):
        kb.G.do(g, lambda e: e.memset(st["vst"][:], 1.0))
        kb.G.do(g, lambda e: e.memset(st["nmst"][:], 0.0))
        for h in range(2):
            kb.G.do(g, lambda e, h=h: e.memset(st["qst"][h][64:65, :], 1.0))

    @blk.sync
    def _(q):
        for h in range(2):
            kb.dma(q, st["kst"][h][65:66, :], kb.inp["onesbf"], slot=0)
    return kb.G.prev


def qkv_production(kb, blk, st, cfg, kind, trange, stage_cols0):
    V, S, P = kb.V, kb.S, kb.P
    xs = cfg["xs"]
    dst = cfg["dst"]
    PS = kb.PS
    ptk = st.setdefault("ptk", {"ev": [None, None], "rot": [None],
                                "p1": [None, None], "fb": [None],
                                "fr": [None], "sq": [None],
                                "rotrd": [None], "fbrd": [None]})
    if "rotrd" not in ptk:
        ptk.update({"rotrd": [None], "fbrd": [None]})
    evtok = ptk["ev"]
    rottok = ptk["rot"]
    p1tok = ptk["p1"]
    fbtok = ptk["fb"]
    frtok = ptk["fr"]
    sqtok = ptk["sq"]
    first = True
    for (t0, t1) in tblocks(*trange):
        Wd = t1 - t0
        par = (t0 // TB) % 2

        @blk.tensor
        def _(p, t0=t0, t1=t1, Wd=Wd, par=par, first=first):
            for c in range(C8):
                P.do(p, lambda e, c=c: e.matmul(
                    PS[par][:, 0:Wd],
                    st["wqkvW"][:, cfg["woff"] + c * 128:cfg["woff"] + c * 128 + 128],
                    xs[:, c * NT0 + t0:c * NT0 + t1],
                    start=(c == 0), stop=(c == C8 - 1)),
                    waits=([cfg["wtok"], cfg["war"], cfg.get("war_v"),
                            cfg["xstok"], evtok[par]] if c == 0 else []),
                    mark=(c == C8 - 1))
        mmtok = P.prev

        @blk.scalar
        def _(s, Wd=Wd, par=par):
            S.do(s, lambda e: e.copy(st["rawst"][par][:, 0:Wd], PS[par][:, 0:Wd]),
                 waits=[mmtok, rottok[0], p1tok[par]])
            evtok[par] = S.prev
            if cfg["l2n"]:
                S.do(s, lambda e: e.activation(kb.ln_scr["aff"][:, 0:Wd],
                                               st["rawst"][par][:, 0:Wd],
                                               AF.Square))
                sqtok[0] = S.prev

        @blk.tensor
        def _(p, Wd=Wd, par=par):
            P.do(p, lambda e: e.matmul(PS[2][:, 0:Wd], cfg["prot"],
                                       st["rawst"][par][:, 0:Wd],
                                       start=True, stop=True),
                 waits=[evtok[par], ptk["rotrd"][0], cfg.get("prottok")],
                 mark=True)
            rmtok = P.prev
            rottok[0] = rmtok
            if cfg["l2n"]:
                P.do(p, lambda e: e.matmul(PS[3][0:2, 0:Wd], kb.bd2[:, :],
                                           kb.ln_scr["aff"][:, 0:Wd],
                                           start=True, stop=True),
                     waits=[sqtok[0], frtok[0]], mark=True)
                segtok = P.prev
            else:
                segtok = None
            cfg["_rm"] = rmtok
            cfg["_seg"] = segtok

        if cfg["l2n"]:
            @blk.scalar
            def _(s, Wd=Wd):
                S.do(s, lambda e: e.activation(st["fr"][0:2, 0:Wd],
                                               PS[3][0:2, 0:Wd], AF.Sqrt,
                                               bias=kb.c_zero[0:2, 0:1]),
                     waits=[cfg["_seg"]])

            @blk.vector
            def _(v, Wd=Wd):
                V.do(v, lambda e: e.tensor_scalar(
                    st["fr"][0:2, 0:Wd], st["fr"][0:2, 0:Wd],
                    1e-12, None, op0=ALU.max), waits=[S.prev])
                V.do(v, lambda e: e.reciprocal(st["fr"][0:2, 0:Wd],
                                               st["fr"][0:2, 0:Wd]))
                frtok[0] = V.prev

            @blk.tensor
            def _(p, Wd=Wd):
                P.do(p, lambda e: e.matmul(PS[6][:, 0:Wd], kb.bd2t[0:2, :],
                                           st["fr"][0:2, 0:Wd],
                                           start=True, stop=True),
                     waits=[frtok[0], ptk["fbrd"][0]], mark=True)
                fbtok[0] = P.prev

        @blk.vector
        def _(v, t0=t0, t1=t1, Wd=Wd, par=par):
            tc, ts = cfg["tcb"], cfg["tsb"]
            qs = cfg["qs_ap"]
            if qs is not None:
                V.do(v, lambda e: e.scalar_tensor_tensor(
                    st["t1st"][:, 0:Wd], st["rawst"][par][:, 0:Wd],
                    qs[:, :], tc[:, t0:t1], op0=ALU.mult, op1=ALU.mult),
                    waits=[evtok[par], cfg["war"]])
            else:
                V.do(v, lambda e: e.scalar_tensor_tensor(
                    st["t1st"][:, 0:Wd], st["rawst"][par][:, 0:Wd],
                    cfg["scl"], tc[:, t0:t1], op0=ALU.mult, op1=ALU.mult),
                    waits=[evtok[par], cfg["war"]])
            V.do(v, lambda e: e.scalar_tensor_tensor(
                st["t3st"][:, 0:Wd], PS[2][:, 0:Wd],
                cfg["scl"], ts[:, t0:t1], op0=ALU.mult, op1=ALU.mult),
                waits=[cfg["_rm"]])
            ptk["rotrd"][0] = V.prev
            V.do(v, lambda e: e.tensor_add(st["t1st"][:, 0:Wd],
                                           st["t1st"][:, 0:Wd],
                                           st["t3st"][:, 0:Wd]))
            p1tok[par] = V.prev
            lc = t0 - stage_cols0
            if cfg["l2n"]:
                for hh in range(2):
                    V.do(v, lambda e, hh=hh, lc=lc: e.tensor_mul(
                        dst[hh][0:64, lc:lc + Wd],
                        st["t1st"][hh * 64:hh * 64 + 64, 0:Wd],
                        PS[6][hh * 64:hh * 64 + 64, 0:Wd]),
                        waits=[fbtok[0]] if hh == 0 else [])
                ptk["fbrd"][0] = V.prev
            else:
                for hh in range(2):
                    V.do(v, lambda e, hh=hh, lc=lc: e.tensor_copy(
                        dst[hh][0:64, lc:lc + Wd],
                        st["t1st"][hh * 64:hh * 64 + 64, 0:Wd]))
            p1tok[par] = V.prev
        first = False
    return P.prev, V.prev


def v_production(kb, blk, st, cfg, kts):
    V, S, P = kb.V, kb.S, kb.P
    PS = kb.PS
    xs = cfg["xs"]
    ptk = st.setdefault("ptk", {})
    evtok = ptk.setdefault("vev", [None, None])
    for i, kt in enumerate(kts):
        par = i % 2

        @blk.tensor
        def _(p, kt=kt, par=par):
            for c in range(C8):
                P.do(p, lambda e, c=c: e.matmul(
                    PS[par][:, 0:128],
                    xs[:, c * NT0 + kt * 128:c * NT0 + kt * 128 + 128],
                    st["wqkvW"][:, cfg["woff"] + c * 128:cfg["woff"] + c * 128 + 128],
                    start=(c == 0), stop=(c == C8 - 1)),
                    waits=[cfg["wtok"], cfg["war"], evtok[par]] if c == 0 else [],
                    mark=(c == C8 - 1))
        mmtok = P.prev

        @blk.scalar
        def _(s, i=i, par=par):
            for hh in range(2):
                S.do(s, lambda e, hh=hh: e.copy(
                    st["vst"][:, i * 130 + hh * 65:i * 130 + hh * 65 + 64],
                    PS[par][:, hh * 64:hh * 64 + 64]),
                    waits=[mmtok, cfg["war"], cfg.get("vinit")]
                    if hh == 0 else [])
            evtok[par] = S.prev
    return S.prev


def attn_scores_pv(kb, blk, st, cfg):
    V, S, P = kb.V, kb.S, kb.P
    PS = kb.PS
    qt0, qtn, kt0 = cfg["qt0"], cfg["qtn"], cfg["kt0"]
    lb = cfg["lookback"]
    nq = qtn - qt0 + 1
    ktn = qtn
    nmcols = []
    SB3 = (4, 5, 7)
    s2tok = [None] * 3
    for hh in range(2):
        for qt in range(qt0, qtn + 1):
            par = (qt - qt0) % 3
            ka = max(qt - lb, kt0)
            nk = (qt - ka + 1) * 128
            qlc = (qt - qt0) * 128
            klc = (ka - kt0) * 128

            @blk.tensor
            def _(p, hh=hh, qt=qt, par=par, nk=nk, qlc=qlc, klc=klc):
                P.do(p, lambda e: e.matmul(
                    PS[SB3[par]][:, 0:nk],
                    st["qst"][hh][0:65, qlc:qlc + 128],
                    st["kst"][hh][0:65, klc:klc + nk],
                    start=True, stop=True),
                    waits=[cfg["qtok"], cfg["ktok"], cfg.get("qones_tok"),
                           s2tok[par]], mark=True)
            mmtok = P.prev

            @blk.vector
            def _(v, hh=hh, qt=qt, par=par, nk=nk):
                col = hh * nq + (qt - qt0)
                V.do(v, lambda e: e.reduce_max(
                    st["nmst"][:, col:col + 1], PS[SB3[par]][:, 0:nk],
                    axis=AX.X, negate=True),
                    waits=[mmtok, cfg.get("nm_war")
                           if (hh == 0 and qt == qt0) else None])
                s2tok[par] = V.prev
    @blk.sync
    def _(q):
        t1 = kb.dma(q, st["nmtr"][:], st["nmst"][:], slot=7,
                    waits=[kb.V.prev], transpose=True)
        for hh in range(2):
            kb.dma(q, st["qst"][hh][65:66, 0:nq * 128],
                   st["nmtr"][hh * nq:hh * nq + nq, 0:128], slot=7,
                   waits=[t1] if hh == 0 else [])
    bias_tok = (kb.dsem[7], kb.dcnt[7])
    sttok = [None] * 3
    pvtok = [None, None]
    exptok = {}
    masktok = {}
    ottok = [None]
    osbtok = [None, None]
    trtok = [None, None]
    for hh in range(2):
        for kt in range(kt0, ktn + 1):
            par = kt % 3
            qa = max(kt, qt0)
            qb = min(kt + lb, qtn)
            nqq = (qb - qa + 1) * 128
            klc = (kt - kt0) * 128
            qlc = (qa - qt0) * 128

            @blk.tensor
            def _(p, hh=hh, kt=kt, par=par, nqq=nqq, klc=klc, qlc=qlc):
                P.do(p, lambda e: e.matmul(
                    PS[SB3[par]][:, 0:nqq],
                    st["kst"][hh][0:66, klc:klc + 128],
                    st["qst"][hh][0:66, qlc:qlc + nqq],
                    start=True, stop=True),
                    waits=[bias_tok, cfg.get("aug_tok"), sttok[par]],
                    mark=True)
            mmtok = P.prev

            @blk.scalar
            def _(s, hh=hh, kt=kt, par=par, nqq=nqq):
                slot = kt % 4
                S.do(s, lambda e: e.activation(
                    st["Pt"][:, slot * 384:slot * 384 + nqq],
                    PS[SB3[par]][:, 0:nqq], AF.Exp,
                    bias=kb.c_zero[:, 0:1]),
                    waits=[mmtok, pvtok[0], pvtok[1]])
                exptok[(hh, kt)] = S.prev
                sttok[par] = S.prev

            @blk.vector
            def _(v, hh=hh, kt=kt, qa=qa, qb=qb):
                slot = kt % 4
                tok = exptok[(hh, kt)]
                for qt in range(qa, qb + 1):
                    o = qt - kt
                    mo = cfg["mask_of"](o)
                    if mo is not None:
                        V.do(v, lambda e, qt=qt, mo=mo: e.tensor_mul(
                            st["Pt"][:, slot * 384 + (qt - qa) * 128:
                                     slot * 384 + (qt - qa) * 128 + 128],
                            st["Pt"][:, slot * 384 + (qt - qa) * 128:
                                     slot * 384 + (qt - qa) * 128 + 128],
                            kb.msk[:, mo * 128:mo * 128 + 128]),
                            waits=[tok])
                masktok[(hh, kt)] = V.prev

            qt = kt
            if qt >= qt0:
                @blk.tensor
                def _(p, hh=hh, qt=qt):
                    ks = range(max(qt - lb, kt0), qt + 1)
                    nk = len(list(ks))
                    for r, kk in enumerate(ks):
                        slot = kk % 4
                        qa2 = max(kk, qt0)
                        off = slot * 384 + (qt - qa2) * 128
                        P.do(p, lambda e, kk=kk, off=off, r=r, nk=nk: e.matmul(
                            kb.PS[2 + hh][:, 0:65],
                            st["Pt"][:, off:off + 128],
                            st["vst"][:, (kk - kt0) * 130 + hh * 65:
                                      (kk - kt0) * 130 + hh * 65 + 65],
                            start=(r == 0), stop=(r == nk - 1)),
                            waits=([masktok[(hh, kk)] for kk in ks] +
                                   [cfg["vtok"], osbtok[hh]]) if r == 0 else [],
                            mark=(r == nk - 1))
                    pvtok[hh] = kb.P.prev

                @blk.vector
                def _(v, hh=hh, qt=qt):
                    V.do(v, lambda e: e.reciprocal(
                        st["rc"][:, hh:hh + 1],
                        kb.PS[2 + hh][:, 64:65]), waits=[pvtok[hh]])
                    V.do(v, lambda e: e.tensor_scalar(
                        st["osb"][:, hh * 64:hh * 64 + 64],
                        kb.PS[2 + hh][:, 0:64],
                        st["rc"][:, hh:hh + 1], None,
                        op0=ALU.mult), waits=[trtok[hh]])
                    osbtok[hh] = V.prev

                @blk.tensor
                def _(p, hh=hh, qt=qt):
                    P.do(p, lambda e: e.transpose(
                        kb.PS[6][0:64, 0:128],
                        st["osb"][:, hh * 64:hh * 64 + 64], kb.idf[:]),
                        waits=[osbtok[hh], ottok[0]], mark=True)
                    trtok[hh] = kb.P.prev
                    cfg["_ott"] = kb.P.prev

                @blk.scalar
                def _(s, hh=hh, qt=qt):
                    qlc = (qt - qt0) * 128
                    S.do(s, lambda e: e.copy(
                        st["otst"][hh * 64:hh * 64 + 64,
                                   cfg["hb"] * QMAX + qlc:
                                   cfg["hb"] * QMAX + qlc + 128],
                        kb.PS[6][0:64, 0:128]), waits=[cfg["_ott"]])
                    ottok[0] = S.prev
    return P.prev, V.prev, S.prev


def attn_wo(kb, blk, st, cfg):
    V, S, P = kb.V, kb.S, kb.P
    PS = kb.PS
    qt0, qtn = cfg["qt0"], cfg["qtn"]
    q0, q1 = qt0 * 128, (qtn + 1) * 128
    wo = cfg["wo_ap"]
    wov = wo.rearrange("(g p) m -> g p m", p=128)
    addtok = [None]
    ldwait = [cfg["wo_war"]]
    for half in range(2):
        ldtoks = []

        @blk.gpsimd
        def _(g, half=half):
            for gi in range(4):
                hb = half * 4 + gi
                ldtoks.append(kb.dma(g, st["woW"][:, gi * 1024:(gi + 1) * 1024],
                                     wov[hb], slot=3,
                                     waits=[ldwait[0]] if gi == 0 else []))
        wotok = ldtoks[-1]
        for dm in range(C8):
            for (t0, t1) in tblocks(q0, q1):
                Wd = t1 - t0
                par = (dm + t0 // TB) % 2

                @blk.tensor
                def _(p, half=half, dm=dm, t0=t0, t1=t1, Wd=Wd, par=par):
                    for gi in range(4):
                        hb = half * 4 + gi
                        P.do(p, lambda e, gi=gi, hb=hb: e.matmul(
                            PS[par][:, 0:Wd],
                            st["woW"][:, gi * 1024 + dm * 128:
                                      gi * 1024 + dm * 128 + 128],
                            st["otst"][:, hb * QMAX + (t0 - q0):
                                       hb * QMAX + (t1 - q0)],
                            start=(gi == 0), stop=(gi == 3)),
                            waits=[wotok, cfg["ottok"], addtok[0]]
                            if gi == 0 else [],
                            mark=(gi == 3))
                mmtok = P.prev

                ldwait[0] = P.prev
                if cfg["dest"] == "resid":
                    @blk.vector
                    def _(v, dm=dm, t0=t0, t1=t1, Wd=Wd, par=par):
                        r = cfg["resid"]
                        if cfg.get("gate_first") and t0 == q0:
                            V.do(v, lambda e: e.scalar_tensor_tensor(
                                r[:, dm * NT0 + t0:dm * NT0 + t0 + 128],
                                PS[par][:, 0:128], kb.cmask[:, 0:1],
                                r[:, dm * NT0 + t0:dm * NT0 + t0 + 128],
                                op0=ALU.mult, op1=ALU.add), waits=[mmtok])
                            V.do(v, lambda e: e.tensor_add(
                                r[:, dm * NT0 + t0 + 128:dm * NT0 + t1],
                                r[:, dm * NT0 + t0 + 128:dm * NT0 + t1],
                                PS[par][:, 128:Wd]))
                        else:
                            V.do(v, lambda e: e.tensor_add(
                                r[:, dm * NT0 + t0:dm * NT0 + t1],
                                r[:, dm * NT0 + t0:dm * NT0 + t1],
                                PS[par][:, 0:Wd]), waits=[mmtok])
                        addtok[0] = V.prev
                else:
                    @blk.vector
                    def _(v, half=half, dm=dm, t0=t0, t1=t1, Wd=Wd, par=par):
                        r = cfg["resid"]
                        if half == 0:
                            V.do(v, lambda e: e.tensor_add(
                                st["accst"][par][:, 0:Wd],
                                r[:, dm * NT0 + t0:dm * NT0 + t1],
                                PS[par][:, 0:Wd]),
                                waits=[mmtok, cfg["acc_war"][par]])
                        else:
                            V.do(v, lambda e: e.tensor_copy(
                                st["accst"][par][:, 0:Wd],
                                PS[par][:, 0:Wd]),
                                waits=[mmtok, cfg["acc_war"][par]])
                        addtok[0] = V.prev
                        cfg["_acc_v"] = V.prev

                    @blk.sync
                    def _(q, half=half, dm=dm, t0=t0, t1=t1, Wd=Wd, par=par):
                        tok = kb.dma(q, cfg["acc_out"][half * D + dm * 128:
                                                       half * D + dm * 128 + 128,
                                                       t0 - HALO:t1 - HALO],
                                     st["accst"][par][:, 0:Wd], slot=1 + 8 * par,
                                     waits=[cfg["_acc_v"]])
                        cfg["acc_war"][par] = tok
    return P.prev, V.prev


def emit_cast_shift(kb, blk, resid, xs, waits=()):
    V = kb.V

    @blk.vector
    def _(v):
        first = [True]
        for j in range(C8):
            for (t0, t1) in tblocks(0, NT0):
                w = []
                if first[0]:
                    w = list(waits)
                    first[0] = False
                if j < C8 // 2:
                    V.do(v, lambda e, j=j, t0=t0, t1=t1: e.tensor_copy(
                        xs[:, j * NT0 + t0:j * NT0 + t1],
                        resid[:, j * NT0 + t0:j * NT0 + t1]), waits=w)
                else:
                    a0 = max(t0, 1)
                    V.do(v, lambda e, j=j, a0=a0, t1=t1: e.tensor_copy(
                        xs[:, j * NT0 + a0:j * NT0 + t1],
                        resid[:, j * NT0 + a0 - 1:j * NT0 + t1 - 1]), waits=w)
                    if t0 == 0:
                        V.do(v, lambda e, j=j: e.tensor_copy(
                            xs[:, j * NT0:j * NT0 + 1],
                            resid[:, j * NT0:j * NT0 + 1]))
    return V.prev


def emit_inner_attn(kb, blk, st, l, waps, tok):
    V, S, P = kb.V, kb.S, kb.P
    dws = tok.pop("acc_dma", None) or [None, None]

    @blk.vector
    def _(v):
        kb.V.do(v, lambda e: e.nop(), waits=dws)

    @blk.scalar
    def _(s):
        kb.S.do(s, lambda e: e.nop(), waits=dws)
    xstok = emit_cast_shift(kb, blk, st["resid"], st["xs"],
                            waits=[tok.get("resid_v")])

    @blk.vector
    def _(v):
        V.do(v, lambda e: e.tensor_scalar(
            kb.protq[:], kb.protF[:], kb.qsW[:, l:l + 1], None, op0=ALU.mult))
        V.do(v, lambda e: e.tensor_scalar(
            kb.protk[:], kb.protF[:], kb.ksW[:, l:l + 1], None, op0=ALU.mult))
    prottok = V.prev

    @blk.gpsimd
    def _(g):
        kb.G.do(g, lambda e: e.memset(st["vst"][:], 1.0),
                waits=[tok.get("pe_free"), tok.get("resid_v"),
                       dws[0], dws[1]])
        for h in range(2):
            kb.G.do(g, lambda e, h=h: e.memset(st["qst"][h][64:65, :], 1.0))
    vinit_tok = kb.G.prev

    @blk.sync
    def _(q):
        for h in range(2):
            kb.dma(q, st["kst"][h][65:66, :], kb.inp["onesbf"], slot=10,
                   waits=[tok.get("pe_free"), dws[0], dws[1]]
                   if h == 0 else [])
    aug_tok = (kb.dsem[10], kb.dcnt[10])

    wv = waps["Wqkv"][l].rearrange("(c p) m -> p c m", p=128)
    stage_war = [tok.get("pe_free")]
    war_v = [tok.get("resid_v")]
    wo_war = [None]
    wtokL = [None]
    wv2 = None

    def load_wqkv(hb, wait):
        toks = []

        @blk.gpsimd
        def _(g):
            for i in range(3):
                toks.append(kb.dma(
                    g, st["wqkvW"][:, i * 1024:(i + 1) * 1024].rearrange(
                        "p (c m) -> p c m", c=C8),
                    wv[:, :, i * D + hb * 128:i * D + (hb + 1) * 128],
                    slot=2, waits=[wait] if i == 0 else []))
        return toks[-1]

    for half in range(2):
        qt0, qtn, kt0 = HALVES_IN[half]
        qrange = (qt0 * 128, (qtn + 1) * 128)
        krange = (kt0 * 128, (qtn + 1) * 128)
        kts = list(range(kt0, qtn + 1))

        @blk.scalar
        def _(s, kt0=kt0, qtn=qtn):
            nk = (qtn - kt0 + 1) * 128
            for hh in range(2):
                S.do(s, lambda e, hh=hh, nk=nk: e.copy(
                    st["kst"][hh][64:65, 0:nk],
                    kb.padrow_sb[0:1, kt0 * 128:kt0 * 128 + nk]),
                    waits=[stage_war[0]] if hh == 0 else [])

        if half == 0:
            wtokL[0] = load_wqkv(0, stage_war[0])
        stok2 = None
        for hb in range(8):
            wtok = wtokL[0]
            cfgq = dict(xs=st["xs"], woff=0, wtok=wtok, war=stage_war[0],
                        prottok=prottok,
                        war_v=war_v[0],
                        xstok=xstok, dst=st["qst"], l2n=True,
                        prot=kb.protq[:], qs_ap=kb.qsW[:, l:l + 1], scl=1.0,
                        tcb=kb.tcB, tsb=kb.tsB)
            qpe, qv = qkv_production(kb, blk, st, cfgq, "q", qrange, qt0 * 128)
            cfgk = dict(xs=st["xs"], woff=1024, wtok=wtok, war=stage_war[0],
                        prottok=prottok,
                        war_v=war_v[0],
                        xstok=xstok, dst=st["kst"], l2n=True,
                        prot=kb.protk[:], qs_ap=kb.ksW[:, l:l + 1], scl=1.0,
                        tcb=kb.tciB, tsb=kb.tsiB)
            kpe, kv = qkv_production(kb, blk, st, cfgk, "k", krange, kt0 * 128)
            cfgv = dict(xs=st["xs"], woff=2048, wtok=wtok, war=stage_war[0],
                        vinit=vinit_tok)
            vtok = v_production(kb, blk, st, cfgv, kts)
            if hb < 7:
                wtokL[0] = load_wqkv(hb + 1, kb.P.prev)
            elif half == 0:
                wtokL[0] = load_wqkv(0, kb.P.prev)
            scfg = dict(qt0=qt0, qtn=qtn, kt0=kt0, lookback=2,
                        mask_of=lambda o: {0: 0, 2: 1}.get(o),
                        hb=hb, qtok=qv, ktok=kv, vtok=vtok,
                        aug_tok=aug_tok, qones_tok=vinit_tok,
                        nm_war=tok.get("nm_war"))
            ptok, vtok2, stok2 = attn_scores_pv(kb, blk, st, scfg)
            stage_war[0] = ptok
            war_v[0] = vtok2
            tok["nm_war"] = (kb.dsem[7], kb.dcnt[7])
        wcfg = dict(qt0=qt0, qtn=qtn, wo_ap=waps["Wo"][l], dest="resid",
                    resid=st["resid"], gate_first=(half == 0),
                    wo_war=wo_war[0], ottok=stok2, pe_tok=stage_war[0])
        wpe, wv2 = attn_wo(kb, blk, st, wcfg)
        wo_war[0] = wpe
        stage_war[0] = wpe
        war_v[0] = wv2
    tok["resid_v"] = wv2
    tok["pe_free"] = stage_war[0]
    return tok


def emit_inner_ffn(kb, blk, st, l, waps, tok):
    V, S, P = kb.V, kb.S, kb.P
    PS = kb.PS
    own = tblocks(HALO, NT0)
    lng = kb.lngW[:, l * C8:(l + 1) * C8]
    lnb = kb.lnbW[:, l * C8:(l + 1) * C8]
    lntok = emit_ln(kb, blk, st["resid"], st["xs"], NT0, own, kb.ln_scr,
                    gamma=lng, beta=lnb, shift=True,
                    extra_wait=tok.get("resid_v"))
    w1v = waps["W1p"][l].rearrange("(c p) m -> p c m", p=128)
    w2v = waps["W2"][l]
    groups = [list(range(g, min(g + 4, NPAIR))) for g in range(0, NPAIR, 4)]

    def load_group(gi, wait):
        prs = groups[gi]
        c0, c1 = prs[0] * 256, (prs[-1] + 1) * 256
        r0, r1 = prs[0] * 128, min((prs[-1] + 1) * 128, DIM_INNER)
        toks = []

        @blk.gpsimd
        def _(g):
            toks.append(kb.dma(
                g, st["w1W"][:, 0:8 * (c1 - c0)].rearrange(
                    "p (c m) -> p c m", c=C8),
                w1v[:, :, c0:c1], slot=4, waits=[wait]))
            t2 = None
            for pi, pr in enumerate(prs):
                rows = min(128, DIM_INNER - pr * 128)
                t2 = kb.dma(g, st["w2W"][0:rows, pi * 1024:(pi + 1) * 1024],
                            w2v[pr * 128:pr * 128 + rows, :], slot=5)
            toks.append(t2)
        return toks

    gwar = [tok.get("pe_free")]
    pa_war = [None]
    pg_war = [None]
    addtok = [None, None]
    fusedwar = [None]
    ldt = load_group(0, gwar[0])
    for gi, prs in enumerate(groups):
        w1t, w2t = ldt
        npr = len(prs)
        ncol = npr * 256
        for (t0, t1) in own:
            Wd = t1 - t0
            fusedtok = {}
            for pi, pr in enumerate(prs):
                rows = min(128, DIM_INNER - pr * 128)

                @blk.tensor
                def _(p, pr=pr, pi=pi, t0=t0, t1=t1, Wd=Wd, rows=rows):
                    for c in range(C8):
                        P.do(p, lambda e, c=c: e.matmul(
                            PS[0][0:rows, 0:Wd],
                            st["w1W"][:, c * ncol + pi * 256:
                                      c * ncol + pi * 256 + rows],
                            st["xs"][:, c * NT0 + t0:c * NT0 + t1],
                            start=(c == 0), stop=(c == C8 - 1)),
                            waits=[w1t, lntok, pa_war[0]] if c == 0 else [],
                            mark=(c == C8 - 1))
                    for c in range(C8):
                        P.do(p, lambda e, c=c: e.matmul(
                            PS[1][0:rows, 0:Wd],
                            st["w1W"][:, c * ncol + pi * 256 + 128:
                                      c * ncol + pi * 256 + 128 + rows],
                            st["xs"][:, c * NT0 + t0:c * NT0 + t1],
                            start=(c == 0), stop=(c == C8 - 1)),
                            waits=[pg_war[0]] if c == 0 else [],
                            mark=(c == C8 - 1))
                mmtok = P.prev

                @blk.scalar
                def _(s, pr=pr, Wd=Wd, rows=rows):
                    S.do(s, lambda e: e.activation(
                        kb.ln_scr["aff"][0:rows, 0:Wd], PS[1][0:rows, 0:Wd],
                        AF.Gelu, bias=kb.b1W[0:rows, l * 44 + 2 * pr + 1:
                                             l * 44 + 2 * pr + 2]),
                        waits=[mmtok])
                    pg_war[0] = S.prev

                @blk.vector
                def _(v, pr=pr, pi=pi, t0=t0, Wd=Wd, rows=rows):
                    V.do(v, lambda e: e.scalar_tensor_tensor(
                        st["fusedst"][:, pi * TB:pi * TB + Wd][0:rows, :],
                        PS[0][0:rows, 0:Wd],
                        kb.b1W[0:rows, l * 44 + 2 * pr:l * 44 + 2 * pr + 1],
                        kb.ln_scr["aff"][0:rows, 0:Wd],
                        op0=ALU.add, op1=ALU.mult),
                        waits=[kb.S.prev, fusedwar[0]])
                    fusedtok[pi] = V.prev
                    pa_war[0] = V.prev
            for dm in range(C8):
                ppar = dm % 2

                @blk.tensor
                def _(p, dm=dm, t0=t0, t1=t1, Wd=Wd, ppar=ppar):
                    for pi, pr in enumerate(prs):
                        rows = min(128, DIM_INNER - pr * 128)
                        P.do(p, lambda e, pi=pi, rows=rows: e.matmul(
                            PS[2 + ppar][:, 0:Wd],
                            st["w2W"][0:rows, pi * 1024 + dm * 128:
                                      pi * 1024 + dm * 128 + 128],
                            st["fusedst"][:, pi * TB:pi * TB + Wd][0:rows, :],
                            start=(pi == 0), stop=(pi == npr - 1)),
                            waits=[w2t, fusedtok[npr - 1], addtok[ppar]]
                            if pi == 0 else [],
                            mark=(pi == npr - 1))
                    if dm == C8 - 1:
                        fusedwar[0] = kb.P.prev
                mmtok2 = P.prev

                @blk.vector
                def _(v, dm=dm, t0=t0, t1=t1, Wd=Wd, ppar=ppar, gi=gi):
                    r = st["resid"]
                    if gi == len(groups) - 1:
                        V.do(v, lambda e: e.scalar_tensor_tensor(
                            r[:, dm * NT0 + t0:dm * NT0 + t1],
                            PS[2 + ppar][:, 0:Wd],
                            kb.b2W[:, l * C8 + dm:l * C8 + dm + 1],
                            r[:, dm * NT0 + t0:dm * NT0 + t1],
                            op0=ALU.add, op1=ALU.add), waits=[mmtok2])
                    else:
                        V.do(v, lambda e: e.tensor_add(
                            r[:, dm * NT0 + t0:dm * NT0 + t1],
                            r[:, dm * NT0 + t0:dm * NT0 + t1],
                            PS[2 + ppar][:, 0:Wd]), waits=[mmtok2])
                    addtok[ppar] = V.prev
        if gi + 1 < len(groups):
            ldt = load_group(gi + 1, fusedwar[0])
    tok["resid_v"] = V.prev
    tok["pe_free"] = P.prev
    return tok


def emit_halo_exchange(kb, blk, st, idx, tok):
    V = kb.V
    cin, cout = kb.cinH, kb.coutH

    @blk.sync
    def _(q):
        for j in range(C8):
            kb.dma(q, cin[j * 128:(j + 1) * 128, :],
                   st["resid"][:, j * NT0 + OWN:j * NT0 + NT0], slot=6,
                   waits=[tok["resid_v"]] if j == 0 else [])
    dtok = (kb.dsem[6], kb.dcnt[6])

    @blk.gpsimd
    def _(g):
        g.wait_ge(*dtok)
        kb.G.do(g, lambda e: e.collective_compute(
            "AllGather", mybir.AluOpType.bypass,
            replica_groups=[[0, 1], [2, 3], [4, 5], [6, 7]],
            ins=[cin.opt()], outs=[cout.opt()]))
    ctok = kb.G.prev

    @blk.sync
    def _(q):
        for j in range(C8):
            kb.dma(q, st["resid"][:, j * NT0:j * NT0 + HALO],
                   cout[j * 128:(j + 1) * 128, :], slot=6,
                   waits=[ctok] if j == 0 else [])
    htok = (kb.dsem[6], kb.dcnt[6])

    @blk.vector
    def _(v):
        V.do(v, lambda e: e.tensor_scalar(
            st["resid"].rearrange("p (c t) -> p c t", c=C8)[:, :, 0:HALO],
            st["resid"].rearrange("p (c t) -> p c t", c=C8)[:, :, 0:HALO],
            kb.cmask[:, 0:1], None, op0=ALU.mult), waits=[htok])
    tok["resid_v"] = V.prev
    return tok


def alloc_block0(kb):
    st = {}
    st["resid"] = kb.sbuf("resid", [128, C8 * NT0], F32)
    st["xs"] = kb.sbuf("xs", [128, C8 * NT0], BF16)
    base = kb.mkslab("ovl", OVLB)
    kb.ovl_base = base
    st.update(alloc_attn_state(kb, base))
    st["rawst"] = [kb.sbuf(f"raw{i}", [128, TB], F32) for i in range(2)]
    st["t1st"] = kb.sbuf("t1st", [128, TB], F32)
    st["t3st"] = kb.sbuf("t3st", [128, TB], F32)
    st["fr"] = kb.sbuf("fr", [2, TB], F32)
    st["w1W"] = kb.sbuf_at("w1W", [128, 8 * 1024], BF16, base, 0)
    st["w2W"] = kb.sbuf_at("w2W", [128, 4 * 1024], BF16, base, 16384)
    st["fusedst"] = kb.sbuf_at("fusedst", [128, 4 * TB], BF16, base, 24576)
    kb.PS = [kb.psum(f"PSB{i}", [128, 512], F32) for i in range(8)]
    kb.ln_scr = alloc_ln_scratch(kb)
    kb.tcB = kb.sbuf("tcB", [128, NT0], BF16)
    kb.tsB = kb.sbuf("tsB", [128, NT0], BF16)
    kb.tciB = kb.sbuf("tciB", [128, NT0], BF16)
    kb.tsiB = kb.sbuf("tsiB", [128, NT0], BF16)
    kb.tcA = kb.sbuf_at("tcA", [128, NT0], BF16,
                        kb.nc.lookup_mloc(kb.tcB).addr, 0)
    kb.tsA = kb.sbuf_at("tsA", [128, NT0], BF16,
                        kb.nc.lookup_mloc(kb.tsB).addr, 0)
    kb.msk = kb.sbuf("mskS", [128, 4 * 128], BF16)
    kb.protF = kb.sbuf("protF", [128, 128], F32)
    kb.protq = kb.sbuf("protq", [128, 128], F32)
    kb.protk = kb.sbuf("protk", [128, 128], F32)
    kb.bd2 = kb.sbuf("bd2S", [128, 2], F32)
    kb.bd2t = kb.sbuf("bd2tS", [2, 128], F32)
    kb.idf = kb.sbuf("idfS", [128, 128], F32)
    kb.padrow_sb = kb.sbuf("padrowS", [1, NT0], BF16)
    kb.cmask = kb.sbuf("cmaskS", [128, 1], F32)
    kb.qsW = kb.sbuf("qsWS", [128, IN_DEPTH], F32)
    kb.ksW = kb.sbuf("ksWS", [128, IN_DEPTH], F32)
    kb.lngW = kb.sbuf("lngWS", [128, IN_DEPTH * C8], F32)
    kb.lnbW = kb.sbuf("lnbWS", [128, IN_DEPTH * C8], F32)
    kb.b1W = kb.sbuf("b1WS", [128, IN_DEPTH * 44], F32)
    kb.b2W = kb.sbuf("b2WS", [128, IN_DEPTH * C8], F32)
    return st


def emit_prelude(kb, blk, st, ins):
    ctok = emit_consts(kb, blk)
    sctok = emit_stage_consts(kb, blk, st, kb.padrow_sb)

    @blk.sync
    def _(q):
        pairs = [
            (kb.tciB, "TciB"),
            (kb.tsiB, "TsiB"), (kb.tcA, "TcA"), (kb.tsA, "TsA"),
            (kb.msk, "MSK"), (kb.protF, "PROTF"), (kb.bd2, "BD2"),
            (kb.bd2t, "BD2T"), (kb.idf, "IDF"), (kb.padrow_sb, "padrowbf"),
            (kb.cmask, "coremask"),
        ]
        if hasattr(kb, "a0w"):
            pairs += [(kb.a0w, "a0_lnw"), (kb.a0b, "a0_lnb")]
        for sb, name in pairs:
            if name in ins:
                kb.dma(q, sb[:], ins[name], slot=0)
        if "qs_col" in ins:
            kb.dma(q, kb.qsW[:], ins["qs_col"], slot=0)
            kb.dma(q, kb.ksW[:], ins["ks_col"], slot=0)
        for sb, name, w in [(kb.lngW, "ln_g", C8), (kb.lnbW, "ln_b", C8),
                            (kb.b1W, "b1p", 44), (kb.b2W, "b2c", C8)]:
            if name in ins:
                kb.dma(q, sb[:].rearrange("p (l c) -> p l c", l=IN_DEPTH),
                       ins[name].rearrange("l p c -> p l c"), slot=0)
    ldtok = (kb.dsem[0], kb.dcnt[0])
    @blk.vector
    def _(v):
        kb.V.do(v, lambda e: e.nop(), waits=[ldtok, ctok, sctok])

    @blk.scalar
    def _(s):
        kb.S.do(s, lambda e: e.nop(), waits=[ldtok, ctok, sctok])

    @blk.tensor
    def _(p):
        kb.P.do(p, lambda e: e.nop(), waits=[ldtok, ctok, sctok], mark=True)

    @blk.gpsimd
    def _(g):
        g.wait_ge(*ldtok)
    return {"resid_v": kb.V.prev, "pe_free": kb.P.prev}


def build_piece_inner(l0=0, nl=1):
    kb = KB()
    nc = kb.nc
    rin = kb.din("residT", [D, NT0])
    waps = {
        "Wqkv": kb.din("Wqkv", [IN_DEPTH, D, 3 * 1024]),
        "Wo": kb.din("Wo", [IN_DEPTH, D, D]),
        "W1p": kb.din("W1p", [IN_DEPTH, D, NPAIR * 256]),
        "W2": kb.din("W2", [IN_DEPTH, DIM_INNER, D]),
    }
    ins = {}
    for name, shape in [("TcB", [128, NT0]), ("TsB", [128, NT0]),
                        ("TciB", [128, NT0]), ("TsiB", [128, NT0]),
                        ("MSK", [128, 4 * 128]), ("padrowbf", [1, NT0]),
                        ("onesbf", [1, KMAX])]:
        ins[name] = kb.din(name, shape, BF16)
    for name, shape in [("PROTF", [128, 128]), ("BD2", [128, 2]),
                        ("BD2T", [2, 128]), ("IDF", [128, 128]),
                        ("coremask", [128, 1]),
                        ("qs_col", [128, IN_DEPTH]),
                        ("ks_col", [128, IN_DEPTH]),
                        ("ln_g", [IN_DEPTH, 128, C8]),
                        ("ln_b", [IN_DEPTH, 128, C8]),
                        ("b1p", [IN_DEPTH, 128, 44]),
                        ("b2c", [IN_DEPTH, 128, C8])]:
        ins[name] = kb.din(name, shape, F32)
    rout = kb.dout("residO", [D, NT0])
    st = alloc_block0(kb)
    kb.open()
    with nc.Block() as blk:
        tok = emit_prelude(kb, blk, st, ins)
        ldtok = emit_load_x(kb, blk, st["resid"], rin)

        @blk.tensor
        def _(p):
            p.wait_ge(*ldtok)

        @blk.vector
        def _(v):
            v.wait_ge(*ldtok)

        @blk.scalar
        def _(s):
            s.wait_ge(*ldtok)
        for l in range(l0, l0 + nl):
            tok = emit_inner_attn(kb, blk, st, l, waps, tok)
            tok = emit_inner_ffn(kb, blk, st, l, waps, tok)

        @blk.sync
        def _(q):
            t = None
            for j in range(C8):
                t = kb.dma(q, rout[j * 128:(j + 1) * 128, :],
                           st["resid"][:, j * NT0:(j + 1) * NT0], slot=1,
                           waits=[tok["resid_v"], tok["pe_free"]]
                           if j == 0 else [])
            q.wait_ge(*t)
    kb.close()
    return kb


def emit_load_tablesB(kb, blk, ins, tok):
    @blk.sync
    def _(q):
        kb.dma(q, kb.tcB[:], ins["TcB"], slot=11,
               waits=[tok["pe_free"], tok["resid_v"]])
        kb.dma(q, kb.tsB[:], ins["TsB"], slot=11)
    t = (kb.dsem[11], kb.dcnt[11])

    @blk.vector
    def _(v):
        kb.V.do(v, lambda e: e.nop(), waits=[t])
    return tok


def emit_a0(kb, blk, st, waps, tok, acc_out):
    V, S, P = kb.V, kb.S, kb.P
    lntok = emit_ln(kb, blk, st["resid"], st["xs"], NT0, tblocks(256, NT0),
                    kb.ln_scr, gamma=kb.a0w, beta=kb.a0b, shift=False,
                    extra_wait=tok.get("resid_v"))
    wv = waps["w_qkvT"].rearrange("(c p) m -> p c m", p=128)
    stage_war = [tok.get("pe_free")]
    war_v = [lntok]
    wo_war = [None]
    wtokL = [None]
    wv2 = None

    def load_wqkv(hb, wait):
        toks = []

        @blk.gpsimd
        def _(g):
            for i in range(3):
                toks.append(kb.dma(
                    g, st["wqkvW"][:, i * 1024:(i + 1) * 1024].rearrange(
                        "p (c m) -> p c m", c=C8),
                    wv[:, :, i * D + hb * 128:i * D + (hb + 1) * 128],
                    slot=2, waits=[wait] if i == 0 else []))
        return toks[-1]

    acc_war = [None, None]
    dma_war = [None, None]
    for half in range(2):
        qt0, qtn, kt0 = HALVES_A0[half]
        qrange = (qt0 * 128, (qtn + 1) * 128)
        krange = (kt0 * 128, (qtn + 1) * 128)
        kts = list(range(kt0, qtn + 1))

        @blk.scalar
        def _(s, kt0=kt0, qtn=qtn):
            nk = (qtn - kt0 + 1) * 128
            for hh in range(2):
                S.do(s, lambda e, hh=hh, nk=nk: e.copy(
                    st["kst"][hh][64:65, 0:nk],
                    kb.padrow_sb[0:1, kt0 * 128:kt0 * 128 + nk]),
                    waits=[stage_war[0], dma_war[0], dma_war[1]]
                    if hh == 0 else [])

        @blk.vector
        def _(v):
            if dma_war[0] is not None:
                kb.V.do(v, lambda e: e.nop(), waits=[dma_war[0], dma_war[1]])

        @blk.gpsimd
        def _(g):
            for h in range(2):
                kb.G.do(g, lambda e, h=h: e.memset(st["qst"][h][64:65, :], 1.0),
                        waits=[stage_war[0], dma_war[0], dma_war[1]]
                        if h == 0 else [])

        @blk.sync
        def _(q):
            for h in range(2):
                kb.dma(q, st["kst"][h][65:66, :], kb.inp["onesbf"], slot=10,
                       waits=[stage_war[0], dma_war[0], dma_war[1]]
                       if h == 0 else [])
        a0_aug_tok = (kb.dsem[10], kb.dcnt[10])
        a0_qones = [kb.G.prev]

        if half == 0:
            wtokL[0] = load_wqkv(0, stage_war[0])
        stok2 = None
        for hb in range(8):
            wtok = wtokL[0]
            cfgq = dict(xs=st["xs"], woff=0, wtok=wtok, war=stage_war[0],
                        war_v=war_v[0], prottok=None, xstok=lntok,
                        dst=st["qst"], l2n=False, prot=kb.protF[:],
                        qs_ap=None, scl=0.125, tcb=kb.tcA, tsb=kb.tsA)
            qpe, qv = qkv_production(kb, blk, st, cfgq, "q", qrange, qt0 * 128)
            cfgk = dict(xs=st["xs"], woff=1024, wtok=wtok, war=stage_war[0],
                        war_v=war_v[0], prottok=None, xstok=lntok,
                        dst=st["kst"], l2n=False, prot=kb.protF[:],
                        qs_ap=None, scl=1.0, tcb=kb.tcA, tsb=kb.tsA)
            kpe, kv = qkv_production(kb, blk, st, cfgk, "k", krange, kt0 * 128)
            cfgv = dict(xs=st["xs"], woff=2048, wtok=wtok, war=stage_war[0],
                        vinit=None)
            vtok = v_production(kb, blk, st, cfgv, kts)
            if hb < 7:
                wtokL[0] = load_wqkv(hb + 1, kb.P.prev)
            elif half == 0:
                wtokL[0] = load_wqkv(0, kb.P.prev)
            scfg = dict(qt0=qt0, qtn=qtn, kt0=kt0, lookback=1,
                        mask_of=lambda o: {0: 2, 1: 3}.get(o),
                        hb=hb, qtok=qv, ktok=kv, vtok=vtok,
                        aug_tok=a0_aug_tok, qones_tok=a0_qones[0],
                        nm_war=tok.get("nm_war"))
            ptok, vtok2, stok2 = attn_scores_pv(kb, blk, st, scfg)
            stage_war[0] = ptok
            war_v[0] = vtok2
            tok["nm_war"] = (kb.dsem[7], kb.dcnt[7])
        wcfg = dict(qt0=qt0, qtn=qtn, wo_ap=waps["w_outT"], dest="acc",
                    resid=st["resid"], acc_out=acc_out, acc_war=acc_war,
                    wo_war=wo_war[0], ottok=stok2, pe_tok=stage_war[0])
        wpe, wv2 = attn_wo(kb, blk, st, wcfg)
        wo_war[0] = wpe
        stage_war[0] = wpe
        war_v[0] = wv2
        dma_war[0] = (kb.dsem[1], kb.dcnt[1])
        dma_war[1] = (kb.dsem[9], kb.dcnt[9])
    tok["resid_v"] = wv2
    tok["pe_free"] = stage_war[0]
    tok["acc_dma"] = [(kb.dsem[1], kb.dcnt[1]), (kb.dsem[9], kb.dcnt[9])]
    return tok


def emit_combine_block0(kb, blk, st, xT, acc_in, x1_out, tok):
    V = kb.V
    ldtoks = {}
    addv = [None, None]
    for j in range(C8):
        for bi, (t0, t1) in enumerate(tblocks(HALO, NT0)):
            Wd = t1 - t0
            par = (j * 4 + bi) % 2

            @blk.sync
            def _(q, j=j, t0=t0, t1=t1, par=par):
                adm = tok.get("acc_dma") or [None, None]
                first = (j == 0 and t0 == HALO)
                xw = [addv[par]]
                if first:
                    xw += [adm[0], adm[1], (kb.dsem[7], kb.dcnt[7]),
                           (kb.dsem[10], kb.dcnt[10]),
                           tok.get("pe_free"), tok.get("resid_v")]
                t1a = kb.dma(q, st["rawst"][par][:, 0:t1 - t0],
                             xT[j * 128:(j + 1) * 128, t0:t1], slot=12 + par,
                             waits=xw)
                t1b = kb.dma(q, st["accst"][par][:, 0:t1 - t0],
                             acc_in[j * 128:(j + 1) * 128, t0 - HALO:t1 - HALO],
                             slot=12 + par)
                t1c = kb.dma(q, (st["t1st"] if par == 0 else st["t3st"])[:, 0:t1 - t0],
                             acc_in[D + j * 128:D + (j + 1) * 128,
                                    t0 - HALO:t1 - HALO],
                             slot=12 + par)
                ldtoks[(j, t0)] = t1c

            @blk.vector
            def _(v, j=j, t0=t0, t1=t1, Wd=Wd, par=par):
                V.do(v, lambda e: e.tensor_add(
                    st["resid"][:, j * NT0 + t0:j * NT0 + t1],
                    st["resid"][:, j * NT0 + t0:j * NT0 + t1],
                    st["rawst"][par][:, 0:Wd]),
                    waits=[ldtoks[(j, t0)], tok.get("resid_v")])
                V.do(v, lambda e: e.tensor_add(
                    st["resid"][:, j * NT0 + t0:j * NT0 + t1],
                    st["resid"][:, j * NT0 + t0:j * NT0 + t1],
                    st["accst"][par][:, 0:Wd]))
                V.do(v, lambda e: e.tensor_add(
                    st["resid"][:, j * NT0 + t0:j * NT0 + t1],
                    st["resid"][:, j * NT0 + t0:j * NT0 + t1],
                    (st["t1st"] if par == 0 else st["t3st"])[:, 0:Wd]))
                addv[par] = V.prev

            @blk.sync
            def _(q, j=j, t0=t0, t1=t1, par=par):
                kb.dma(q, x1_out[j * 128:(j + 1) * 128, t0 - HALO:t1 - HALO],
                       st["resid"][:, j * NT0 + t0:j * NT0 + t1], slot=1,
                       waits=[addv[par]])
    tok["x1_dma"] = (kb.dsem[1], kb.dcnt[1])
    return tok


def build_block0(with_exchange=True, dev=True):
    kb = K_B0 = KB()
    nc = kb.nc
    xT = kb.din("xT", [D, NT0])
    waps = {
        "Wqkv": kb.din("Wqkv", [IN_DEPTH, D, 3 * 1024]),
        "Wo": kb.din("Wo", [IN_DEPTH, D, D]),
        "W1p": kb.din("W1p", [IN_DEPTH, D, NPAIR * 256]),
        "W2": kb.din("W2", [IN_DEPTH, DIM_INNER, D]),
        "w_qkvT": kb.din("w_qkvT", [D, 3 * 1024]),
        "w_outT": kb.din("w_outT", [D, D]),
    }
    ins = {}
    for name, shape in [("TcB", [128, NT0]), ("TsB", [128, NT0]),
                        ("TciB", [128, NT0]), ("TsiB", [128, NT0]),
                        ("TcA", [128, NT0]), ("TsA", [128, NT0]),
                        ("MSK", [128, 4 * 128]), ("padrowbf", [1, NT0]),
                        ("onesbf", [1, KMAX])]:
        ins[name] = kb.din(name, shape, BF16)
    for name, shape in [("PROTF", [128, 128]), ("BD2", [128, 2]),
                        ("BD2T", [2, 128]), ("IDF", [128, 128]),
                        ("coremask", [128, 1]),
                        ("qs_col", [128, IN_DEPTH]),
                        ("ks_col", [128, IN_DEPTH]),
                        ("ln_g", [IN_DEPTH, 128, C8]),
                        ("ln_b", [IN_DEPTH, 128, C8]),
                        ("b1p", [IN_DEPTH, 128, 44]),
                        ("b2c", [IN_DEPTH, 128, C8]),
                        ("a0_lnw", [128, C8]), ("a0_lnb", [128, C8])]:
        ins[name] = kb.din(name, shape, F32)
    x1o = kb.dout("x1T", [D, OWN])
    spill_acc = kb.dint("spill_acc", [2 * D, OWN])
    st = alloc_block0(kb)
    kb.a0w = kb.sbuf("a0w", [128, C8], F32)
    kb.a0b = kb.sbuf("a0b", [128, C8], F32)
    kb.cinH = kb.dint("cinH", [D, HALO])
    kb.coutH = kb.dint("coutH", [2 * D, HALO])
    kb.open()
    with nc.Block() as blk:
        tok = emit_prelude(kb, blk, st, ins)
        ldtok = emit_load_x(kb, blk, st["resid"], xT)

        @blk.tensor
        def _(p):
            p.wait_ge(*ldtok)

        @blk.vector
        def _(v):
            v.wait_ge(*ldtok)

        @blk.scalar
        def _(s):
            s.wait_ge(*ldtok)
        lt = emit_ln(kb, blk, st["resid"], st["resid"], NT0, tblocks(0, NT0),
                     kb.ln_scr)
        tok["resid_v"] = lt
        tok = emit_a0(kb, blk, st, waps, tok, spill_acc)
        tok = emit_load_tablesB(kb, blk, ins, tok)
        for l in range(IN_DEPTH):
            tok = emit_inner_attn(kb, blk, st, l, waps, tok)
            tok = emit_inner_ffn(kb, blk, st, l, waps, tok)
            if with_exchange and l < IN_DEPTH - 1:
                tok = emit_halo_exchange(kb, blk, st, l, tok)
        tok = emit_combine_block0(kb, blk, st, xT, spill_acc, x1o, tok)

        @blk.sync
        def _(q):
            q.wait_ge(*tok["x1_dma"])
    kb.close()
    return kb


def build_piece_a0():
    kb = KB()
    nc = kb.nc
    rin = kb.din("residT", [D, NT0])
    waps = {
        "w_qkvT": kb.din("w_qkvT", [D, 3 * 1024]),
        "w_outT": kb.din("w_outT", [D, D]),
    }
    ins = {}
    for name, shape in [("TcA", [128, NT0]), ("TsA", [128, NT0]),
                        ("MSK", [128, 4 * 128]), ("padrowbf", [1, NT0]),
                        ("onesbf", [1, KMAX])]:
        ins[name] = kb.din(name, shape, BF16)
    for name, shape in [("PROTF", [128, 128]), ("BD2", [128, 2]),
                        ("BD2T", [2, 128]), ("IDF", [128, 128]),
                        ("coremask", [128, 1]),
                        ("a0_lnw", [128, C8]), ("a0_lnb", [128, C8])]:
        ins[name] = kb.din(name, shape, F32)
    acc = kb.dout("acc", [2 * D, OWN])
    st = alloc_block0(kb)
    kb.a0w = kb.sbuf("a0w", [128, C8], F32)
    kb.a0b = kb.sbuf("a0b", [128, C8], F32)
    kb.open()
    with nc.Block() as blk:
        tok = emit_prelude(kb, blk, st, ins)
        ldtok = emit_load_x(kb, blk, st["resid"], rin)

        @blk.tensor
        def _(p):
            p.wait_ge(*ldtok)

        @blk.vector
        def _(v):
            v.wait_ge(*ldtok)

        @blk.scalar
        def _(s):
            s.wait_ge(*ldtok)
        tok = emit_a0(kb, blk, st, waps, tok, acc)

        @blk.sync
        def _(q):
            q.wait_ge(*tok["acc_dma"][0])
            q.wait_ge(*tok["acc_dma"][1])
    kb.close()
    return kb


NB1 = OWN
FT = 4096
OVL1 = 75840
RG1 = [[0, 1], [2, 3], [4, 5], [6, 7]]


def alloc_block1(kb, fused_base=None):
    st = {}
    if fused_base is None:
        st["x1"] = kb.sbuf("x1", [128, C8 * NB1], F32)
        st["xob"] = kb.sbuf("xob", [128, C8 * NB1], BF16)
        st["xpb"] = kb.sbuf("xpb", [128, C8 * NB1], BF16)
        st["cstat"] = kb.sbuf("cstat", [128, 128], F32)
        st["fcol"] = kb.sbuf("fcol", [128, 32], F32)
        st["qmcol"] = kb.sbuf("qmcol", [128, 16], F32)
        kb.gcl = kb.sbuf("gclS", [128, C8], F32)
        kb.gch = kb.sbuf("gchS", [128, C8], F32)
        kb.rtq = kb.sbuf("rtqS", [128, C8], F32)
        kb.rtkv = kb.sbuf("rtkvS", [128, C8], F32)
        kb.idf = kb.sbuf("idfS", [128, 128], F32)
        kb.idb = kb.sbuf("idbS", [128, 128], BF16)
        kb.cmask = kb.sbuf("cmaskS", [128, 1], F32)
        kb.ocb = kb.sbuf("ocbS", [128, 1], BF16)
        alloc_consts(kb)
        base = kb.mkslab("ovl1", OVL1)
    else:
        fb = fused_base
        st["x1"] = kb.sbuf_at("x1", [128, C8 * NB1], F32, fb, 0)
        st["xob"] = kb.sbuf_at("xob", [128, C8 * NB1], BF16, fb, 65536)
        st["xpb"] = kb.sbuf_at("xpb", [128, C8 * NB1], BF16, fb, 98304)
        base = fb + 131072
        so = [131072 + OVL1]

        def sm(name, shape, dtype):
            nb = int(np.prod(shape[1:])) * (2 if dtype == BF16 else 4)
            t = kb.sbuf_at(name, shape, dtype, fb, so[0])
            so[0] += (nb + 31) // 32 * 32
            return t
        st["cstat"] = sm("cstat", [128, 128], F32)
        st["fcol"] = sm("fcol", [128, 32], F32)
        st["qmcol"] = sm("qmcol", [128, 16], F32)
        kb.gcl = sm("gclS", [128, C8], F32)
        kb.gch = sm("gchS", [128, C8], F32)
        kb.rtq = sm("rtqS", [128, C8], F32)
        kb.rtkv = sm("rtkvS", [128, C8], F32)
        kb.idb = sm("idbS", [128, 128], BF16)
        kb.ocb = sm("ocbS", [128, 1], BF16)
        st["b1_consts"] = {
            "idf": sm("idfB1", [128, 128], F32),
            "cmask": sm("cmaskB1", [128, 1], F32),
            "oc": sm("ocB1", [128, 1], F32),
            "orow": sm("orowB1", [1, 128], F32),
            "c_eps": sm("cepsB1", [128, 1], F32),
            "c_zero": sm("czeroB1", [128, 1], F32),
        }
        assert fb + so[0] <= 229344, so[0]
    kb.ovl_base = base

    def at(name, shape, dtype, off):
        return kb.sbuf_at(name, shape, dtype, base, off)

    st["xnr"] = at("xnr", [128, C8 * TB], F32, 0)
    st["m1row"] = at("m1row", [1, 2 * FT], F32, 32768)
    st["ln_scr"] = {
        "pA": None, "pB": None, "bc0": None, "bc1": None,
        "sq": at("sq1", [128, 2 * TB], F32, 65536),
        "aff": at("aff1", [128, TB], F32, 69632),
        "rows": at("rows1", [1, 6 * TB], F32, 16384),
    }
    st["gtmp"] = at("gtmp", [128, C8 * NB1], BF16, 0)
    st["diff32"] = at("diff32", [128, NB1], F32, 65536)
    st["srowA"] = at("srowA", [1, NB1], F32, 65536)
    st["srowB"] = at("srowB", [1, NB1], F32, 16384)
    st["c1row"] = at("c1row", [1, FT], F32, 10016)
    st["acol"] = at("acol", [128, 2], F32, 26400)
    st["rit"] = at("rit", [128, 2 * 128], F32, 0)
    st["cells"] = at("cells", [1, 64], F32, 1024)
    st["ldq"] = at("ldq", [16, 128], F32, 1280)
    st["raccum"] = at("raccum", [128, 2], F32, 1792)
    st["kvml"] = at("kvml", [1, FT], BF16, 1824)
    st["nrow"] = at("nrow", [1, FT], F32, 0)
    st["ldn"] = at("ldn", [32, 128], F32, 16384)
    st["nsq"] = at("nsq", [128, 2 * 128], BF16, 26432)
    st["qt1"] = at("qt1", [128, C8 * NB1], BF16, 0)
    st["wvs"] = at("wvs", [128, C8 * 1024], BF16, 32768)
    st["wkb"] = at("wkb", [128, C8 * 1024], BF16, 49152)
    st["ktm"] = at("ktm", [128, 2 * 1024], BF16, 65536)
    st["wqst"] = at("wqst", [128, 2 * 1024], BF16, 65536)
    st["vtm"] = at("vtm", [128, 2 * 512], BF16, 69632)
    st["pst"] = at("pst", [128, NB1], BF16, 71680)
    st["wvstage"] = at("wvstage", [128, 2 * 512], F32, 71680)
    st["rc1"] = at("rc1", [128, 16], F32, 75776)
    st["tt1"] = at("tt1", [128, 256], F32, 53248)
    st["qlh"] = [at(f"qlh{i}", [66, NB1], BF16, i * 4096) for i in range(2)]
    st["klh"] = [at(f"klh{i}", [66, FT], BF16, 8192 * (i + 1))
                 for i in range(2)]
    st["vlh"] = at("vlh", [128, 32 * 130], BF16, 24576)
    st["olh"] = at("olh", [128, NB1], BF16, 32928)
    st["wfst"] = at("wfst", [128, C8 * 128], F32, 37024)
    st["wst"] = at("wst", [128, C8 * 128], BF16, 41120)
    st["won"] = at("won", [128, 1024], BF16, 43168)
    st["pstg"] = at("pstg", [128, 2 * 256], BF16, 45216)
    st["nm1"] = at("nm1", [128, 128], BF16, 46240)
    st["nm1t"] = at("nm1t", [128, 128], BF16, 46496)
    st["osb1"] = at("osb1", [128, 64], F32, 46752)
    st["Pq"] = at("Pq", [128, 32 * 256], BF16, 53248)
    st["prow"] = [at(f"prow{i}", [128, 512], BF16, i * 1024)
                  for i in range(2)]
    st["vres"] = kb.sbuf_at("vres", [128, 32 * 1024], BF16,
                            kb.nc.lookup_mloc(st["xob"]).addr, 0)
    if fused_base is None:
        kb.PS = [kb.psum(f"PSB{i}", [128, 512], F32) for i in range(8)]
    st["ln_scr"]["pA"] = kb.PS[4]
    st["ln_scr"]["pB"] = kb.PS[5]
    st["ln_scr"]["bc0"] = kb.PS[6]
    st["ln_scr"]["bc1"] = kb.PS[7]
    if fused_base is None:
        kb.ln_scr = st["ln_scr"]
    return st


def emit_barrier(kb, blk, toks):
    toks = [t for t in toks if t is not None]

    @blk.sync
    def _(q):
        for t in toks:
            q.wait_ge(t[0], t[1])

    @blk.vector
    def _(v):
        kb.V.do(v, lambda e: e.nop(), waits=toks)

    @blk.scalar
    def _(s):
        kb.S.do(s, lambda e: e.nop(), waits=toks)

    @blk.tensor
    def _(p):
        kb.P.do(p, lambda e: e.nop(), waits=toks, mark=True)

    @blk.gpsimd
    def _(g):
        kb.G.do(g, lambda e: e.nop(), waits=toks)
    return {"v": kb.V.prev, "s": kb.S.prev, "p": kb.P.prev, "g": kb.G.prev}


def bar_toks(bar):
    return [bar["v"], bar["s"], bar["p"], bar["g"]]


def emit_b1_prelude(kb, blk, ins):
    ctok = emit_consts(kb, blk)

    @blk.gpsimd
    def _(g):
        kb.G.do(g, lambda e: e.memset(kb.ocb[:], 1.0))
    mtok = kb.G.prev

    @blk.sync
    def _(q):
        for sb, name in [(kb.idf, "IDF"), (kb.idb, "IDB"),
                         (kb.cmask, "coremask"), (kb.gcl, "gl_col"),
                         (kb.gch, "gh_col"), (kb.rtq, "rt_q"),
                         (kb.rtkv, "rt_kv")]:
            kb.dma(q, sb[:], ins[name], slot=0)
    ldtok = (kb.dsem[0], kb.dcnt[0])
    return emit_barrier(kb, blk, [ctok, mtok, ldtok])


def emit_b1_ln_s(kb, blk, st, x1T, tok):
    V, S, P = kb.V, kb.S, kb.P
    PS = kb.PS
    ldtok = emit_load_x(kb, blk, st["x1"], x1T, nt=NB1)

    @blk.tensor
    def _(p):
        p.wait_ge(*ldtok)

    @blk.vector
    def _(v):
        v.wait_ge(*ldtok)

    @blk.scalar
    def _(s):
        s.wait_ge(*ldtok)
    rr = [None]
    srow = [None]
    for bi, (t0, t1) in enumerate(tblocks(0, NB1)):
        W = t1 - t0
        lt = emit_ln(kb, blk, st["x1"], st["xnr"], NB1, [(t0, t1)],
                     kb.ln_scr, dst_bf16=st["xob"], nt_dst=TB, dst_off=t0,
                     extra_wait=rr[0])

        @blk.tensor
        def _(p, bi=bi, W=W, lt=lt):
            for r in range(2):
                for j in range(C8):
                    P.do(p, lambda e, r=r, j=j: e.matmul(
                        PS[r][0:1, 0:W],
                        (kb.rtq if r == 0 else kb.rtkv)[:, j:j + 1],
                        st["xnr"][:, j * TB:j * TB + W],
                        start=(j == 0), stop=(j == C8 - 1)),
                        waits=[lt, srow[0]] if (r == 0 and j == 0) else [],
                        mark=(r == 1 and j == C8 - 1))
            rr[0] = P.prev
        mmt = P.prev

        @blk.vector
        def _(v, t0=t0, W=W, mmt=mmt):
            for r in range(2):
                V.do(v, lambda e, r=r: e.tensor_copy(
                    st["m1row"][0:1, r * FT + t0:r * FT + t0 + W],
                    PS[r][0:1, 0:W]), waits=[mmt] if r == 0 else [])
            srow[0] = V.prev

        @blk.sync
        def _(q, t0=t0, t1=t1, lt=lt):
            for j in range(C8):
                kb.dma(q, kb.cin1[j * 128:(j + 1) * 128, t0:t1],
                       st["xob"][:, j * NB1 + t0:j * NB1 + t1], slot=6,
                       waits=[lt] if j == 0 else [])

    @blk.sync
    def _(q):
        kb.dma(q, kb.cins[0:1, :], st["m1row"][0:1, 0:NB1], slot=6,
               waits=[kb.V.prev])
        kb.dma(q, kb.cins[1:2, :], st["m1row"][0:1, FT:FT + NB1], slot=6)
    tok["gin"] = (kb.dsem[6], kb.dcnt[6])
    return tok


def emit_b1_gather(kb, blk, st, tok):
    V = kb.V

    @blk.gpsimd
    def _(g):
        g.wait_ge(*tok["gin"])
        kb.G.do(g, lambda e: e.collective_compute(
            "AllGather", mybir.AluOpType.bypass, replica_groups=RG1,
            ins=[kb.cin1.opt()], outs=[kb.cout1.opt()]))
        kb.G.do(g, lambda e: e.collective_compute(
            "AllGather", mybir.AluOpType.bypass, replica_groups=RG1,
            ins=[kb.cins.opt()], outs=[kb.couts.opt()]))
    ctok = kb.G.prev

    @blk.sync
    def _(q):
        for j in range(C8):
            kb.dma(q, st["xpb"][:, j * NB1:(j + 1) * NB1],
                   kb.cout1[D + j * 128:D + (j + 1) * 128, :], slot=6,
                   waits=[ctok] if j == 0 else [])
            kb.dma(q, st["gtmp"][:, j * NB1:(j + 1) * NB1],
                   kb.cout1[j * 128:(j + 1) * 128, :], slot=6)
    xg = (kb.dsem[6], kb.dcnt[6])
    @blk.vector
    def _(v):
        for j in range(C8):
            V.do(v, lambda e, j=j: e.tensor_sub(
                st["diff32"][:, 0:NB1],
                st["gtmp"][:, j * NB1:(j + 1) * NB1],
                st["xpb"][:, j * NB1:(j + 1) * NB1]),
                waits=[xg] if j == 0 else [])
            V.do(v, lambda e, j=j: e.scalar_tensor_tensor(
                st["xpb"][:, j * NB1:(j + 1) * NB1],
                st["diff32"][:, 0:NB1], kb.cmask[:, 0:1],
                st["xpb"][:, j * NB1:(j + 1) * NB1],
                op0=ALU.mult, op1=ALU.add))
    xv = V.prev

    bv = [xv]
    for r in range(2):
        @blk.sync
        def _(q, r=r, bv=bv):
            kb.dma(q, st["srowA"][0:1, :], kb.couts[r:r + 1, :], slot=6,
                   waits=[bv[0]])
            kb.dma(q, st["srowB"][0:1, :], kb.couts[2 + r:3 + r, :], slot=6)
        sg = (kb.dsem[6], kb.dcnt[6])

        @blk.vector
        def _(v, r=r, sg=sg, bv=bv):
            V.do(v, lambda e: e.tensor_sub(
                st["srowA"][0:1, :], st["srowA"][0:1, :],
                st["srowB"][0:1, :]), waits=[sg])
            V.do(v, lambda e: e.scalar_tensor_tensor(
                st["m1row"][0:1, r * FT + NB1:r * FT + FT],
                st["srowA"][0:1, :], kb.cmask[0:1, 0:1],
                st["srowB"][0:1, :], op0=ALU.mult, op1=ALU.add))
            bv[0] = kb.V.prev
    tok["xnv"] = kb.V.prev
    return tok


def emit_b1_routing(kb, blk, st, tok):
    V, S, P = kb.V, kb.S, kb.P
    PS = kb.PS

    @blk.sync
    def _(q):
        kb.dma(q, kb.rscr[0:1, :], st["m1row"][0:1, :], slot=11,
               waits=[tok["xnv"]])
    t0 = (kb.dsem[11], kb.dcnt[11])

    @blk.sync
    def _(q):
        for r in range(2):
            kb.dma(q, st["rit"][:, r * 128:r * 128 + 32],
                   kb.rscr[0:1, r * FT:(r + 1) * FT].rearrange(
                       "a (p c) -> (a p) c", p=128),
                   slot=11, waits=[t0] if r == 0 else [])
    t1 = (kb.dsem[11], kb.dcnt[11])
    logk = [float(np.log(KQ)), float(np.log(KKV))]
    acell = [st["cells"][0:1, r:r + 1] for r in range(2)]
    lncell = [st["cells"][0:1, 2 + r:3 + r] for r in range(2)]
    apcell = [st["cells"][0:1, 4 + r:5 + r] for r in range(2)]

    @blk.gpsimd
    def _(g):
        kb.G.do(g, lambda e: e.memset(st["cells"][:], 0.0), waits=[t1])
    cmt = kb.G.prev

    @blk.vector
    def _(v):
        for r in range(2):
            V.do(v, lambda e, r=r: e.tensor_copy(
                st["rit"][:, r * 128 + 64:r * 128 + 96],
                st["rit"][:, r * 128:r * 128 + 32]),
                waits=[t1, cmt] if r == 0 else [])
    inittok = V.prev
    abc = [None, None]
    for it in range(N_ITERS):
        for r in range(2):
            @blk.vector
            def _(v, r=r, it=it):
                V.do(v, lambda e: e.tensor_sub(
                    st["rit"][:, r * 128 + 32:r * 128 + 64],
                    st["rit"][:, r * 128:r * 128 + 32],
                    st["rit"][:, r * 128 + 64:r * 128 + 96]),
                    waits=[inittok] if it == 0 and r == 0 else [])
            ttok = V.prev

            @blk.scalar
            def _(s, r=r, it=it, ttok=ttok):
                bias = kb.c_zero[:, 0:1] if it == 0 \
                    else st["acol"][:, r:r + 1]
                S.do(s, lambda e: e.activation(
                    st["rit"][:, r * 128 + 96:r * 128 + 128],
                    st["rit"][:, r * 128 + 32:r * 128 + 64],
                    AF.Exp, bias=bias,
                    accum_out=st["raccum"][:, r:r + 1]), waits=[ttok])
            utok = S.prev

            @blk.tensor
            def _(p, r=r, utok=utok):
                P.do(p, lambda e: e.matmul(
                    PS[6][0:1, r:r + 1], kb.oc[:, 0:1],
                    st["raccum"][:, r:r + 1], start=True, stop=True),
                    waits=[utok], mark=True)
            ttot = P.prev

            @blk.scalar
            def _(s, r=r, ttot=ttot):
                S.do(s, lambda e: e.activation(
                    lncell[r], PS[6][0:1, r:r + 1], AF.Ln,
                    bias=kb.c_zero[0:1, 0:1]), waits=[ttot])
            lnt = S.prev

            @blk.vector
            def _(v, r=r, lnt=lnt):
                V.do(v, lambda e: e.scalar_tensor_tensor(
                    acell[r], lncell[r], -1.0, apcell[r],
                    op0=ALU.mult, op1=ALU.add), waits=[lnt])
                V.do(v, lambda e: e.tensor_scalar(
                    acell[r], acell[r], float(logk[r]), None, op0=ALU.add))
                V.do(v, lambda e: e.tensor_copy(apcell[r], acell[r]))
            atok = V.prev

            @blk.tensor
            def _(p, r=r, atok=atok):
                P.do(p, lambda e: e.matmul(
                    PS[4 + r][:, 0:1], kb.orow[:, :], acell[r],
                    start=True, stop=True),
                    waits=[atok, abc[r]], mark=True)
            btok = P.prev

            @blk.vector
            def _(v, r=r, btok=btok):
                V.do(v, lambda e: e.tensor_copy(
                    st["acol"][:, r:r + 1], PS[4 + r][:, 0:1]),
                    waits=[btok])
            acopy = V.prev

            @blk.scalar
            def _(s, r=r, acopy=acopy):
                S.do(s, lambda e: e.activation(
                    st["rit"][:, r * 128 + 64:r * 128 + 96],
                    st["rit"][:, r * 128:r * 128 + 32],
                    AF.Relu, bias=st["acol"][:, r:r + 1]), waits=[acopy])
                abc[r] = kb.S.prev
    rtok = kb.S.prev
    e0 = st["cells"][0:1, 8:9]
    e1 = st["cells"][0:1, 9:10]
    sp = st["cells"][0:1, 10:11]
    so = st["cells"][0:1, 11:12]
    for r, nsel in ((0, NQ), (1, NKV)):
        row = st["m1row"][0:1, r * FT:(r + 1) * FT]

        @blk.vector
        def _(v, r=r, nsel=nsel, row=row):
            V.do(v, lambda e: e.tensor_scalar(
                row, row, apcell[r], 0.0, op0=ALU.add, op1=ALU.is_ge),
                waits=[rtok] if r == 0 else [])
            V.do(v, lambda e: e.tensor_tensor_scan(
                st["c1row"][0:1, :], row, row, 0.0,
                op0=ALU.add, op1=ALU.bypass))
            V.do(v, lambda e: e.tensor_copy(e0, st["c1row"][0:1, NB1 - 1:NB1]))
            V.do(v, lambda e: e.tensor_copy(e1, st["c1row"][0:1, FT - 1:FT]))
            V.do(v, lambda e: e.tensor_sub(sp, e1, e0))
            V.do(v, lambda e: e.tensor_mul(sp, sp, kb.cmask[0:1, 0:1]))
            V.do(v, lambda e: e.tensor_mul(so, e0, kb.cmask[0:1, 0:1]))
            V.do(v, lambda e: e.tensor_scalar(
                st["c1row"][0:1, 0:NB1], st["c1row"][0:1, 0:NB1], sp,
                float(nsel) + 0.5, op0=ALU.add, op1=ALU.is_le))
            V.do(v, lambda e: e.tensor_scalar(
                st["c1row"][0:1, NB1:FT], st["c1row"][0:1, NB1:FT], so,
                float(nsel) + 0.5, op0=ALU.subtract, op1=ALU.is_le))
            V.do(v, lambda e: e.tensor_mul(row, row, st["c1row"][0:1, :]))

    @blk.vector
    def _(v):
        V.do(v, lambda e: e.tensor_scalar(
            st["kvml"][0:1, :], st["m1row"][0:1, FT:2 * FT], -1.0, 1.0e30,
            op0=ALU.add, op1=ALU.mult))
    mtok = kb.V.prev

    @blk.sync
    def _(q):
        kb.dma(q, kb.kvmd[0:1, :], st["kvml"][0:1, :], slot=11,
               waits=[mtok])
        kb.dma(q, kb.rqm[0:1, :], st["m1row"][0:1, 0:NB1], slot=11)
    d1 = (kb.dsem[11], kb.dcnt[11])

    @blk.sync
    def _(q):
        kb.dma(q, st["ldq"][:], kb.rqm[0:1, :].rearrange(
            "a (p c) -> (a p) c", p=16), slot=11, waits=[d1])
    d2 = (kb.dsem[11], kb.dcnt[11])

    @blk.tensor
    def _(p):
        P.do(p, lambda e: e.matmul(PS[3][0:128, 0:16], st["ldq"][:],
                                   kb.idf[0:16, 0:16], start=True, stop=True),
             waits=[d2], mark=True)
    qmt = P.prev

    @blk.vector
    def _(v):
        V.do(v, lambda e: e.tensor_copy(st["qmcol"][:], PS[3][0:128, 0:16]),
             waits=[qmt])
    tok["maskv"] = kb.V.prev
    return tok


def emit_b1_norms(kb, blk, st, tok):
    V, S, P = kb.V, kb.S, kb.P
    PS = kb.PS
    warP = [None]
    mmtk = {}
    for mt in range(32):
        srcb = st["xob"] if mt < 16 else st["xpb"]
        c0 = (mt % 16) * 128

        @blk.scalar
        def _(s, mt=mt, srcb=srcb, c0=c0):
            for j in range(C8):
                war = (mmtk.get((mt, j - 2)) if j >= 2
                       else mmtk.get((mt - 1, 6 + j)))
                S.do(s, lambda e, j=j: e.activation(
                    st["nsq"][:, (j % 2) * 128:(j % 2) * 128 + 128],
                    srcb[:, j * NB1 + c0:j * NB1 + c0 + 128], AF.Square),
                    waits=[war] +
                    ([tok["maskv"]] if mt == 0 and j == 0 else []))
                sqt = kb.S.prev

                @blk.tensor
                def _(p, j=j, mt=mt, sqt=sqt):
                    kb.P.do(p, lambda e: e.matmul(
                        PS[2][0:1, 0:128], kb.ocb[:, 0:1],
                        st["nsq"][:, (j % 2) * 128:(j % 2) * 128 + 128],
                        start=(j == 0), stop=(j == C8 - 1)),
                        waits=[sqt, warP[0]] if j == 0 else [sqt],
                        mark=True)
                    mmtk[(mt, j)] = kb.P.prev

        @blk.vector
        def _(v, mt=mt):
            V.do(v, lambda e: e.tensor_copy(
                st["nrow"][0:1, mt * 128:mt * 128 + 128], PS[2][0:1, 0:128]),
                waits=[mmtk[(mt, C8 - 1)]])
            warP[0] = kb.V.prev
    rowv = kb.V.prev

    @blk.scalar
    def _(s):
        S.do(s, lambda e: e.activation(st["nrow"][0:1, :], st["nrow"][0:1, :],
                                       AF.Sqrt, bias=kb.c_zero[0:1, 0:1]),
             waits=[rowv])

    @blk.vector
    def _(v):
        V.do(v, lambda e: e.tensor_scalar(
            st["nrow"][0:1, :], st["nrow"][0:1, :], 1e-12, None,
            op0=ALU.max), waits=[kb.S.prev])
        V.do(v, lambda e: e.reciprocal(st["nrow"][0:1, :], st["nrow"][0:1, :]))
        V.do(v, lambda e: e.tensor_scalar(
            st["nrow"][0:1, :], st["nrow"][0:1, :], 32.0, None, op0=ALU.mult))
    ntok = kb.V.prev

    @blk.sync
    def _(q):
        kb.dma(q, kb.rfn[0:1, :], st["nrow"][0:1, :], slot=11, waits=[ntok])
    d1 = (kb.dsem[11], kb.dcnt[11])

    @blk.sync
    def _(q):
        kb.dma(q, st["ldn"][:], kb.rfn[0:1, :].rearrange(
            "a (p c) -> (a p) c", p=32), slot=11, waits=[d1])
    d2 = (kb.dsem[11], kb.dcnt[11])

    @blk.tensor
    def _(p):
        P.do(p, lambda e: e.matmul(PS[3][0:128, 0:32], st["ldn"][:],
                                   kb.idf[0:32, 0:32], start=True, stop=True),
             waits=[d2, tok["maskv"]], mark=True)
    fmt = P.prev

    @blk.vector
    def _(v):
        V.do(v, lambda e: e.tensor_copy(st["fcol"][:], PS[3][0:128, 0:32]),
             waits=[fmt])
    tok["fcolv"] = kb.V.prev
    return tok


def emit_b1_l1a_S(kb, blk, st, waps, tok, bar):
    V, S, P = kb.V, kb.S, kb.P
    PS = kb.PS
    wv_done = [None]
    for j in range(C8):
        dtk = [None]

        @blk.sync
        def _(q, j=j):
            for hf in range(2):
                dtk[0] = kb.dma(q, st["wvstage"][:, hf * 512:(hf + 1) * 512],
                                waps["wv1T"][j * 128:(j + 1) * 128,
                                             hf * 512:(hf + 1) * 512], slot=7,
                                waits=([wv_done[0]] + bar_toks(bar))
                                if hf == 0 else [])

        @blk.vector
        def _(v, j=j):
            V.do(v, lambda e: e.reduce_max(
                st["rc1"][:, 0:1], st["wvstage"][:, 0:512],
                axis=AX.X, negate=True), waits=[dtk[0]])
            V.do(v, lambda e: e.reduce_max(
                st["rc1"][:, 1:2], st["wvstage"][:, 512:1024],
                axis=AX.X, negate=True))
            V.do(v, lambda e: e.tensor_tensor(
                st["rc1"][:, 0:1], st["rc1"][:, 0:1], st["rc1"][:, 1:2],
                op=ALU.min))
        nmt = V.prev

        @blk.scalar
        def _(s, j=j, nmt=nmt):
            for hf in range(2):
                S.do(s, lambda e, hf=hf: e.activation(
                    st["wvstage"][:, hf * 512:(hf + 1) * 512],
                    st["wvstage"][:, hf * 512:(hf + 1) * 512], AF.Exp,
                    bias=st["rc1"][:, 0:1],
                    accum_out=st["rc1"][:, 2 + hf:3 + hf]),
                    waits=[nmt] if hf == 0 else [])
        expt = S.prev

        @blk.vector
        def _(v, j=j, expt=expt):
            V.do(v, lambda e: e.tensor_add(
                st["rc1"][:, 2:3], st["rc1"][:, 2:3], st["rc1"][:, 3:4]),
                waits=[expt])
            V.do(v, lambda e: e.reciprocal(st["rc1"][:, 2:3],
                                           st["rc1"][:, 2:3]))
            for hf in range(2):
                V.do(v, lambda e, hf=hf: e.tensor_scalar(
                    st["wvs"][:, j * 1024 + hf * 512:j * 1024 + hf * 512 + 512],
                    st["wvstage"][:, hf * 512:(hf + 1) * 512],
                    st["rc1"][:, 2:3], 1.0 / 1024.0,
                    op0=ALU.mult, op1=ALU.mult))
            wv_done[0] = V.prev
    wvs_done = kb.V.prev
    @blk.gpsimd
    def _(g):
        kb.dma(g, st["wkb"][:].rearrange("p (c m) -> p c m", c=C8),
               waps["wk1"].rearrange("(c p) m -> p c m", p=128), slot=2,
               waits=bar_toks(bar))
    wktok = (kb.dsem[2], kb.dcnt[2])
    wqv = waps["wq1"].rearrange("(c p) m -> p c m", p=128)
    evq = [None, None]
    for dd in range(C8):
        par = dd % 2
        wslot = 3 if par == 0 else 14

        @blk.gpsimd
        def _(g, dd=dd, par=par, wslot=wslot):
            kb.dma(g, st["wqst"][:, par * 1024:par * 1024 + 1024].rearrange(
                "p (c m) -> p c m", c=C8),
                wqv[:, :, dd * 128:(dd + 1) * 128], slot=wslot,
                waits=[evq[par]] + (bar_toks(bar) if dd == 0 else []))
        wqt = (kb.dsem[wslot], kb.dcnt[wslot])
        for tt in range(16):
            @blk.tensor
            def _(p, dd=dd, tt=tt, par=par, wqt=wqt):
                for c in range(C8):
                    P.do(p, lambda e, c=c: e.matmul(
                        PS[par][:, 0:128],
                        st["wqst"][:, par * 1024 + c * 128:
                                   par * 1024 + c * 128 + 128],
                        st["xob"][:, c * NB1 + tt * 128:c * NB1 + tt * 128 + 128],
                        start=(c == 0), stop=(c == C8 - 1)),
                        waits=[wqt, evq[par]] if c == 0 else [],
                        mark=(c == C8 - 1))
            mmt = P.prev

            @blk.scalar
            def _(s, dd=dd, tt=tt, par=par, mmt=mmt):
                S.do(s, lambda e: e.activation(
                    st["qt1"][:, dd * NB1 + tt * 128:dd * NB1 + tt * 128 + 128],
                    PS[par][:, 0:128], AF.Copy, bias=0.0,
                    scale=1.0 / 64.0), waits=[mmt])
                evq[par] = kb.S.prev
    qt_done = kb.S.prev
    qt_pe = kb.P.prev
    wq_dma_done = (kb.dsem[3], kb.dcnt[3])
    stpe = [None, None]
    stw = [None] * 4
    spb = [None]
    spv = [None]
    pvsum = [None]
    vcp = [None]
    for mt in range(32):
        srcb = st["xob"] if mt < 16 else st["xpb"]
        c0 = (mt % 16) * 128
        slot = mt % 2
        evk = [None]
        for dd in range(C8):
            @blk.tensor
            def _(p, dd=dd, c0=c0, srcb=srcb):
                for c in range(C8):
                    P.do(p, lambda e, c=c: e.matmul(
                        PS[2][:, 0:128],
                        st["wkb"][:, c * 1024 + dd * 128:c * 1024 + dd * 128 + 128],
                        srcb[:, c * NB1 + c0:c * NB1 + c0 + 128],
                        start=(c == 0), stop=(c == C8 - 1)),
                        waits=[wktok, evk[0]] if c == 0 else [],
                        mark=(c == C8 - 1))
            mmt = P.prev

            @blk.scalar
            def _(s, dd=dd, slot=slot, mt=mt, mmt=mmt):
                S.do(s, lambda e: e.copy(
                    st["ktm"][:, slot * 1024 + dd * 128:slot * 1024 + dd * 128 + 128],
                    PS[2][:, 0:128]),
                    waits=[mmt, stpe[slot],
                           wq_dma_done if mt < 2 else None]
                    if dd == 0 else [mmt])
                evk[0] = kb.S.prev
        ktok = kb.S.prev
        for vv in range(2):
            @blk.tensor
            def _(p, vv=vv, c0=c0, srcb=srcb):
                for c in range(C8):
                    P.do(p, lambda e, c=c: e.matmul(
                        PS[3][:, 0:512],
                        srcb[:, c * NB1 + c0:c * NB1 + c0 + 128],
                        st["wvs"][:, c * 1024 + vv * 512:c * 1024 + vv * 512 + 512],
                        start=(c == 0), stop=(c == C8 - 1)),
                        waits=[wvs_done, vcp[0]] if c == 0 else [],
                        mark=(c == C8 - 1))
            mmt = P.prev

            @blk.scalar
            def _(s, vv=vv, mmt=mmt):
                S.do(s, lambda e: e.copy(
                    st["vtm"][:, vv * 512:vv * 512 + 512],
                    PS[3][:, 0:512]),
                    waits=[mmt, spv[0]] if vv == 0 else [mmt])
                vcp[0] = kb.S.prev
        vtok = kb.S.prev

        @blk.sync
        def _(q, mt=mt, vtok=vtok):
            spv[0] = kb.dma(q, kb.spillV[mt * 128:(mt + 1) * 128, :],
                            st["vtm"][:, 0:1024], slot=9, waits=[vtok])
        nmlast = [None]
        for nb in range(4):
            @blk.tensor
            def _(p, nb=nb, slot=slot, ktok=ktok):
                for dd in range(C8):
                    P.do(p, lambda e, dd=dd: e.matmul(
                        PS[4 + nb][:, 0:512],
                        st["ktm"][:, slot * 1024 + dd * 128:
                                  slot * 1024 + dd * 128 + 128],
                        st["qt1"][:, dd * NB1 + nb * 512:dd * NB1 + nb * 512 + 512],
                        start=(dd == 0), stop=(dd == C8 - 1)),
                        waits=[ktok, qt_done, stw[nb]] if dd == 0 else [],
                        mark=(dd == C8 - 1))
                if nb == 3:
                    stpe[slot] = P.prev
            mmt = P.prev

            @blk.vector
            def _(v, nb=nb, mmt=mmt):
                V.do(v, lambda e: e.reduce_max(
                    st["rc1"][:, nb:nb + 1], PS[4 + nb][:, 0:512],
                    axis=AX.X, negate=True), waits=[mmt])
                nmlast[0] = kb.V.prev

        @blk.vector
        def _(v, mt=mt):
            V.do(v, lambda e: e.tensor_tensor(
                st["rc1"][:, 0:1], st["rc1"][:, 0:1], st["rc1"][:, 1:2],
                op=ALU.min), waits=[nmlast[0]])
            V.do(v, lambda e: e.tensor_tensor(
                st["rc1"][:, 2:3], st["rc1"][:, 2:3], st["rc1"][:, 3:4],
                op=ALU.min))
            V.do(v, lambda e: e.tensor_tensor(
                st["cstat"][:, mt:mt + 1], st["rc1"][:, 0:1],
                st["rc1"][:, 2:3], op=ALU.min))
        nmt = kb.V.prev
        for nb in range(4):
            @blk.scalar
            def _(s, nb=nb, mt=mt, nmt=nmt):
                S.do(s, lambda e: e.activation(
                    st["pst"][:, nb * 512:nb * 512 + 512],
                    PS[4 + nb][:, 0:512], AF.Exp,
                    bias=st["cstat"][:, mt:mt + 1],
                    accum_out=st["rc1"][:, 4 + nb:5 + nb]),
                    waits=[nmt, spb[0], pvsum[0]] if nb == 0 else [])
                stw[nb] = kb.S.prev
        expt = kb.S.prev

        @blk.vector
        def _(v, mt=mt, expt=expt):
            V.do(v, lambda e: e.tensor_add(
                st["rc1"][:, 4:5], st["rc1"][:, 4:5], st["rc1"][:, 5:6]),
                waits=[expt])
            V.do(v, lambda e: e.tensor_add(
                st["rc1"][:, 6:7], st["rc1"][:, 6:7], st["rc1"][:, 7:8]))
            V.do(v, lambda e: e.tensor_add(
                st["cstat"][:, 32 + mt:33 + mt], st["rc1"][:, 4:5],
                st["rc1"][:, 6:7]))
            pvsum[0] = kb.V.prev

        @blk.sync
        def _(q, mt=mt, expt=expt):
            spb[0] = kb.dma(q, kb.spillP[mt * 128:(mt + 1) * 128, :],
                            st["pst"][:, 0:NB1], slot=1, waits=[expt])
    tok["spillP"] = (kb.dsem[1], kb.dcnt[1])
    tok["spillV"] = (kb.dsem[9], kb.dcnt[9])
    return tok


def emit_b1_merge(kb, blk, st, tok, bar):
    V, S = kb.V, kb.S

    @blk.sync
    def _(q):
        kb.dma(q, kb.cing[:], st["cstat"][:, 0:64], slot=6,
               waits=bar_toks(bar))

    @blk.gpsimd
    def _(g):
        g.wait_ge(kb.dsem[6], kb.dcnt[6])
        kb.G.do(g, lambda e: e.collective_compute(
            "AllGather", mybir.AluOpType.bypass, replica_groups=RG1,
            ins=[kb.cing.opt()], outs=[kb.coutg.opt()]))
    ctok = kb.G.prev

    @blk.sync
    def _(q):
        kb.dma(q, st["tt1"][:, 0:64], kb.coutg[0:128, :], slot=6,
               waits=[ctok])
        kb.dma(q, st["tt1"][:, 64:128], kb.coutg[128:256, :], slot=6)
    gtok = (kb.dsem[6], kb.dcnt[6])

    @blk.vector
    def _(v):
        V.do(v, lambda e: e.tensor_sub(st["tt1"][:, 0:64], st["tt1"][:, 0:64],
                                       st["tt1"][:, 64:128]), waits=[gtok])
        V.do(v, lambda e: e.scalar_tensor_tensor(
            st["tt1"][:, 128:192], st["tt1"][:, 0:64], kb.cmask[:, 0:1],
            st["tt1"][:, 64:128], op0=ALU.mult, op1=ALU.add))
        for half in range(2):
            for dst in (192, 224):
                V.do(v, lambda e, half=half, dst=dst: e.tensor_copy(
                    st["tt1"][:, dst + half * 16:dst + half * 16 + 16],
                    st["tt1"][:, dst - 64 + (1 - half) * 16:
                              dst - 64 + (1 - half) * 16 + 16]))
        V.do(v, lambda e: e.tensor_tensor(
            st["cstat"][:, 96:128], st["cstat"][:, 0:32],
            st["tt1"][:, 192:224], op=ALU.min))
        V.do(v, lambda e: e.tensor_sub(
            st["tt1"][:, 0:32], st["cstat"][:, 96:128], st["cstat"][:, 0:32]))
        V.do(v, lambda e: e.tensor_sub(
            st["tt1"][:, 32:64], st["cstat"][:, 96:128],
            st["tt1"][:, 192:224]))
    dtok = V.prev

    @blk.scalar
    def _(s):
        S.do(s, lambda e: e.activation(st["tt1"][:, 0:32], st["tt1"][:, 0:32],
                                       AF.Exp, bias=kb.c_zero[:, 0:1]),
             waits=[dtok])
        S.do(s, lambda e: e.activation(st["tt1"][:, 32:64],
                                       st["tt1"][:, 32:64],
                                       AF.Exp, bias=kb.c_zero[:, 0:1]))
    etok = S.prev

    @blk.vector
    def _(v):
        V.do(v, lambda e: e.tensor_mul(st["tt1"][:, 64:96],
                                       st["cstat"][:, 32:64],
                                       st["tt1"][:, 0:32]), waits=[etok])
        V.do(v, lambda e: e.tensor_mul(st["tt1"][:, 96:128],
                                       st["tt1"][:, 224:256],
                                       st["tt1"][:, 32:64]))
        V.do(v, lambda e: e.tensor_add(st["tt1"][:, 64:96],
                                       st["tt1"][:, 64:96],
                                       st["tt1"][:, 96:128]))
        V.do(v, lambda e: e.reciprocal(st["tt1"][:, 64:96],
                                       st["tt1"][:, 64:96]))
        V.do(v, lambda e: e.tensor_mul(st["cstat"][:, 64:96],
                                       st["tt1"][:, 0:32],
                                       st["tt1"][:, 64:96]))
    tok["facv"] = V.prev
    return tok


def emit_b1_lh(kb, blk, st, waps, tok, bar, kind):
    V, S, P = kb.V, kb.S, kb.P
    PS = kb.PS
    heavy = kind == "heavy"
    gcol = kb.gch if heavy else kb.gcl
    wq_ap = waps["wq_h" if heavy else "wq_l"]
    wkv_ap = waps["wkv_hT" if heavy else "wkvl_r"]
    wo_ap = waps["wo_h" if heavy else "wo_l"]
    ktiles = 32 if heavy else 16
    trq = [tok.get("lh_trq")]
    cps = [tok.get("lh_cps")]
    for g_ in range(4):
        firstw = bar_toks(bar) if (g_ == 0 and not heavy) else []
        attn_pe = tok.get("lh_pe")
        bias_prev = tok.get("lh_bias")
        wo_v = tok.get("lh_wov")
        @blk.gpsimd
        def _(gq, attn_pe=attn_pe, firstw=firstw, g_=g_):
            kb.G.do(gq, lambda e: e.memset(st["vlh"][:], 1.0),
                    waits=[attn_pe] + firstw)
            if g_ == 0:
                kb.G.do(gq, lambda e: e.memset(st["nm1"][:], 0.0))
                for hh in range(2):
                    kb.G.do(gq, lambda e, hh=hh: e.memset(
                        st["qlh"][hh][64:65, :], 1.0))
                    if not heavy:
                        kb.G.do(gq, lambda e, hh=hh: e.memset(
                            st["klh"][hh][64:65, :], 0.0))
        mst = kb.G.prev
        if g_ == 0:
            @blk.sync
            def _(q, mst=mst, attn_pe=attn_pe):
                for hh in range(2):
                    kb.dma(q, st["klh"][hh][65:66, :],
                           kb.inp["onesb1"][0:1, :], slot=8,
                           waits=[mst, attn_pe] if hh == 0 else [])
                    if heavy:
                        kb.dma(q, st["klh"][hh][64:65, :], kb.kvmd[0:1, :],
                               slot=8)
            mrow = (kb.dsem[8], kb.dcnt[8])
            tok["lh_mrow"] = mrow
        else:
            mrow = tok["lh_mrow"]
        prod_toks = []
        for ss in range(3):
            isq = ss == 0

            @blk.sync
            def _(q, g_=g_, ss=ss, wo_v=wo_v, firstw=firstw):
                if isq := (ss == 0):
                    src = wq_ap.rearrange("(c p) m -> p c m", p=128)[
                        :, :, g_ * 128:(g_ + 1) * 128]
                else:
                    cb = (2 * g_ + ss - 1) * 128
                    src = wkv_ap.rearrange("(c p) m -> p c m", p=128)[
                        :, :, cb:cb + 128]
                kb.dma(q, st["wfst"][:].rearrange("p (c m) -> p c m", c=C8),
                       src, slot=10,
                       waits=[tok.get("lh_foldpe"), wo_v] + firstw)
            wqt = (kb.dsem[10], kb.dcnt[10])

            @blk.vector
            def _(v, ss=ss, isq=isq, wqt=wqt):
                for c in range(C8):
                    V.do(v, lambda e, c=c: e.tensor_scalar(
                        st["wst"][:, c * 128:c * 128 + 128],
                        st["wfst"][:, c * 128:c * 128 + 128],
                        gcol[:, c:c + 1], 0.125 if isq else 1.0,
                        op0=ALU.mult, op1=ALU.mult),
                        waits=[wqt, tok.get("lh_foldpe")] if c == 0 else [])
            foldv = kb.V.prev
            evp = [None, None]
            ntiles = 16 if isq else ktiles
            for t in range(ntiles):
                par = t % 2
                srcb = st["xob"] if t < 16 else st["xpb"]
                c0 = (t % 16) * 128

                @blk.tensor
                def _(p, t=t, par=par, srcb=srcb, c0=c0, foldv=foldv,
                      mst=mst):
                    for c in range(C8):
                        P.do(p, lambda e, c=c: e.matmul(
                            PS[par][:, 0:128],
                            srcb[:, c * NB1 + c0:c * NB1 + c0 + 128],
                            st["wst"][:, c * 128:c * 128 + 128],
                            start=(c == 0), stop=(c == C8 - 1)),
                            waits=[foldv, evp[par], mst] if c == 0 else [],
                            mark=(c == C8 - 1))
                mmt = P.prev

                @blk.vector
                def _(v, t=t, par=par, mmt=mmt, trq=trq):
                    V.do(v, lambda e: e.tensor_scalar(
                        st["pstg"][:, par * 256:par * 256 + 128],
                        PS[par][:, 0:128], st["fcol"][:, t:t + 1], None,
                        op0=ALU.mult), waits=[mmt, trq[0]])
                    evp[par] = kb.V.prev
                if not isq:
                    @blk.scalar
                    def _(s, t=t, par=par, ss=ss, mst=mst):
                        S.do(s, lambda e: e.copy(
                            st["vlh"][:, t * 130 + (ss - 1) * 65:
                                      t * 130 + (ss - 1) * 65 + 64],
                            st["pstg"][:, par * 256 + 64:par * 256 + 128]),
                            waits=[evp[par], mst])

                @blk.tensor
                def _(p, t=t, par=par, isq=isq, cps=cps, trq=trq):
                    ncols = 128 if isq else 64
                    P.do(p, lambda e: e.matmul(
                        PS[6][0:ncols, 0:128],
                        st["pstg"][:, par * 256:par * 256 + ncols],
                        kb.idb[:], start=True, stop=True),
                        waits=[evp[par], cps[0]], mark=True)
                    trq[0] = kb.P.prev

                @blk.scalar
                def _(s, t=t, isq=isq, ss=ss, trq=trq, cps=cps, mrow=mrow):
                    if isq:
                        for hh in range(2):
                            S.do(s, lambda e, hh=hh: e.copy(
                                st["qlh"][hh][0:64, t * 128:t * 128 + 128],
                                PS[6][hh * 64:hh * 64 + 64, 0:128]),
                                waits=[trq[0]] if hh == 0 else [])
                    else:
                        S.do(s, lambda e: e.copy(
                            st["klh"][ss - 1][0:64, t * 128:t * 128 + 128],
                            PS[6][0:64, 0:128]),
                            waits=[trq[0], mrow] if t == 0 else [trq[0]])
                    cps[0] = kb.S.prev
            prod_toks.append(kb.S.prev)
            tok["lh_foldpe"] = kb.P.prev
        prod_s = kb.S.prev
        @blk.gpsimd
        def _(gq, g_=g_, wo_v=wo_v):
            kb.dma(gq, st["won"][:], wo_ap[g_ * 128:(g_ + 1) * 128, :],
                   slot=14, waits=[wo_v])
        wot = (kb.dsem[14], kb.dcnt[14])
        SB3 = (4, 5, 7)
        s2w = [None] * 3
        for hh in range(2):
            for qt in range(16):
                jbs = range(8) if heavy else [0]
                for ji in list(jbs):
                    par = ji % 3
                    if heavy:
                        j0, jn = ji * 512, 512
                    else:
                        j0, jn = qt * 128, 128

                    @blk.tensor
                    def _(p, hh=hh, qt=qt, par=par, j0=j0, jn=jn,
                          prod_s=prod_s, mrow=mrow, mst=mst, s2w=s2w):
                        P.do(p, lambda e: e.matmul(
                            PS[SB3[par]][:, 0:jn],
                            st["qlh"][hh][0:65, qt * 128:qt * 128 + 128],
                            st["klh"][hh][0:65, j0:j0 + jn],
                            start=True, stop=True),
                            waits=[prod_s, mrow, mst, s2w[par]], mark=True)
                    mmt = P.prev

                    @blk.vector
                    def _(v, ji=ji, par=par, jn=jn, mmt=mmt,
                          bias_prev=bias_prev, s2w=s2w):
                        dst = 8 if ji == 0 else 9
                        V.do(v, lambda e: e.reduce_max(
                            st["rc1"][:, dst:dst + 1], PS[SB3[par]][:, 0:jn],
                            axis=AX.X, negate=True),
                            waits=[mmt, bias_prev])
                        if ji > 0:
                            V.do(v, lambda e: e.tensor_tensor(
                                st["rc1"][:, 8:9], st["rc1"][:, 8:9],
                                st["rc1"][:, 9:10], op=ALU.min))
                        s2w[par] = kb.V.prev

                @blk.vector
                def _(v, hh=hh, qt=qt):
                    V.do(v, lambda e: e.tensor_copy(
                        st["nm1"][:, hh * 16 + qt:hh * 16 + qt + 1],
                        st["rc1"][:, 8:9]))
        nmv = kb.V.prev
        @blk.sync
        def _(q, nmv=nmv, attn_pe=attn_pe):
            t1 = kb.dma(q, st["nm1t"][:], st["nm1"][:], slot=7,
                        waits=[nmv, attn_pe], transpose=True)
            for hh in range(2):
                kb.dma(q, st["qlh"][hh][65:66, 0:NB1],
                       st["nm1t"][hh * 16:hh * 16 + 16, 0:128], slot=7,
                       waits=[t1] if hh == 0 else [])
        bias_tok = (kb.dsem[7], kb.dcnt[7])
        tok["lh_bias"] = bias_tok
        stw2 = [None] * 3
        pvv = [None, None]
        trt = [None]
        ocp = [None]
        for hh in range(2):
            qbs = 8 if heavy else 16
            nq = 256 if heavy else 128
            for qb in range(qbs):
                q0 = qb * nq
                kts = list(range(32)) if heavy else [qb]
                expl = None
                etoks = []
                for ki, kt in enumerate(kts):
                    par = ki % 3

                    @blk.tensor
                    def _(p, hh=hh, kt=kt, par=par, q0=q0, nq=nq,
                          bias_tok=bias_tok, stw2=stw2):
                        P.do(p, lambda e: e.matmul(
                            PS[SB3[par]][:, 0:nq],
                            st["klh"][hh][0:66, kt * 128:kt * 128 + 128],
                            st["qlh"][hh][0:66, q0:q0 + nq],
                            start=True, stop=True),
                            waits=[bias_tok, stw2[par]], mark=True)
                    mmt = P.prev

                    @blk.scalar
                    def _(s, ki=ki, qb=qb, par=par, nq=nq, mmt=mmt,
                          pvv=pvv, stw2=stw2):
                        off = ki * 256 if heavy else (qb % 2) * 128
                        S.do(s, lambda e: e.activation(
                            st["Pq"][:, off:off + nq],
                            PS[SB3[par]][:, 0:nq], AF.Exp,
                            bias=kb.c_zero[:, 0:1]),
                            waits=[mmt, pvv[0], pvv[1]] if ki == 0
                            else [mmt])
                        stw2[par] = kb.S.prev
                        etoks.append(kb.S.prev)
                    expl = kb.S.prev
                for q2 in range(2 if heavy else 1):
                    pb = (qb * 2 + q2) % 2
                    qt = (q0 + q2 * 128) // 128

                    @blk.tensor
                    def _(p, hh=hh, q2=q2, pb=pb, qb=qb, kts=kts,
                          etoks=etoks, pvv=pvv):
                        for r, kt in enumerate(kts):
                            off = (r * 256 + q2 * 128) if heavy \
                                else (qb % 2) * 128
                            P.do(p, lambda e, kt=kt, off=off, r=r: e.matmul(
                                PS[2 + pb][:, 0:65],
                                st["Pq"][:, off:off + 128],
                                st["vlh"][:, kt * 130 + hh * 65:
                                          kt * 130 + hh * 65 + 65],
                                start=(r == 0), stop=(r == len(kts) - 1)),
                                waits=[etoks[r], pvv[pb]] if r == 0
                                else [etoks[r]],
                                mark=(r == len(kts) - 1))
                    pvt = P.prev

                    @blk.vector
                    def _(v, qt=qt, pb=pb, pvt=pvt, trt=trt, pvv=pvv):
                        V.do(v, lambda e: e.reciprocal(
                            st["rc1"][:, 12:13], PS[2 + pb][:, 64:65]),
                            waits=[pvt])
                        if heavy:
                            V.do(v, lambda e: e.tensor_mul(
                                st["rc1"][:, 12:13], st["rc1"][:, 12:13],
                                st["qmcol"][:, qt:qt + 1]))
                        V.do(v, lambda e: e.tensor_scalar(
                            st["osb1"][:, 0:64], PS[2 + pb][:, 0:64],
                            st["rc1"][:, 12:13], None, op0=ALU.mult),
                            waits=[trt[0]])
                        pvv[pb] = kb.V.prev

                    @blk.tensor
                    def _(p, pb=pb, pvv=pvv, ocp=ocp, trt=trt):
                        P.do(p, lambda e: e.transpose(
                            PS[6][0:64, 0:128], st["osb1"][:, 0:64],
                            kb.idf[:]), waits=[pvv[pb], ocp[0]], mark=True)
                        trt[0] = kb.P.prev

                    @blk.scalar
                    def _(s, hh=hh, qt=qt, trt=trt, ocp=ocp):
                        S.do(s, lambda e: e.copy(
                            st["olh"][hh * 64:hh * 64 + 64,
                                      qt * 128:qt * 128 + 128],
                            PS[6][0:64, 0:128]), waits=[trt[0]])
                        ocp[0] = kb.S.prev
                        cps[0] = kb.S.prev
        tok["lh_pe"] = kb.P.prev
        olh_done = kb.S.prev
        wov = [None, None]
        for dc in range(C8):
            for qb in range(4):
                par = (dc * 4 + qb) % 2

                @blk.tensor
                def _(p, dc=dc, qb=qb, par=par, wot=wot,
                      olh_done=olh_done, wov=wov):
                    P.do(p, lambda e: e.matmul(
                        PS[6 + par][:, 0:512],
                        st["won"][:, dc * 128:dc * 128 + 128],
                        st["olh"][:, qb * 512:qb * 512 + 512],
                        start=True, stop=True),
                        waits=[wot, olh_done, wov[par]]
                        if (dc == 0 and qb == 0) else [wov[par]],
                        mark=True)
                mmt = P.prev

                @blk.vector
                def _(v, dc=dc, qb=qb, par=par, mmt=mmt, wov=wov):
                    V.do(v, lambda e: e.tensor_add(
                        st["x1"][:, dc * NB1 + qb * 512:dc * NB1 + qb * 512 + 512],
                        st["x1"][:, dc * NB1 + qb * 512:dc * NB1 + qb * 512 + 512],
                        PS[6 + par][:, 0:512]), waits=[mmt])
                    wov[par] = kb.V.prev
        tok["lh_wov"] = kb.V.prev
        tok["lh_pe"] = kb.P.prev
    tok["lh_trq"] = trq[0]
    tok["lh_cps"] = cps[0]
    return tok


def emit_b1_pv(kb, blk, st, tok, bar):
    V, S, P = kb.V, kb.S, kb.P
    PS = kb.PS

    @blk.sync
    def _(q):
        for mt in range(32):
            kb.dma(q, st["vres"][:, mt * 1024:(mt + 1) * 1024],
                   kb.spillV[mt * 128:(mt + 1) * 128, :], slot=0,
                   waits=(bar_toks(bar) + [tok["spillV"]])
                   if mt == 0 else [])
    vg = (kb.dsem[0], kb.dcnt[0])
    eyev = [None, None]
    for mt in range(16):
        for dm in range(C8):
            par = (mt * C8 + dm) % 2

            @blk.tensor
            def _(p, mt=mt, dm=dm, par=par, vg=vg, eyev=eyev):
                P.do(p, lambda e: e.matmul(
                    PS[par][0:128, 0:128],
                    st["vres"][:, mt * 1024 + dm * 128:mt * 1024 + dm * 128 + 128],
                    kb.idb[:], start=True, stop=True),
                    waits=[vg, eyev[par]] if (mt == 0 and dm < 2)
                    else [eyev[par]], mark=True)
            mmt = P.prev

            @blk.vector
            def _(v, mt=mt, dm=dm, par=par, mmt=mmt, eyev=eyev):
                V.do(v, lambda e: e.tensor_sub(
                    st["x1"][:, dm * NB1 + mt * 128:dm * NB1 + mt * 128 + 128],
                    st["x1"][:, dm * NB1 + mt * 128:dm * NB1 + mt * 128 + 128],
                    PS[par][0:128, 0:128]), waits=[mmt])
                eyev[par] = kb.V.prev
    eyepe = kb.P.prev

    @blk.vector
    def _(v):
        for mt in range(32):
            V.do(v, lambda e, mt=mt: e.tensor_scalar(
                st["vres"][:, mt * 1024:(mt + 1) * 1024],
                st["vres"][:, mt * 1024:(mt + 1) * 1024],
                st["cstat"][:, 64 + mt:65 + mt], None, op0=ALU.mult),
                waits=[eyepe, vg] if mt == 0 else [])
    fscale = kb.V.prev
    pmark = {}
    addv = [None]
    for nb in range(4):
        for mt in range(32):
            slot = mt % 2

            @blk.sync
            def _(q, nb=nb, mt=mt, slot=slot, addv=addv):
                w = [pmark.get((nb, mt - 2)) if (nb, mt - 2) in pmark
                     else pmark.get((nb - 1, 30 + mt))]
                if nb == 0 and mt < 2:
                    w = [tok["spillP"], fscale]
                kb.dma(q, st["prow"][slot][:],
                       kb.spillP[mt * 128:(mt + 1) * 128,
                                 nb * 512:(nb + 1) * 512],
                       slot=12 + slot, waits=w)
            ptok = (kb.dsem[12 + slot], kb.dcnt[12 + slot])

            @blk.tensor
            def _(p, nb=nb, mt=mt, slot=slot, ptok=ptok, addv=addv):
                for dm in range(C8):
                    P.do(p, lambda e, dm=dm: e.matmul(
                        PS[dm][:, 0:512],
                        st["vres"][:, mt * 1024 + dm * 128:
                                   mt * 1024 + dm * 128 + 128],
                        st["prow"][slot][:],
                        start=(mt == 0), stop=(mt == 31)),
                        waits=[ptok, fscale, addv[0]]
                        if (dm == 0 and mt == 0) else
                        ([ptok] if dm == 0 else []),
                        mark=(dm == C8 - 1))
                pmark[(nb, mt)] = kb.P.prev

        @blk.vector
        def _(v, nb=nb, addv=addv):
            for dm in range(C8):
                V.do(v, lambda e, dm=dm: e.tensor_add(
                    st["x1"][:, dm * NB1 + nb * 512:dm * NB1 + nb * 512 + 512],
                    st["x1"][:, dm * NB1 + nb * 512:dm * NB1 + nb * 512 + 512],
                    PS[dm][:, 0:512]),
                    waits=[pmark[(nb, 31)]] if dm == 0 else [])
            addv[0] = kb.V.prev
    tok["pvv"] = kb.V.prev
    return tok


def emit_b1_out(kb, blk, st, outT, tok):
    @blk.sync
    def _(q):
        last = None
        for j in range(C8):
            last = kb.dma(q, outT[j * 128:(j + 1) * 128, :],
                          st["x1"][:, j * NB1:(j + 1) * NB1], slot=15,
                          waits=[tok["pvv"]] if j == 0 else [])
        q.wait_ge(*last)


B1_PHASES = ["ln_s", "gather", "routing", "norms", "l1aS", "merge",
             "light", "heavy", "pv", "out"]


def build_block1(stop_after=None):
    kb = KB()
    nc = kb.nc
    x1T = kb.din("x1T", [D, OWN])
    waps = {}
    for name, shape in [("wq1", [D, D]), ("wk1", [D, D]), ("wv1T", [D, D]),
                        ("wq_l", [D, 512]), ("wkvl_r", [D, D]),
                        ("wo_l", [512, D]), ("wq_h", [D, 512]),
                        ("wkv_hT", [D, D]), ("wo_h", [512, D])]:
        waps[name] = kb.din(name, shape, F32)
    ins = {}
    for name, shape, dt in [("IDF", [128, 128], F32),
                            ("coremask", [128, 1], F32),
                            ("gl_col", [128, C8], F32),
                            ("gh_col", [128, C8], F32),
                            ("rt_q", [128, C8], F32),
                            ("rt_kv", [128, C8], F32),
                            ("IDB", [128, 128], BF16),
                            ("onesb1", [1, FT], BF16)]:
        ins[name] = kb.din(name, shape, dt)
    outT = kb.dout("outT", [D, OWN])
    kb.spillP = kb.dint("spillP", [FT, NB1], BF16)
    kb.spillV = kb.dint("spillV", [FT, D], BF16)
    kb.cin1 = kb.dint("cin1", [D, NB1], BF16)
    kb.cout1 = kb.dint("cout1", [2 * D, NB1], BF16)
    kb.cins = kb.dint("cins", [2, NB1])
    kb.couts = kb.dint("couts", [4, NB1])
    kb.cing = kb.dint("cing", [128, 64])
    kb.coutg = kb.dint("coutg", [256, 64])
    kb.rscr = kb.dint("rscr", [1, 2 * FT])
    kb.rqm = kb.dint("rqm", [1, NB1])
    kb.kvmd = kb.dint("kvmd", [1, FT], BF16)
    kb.rfn = kb.dint("rfn", [1, FT])
    st = alloc_block1(kb)
    kb.open()
    nphase = B1_PHASES.index(stop_after) + 1 if stop_after else len(B1_PHASES)
    ph = B1_PHASES[:nphase]
    with nc.Block() as blk:
        bar = emit_b1_prelude(kb, blk, ins)
        tok = {}
        tok = emit_b1_ln_s(kb, blk, st, x1T, tok)
        if "gather" in ph:
            tok = emit_b1_gather(kb, blk, st, tok)
        if "routing" in ph:
            tok = emit_b1_routing(kb, blk, st, tok)
        if "norms" in ph:
            tok = emit_b1_norms(kb, blk, st, tok)
        if "l1aS" in ph:
            bar = emit_barrier(kb, blk, [tok.get("maskv"), tok.get("fcolv"),
                                         (kb.dsem[11], kb.dcnt[11]),
                                         (kb.dsem[6], kb.dcnt[6])])
            tok = emit_b1_l1a_S(kb, blk, st, waps, tok, bar)
        if "merge" in ph:
            bar = emit_barrier(kb, blk, [tok["spillP"], tok["spillV"]])
            tok = emit_b1_merge(kb, blk, st, tok, bar)
        if "light" in ph:
            bar = emit_barrier(kb, blk, [tok["facv"]])
            tok = emit_b1_lh(kb, blk, st, waps, tok, bar, "light")
        if "heavy" in ph:
            tok = emit_b1_lh(kb, blk, st, waps, tok, bar, "heavy")
        if "pv" in ph:
            bar = emit_barrier(kb, blk, [tok["lh_wov"], tok["lh_pe"],
                                         tok.get("lh_bias"),
                                         (kb.dsem[8], kb.dcnt[8]),
                                         (kb.dsem[10], kb.dcnt[10]),
                                         (kb.dsem[14], kb.dcnt[14])])
            tok = emit_b1_pv(kb, blk, st, tok, bar)
        if "out" in ph:
            emit_b1_out(kb, blk, st, outT, tok)
        else:
            emit_barrier(kb, blk, [(kb.dsem[i], kb.dcnt[i])
                                   for i in range(16)] +
                         [kb.V.prev, kb.S.prev, kb.G.prev])
    kb.close()
    return kb


def build_fused():
    kb = KB()
    nc = kb.nc
    xT = kb.din("xT", [D, NT0])
    waps = {
        "Wqkv": kb.din("Wqkv", [IN_DEPTH, D, 3 * 1024]),
        "Wo": kb.din("Wo", [IN_DEPTH, D, D]),
        "W1p": kb.din("W1p", [IN_DEPTH, D, NPAIR * 256]),
        "W2": kb.din("W2", [IN_DEPTH, DIM_INNER, D]),
        "w_qkvT": kb.din("w_qkvT", [D, 3 * 1024]),
        "w_outT": kb.din("w_outT", [D, D]),
    }
    ins = {}
    for name, shape in [("TcB", [128, NT0]), ("TsB", [128, NT0]),
                        ("TciB", [128, NT0]), ("TsiB", [128, NT0]),
                        ("TcA", [128, NT0]), ("TsA", [128, NT0]),
                        ("MSK", [128, 4 * 128]), ("padrowbf", [1, NT0]),
                        ("onesbf", [1, KMAX])]:
        ins[name] = kb.din(name, shape, BF16)
    for name, shape in [("PROTF", [128, 128]), ("BD2", [128, 2]),
                        ("BD2T", [2, 128]), ("IDF", [128, 128]),
                        ("coremask", [128, 1]),
                        ("qs_col", [128, IN_DEPTH]),
                        ("ks_col", [128, IN_DEPTH]),
                        ("ln_g", [IN_DEPTH, 128, C8]),
                        ("ln_b", [IN_DEPTH, 128, C8]),
                        ("b1p", [IN_DEPTH, 128, 44]),
                        ("b2c", [IN_DEPTH, 128, C8]),
                        ("a0_lnw", [128, C8]), ("a0_lnb", [128, C8])]:
        ins[name] = kb.din(name, shape, F32)
    for name, shape in [("wq1", [D, D]), ("wk1", [D, D]), ("wv1T", [D, D]),
                        ("wq_l", [D, 512]), ("wkvl_r", [D, D]),
                        ("wo_l", [512, D]), ("wq_h", [D, 512]),
                        ("wkv_hT", [D, D]), ("wo_h", [512, D]),
                        ("gl_col", [128, C8]), ("gh_col", [128, C8]),
                        ("rt_q", [128, C8]), ("rt_kv", [128, C8])]:
        waps[name] = ins[name] = kb.din(name, shape, F32)
    ins["IDB"] = kb.din("IDB", [128, 128], BF16)
    ins["onesb1"] = kb.din("onesb1", [1, FT], BF16)
    outT = kb.dout("outT", [D, OWN])
    x1o = kb.dint("x1T", [D, OWN])
    spill_acc = kb.dint("spill_acc", [2 * D, OWN])
    st = alloc_block0(kb)
    ln_scr0 = kb.ln_scr
    kb.a0w = kb.sbuf("a0w", [128, C8], F32)
    kb.a0b = kb.sbuf("a0b", [128, C8], F32)
    kb.cinH = kb.dint("cinH", [D, HALO])
    kb.coutH = kb.dint("coutH", [2 * D, HALO])
    kb.spillP = kb.dint("spillP", [FT, NB1], BF16)
    kb.spillV = kb.dint("spillV", [FT, D], BF16)
    kb.cin1 = kb.dint("cin1", [D, NB1], BF16)
    kb.cout1 = kb.dint("cout1", [2 * D, NB1], BF16)
    kb.cins = kb.dint("cins", [2, NB1])
    kb.couts = kb.dint("couts", [4, NB1])
    kb.cing = kb.dint("cing", [128, 64])
    kb.coutg = kb.dint("coutg", [256, 64])
    kb.rscr = kb.dint("rscr", [1, 2 * FT])
    kb.rqm = kb.dint("rqm", [1, NB1])
    kb.kvmd = kb.dint("kvmd", [1, FT], BF16)
    kb.rfn = kb.dint("rfn", [1, FT])
    st1 = alloc_block1(kb, fused_base=kb.nc.lookup_mloc(st["resid"]).addr)
    kb.open()
    with nc.Block() as blk:
        tok = emit_prelude(kb, blk, st, ins)
        ldtok = emit_load_x(kb, blk, st["resid"], xT)

        @blk.tensor
        def _(p):
            p.wait_ge(*ldtok)

        @blk.vector
        def _(v):
            v.wait_ge(*ldtok)

        @blk.scalar
        def _(s):
            s.wait_ge(*ldtok)
        lt = emit_ln(kb, blk, st["resid"], st["resid"], NT0, tblocks(0, NT0),
                     kb.ln_scr)
        tok["resid_v"] = lt
        tok = emit_a0(kb, blk, st, waps, tok, spill_acc)
        tok = emit_load_tablesB(kb, blk, ins, tok)
        for l in range(IN_DEPTH):
            tok = emit_inner_attn(kb, blk, st, l, waps, tok)
            tok = emit_inner_ffn(kb, blk, st, l, waps, tok)
            if l < IN_DEPTH - 1:
                tok = emit_halo_exchange(kb, blk, st, l, tok)
        tok = emit_combine_block0(kb, blk, st, xT, spill_acc, x1o, tok)
        kb.ln_scr = st1["ln_scr"]
        for nm, ap in st1["b1_consts"].items():
            setattr(kb, nm, ap)
        bar0 = emit_barrier(kb, blk, [tok["x1_dma"], kb.V.prev, kb.S.prev,
                                      kb.G.prev, kb.P.prev] +
                            [(kb.dsem[i], kb.dcnt[i]) for i in range(16)])
        bar = emit_b1_prelude(kb, blk, ins)
        tok1 = {}
        tok1 = emit_b1_ln_s(kb, blk, st1, x1o, tok1)
        tok1 = emit_b1_gather(kb, blk, st1, tok1)
        tok1 = emit_b1_routing(kb, blk, st1, tok1)
        tok1 = emit_b1_norms(kb, blk, st1, tok1)
        bar = emit_barrier(kb, blk, [tok1.get("maskv"), tok1.get("fcolv"),
                                     (kb.dsem[11], kb.dcnt[11]),
                                     (kb.dsem[6], kb.dcnt[6])])
        tok1 = emit_b1_l1a_S(kb, blk, st1, waps, tok1, bar)
        bar = emit_barrier(kb, blk, [tok1["spillP"], tok1["spillV"]])
        tok1 = emit_b1_merge(kb, blk, st1, tok1, bar)
        bar = emit_barrier(kb, blk, [tok1["facv"]])
        tok1 = emit_b1_lh(kb, blk, st1, waps, tok1, bar, "light")
        tok1 = emit_b1_lh(kb, blk, st1, waps, tok1, bar, "heavy")
        bar = emit_barrier(kb, blk, [tok1["lh_wov"], tok1["lh_pe"],
                                     tok1.get("lh_bias"),
                                     (kb.dsem[8], kb.dcnt[8]),
                                     (kb.dsem[10], kb.dcnt[10]),
                                     (kb.dsem[14], kb.dcnt[14])])
        tok1 = emit_b1_pv(kb, blk, st1, tok1, bar)
        emit_b1_out(kb, blk, st1, outT, tok1)
    kb.close()
    return kb


def kernel(**inputs):
    cores = [make_core_inputs(inputs, c) for c in range(NCORES)]
    kb = build_fused()
    maps = [{k: g[k] for k in kb.inp} for g in cores]
    res = run_bass_kernel_spmd(kb.nc, maps, core_ids=list(range(NCORES)))
    out = np.empty((B, T, D), np.float32)
    for c in range(NCORES):
        b, h = c // 2, c % 2
        out[b, h * OWN:(h + 1) * OWN] = np.asarray(res.results[c]["outT"]).T
    return out


# version 09
# speedup vs baseline: 1.1929x; 1.0017x over previous
import numpy as np
import ml_dtypes
import concourse.bass as bass
import concourse.mybir as mybir
from concourse.bass_utils import run_bass_kernel_spmd
from contextlib import ExitStack

F32 = mybir.dt.float32
BF16 = mybir.dt.bfloat16
AF = mybir.ActivationFunctionType
ALU = mybir.AluOpType
AX = mybir.AxisListType

B, T, D = 4, 4096, 1024
NCORES = 8
OWN = 2048
HALO = 384
NT0 = HALO + OWN
C8 = 8
TB = 512
NTB = 5
NEG = -1.0e30
EPS = 1e-5

A0_HEADS, A0_DH, A0_WIN = 16, 64, 32
IN_DEPTH, IN_HEADS, IN_DH, IN_WIN = 6, 16, 64, 256
DIM_INNER = 2730
NPAIR = (DIM_INNER + 127) // 128
QK_SCALE = 8.0
L1_HEADS, L1_DH, L1_HID, L1_WIN = 8, 64, 512, 128
NQ, NKV = 1024, 2048
KQ, KKV = 1152.0, 2304.0
N_ITERS = 50


def _rope_tables_a0(pos):
    inv_freq = 1.0 / 10000.0 ** (np.arange(0, A0_DH, 2, dtype=np.float64) / A0_DH)
    p = np.maximum(pos, 0).astype(np.float64)
    fr = p[:, None] * inv_freq[None, :]
    fr = np.concatenate([fr, fr], -1)
    cos = np.cos(fr).astype(np.float32)
    sin = np.sin(fr).astype(np.float32)
    return cos.T.copy(), sin.T.copy()


def _xpos_tables(pos, n):
    half = np.arange(0, IN_DH, 2, dtype=np.float32)
    inv_freq = (1.0 / 10000.0 ** (half / IN_DH)).astype(np.float32)
    p = np.maximum(pos, 0).astype(np.float32)
    fr = p[:, None] * inv_freq[None, :]
    fr = np.concatenate([fr, fr], -1)
    cos, sin = np.cos(fr), np.sin(fr)
    base = ((half + 0.4 * IN_DH) / (1.4 * IN_DH)).astype(np.float32)
    power = ((p - n // 2) / (IN_WIN // 2)).astype(np.float32)
    scale = base[None, :] ** power[:, None]
    scale = np.concatenate([scale, scale], -1).astype(np.float32)
    inv = (1.0 / scale).astype(np.float32)
    return ((QK_SCALE * cos * scale).T.copy(),
            (QK_SCALE * sin * scale).T.copy(),
            (cos * inv).T.copy(), (sin * inv).T.copy())


def _rot_sign_matrix(dh=64):
    P = np.zeros((128, 128), dtype=np.float32)
    for dp in range(128):
        base = (dp // dh) * dh
        o = dp % dh
        if o < dh // 2:
            P[base + o + dh // 2, dp] = -1.0
        else:
            P[base + o - dh // 2, dp] = 1.0
    return P


def _masks():
    k = np.arange(128)[:, None]
    q = np.arange(128)[None, :]
    m0 = (k <= q).astype(np.float32)
    m2 = (k >= q).astype(np.float32)
    md = (((q - k) >= 0) & ((q - k) < A0_WIN)).astype(np.float32)
    mp = ((k - q) > (128 - A0_WIN)).astype(np.float32)
    return np.stack([m0, m2, md, mp])


def _bf(x):
    return np.ascontiguousarray(np.asarray(x).astype(ml_dtypes.bfloat16))


def _f32(x):
    return np.ascontiguousarray(np.asarray(x, dtype=np.float32))


def make_core_inputs(ins, c):
    b, h = c // 2, c % 2
    own0 = h * OWN
    g = {}
    x = np.asarray(ins["L0_attn_x"][b])
    lo = own0 - HALO
    sh = np.zeros((NT0, D), dtype=np.float32)
    s0 = max(lo, 0)
    sh[s0 - lo:, :] = x[s0:own0 + OWN]
    g["xT"] = _f32(sh.T)

    pos = np.arange(lo, own0 + OWN)
    g["padrowbf"] = _bf(np.where(pos < 0, np.float32(NEG), 0.0)[None, :])
    g["coremask"] = np.full((128, 1), 1.0 if h == 1 else 0.0, np.float32)

    cosA, sinA = _rope_tables_a0(pos)
    dup = lambda t: np.concatenate([t, t], axis=0)
    g["TcA"], g["TsA"] = _bf(dup(cosA)), _bf(dup(sinA))
    Tc, Ts, Tci, Tsi = _xpos_tables(pos, T)
    g["TcB"], g["TsB"] = _bf(dup(Tc)), _bf(dup(Ts))
    g["TciB"], g["TsiB"] = _bf(dup(Tci)), _bf(dup(Tsi))
    g["MSK"] = _bf(_masks().transpose(1, 0, 2).reshape(128, 4 * 128))
    g["onesbf"] = _bf(np.ones((1, 1408)))
    g["PROTF"] = _f32(_rot_sign_matrix())
    bd2 = np.zeros((128, 2), np.float32)
    bd2[0:64, 0] = 1.0
    bd2[64:128, 1] = 1.0
    g["BD2"] = bd2
    g["BD2T"] = _f32(bd2.T)
    g["IDF"] = _f32(np.eye(128))

    g["w_qkvT"] = _f32(np.asarray(ins["L0_attn_w_qkv"]).T)
    g["w_outT"] = _f32(np.asarray(ins["L0_attn_w_out"]).T)
    g["a0_lnw"] = _f32(np.asarray(ins["L0_attn_ln_w"]).reshape(C8, 128).T)
    g["a0_lnb"] = _f32(np.asarray(ins["L0_attn_ln_b"]).reshape(C8, 128).T)

    g["Wqkv"] = _f32(ins["L0_ffn_Wqkv"])
    g["Wo"] = _f32(ins["L0_ffn_Wo"])
    qs, ks = np.asarray(ins["L0_ffn_q_scale"]), np.asarray(ins["L0_ffn_k_scale"])
    g["qs_col"] = _f32(np.tile(qs, (1, 2)).T)
    g["ks_col"] = _f32(np.tile(ks, (1, 2)).T)
    g["ln_g"] = _f32(np.asarray(ins["L0_ffn_ln_g"]).reshape(IN_DEPTH, C8, 128).transpose(0, 2, 1))
    g["ln_b"] = _f32(np.asarray(ins["L0_ffn_ln_b"]).reshape(IN_DEPTH, C8, 128).transpose(0, 2, 1))
    W1 = np.asarray(ins["L0_ffn_W1"])
    b1 = np.asarray(ins["L0_ffn_b1"])
    W1p = np.zeros((IN_DEPTH, D, NPAIR * 256), np.float32)
    for p in range(NPAIR):
        a0, a1 = p * 128, min((p + 1) * 128, DIM_INNER)
        W1p[:, :, p * 256:p * 256 + (a1 - a0)] = W1[:, :, a0:a1]
        W1p[:, :, p * 256 + 128:p * 256 + 128 + (a1 - a0)] = \
            W1[:, :, DIM_INNER + a0:DIM_INNER + a1]
    g["W1p"] = W1p
    b1p = np.zeros((IN_DEPTH, 2 * NPAIR, 128), np.float32)
    for p in range(NPAIR):
        a0, a1 = p * 128, min((p + 1) * 128, DIM_INNER)
        b1p[:, 2 * p, :a1 - a0] = b1[:, a0:a1]
        b1p[:, 2 * p + 1, :a1 - a0] = b1[:, DIM_INNER + a0:DIM_INNER + a1]
    g["b1p"] = _f32(b1p.transpose(0, 2, 1))
    g["W2"] = _f32(ins["L0_ffn_W2"])
    g["b2c"] = _f32(np.asarray(ins["L0_ffn_b2"]).reshape(IN_DEPTH, C8, 128).transpose(0, 2, 1))

    g["wq1"] = _f32(ins["L1_attn_wq"])
    g["wk1"] = _f32(ins["L1_attn_wk"])
    g["wv1T"] = _f32(np.asarray(ins["L1_attn_wv"]).T)

    g["IDB"] = _bf(np.eye(128))
    g["onesb1"] = _bf(np.ones((1, 4096)))
    wkvl = np.asarray(ins["L1_ffn_wkv_light"])
    cols = []
    for hh in range(8):
        cols.append(wkvl[:, 64 * hh:64 * hh + 64])
        cols.append(wkvl[:, 512 + 64 * hh:512 + 64 * hh + 64])
    g["wkvl_r"] = _f32(np.concatenate(cols, axis=1))
    g["gl_col"] = _f32(np.asarray(ins["L1_ffn_gamma_light"]).reshape(C8, 128).T)
    g["gh_col"] = _f32(np.asarray(ins["L1_ffn_gamma_heavy"]).reshape(C8, 128).T)
    g["wq_l"] = _f32(ins["L1_ffn_wq_light"])
    g["wkv_l"] = _f32(ins["L1_ffn_wkv_light"])
    g["wo_l"] = _f32(ins["L1_ffn_wo_light"])
    g["rt_q"] = _f32(np.asarray(ins["L1_ffn_q_routing_token"]).reshape(C8, 128).T)
    g["rt_kv"] = _f32(np.asarray(ins["L1_ffn_kv_routing_token"]).reshape(C8, 128).T)
    g["wq_h"] = _f32(ins["L1_ffn_wq_heavy"])
    g["wkv_hT"] = _f32(np.asarray(ins["L1_ffn_wkv_heavy"]).T)
    g["wo_h"] = _f32(ins["L1_ffn_wo_heavy"])
    return g


class Chain:

    def __init__(self, nc, sem, auto=True, wcache=None):
        self.nc = nc
        self.sem = sem
        self.auto = auto
        self.val = 0
        self.prev = None
        self.wcache = wcache if wcache is not None else {}

    def _wait(self, eng, t):
        key = (eng.engine, id(t[0]))
        if self.wcache.get(key, -1) >= t[1]:
            return
        self.wcache[key] = t[1]
        eng.wait_ge(t[0], t[1])

    def do(self, eng, build, waits=(), mark=None):
        for t in waits:
            if t is not None:
                self._wait(eng, t)
        if self.auto and self.prev is not None:
            self._wait(eng, self.prev)
        ins = build(eng)
        if self.auto or mark:
            ins.then_inc(self.sem, 1)
            self.val += 1
            self.prev = (self.sem, self.val)
        return self.prev


class KB:
    def __init__(self, dev_taps=()):
        self.nc = bass.Bass("TRN2", target_bir_lowering=False)
        self.dev_taps = set(dev_taps)
        self.ctx = ExitStack()
        self.inp = {}
        self.outp = {}
        self.sb = {}
        self.sems = {}
        self.V = None
        self.S = None
        self.P = None
        self.G = None

    def din(self, name, shape, dtype=F32):
        ap = self.nc.dram_tensor(name, list(shape), dtype, kind="ExternalInput").ap()
        self.inp[name] = ap
        return ap

    def dout(self, name, shape, dtype=F32):
        ap = self.nc.dram_tensor(name, list(shape), dtype, kind="ExternalOutput").ap()
        self.outp[name] = ap
        return ap

    def dint(self, name, shape, dtype=F32, addr_space="Local"):
        return self.nc.dram_tensor(name, list(shape), dtype, kind="Internal",
                                   addr_space=addr_space).ap()

    def sbuf(self, name, shape, dtype=F32):
        t = self.ctx.enter_context(self.nc.sbuf_tensor(name, list(shape), dtype))
        self.sb[name] = t
        return t

    def mkslab(self, name, nbytes):
        slab = self.ctx.enter_context(
            self.nc.sbuf_tensor(name, [128, nbytes], mybir.dt.uint8))
        return self.nc.lookup_mloc(slab).addr

    def sbuf_at(self, name, shape, dtype, base, off):
        t = self.nc.alloc_sbuf_tensor_at(name, list(shape), dtype,
                                         offset=base + off)
        self.sb[name] = t
        return t

    def psum(self, name, shape, dtype=F32):
        return self.ctx.enter_context(self.nc.psum_tensor(name, list(shape), dtype))

    def sem(self, name):
        s = self.ctx.enter_context(self.nc.semaphore(name))
        self.sems[name] = s
        return s

    def open(self):
        self.wcache = {}
        self.V = Chain(self.nc, self.sem("cv"), wcache=self.wcache)
        self.S = Chain(self.nc, self.sem("cs"), wcache=self.wcache)
        self.P = Chain(self.nc, self.sem("cp"), auto=False,
                       wcache=self.wcache)
        self.G = Chain(self.nc, self.sem("cg"), wcache=self.wcache)
        self.dsem = [self.sem(f"d{i}") for i in range(16)]
        self.dcnt = [0] * 16

    def dma(self, qeng, out, in_, slot, waits=(), **kw):
        for t in waits:
            if t is not None:
                key = (qeng.engine, id(t[0]))
                if self.wcache.get(key, -1) >= t[1]:
                    continue
                self.wcache[key] = t[1]
                qeng.wait_ge(t[0], t[1])
        qeng.dma_start(out=out, in_=in_, **kw).then_inc(self.dsem[slot], 16)
        self.dcnt[slot] += 16
        return (self.dsem[slot], self.dcnt[slot])

    def close(self):
        self.ctx.close()


def emit_load_x(kb, blk, resid, xT, nt=NT0):
    toks = []

    @blk.sync
    def _(q):
        for j in range(C8):
            toks.append(kb.dma(q, resid[:, j * nt:(j + 1) * nt],
                               xT[j * 128:(j + 1) * 128, :], slot=8))
    return toks[-1]


def emit_ln(kb, blk, src, dst, nt, ranges, scr,
            gamma=None, beta=None, shift=False, extra_wait=None,
            dst_bf16=None, nt_dst=None, dst_off=0):
    V, S, P = kb.V, kb.S, kb.P
    pA, pB = scr["pA"], scr["pB"]
    bc = [scr["bc0"], scr["bc1"]]
    sq, rows, aff = scr["sq"], scr["rows"], scr["aff"]
    ndst = nt if nt_dst is None else nt_dst
    pe_stats_tok = [extra_wait]
    vec_rows_tok = [None]
    aff_tok = [None, None]

    def col(j, t0, t1, buf=src, n=None, noshift=False, off=0):
        n = nt if n is None else n
        sh = 1 if (shift and j >= C8 // 2 and not noshift) else 0
        return buf[:, j * n + t0 - off - sh: j * n + t1 - off - sh]

    for (t0, t1) in ranges:
        W = t1 - t0

        stage_tok = [None] * 4
        for stage in range(4):
            @blk.scalar
            def _(s, t0=t0, t1=t1, stage=stage):
                for j in range(2 * stage, 2 * stage + 2):
                    S.do(s, lambda e, j=j: e.activation(
                        sq[:, (j % 2) * TB:(j % 2) * TB + W], col(j, t0, t1),
                        AF.Square),
                        waits=[pe_stats_tok[0] if stage == 0 and j == 0
                               else None,
                               stage_tok[stage - 1] if stage > 0 and
                               j == 2 * stage else None])
            sq_done = S.prev

            @blk.tensor
            def _(p, t0=t0, t1=t1, stage=stage):
                if stage == 0:
                    for j in range(C8):
                        P.do(p, lambda e, j=j: e.matmul(
                            pA[0:1, 0:W], kb.oc[:, 0:1], col(j, t0, t1),
                            start=(j == 0), stop=(j == C8 - 1)),
                            waits=[vec_rows_tok[0], pe_stats_tok[0]]
                            if j == 0 else [])
                for j in range(2 * stage, 2 * stage + 2):
                    P.do(p, lambda e, j=j: e.matmul(
                        pB[0:1, 0:W], kb.oc[:, 0:1],
                        sq[:, (j % 2) * TB:(j % 2) * TB + W],
                        start=(j == 0), stop=(j == C8 - 1)),
                        waits=[sq_done if j == 2 * stage else None],
                        mark=(j % 2 == 1))
                stage_tok[stage] = P.prev
        stats_done = P.prev

        @blk.vector
        def _(v):
            V.do(v, lambda e: e.tensor_scalar(
                rows[0:1, 0*TB:0*TB+W], pA[0:1, 0:W], 1.0 / D, None, op0=ALU.mult),
                waits=[stats_done])
            V.do(v, lambda e: e.tensor_mul(rows[0:1, 4*TB:4*TB+W], rows[0:1, 0*TB:0*TB+W],
                                           rows[0:1, 0*TB:0*TB+W]))
            V.do(v, lambda e: e.scalar_tensor_tensor(
                rows[0:1, 4*TB:4*TB+W], pB[0:1, 0:W], 1.0 / D, rows[0:1, 4*TB:4*TB+W],
                op0=ALU.mult, op1=ALU.subtract))

        @blk.scalar
        def _(s):
            S.do(s, lambda e: e.activation(rows[0:1, 3*TB:3*TB+W], rows[0:1, 4*TB:4*TB+W],
                                           AF.Sqrt, bias=kb.c_eps[0:1, 0:1]),
                 waits=[V.prev])

        @blk.vector
        def _(v):
            V.do(v, lambda e: e.reciprocal(rows[0:1, 4*TB:4*TB+W], rows[0:1, 3*TB:3*TB+W]),
                 waits=[S.prev])
            tok = V.do(v, lambda e: e.tensor_mul(rows[0:1, 5*TB:5*TB+W], rows[0:1, 0*TB:0*TB+W],
                                                 rows[0:1, 4*TB:4*TB+W]))
            vec_rows_tok[0] = tok
            pe_stats_tok[0] = tok

        @blk.tensor
        def _(p):
            P.do(p, lambda e: e.matmul(bc[0][:, 0:W], kb.orow[:, :], rows[0:1, 4*TB:4*TB+W],
                                       start=True, stop=True),
                 waits=[V.prev, aff_tok[0]])
            P.do(p, lambda e: e.matmul(bc[1][:, 0:W], kb.orow[:, :], rows[0:1, 5*TB:5*TB+W],
                                       start=True, stop=True),
                 waits=[aff_tok[1]], mark=True)
        bc_done = P.prev

        @blk.vector
        def _(v, t0=t0, t1=t1):
            for j in range(C8):
                V.do(v, lambda e, j=j: e.tensor_mul(
                    aff[:, 0:W], col(j, t0, t1), bc[0][:, 0:W]), waits=[bc_done])
                if gamma is None:
                    tok = V.do(v, lambda e, j=j: e.tensor_sub(
                        col(j, t0, t1, buf=dst, n=ndst, noshift=True,
                            off=dst_off),
                        aff[:, 0:W], bc[1][:, 0:W]))
                else:
                    V.do(v, lambda e: e.tensor_sub(aff[:, 0:W], aff[:, 0:W], bc[1][:, 0:W]))
                    tok = V.do(v, lambda e, j=j: e.tensor_scalar(
                        col(j, t0, t1, buf=dst, n=ndst, noshift=True,
                            off=dst_off), aff[:, 0:W],
                        gamma[:, j:j + 1], beta[:, j:j + 1],
                        op0=ALU.mult, op1=ALU.add))
                if dst_bf16 is not None:
                    tok = V.do(v, lambda e, j=j: e.tensor_copy(
                        col(j, t0, t1, buf=dst_bf16, noshift=True),
                        col(j, t0, t1, buf=dst, n=ndst, noshift=True,
                            off=dst_off)))
                if j == 0:
                    aff_tok[0] = tok
            aff_tok[1] = tok
    return V.prev


def alloc_ln_scratch(kb):
    return {
        "pA": kb.PS[4],
        "pB": kb.PS[5],
        "bc0": kb.PS[6],
        "bc1": kb.PS[7],
        "sq": kb.sbuf_at("sq", [128, 2 * TB], F32, kb.ovl_base, 31584),
        "rows": kb.sbuf_at("rows", [1, 6 * TB], F32, kb.ovl_base,
                           31584 + 4096),
        "aff": kb.sbuf("aff", [128, TB], F32),
    }


def emit_consts(kb, blk):
    if not hasattr(kb, "oc"):
        alloc_consts(kb)

    @blk.gpsimd
    def _(g):
        kb.G.do(g, lambda e: e.memset(kb.oc[:], 1.0))
        kb.G.do(g, lambda e: e.memset(kb.orow[:], 1.0))
        kb.G.do(g, lambda e: e.memset(kb.c_eps[:], float(EPS)))
        kb.G.do(g, lambda e: e.memset(kb.c_zero[:], 0.0))
    return kb.G.prev


def alloc_consts(kb):
    kb.oc = kb.sbuf("ones_c", [128, 1], F32)
    kb.orow = kb.sbuf("ones_r", [1, 128], F32)
    kb.c_eps = kb.sbuf("c_eps", [128, 1], F32)
    kb.c_zero = kb.sbuf("c_zero", [128, 1], F32)


def tblocks(t0, t1, step=TB):
    return [(a, min(a + step, t1)) for a in range(t0, t1, step)]


QMAX = 1152
KMAX = 1408
KTILES = 11

HALVES_IN = [(2, 10, 0), (11, 18, 9)]
HALVES_A0 = [(3, 10, 2), (11, 18, 10)]


def alloc_attn_state(kb, base):
    st = {}
    o = [0]

    def at(name, shape, dtype):
        nb = int(np.prod(shape[1:])) if len(shape) > 2 else shape[1]
        nb = nb * (2 if dtype == BF16 else 4)
        t = kb.sbuf_at(name, shape, dtype, base, o[0])
        o[0] += (nb + 31) // 32 * 32
        return t

    st["wqkvW"] = at("wqkvW", [128, 3 * 1024], BF16)
    st["woW"] = at("woW", [128, 4 * 1024], BF16)
    st["qst"] = [at(f"qst{h}", [66, QMAX], BF16) for h in range(2)]
    st["kst"] = [at(f"kst{h}", [66, KMAX], BF16) for h in range(2)]
    st["vst"] = at("vst", [128, KTILES * 130], BF16)
    st["Pt"] = at("Pt", [128, 4 * 384], BF16)
    st["nmst"] = at("nmst", [128, 128], BF16)
    st["nmtr"] = at("nmtr", [128, 128], BF16)
    st["osb"] = at("osb", [128, 128], F32)
    st["rc"] = at("rc", [128, 2], F32)
    st["accst"] = [kb.sbuf_at(f"accst{i}", [128, TB], F32, base,
                              16384 + 2048 * i) for i in range(2)]
    st["otst"] = at("otst", [128, 8 * QMAX], BF16)
    assert o[0] <= OVLB, o[0]
    return st


OVLB = 50176


def emit_stage_consts(kb, blk, st, padrow_sb):
    V, S = kb.V, kb.S

    @blk.gpsimd
    def _(g):
        kb.G.do(g, lambda e: e.memset(st["vst"][:], 1.0))
        kb.G.do(g, lambda e: e.memset(st["nmst"][:], 0.0))
        for h in range(2):
            kb.G.do(g, lambda e, h=h: e.memset(st["qst"][h][64:65, :], 1.0))

    @blk.sync
    def _(q):
        for h in range(2):
            kb.dma(q, st["kst"][h][65:66, :], kb.inp["onesbf"], slot=0)
    return kb.G.prev


def qkv_production(kb, blk, st, cfg, kind, trange, stage_cols0):
    V, S, P = kb.V, kb.S, kb.P
    xs = cfg["xs"]
    dst = cfg["dst"]
    PS = kb.PS
    ptk = st.setdefault("ptk", {"ev": [None, None], "rot": [None],
                                "p1": [None, None], "fb": [None],
                                "fr": [None], "sq": [None],
                                "rotrd": [None], "fbrd": [None]})
    if "rotrd" not in ptk:
        ptk.update({"rotrd": [None], "fbrd": [None]})
    evtok = ptk["ev"]
    rottok = ptk["rot"]
    p1tok = ptk["p1"]
    fbtok = ptk["fb"]
    frtok = ptk["fr"]
    sqtok = ptk["sq"]
    first = True
    for (t0, t1) in tblocks(*trange):
        Wd = t1 - t0
        par = (t0 // TB) % 2

        @blk.tensor
        def _(p, t0=t0, t1=t1, Wd=Wd, par=par, first=first):
            for c in range(C8):
                P.do(p, lambda e, c=c: e.matmul(
                    PS[par][:, 0:Wd],
                    st["wqkvW"][:, cfg["woff"] + c * 128:cfg["woff"] + c * 128 + 128],
                    xs[:, c * NT0 + t0:c * NT0 + t1],
                    start=(c == 0), stop=(c == C8 - 1)),
                    waits=([cfg["wtok"], cfg["war"], cfg.get("war_v"),
                            cfg["xstok"], evtok[par]] if c == 0 else []),
                    mark=(c == C8 - 1))
        mmtok = P.prev

        @blk.scalar
        def _(s, Wd=Wd, par=par):
            S.do(s, lambda e: e.copy(st["rawst"][par][:, 0:Wd], PS[par][:, 0:Wd]),
                 waits=[mmtok, rottok[0], p1tok[par]])
            evtok[par] = S.prev
            if cfg["l2n"]:
                S.do(s, lambda e: e.activation(kb.ln_scr["aff"][:, 0:Wd],
                                               st["rawst"][par][:, 0:Wd],
                                               AF.Square))
                sqtok[0] = S.prev

        @blk.tensor
        def _(p, Wd=Wd, par=par):
            P.do(p, lambda e: e.matmul(PS[2][:, 0:Wd], cfg["prot"],
                                       st["rawst"][par][:, 0:Wd],
                                       start=True, stop=True),
                 waits=[evtok[par], ptk["rotrd"][0], cfg.get("prottok")],
                 mark=True)
            rmtok = P.prev
            rottok[0] = rmtok
            if cfg["l2n"]:
                P.do(p, lambda e: e.matmul(PS[3][0:2, 0:Wd], kb.bd2[:, :],
                                           kb.ln_scr["aff"][:, 0:Wd],
                                           start=True, stop=True),
                     waits=[sqtok[0], frtok[0]], mark=True)
                segtok = P.prev
            else:
                segtok = None
            cfg["_rm"] = rmtok
            cfg["_seg"] = segtok

        if cfg["l2n"]:
            @blk.scalar
            def _(s, Wd=Wd):
                S.do(s, lambda e: e.activation(st["fr"][0:2, 0:Wd],
                                               PS[3][0:2, 0:Wd], AF.Sqrt,
                                               bias=kb.c_zero[0:2, 0:1]),
                     waits=[cfg["_seg"]])

            @blk.vector
            def _(v, Wd=Wd):
                V.do(v, lambda e: e.tensor_scalar(
                    st["fr"][0:2, 0:Wd], st["fr"][0:2, 0:Wd],
                    1e-12, None, op0=ALU.max), waits=[S.prev])
                V.do(v, lambda e: e.reciprocal(st["fr"][0:2, 0:Wd],
                                               st["fr"][0:2, 0:Wd]))
                frtok[0] = V.prev

            @blk.tensor
            def _(p, Wd=Wd):
                P.do(p, lambda e: e.matmul(PS[6][:, 0:Wd], kb.bd2t[0:2, :],
                                           st["fr"][0:2, 0:Wd],
                                           start=True, stop=True),
                     waits=[frtok[0], ptk["fbrd"][0]], mark=True)
                fbtok[0] = P.prev

        @blk.vector
        def _(v, t0=t0, t1=t1, Wd=Wd, par=par):
            tc, ts = cfg["tcb"], cfg["tsb"]
            qs = cfg["qs_ap"]
            if qs is not None:
                V.do(v, lambda e: e.scalar_tensor_tensor(
                    st["t1st"][:, 0:Wd], st["rawst"][par][:, 0:Wd],
                    qs[:, :], tc[:, t0:t1], op0=ALU.mult, op1=ALU.mult),
                    waits=[evtok[par], cfg["war"]])
            else:
                V.do(v, lambda e: e.scalar_tensor_tensor(
                    st["t1st"][:, 0:Wd], st["rawst"][par][:, 0:Wd],
                    cfg["scl"], tc[:, t0:t1], op0=ALU.mult, op1=ALU.mult),
                    waits=[evtok[par], cfg["war"]])
            V.do(v, lambda e: e.scalar_tensor_tensor(
                st["t3st"][:, 0:Wd], PS[2][:, 0:Wd],
                cfg["scl"], ts[:, t0:t1], op0=ALU.mult, op1=ALU.mult),
                waits=[cfg["_rm"]])
            ptk["rotrd"][0] = V.prev
            V.do(v, lambda e: e.tensor_add(st["t1st"][:, 0:Wd],
                                           st["t1st"][:, 0:Wd],
                                           st["t3st"][:, 0:Wd]))
            p1tok[par] = V.prev
            lc = t0 - stage_cols0
            if cfg["l2n"]:
                for hh in range(2):
                    V.do(v, lambda e, hh=hh, lc=lc: e.tensor_mul(
                        dst[hh][0:64, lc:lc + Wd],
                        st["t1st"][hh * 64:hh * 64 + 64, 0:Wd],
                        PS[6][hh * 64:hh * 64 + 64, 0:Wd]),
                        waits=[fbtok[0]] if hh == 0 else [])
                ptk["fbrd"][0] = V.prev
            else:
                for hh in range(2):
                    V.do(v, lambda e, hh=hh, lc=lc: e.tensor_copy(
                        dst[hh][0:64, lc:lc + Wd],
                        st["t1st"][hh * 64:hh * 64 + 64, 0:Wd]))
            p1tok[par] = V.prev
        first = False
    return P.prev, V.prev


def v_production(kb, blk, st, cfg, kts):
    V, S, P = kb.V, kb.S, kb.P
    PS = kb.PS
    xs = cfg["xs"]
    ptk = st.setdefault("ptk", {})
    evtok = ptk.setdefault("vev", [None, None])
    for i, kt in enumerate(kts):
        par = i % 2

        @blk.tensor
        def _(p, kt=kt, par=par):
            for c in range(C8):
                P.do(p, lambda e, c=c: e.matmul(
                    PS[par][:, 0:128],
                    xs[:, c * NT0 + kt * 128:c * NT0 + kt * 128 + 128],
                    st["wqkvW"][:, cfg["woff"] + c * 128:cfg["woff"] + c * 128 + 128],
                    start=(c == 0), stop=(c == C8 - 1)),
                    waits=[cfg["wtok"], cfg["war"], evtok[par]] if c == 0 else [],
                    mark=(c == C8 - 1))
        mmtok = P.prev

        @blk.scalar
        def _(s, i=i, par=par):
            for hh in range(2):
                S.do(s, lambda e, hh=hh: e.copy(
                    st["vst"][:, i * 130 + hh * 65:i * 130 + hh * 65 + 64],
                    PS[par][:, hh * 64:hh * 64 + 64]),
                    waits=[mmtok, cfg["war"], cfg.get("vinit")]
                    if hh == 0 else [])
            evtok[par] = S.prev
    return S.prev


def attn_scores_pv(kb, blk, st, cfg):
    V, S, P = kb.V, kb.S, kb.P
    PS = kb.PS
    qt0, qtn, kt0 = cfg["qt0"], cfg["qtn"], cfg["kt0"]
    lb = cfg["lookback"]
    nq = qtn - qt0 + 1
    ktn = qtn
    nmcols = []
    SB3 = (4, 5, 7)
    s2tok = [None] * 3
    for hh in range(2):
        for qt in range(qt0, qtn + 1):
            par = (qt - qt0) % 3
            ka = max(qt - lb, kt0)
            nk = (qt - ka + 1) * 128
            qlc = (qt - qt0) * 128
            klc = (ka - kt0) * 128

            @blk.tensor
            def _(p, hh=hh, qt=qt, par=par, nk=nk, qlc=qlc, klc=klc):
                P.do(p, lambda e: e.matmul(
                    PS[SB3[par]][:, 0:nk],
                    st["qst"][hh][0:65, qlc:qlc + 128],
                    st["kst"][hh][0:65, klc:klc + nk],
                    start=True, stop=True),
                    waits=[cfg["qtok"], cfg["ktok"], cfg.get("qones_tok"),
                           s2tok[par]], mark=True)
            mmtok = P.prev

            @blk.vector
            def _(v, hh=hh, qt=qt, par=par, nk=nk):
                col = hh * nq + (qt - qt0)
                V.do(v, lambda e: e.reduce_max(
                    st["nmst"][:, col:col + 1], PS[SB3[par]][:, 0:nk],
                    axis=AX.X, negate=True),
                    waits=[mmtok, cfg.get("nm_war")
                           if (hh == 0 and qt == qt0) else None])
                s2tok[par] = V.prev
    @blk.sync
    def _(q):
        t1 = kb.dma(q, st["nmtr"][:], st["nmst"][:], slot=7,
                    waits=[kb.V.prev], transpose=True)
        for hh in range(2):
            kb.dma(q, st["qst"][hh][65:66, 0:nq * 128],
                   st["nmtr"][hh * nq:hh * nq + nq, 0:128], slot=7,
                   waits=[t1] if hh == 0 else [])
    bias_tok = (kb.dsem[7], kb.dcnt[7])
    sttok = [None] * 3
    sreaders = {}
    pvtok = [None, None]
    exptok = {}
    masktok = {}
    ottok = [None]
    osbtok = [None, None]
    trtok = [None, None]
    for hh in range(2):
        for kt in range(kt0, ktn + 1):
            par = kt % 3
            qa = max(kt, qt0)
            qb = min(kt + lb, qtn)
            nqq = (qb - qa + 1) * 128
            klc = (kt - kt0) * 128
            qlc = (qa - qt0) * 128

            @blk.tensor
            def _(p, hh=hh, kt=kt, par=par, nqq=nqq, klc=klc, qlc=qlc):
                P.do(p, lambda e: e.matmul(
                    PS[SB3[par]][:, 0:nqq],
                    st["kst"][hh][0:66, klc:klc + 128],
                    st["qst"][hh][0:66, qlc:qlc + nqq],
                    start=True, stop=True),
                    waits=[bias_tok, cfg.get("aug_tok"), sttok[par]],
                    mark=True)
            mmtok = P.prev

            @blk.scalar
            def _(s, hh=hh, kt=kt, par=par, nqq=nqq):
                slot = kt % 4
                S.do(s, lambda e: e.activation(
                    st["Pt"][:, slot * 384:slot * 384 + nqq],
                    PS[SB3[par]][:, 0:nqq], AF.Exp,
                    bias=kb.c_zero[:, 0:1]),
                    waits=[mmtok, sreaders.get(kt % 4)])
                exptok[(hh, kt)] = S.prev
                sttok[par] = S.prev

            @blk.vector
            def _(v, hh=hh, kt=kt, qa=qa, qb=qb):
                slot = kt % 4
                tok = exptok[(hh, kt)]
                for qt in range(qa, qb + 1):
                    o = qt - kt
                    mo = cfg["mask_of"](o)
                    if mo is not None:
                        V.do(v, lambda e, qt=qt, mo=mo: e.tensor_mul(
                            st["Pt"][:, slot * 384 + (qt - qa) * 128:
                                     slot * 384 + (qt - qa) * 128 + 128],
                            st["Pt"][:, slot * 384 + (qt - qa) * 128:
                                     slot * 384 + (qt - qa) * 128 + 128],
                            kb.msk[:, mo * 128:mo * 128 + 128]),
                            waits=[tok])
                masktok[(hh, kt)] = V.prev

            qt = kt
            if qt >= qt0:
                @blk.tensor
                def _(p, hh=hh, qt=qt):
                    ks = range(max(qt - lb, kt0), qt + 1)
                    nk = len(list(ks))
                    for r, kk in enumerate(ks):
                        slot = kk % 4
                        qa2 = max(kk, qt0)
                        off = slot * 384 + (qt - qa2) * 128
                        P.do(p, lambda e, kk=kk, off=off, r=r, nk=nk: e.matmul(
                            kb.PS[2 + hh][:, 0:65],
                            st["Pt"][:, off:off + 128],
                            st["vst"][:, (kk - kt0) * 130 + hh * 65:
                                      (kk - kt0) * 130 + hh * 65 + 65],
                            start=(r == 0), stop=(r == nk - 1)),
                            waits=([masktok[(hh, kk)] for kk in ks] +
                                   [cfg["vtok"], osbtok[hh]]) if r == 0 else [],
                            mark=(r == nk - 1))
                    pvtok[hh] = kb.P.prev
                    for kk in range(max(qt - lb, kt0), qt + 1):
                        sreaders[kk % 4] = kb.P.prev

                @blk.vector
                def _(v, hh=hh, qt=qt):
                    V.do(v, lambda e: e.reciprocal(
                        st["rc"][:, hh:hh + 1],
                        kb.PS[2 + hh][:, 64:65]), waits=[pvtok[hh]])
                    V.do(v, lambda e: e.tensor_scalar(
                        st["osb"][:, hh * 64:hh * 64 + 64],
                        kb.PS[2 + hh][:, 0:64],
                        st["rc"][:, hh:hh + 1], None,
                        op0=ALU.mult), waits=[trtok[hh]])
                    osbtok[hh] = V.prev

                @blk.tensor
                def _(p, hh=hh, qt=qt):
                    P.do(p, lambda e: e.transpose(
                        kb.PS[6][0:64, 0:128],
                        st["osb"][:, hh * 64:hh * 64 + 64], kb.idf[:]),
                        waits=[osbtok[hh], ottok[0]], mark=True)
                    trtok[hh] = kb.P.prev
                    cfg["_ott"] = kb.P.prev

                @blk.scalar
                def _(s, hh=hh, qt=qt):
                    qlc = (qt - qt0) * 128
                    S.do(s, lambda e: e.copy(
                        st["otst"][hh * 64:hh * 64 + 64,
                                   cfg["hb"] * QMAX + qlc:
                                   cfg["hb"] * QMAX + qlc + 128],
                        kb.PS[6][0:64, 0:128]), waits=[cfg["_ott"]])
                    ottok[0] = S.prev
    return P.prev, V.prev, S.prev


def attn_wo(kb, blk, st, cfg):
    V, S, P = kb.V, kb.S, kb.P
    PS = kb.PS
    qt0, qtn = cfg["qt0"], cfg["qtn"]
    q0, q1 = qt0 * 128, (qtn + 1) * 128
    wo = cfg["wo_ap"]
    wov = wo.rearrange("(g p) m -> g p m", p=128)
    addtok = [None]
    ldwait = [cfg["wo_war"]]
    for half in range(2):
        ldtoks = []

        @blk.gpsimd
        def _(g, half=half):
            for gi in range(4):
                hb = half * 4 + gi
                ldtoks.append(kb.dma(g, st["woW"][:, gi * 1024:(gi + 1) * 1024],
                                     wov[hb], slot=3,
                                     waits=[ldwait[0]] if gi == 0 else []))
        wotok = ldtoks[-1]
        for dm in range(C8):
            for (t0, t1) in tblocks(q0, q1):
                Wd = t1 - t0
                par = (dm + t0 // TB) % 2

                @blk.tensor
                def _(p, half=half, dm=dm, t0=t0, t1=t1, Wd=Wd, par=par):
                    for gi in range(4):
                        hb = half * 4 + gi
                        P.do(p, lambda e, gi=gi, hb=hb: e.matmul(
                            PS[par][:, 0:Wd],
                            st["woW"][:, gi * 1024 + dm * 128:
                                      gi * 1024 + dm * 128 + 128],
                            st["otst"][:, hb * QMAX + (t0 - q0):
                                       hb * QMAX + (t1 - q0)],
                            start=(gi == 0), stop=(gi == 3)),
                            waits=[wotok, cfg["ottok"], addtok[0]]
                            if gi == 0 else [],
                            mark=(gi == 3))
                mmtok = P.prev

                ldwait[0] = P.prev
                if cfg["dest"] == "resid":
                    @blk.vector
                    def _(v, dm=dm, t0=t0, t1=t1, Wd=Wd, par=par):
                        r = cfg["resid"]
                        if cfg.get("gate_first") and t0 == q0:
                            V.do(v, lambda e: e.scalar_tensor_tensor(
                                r[:, dm * NT0 + t0:dm * NT0 + t0 + 128],
                                PS[par][:, 0:128], kb.cmask[:, 0:1],
                                r[:, dm * NT0 + t0:dm * NT0 + t0 + 128],
                                op0=ALU.mult, op1=ALU.add), waits=[mmtok])
                            V.do(v, lambda e: e.tensor_add(
                                r[:, dm * NT0 + t0 + 128:dm * NT0 + t1],
                                r[:, dm * NT0 + t0 + 128:dm * NT0 + t1],
                                PS[par][:, 128:Wd]))
                        else:
                            V.do(v, lambda e: e.tensor_add(
                                r[:, dm * NT0 + t0:dm * NT0 + t1],
                                r[:, dm * NT0 + t0:dm * NT0 + t1],
                                PS[par][:, 0:Wd]), waits=[mmtok])
                        addtok[0] = V.prev
                else:
                    @blk.vector
                    def _(v, half=half, dm=dm, t0=t0, t1=t1, Wd=Wd, par=par):
                        r = cfg["resid"]
                        if half == 0:
                            V.do(v, lambda e: e.tensor_add(
                                st["accst"][par][:, 0:Wd],
                                r[:, dm * NT0 + t0:dm * NT0 + t1],
                                PS[par][:, 0:Wd]),
                                waits=[mmtok, cfg["acc_war"][par]])
                        else:
                            V.do(v, lambda e: e.tensor_copy(
                                st["accst"][par][:, 0:Wd],
                                PS[par][:, 0:Wd]),
                                waits=[mmtok, cfg["acc_war"][par]])
                        addtok[0] = V.prev
                        cfg["_acc_v"] = V.prev

                    @blk.sync
                    def _(q, half=half, dm=dm, t0=t0, t1=t1, Wd=Wd, par=par):
                        tok = kb.dma(q, cfg["acc_out"][half * D + dm * 128:
                                                       half * D + dm * 128 + 128,
                                                       t0 - HALO:t1 - HALO],
                                     st["accst"][par][:, 0:Wd], slot=1 + 8 * par,
                                     waits=[cfg["_acc_v"]])
                        cfg["acc_war"][par] = tok
    return P.prev, V.prev


def emit_cast_shift(kb, blk, resid, xs, waits=()):
    V = kb.V

    @blk.vector
    def _(v):
        first = [True]
        for j in range(C8):
            for (t0, t1) in tblocks(0, NT0):
                w = []
                if first[0]:
                    w = list(waits)
                    first[0] = False
                if j < C8 // 2:
                    V.do(v, lambda e, j=j, t0=t0, t1=t1: e.tensor_copy(
                        xs[:, j * NT0 + t0:j * NT0 + t1],
                        resid[:, j * NT0 + t0:j * NT0 + t1]), waits=w)
                else:
                    a0 = max(t0, 1)
                    V.do(v, lambda e, j=j, a0=a0, t1=t1: e.tensor_copy(
                        xs[:, j * NT0 + a0:j * NT0 + t1],
                        resid[:, j * NT0 + a0 - 1:j * NT0 + t1 - 1]), waits=w)
                    if t0 == 0:
                        V.do(v, lambda e, j=j: e.tensor_copy(
                            xs[:, j * NT0:j * NT0 + 1],
                            resid[:, j * NT0:j * NT0 + 1]))
    return V.prev


def emit_inner_attn(kb, blk, st, l, waps, tok):
    V, S, P = kb.V, kb.S, kb.P
    dws = tok.pop("acc_dma", None) or [None, None]

    @blk.vector
    def _(v):
        kb.V.do(v, lambda e: e.nop(), waits=dws)

    @blk.scalar
    def _(s):
        kb.S.do(s, lambda e: e.nop(), waits=dws)
    xstok = emit_cast_shift(kb, blk, st["resid"], st["xs"],
                            waits=[tok.get("resid_v")])

    @blk.vector
    def _(v):
        V.do(v, lambda e: e.tensor_scalar(
            kb.protq[:], kb.protF[:], kb.qsW[:, l:l + 1], None, op0=ALU.mult))
        V.do(v, lambda e: e.tensor_scalar(
            kb.protk[:], kb.protF[:], kb.ksW[:, l:l + 1], None, op0=ALU.mult))
    prottok = V.prev

    @blk.gpsimd
    def _(g):
        kb.G.do(g, lambda e: e.memset(st["vst"][:], 1.0),
                waits=[tok.get("pe_free"), tok.get("resid_v"),
                       dws[0], dws[1]])
        for h in range(2):
            kb.G.do(g, lambda e, h=h: e.memset(st["qst"][h][64:65, :], 1.0))
    vinit_tok = kb.G.prev

    @blk.sync
    def _(q):
        for h in range(2):
            kb.dma(q, st["kst"][h][65:66, :], kb.inp["onesbf"], slot=10,
                   waits=[tok.get("pe_free"), dws[0], dws[1]]
                   if h == 0 else [])
    aug_tok = (kb.dsem[10], kb.dcnt[10])

    wv = waps["Wqkv"][l].rearrange("(c p) m -> p c m", p=128)
    stage_war = [tok.get("pe_free")]
    war_v = [tok.get("resid_v")]
    wo_war = [None]
    wtokL = [None]
    wv2 = None

    def load_wqkv(hb, wait):
        toks = []

        @blk.gpsimd
        def _(g):
            for i in range(3):
                toks.append(kb.dma(
                    g, st["wqkvW"][:, i * 1024:(i + 1) * 1024].rearrange(
                        "p (c m) -> p c m", c=C8),
                    wv[:, :, i * D + hb * 128:i * D + (hb + 1) * 128],
                    slot=2, waits=[wait] if i == 0 else []))
        return toks[-1]

    for half in range(2):
        qt0, qtn, kt0 = HALVES_IN[half]
        qrange = (qt0 * 128, (qtn + 1) * 128)
        krange = (kt0 * 128, (qtn + 1) * 128)
        kts = list(range(kt0, qtn + 1))

        @blk.scalar
        def _(s, kt0=kt0, qtn=qtn):
            nk = (qtn - kt0 + 1) * 128
            for hh in range(2):
                S.do(s, lambda e, hh=hh, nk=nk: e.copy(
                    st["kst"][hh][64:65, 0:nk],
                    kb.padrow_sb[0:1, kt0 * 128:kt0 * 128 + nk]),
                    waits=[stage_war[0]] if hh == 0 else [])

        if half == 0:
            wtokL[0] = load_wqkv(0, stage_war[0])
        stok2 = None
        for hb in range(8):
            wtok = wtokL[0]
            cfgq = dict(xs=st["xs"], woff=0, wtok=wtok, war=stage_war[0],
                        prottok=prottok,
                        war_v=war_v[0],
                        xstok=xstok, dst=st["qst"], l2n=True,
                        prot=kb.protq[:], qs_ap=kb.qsW[:, l:l + 1], scl=1.0,
                        tcb=kb.tcB, tsb=kb.tsB)
            qpe, qv = qkv_production(kb, blk, st, cfgq, "q", qrange, qt0 * 128)
            cfgk = dict(xs=st["xs"], woff=1024, wtok=wtok, war=stage_war[0],
                        prottok=prottok,
                        war_v=war_v[0],
                        xstok=xstok, dst=st["kst"], l2n=True,
                        prot=kb.protk[:], qs_ap=kb.ksW[:, l:l + 1], scl=1.0,
                        tcb=kb.tciB, tsb=kb.tsiB)
            kpe, kv = qkv_production(kb, blk, st, cfgk, "k", krange, kt0 * 128)
            cfgv = dict(xs=st["xs"], woff=2048, wtok=wtok, war=stage_war[0],
                        vinit=vinit_tok)
            vtok = v_production(kb, blk, st, cfgv, kts)
            if hb < 7:
                wtokL[0] = load_wqkv(hb + 1, kb.P.prev)
            elif half == 0:
                wtokL[0] = load_wqkv(0, kb.P.prev)
            scfg = dict(qt0=qt0, qtn=qtn, kt0=kt0, lookback=2,
                        mask_of=lambda o: {0: 0, 2: 1}.get(o),
                        hb=hb, qtok=qv, ktok=kv, vtok=vtok,
                        aug_tok=aug_tok, qones_tok=vinit_tok,
                        nm_war=tok.get("nm_war"))
            ptok, vtok2, stok2 = attn_scores_pv(kb, blk, st, scfg)
            stage_war[0] = ptok
            war_v[0] = vtok2
            tok["nm_war"] = (kb.dsem[7], kb.dcnt[7])
        wcfg = dict(qt0=qt0, qtn=qtn, wo_ap=waps["Wo"][l], dest="resid",
                    resid=st["resid"], gate_first=(half == 0),
                    wo_war=wo_war[0], ottok=stok2, pe_tok=stage_war[0])
        wpe, wv2 = attn_wo(kb, blk, st, wcfg)
        wo_war[0] = wpe
        stage_war[0] = wpe
        war_v[0] = wv2
    tok["resid_v"] = wv2
    tok["pe_free"] = stage_war[0]
    return tok


def emit_inner_ffn(kb, blk, st, l, waps, tok):
    V, S, P = kb.V, kb.S, kb.P
    PS = kb.PS
    own = tblocks(HALO, NT0)
    lng = kb.lngW[:, l * C8:(l + 1) * C8]
    lnb = kb.lnbW[:, l * C8:(l + 1) * C8]
    lntok = emit_ln(kb, blk, st["resid"], st["xs"], NT0, own, kb.ln_scr,
                    gamma=lng, beta=lnb, shift=True,
                    extra_wait=tok.get("resid_v"))
    w1v = waps["W1p"][l].rearrange("(c p) m -> p c m", p=128)
    w2v = waps["W2"][l]
    groups = [list(range(g, min(g + 4, NPAIR))) for g in range(0, NPAIR, 4)]

    def load_group(gi, wait):
        prs = groups[gi]
        c0, c1 = prs[0] * 256, (prs[-1] + 1) * 256
        r0, r1 = prs[0] * 128, min((prs[-1] + 1) * 128, DIM_INNER)
        toks = []

        @blk.gpsimd
        def _(g):
            toks.append(kb.dma(
                g, st["w1W"][:, 0:8 * (c1 - c0)].rearrange(
                    "p (c m) -> p c m", c=C8),
                w1v[:, :, c0:c1], slot=4, waits=[wait]))
            t2 = None
            for pi, pr in enumerate(prs):
                rows = min(128, DIM_INNER - pr * 128)
                t2 = kb.dma(g, st["w2W"][0:rows, pi * 1024:(pi + 1) * 1024],
                            w2v[pr * 128:pr * 128 + rows, :], slot=5)
            toks.append(t2)
        return toks

    gwar = [tok.get("pe_free")]
    pa_war = [None]
    pg_war = [None]
    addtok = [None, None]
    fusedwar = [None]
    ldt = load_group(0, gwar[0])
    for gi, prs in enumerate(groups):
        w1t, w2t = ldt
        npr = len(prs)
        ncol = npr * 256
        for (t0, t1) in own:
            Wd = t1 - t0
            fusedtok = {}
            for pi, pr in enumerate(prs):
                rows = min(128, DIM_INNER - pr * 128)

                @blk.tensor
                def _(p, pr=pr, pi=pi, t0=t0, t1=t1, Wd=Wd, rows=rows):
                    for c in range(C8):
                        P.do(p, lambda e, c=c: e.matmul(
                            PS[0][0:rows, 0:Wd],
                            st["w1W"][:, c * ncol + pi * 256:
                                      c * ncol + pi * 256 + rows],
                            st["xs"][:, c * NT0 + t0:c * NT0 + t1],
                            start=(c == 0), stop=(c == C8 - 1)),
                            waits=[w1t, lntok, pa_war[0]] if c == 0 else [],
                            mark=(c == C8 - 1))
                    for c in range(C8):
                        P.do(p, lambda e, c=c: e.matmul(
                            PS[1][0:rows, 0:Wd],
                            st["w1W"][:, c * ncol + pi * 256 + 128:
                                      c * ncol + pi * 256 + 128 + rows],
                            st["xs"][:, c * NT0 + t0:c * NT0 + t1],
                            start=(c == 0), stop=(c == C8 - 1)),
                            waits=[pg_war[0]] if c == 0 else [],
                            mark=(c == C8 - 1))
                mmtok = P.prev

                @blk.scalar
                def _(s, pr=pr, Wd=Wd, rows=rows):
                    S.do(s, lambda e: e.activation(
                        kb.ln_scr["aff"][0:rows, 0:Wd], PS[1][0:rows, 0:Wd],
                        AF.Gelu, bias=kb.b1W[0:rows, l * 44 + 2 * pr + 1:
                                             l * 44 + 2 * pr + 2]),
                        waits=[mmtok])
                    pg_war[0] = S.prev

                @blk.vector
                def _(v, pr=pr, pi=pi, t0=t0, Wd=Wd, rows=rows):
                    V.do(v, lambda e: e.scalar_tensor_tensor(
                        st["fusedst"][:, pi * TB:pi * TB + Wd][0:rows, :],
                        PS[0][0:rows, 0:Wd],
                        kb.b1W[0:rows, l * 44 + 2 * pr:l * 44 + 2 * pr + 1],
                        kb.ln_scr["aff"][0:rows, 0:Wd],
                        op0=ALU.add, op1=ALU.mult),
                        waits=[kb.S.prev, fusedwar[0]])
                    fusedtok[pi] = V.prev
                    pa_war[0] = V.prev
            for dm in range(C8):
                ppar = dm % 2

                @blk.tensor
                def _(p, dm=dm, t0=t0, t1=t1, Wd=Wd, ppar=ppar):
                    for pi, pr in enumerate(prs):
                        rows = min(128, DIM_INNER - pr * 128)
                        P.do(p, lambda e, pi=pi, rows=rows: e.matmul(
                            PS[2 + ppar][:, 0:Wd],
                            st["w2W"][0:rows, pi * 1024 + dm * 128:
                                      pi * 1024 + dm * 128 + 128],
                            st["fusedst"][:, pi * TB:pi * TB + Wd][0:rows, :],
                            start=(pi == 0), stop=(pi == npr - 1)),
                            waits=[w2t, fusedtok[npr - 1], addtok[ppar]]
                            if pi == 0 else [],
                            mark=(pi == npr - 1))
                    if dm == C8 - 1:
                        fusedwar[0] = kb.P.prev
                mmtok2 = P.prev

                @blk.vector
                def _(v, dm=dm, t0=t0, t1=t1, Wd=Wd, ppar=ppar, gi=gi):
                    r = st["resid"]
                    if gi == len(groups) - 1:
                        V.do(v, lambda e: e.scalar_tensor_tensor(
                            r[:, dm * NT0 + t0:dm * NT0 + t1],
                            PS[2 + ppar][:, 0:Wd],
                            kb.b2W[:, l * C8 + dm:l * C8 + dm + 1],
                            r[:, dm * NT0 + t0:dm * NT0 + t1],
                            op0=ALU.add, op1=ALU.add), waits=[mmtok2])
                    else:
                        V.do(v, lambda e: e.tensor_add(
                            r[:, dm * NT0 + t0:dm * NT0 + t1],
                            r[:, dm * NT0 + t0:dm * NT0 + t1],
                            PS[2 + ppar][:, 0:Wd]), waits=[mmtok2])
                    addtok[ppar] = V.prev
        if gi + 1 < len(groups):
            ldt = load_group(gi + 1, fusedwar[0])
    tok["resid_v"] = V.prev
    tok["pe_free"] = P.prev
    return tok


def emit_halo_exchange(kb, blk, st, idx, tok):
    V = kb.V
    cin, cout = kb.cinH, kb.coutH

    @blk.sync
    def _(q):
        for j in range(C8):
            kb.dma(q, cin[j * 128:(j + 1) * 128, :],
                   st["resid"][:, j * NT0 + OWN:j * NT0 + NT0], slot=6,
                   waits=[tok["resid_v"]] if j == 0 else [])
    dtok = (kb.dsem[6], kb.dcnt[6])

    @blk.gpsimd
    def _(g):
        g.wait_ge(*dtok)
        kb.G.do(g, lambda e: e.collective_compute(
            "AllGather", mybir.AluOpType.bypass,
            replica_groups=[[0, 1], [2, 3], [4, 5], [6, 7]],
            ins=[cin.opt()], outs=[cout.opt()]))
    ctok = kb.G.prev

    @blk.sync
    def _(q):
        for j in range(C8):
            kb.dma(q, st["resid"][:, j * NT0:j * NT0 + HALO],
                   cout[j * 128:(j + 1) * 128, :], slot=6,
                   waits=[ctok] if j == 0 else [])
    htok = (kb.dsem[6], kb.dcnt[6])

    @blk.vector
    def _(v):
        V.do(v, lambda e: e.tensor_scalar(
            st["resid"].rearrange("p (c t) -> p c t", c=C8)[:, :, 0:HALO],
            st["resid"].rearrange("p (c t) -> p c t", c=C8)[:, :, 0:HALO],
            kb.cmask[:, 0:1], None, op0=ALU.mult), waits=[htok])
    tok["resid_v"] = V.prev
    return tok


def alloc_block0(kb):
    st = {}
    st["resid"] = kb.sbuf("resid", [128, C8 * NT0], F32)
    st["xs"] = kb.sbuf("xs", [128, C8 * NT0], BF16)
    base = kb.mkslab("ovl", OVLB)
    kb.ovl_base = base
    st.update(alloc_attn_state(kb, base))
    st["rawst"] = [kb.sbuf(f"raw{i}", [128, TB], F32) for i in range(2)]
    st["t1st"] = kb.sbuf("t1st", [128, TB], F32)
    st["t3st"] = kb.sbuf("t3st", [128, TB], F32)
    st["fr"] = kb.sbuf("fr", [2, TB], F32)
    st["w1W"] = kb.sbuf_at("w1W", [128, 8 * 1024], BF16, base, 0)
    st["w2W"] = kb.sbuf_at("w2W", [128, 4 * 1024], BF16, base, 16384)
    st["fusedst"] = kb.sbuf_at("fusedst", [128, 4 * TB], BF16, base, 24576)
    kb.PS = [kb.psum(f"PSB{i}", [128, 512], F32) for i in range(8)]
    kb.ln_scr = alloc_ln_scratch(kb)
    kb.tcB = kb.sbuf("tcB", [128, NT0], BF16)
    kb.tsB = kb.sbuf("tsB", [128, NT0], BF16)
    kb.tciB = kb.sbuf("tciB", [128, NT0], BF16)
    kb.tsiB = kb.sbuf("tsiB", [128, NT0], BF16)
    kb.tcA = kb.sbuf_at("tcA", [128, NT0], BF16,
                        kb.nc.lookup_mloc(kb.tcB).addr, 0)
    kb.tsA = kb.sbuf_at("tsA", [128, NT0], BF16,
                        kb.nc.lookup_mloc(kb.tsB).addr, 0)
    kb.msk = kb.sbuf("mskS", [128, 4 * 128], BF16)
    kb.protF = kb.sbuf("protF", [128, 128], F32)
    kb.protq = kb.sbuf("protq", [128, 128], F32)
    kb.protk = kb.sbuf("protk", [128, 128], F32)
    kb.bd2 = kb.sbuf("bd2S", [128, 2], F32)
    kb.bd2t = kb.sbuf("bd2tS", [2, 128], F32)
    kb.idf = kb.sbuf("idfS", [128, 128], F32)
    kb.padrow_sb = kb.sbuf("padrowS", [1, NT0], BF16)
    kb.cmask = kb.sbuf("cmaskS", [128, 1], F32)
    kb.qsW = kb.sbuf("qsWS", [128, IN_DEPTH], F32)
    kb.ksW = kb.sbuf("ksWS", [128, IN_DEPTH], F32)
    kb.lngW = kb.sbuf("lngWS", [128, IN_DEPTH * C8], F32)
    kb.lnbW = kb.sbuf("lnbWS", [128, IN_DEPTH * C8], F32)
    kb.b1W = kb.sbuf("b1WS", [128, IN_DEPTH * 44], F32)
    kb.b2W = kb.sbuf("b2WS", [128, IN_DEPTH * C8], F32)
    return st


def emit_prelude(kb, blk, st, ins):
    ctok = emit_consts(kb, blk)
    sctok = emit_stage_consts(kb, blk, st, kb.padrow_sb)

    @blk.sync
    def _(q):
        pairs = [
            (kb.tciB, "TciB"),
            (kb.tsiB, "TsiB"), (kb.tcA, "TcA"), (kb.tsA, "TsA"),
            (kb.msk, "MSK"), (kb.protF, "PROTF"), (kb.bd2, "BD2"),
            (kb.bd2t, "BD2T"), (kb.idf, "IDF"), (kb.padrow_sb, "padrowbf"),
            (kb.cmask, "coremask"),
        ]
        if hasattr(kb, "a0w"):
            pairs += [(kb.a0w, "a0_lnw"), (kb.a0b, "a0_lnb")]
        for sb, name in pairs:
            if name in ins:
                kb.dma(q, sb[:], ins[name], slot=0)
        if "qs_col" in ins:
            kb.dma(q, kb.qsW[:], ins["qs_col"], slot=0)
            kb.dma(q, kb.ksW[:], ins["ks_col"], slot=0)
        for sb, name, w in [(kb.lngW, "ln_g", C8), (kb.lnbW, "ln_b", C8),
                            (kb.b1W, "b1p", 44), (kb.b2W, "b2c", C8)]:
            if name in ins:
                kb.dma(q, sb[:].rearrange("p (l c) -> p l c", l=IN_DEPTH),
                       ins[name].rearrange("l p c -> p l c"), slot=0)
    ldtok = (kb.dsem[0], kb.dcnt[0])
    @blk.vector
    def _(v):
        kb.V.do(v, lambda e: e.nop(), waits=[ldtok, ctok, sctok])

    @blk.scalar
    def _(s):
        kb.S.do(s, lambda e: e.nop(), waits=[ldtok, ctok, sctok])

    @blk.tensor
    def _(p):
        kb.P.do(p, lambda e: e.nop(), waits=[ldtok, ctok, sctok], mark=True)

    @blk.gpsimd
    def _(g):
        g.wait_ge(*ldtok)
    return {"resid_v": kb.V.prev, "pe_free": kb.P.prev}


def build_piece_inner(l0=0, nl=1):
    kb = KB()
    nc = kb.nc
    rin = kb.din("residT", [D, NT0])
    waps = {
        "Wqkv": kb.din("Wqkv", [IN_DEPTH, D, 3 * 1024]),
        "Wo": kb.din("Wo", [IN_DEPTH, D, D]),
        "W1p": kb.din("W1p", [IN_DEPTH, D, NPAIR * 256]),
        "W2": kb.din("W2", [IN_DEPTH, DIM_INNER, D]),
    }
    ins = {}
    for name, shape in [("TcB", [128, NT0]), ("TsB", [128, NT0]),
                        ("TciB", [128, NT0]), ("TsiB", [128, NT0]),
                        ("MSK", [128, 4 * 128]), ("padrowbf", [1, NT0]),
                        ("onesbf", [1, KMAX])]:
        ins[name] = kb.din(name, shape, BF16)
    for name, shape in [("PROTF", [128, 128]), ("BD2", [128, 2]),
                        ("BD2T", [2, 128]), ("IDF", [128, 128]),
                        ("coremask", [128, 1]),
                        ("qs_col", [128, IN_DEPTH]),
                        ("ks_col", [128, IN_DEPTH]),
                        ("ln_g", [IN_DEPTH, 128, C8]),
                        ("ln_b", [IN_DEPTH, 128, C8]),
                        ("b1p", [IN_DEPTH, 128, 44]),
                        ("b2c", [IN_DEPTH, 128, C8])]:
        ins[name] = kb.din(name, shape, F32)
    rout = kb.dout("residO", [D, NT0])
    st = alloc_block0(kb)
    kb.open()
    with nc.Block() as blk:
        tok = emit_prelude(kb, blk, st, ins)
        ldtok = emit_load_x(kb, blk, st["resid"], rin)

        @blk.tensor
        def _(p):
            p.wait_ge(*ldtok)

        @blk.vector
        def _(v):
            v.wait_ge(*ldtok)

        @blk.scalar
        def _(s):
            s.wait_ge(*ldtok)
        for l in range(l0, l0 + nl):
            tok = emit_inner_attn(kb, blk, st, l, waps, tok)
            tok = emit_inner_ffn(kb, blk, st, l, waps, tok)

        @blk.sync
        def _(q):
            t = None
            for j in range(C8):
                t = kb.dma(q, rout[j * 128:(j + 1) * 128, :],
                           st["resid"][:, j * NT0:(j + 1) * NT0], slot=1,
                           waits=[tok["resid_v"], tok["pe_free"]]
                           if j == 0 else [])
            q.wait_ge(*t)
    kb.close()
    return kb


def emit_load_tablesB(kb, blk, ins, tok):
    @blk.sync
    def _(q):
        kb.dma(q, kb.tcB[:], ins["TcB"], slot=11,
               waits=[tok["pe_free"], tok["resid_v"]])
        kb.dma(q, kb.tsB[:], ins["TsB"], slot=11)
    t = (kb.dsem[11], kb.dcnt[11])

    @blk.vector
    def _(v):
        kb.V.do(v, lambda e: e.nop(), waits=[t])
    return tok


def emit_a0(kb, blk, st, waps, tok, acc_out):
    V, S, P = kb.V, kb.S, kb.P
    lntok = emit_ln(kb, blk, st["resid"], st["xs"], NT0, tblocks(256, NT0),
                    kb.ln_scr, gamma=kb.a0w, beta=kb.a0b, shift=False,
                    extra_wait=tok.get("resid_v"))
    wv = waps["w_qkvT"].rearrange("(c p) m -> p c m", p=128)
    stage_war = [tok.get("pe_free")]
    war_v = [lntok]
    wo_war = [None]
    wtokL = [None]
    wv2 = None

    def load_wqkv(hb, wait):
        toks = []

        @blk.gpsimd
        def _(g):
            for i in range(3):
                toks.append(kb.dma(
                    g, st["wqkvW"][:, i * 1024:(i + 1) * 1024].rearrange(
                        "p (c m) -> p c m", c=C8),
                    wv[:, :, i * D + hb * 128:i * D + (hb + 1) * 128],
                    slot=2, waits=[wait] if i == 0 else []))
        return toks[-1]

    acc_war = [None, None]
    dma_war = [None, None]
    for half in range(2):
        qt0, qtn, kt0 = HALVES_A0[half]
        qrange = (qt0 * 128, (qtn + 1) * 128)
        krange = (kt0 * 128, (qtn + 1) * 128)
        kts = list(range(kt0, qtn + 1))

        @blk.scalar
        def _(s, kt0=kt0, qtn=qtn):
            nk = (qtn - kt0 + 1) * 128
            for hh in range(2):
                S.do(s, lambda e, hh=hh, nk=nk: e.copy(
                    st["kst"][hh][64:65, 0:nk],
                    kb.padrow_sb[0:1, kt0 * 128:kt0 * 128 + nk]),
                    waits=[stage_war[0], dma_war[0], dma_war[1]]
                    if hh == 0 else [])

        @blk.vector
        def _(v):
            if dma_war[0] is not None:
                kb.V.do(v, lambda e: e.nop(), waits=[dma_war[0], dma_war[1]])

        @blk.gpsimd
        def _(g):
            for h in range(2):
                kb.G.do(g, lambda e, h=h: e.memset(st["qst"][h][64:65, :], 1.0),
                        waits=[stage_war[0], dma_war[0], dma_war[1]]
                        if h == 0 else [])

        @blk.sync
        def _(q):
            for h in range(2):
                kb.dma(q, st["kst"][h][65:66, :], kb.inp["onesbf"], slot=10,
                       waits=[stage_war[0], dma_war[0], dma_war[1]]
                       if h == 0 else [])
        a0_aug_tok = (kb.dsem[10], kb.dcnt[10])
        a0_qones = [kb.G.prev]

        if half == 0:
            wtokL[0] = load_wqkv(0, stage_war[0])
        stok2 = None
        for hb in range(8):
            wtok = wtokL[0]
            cfgq = dict(xs=st["xs"], woff=0, wtok=wtok, war=stage_war[0],
                        war_v=war_v[0], prottok=None, xstok=lntok,
                        dst=st["qst"], l2n=False, prot=kb.protF[:],
                        qs_ap=None, scl=0.125, tcb=kb.tcA, tsb=kb.tsA)
            qpe, qv = qkv_production(kb, blk, st, cfgq, "q", qrange, qt0 * 128)
            cfgk = dict(xs=st["xs"], woff=1024, wtok=wtok, war=stage_war[0],
                        war_v=war_v[0], prottok=None, xstok=lntok,
                        dst=st["kst"], l2n=False, prot=kb.protF[:],
                        qs_ap=None, scl=1.0, tcb=kb.tcA, tsb=kb.tsA)
            kpe, kv = qkv_production(kb, blk, st, cfgk, "k", krange, kt0 * 128)
            cfgv = dict(xs=st["xs"], woff=2048, wtok=wtok, war=stage_war[0],
                        vinit=None)
            vtok = v_production(kb, blk, st, cfgv, kts)
            if hb < 7:
                wtokL[0] = load_wqkv(hb + 1, kb.P.prev)
            elif half == 0:
                wtokL[0] = load_wqkv(0, kb.P.prev)
            scfg = dict(qt0=qt0, qtn=qtn, kt0=kt0, lookback=1,
                        mask_of=lambda o: {0: 2, 1: 3}.get(o),
                        hb=hb, qtok=qv, ktok=kv, vtok=vtok,
                        aug_tok=a0_aug_tok, qones_tok=a0_qones[0],
                        nm_war=tok.get("nm_war"))
            ptok, vtok2, stok2 = attn_scores_pv(kb, blk, st, scfg)
            stage_war[0] = ptok
            war_v[0] = vtok2
            tok["nm_war"] = (kb.dsem[7], kb.dcnt[7])
        wcfg = dict(qt0=qt0, qtn=qtn, wo_ap=waps["w_outT"], dest="acc",
                    resid=st["resid"], acc_out=acc_out, acc_war=acc_war,
                    wo_war=wo_war[0], ottok=stok2, pe_tok=stage_war[0])
        wpe, wv2 = attn_wo(kb, blk, st, wcfg)
        wo_war[0] = wpe
        stage_war[0] = wpe
        war_v[0] = wv2
        dma_war[0] = (kb.dsem[1], kb.dcnt[1])
        dma_war[1] = (kb.dsem[9], kb.dcnt[9])
    tok["resid_v"] = wv2
    tok["pe_free"] = stage_war[0]
    tok["acc_dma"] = [(kb.dsem[1], kb.dcnt[1]), (kb.dsem[9], kb.dcnt[9])]
    return tok


def emit_combine_block0(kb, blk, st, xT, acc_in, x1_out, tok):
    V = kb.V
    ldtoks = {}
    addv = [None, None]
    for j in range(C8):
        for bi, (t0, t1) in enumerate(tblocks(HALO, NT0)):
            Wd = t1 - t0
            par = (j * 4 + bi) % 2

            @blk.sync
            def _(q, j=j, t0=t0, t1=t1, par=par):
                adm = tok.get("acc_dma") or [None, None]
                first = (j == 0 and t0 == HALO)
                xw = [addv[par]]
                if first:
                    xw += [adm[0], adm[1], (kb.dsem[7], kb.dcnt[7]),
                           (kb.dsem[10], kb.dcnt[10]),
                           tok.get("pe_free"), tok.get("resid_v")]
                t1a = kb.dma(q, st["rawst"][par][:, 0:t1 - t0],
                             xT[j * 128:(j + 1) * 128, t0:t1], slot=12 + par,
                             waits=xw)
                t1b = kb.dma(q, st["accst"][par][:, 0:t1 - t0],
                             acc_in[j * 128:(j + 1) * 128, t0 - HALO:t1 - HALO],
                             slot=12 + par)
                t1c = kb.dma(q, (st["t1st"] if par == 0 else st["t3st"])[:, 0:t1 - t0],
                             acc_in[D + j * 128:D + (j + 1) * 128,
                                    t0 - HALO:t1 - HALO],
                             slot=12 + par)
                ldtoks[(j, t0)] = t1c

            @blk.vector
            def _(v, j=j, t0=t0, t1=t1, Wd=Wd, par=par):
                V.do(v, lambda e: e.tensor_add(
                    st["resid"][:, j * NT0 + t0:j * NT0 + t1],
                    st["resid"][:, j * NT0 + t0:j * NT0 + t1],
                    st["rawst"][par][:, 0:Wd]),
                    waits=[ldtoks[(j, t0)], tok.get("resid_v")])
                V.do(v, lambda e: e.tensor_add(
                    st["resid"][:, j * NT0 + t0:j * NT0 + t1],
                    st["resid"][:, j * NT0 + t0:j * NT0 + t1],
                    st["accst"][par][:, 0:Wd]))
                V.do(v, lambda e: e.tensor_add(
                    st["resid"][:, j * NT0 + t0:j * NT0 + t1],
                    st["resid"][:, j * NT0 + t0:j * NT0 + t1],
                    (st["t1st"] if par == 0 else st["t3st"])[:, 0:Wd]))
                addv[par] = V.prev

            @blk.sync
            def _(q, j=j, t0=t0, t1=t1, par=par):
                kb.dma(q, x1_out[j * 128:(j + 1) * 128, t0 - HALO:t1 - HALO],
                       st["resid"][:, j * NT0 + t0:j * NT0 + t1], slot=1,
                       waits=[addv[par]])
    tok["x1_dma"] = (kb.dsem[1], kb.dcnt[1])
    return tok


def build_block0(with_exchange=True, dev=True):
    kb = K_B0 = KB()
    nc = kb.nc
    xT = kb.din("xT", [D, NT0])
    waps = {
        "Wqkv": kb.din("Wqkv", [IN_DEPTH, D, 3 * 1024]),
        "Wo": kb.din("Wo", [IN_DEPTH, D, D]),
        "W1p": kb.din("W1p", [IN_DEPTH, D, NPAIR * 256]),
        "W2": kb.din("W2", [IN_DEPTH, DIM_INNER, D]),
        "w_qkvT": kb.din("w_qkvT", [D, 3 * 1024]),
        "w_outT": kb.din("w_outT", [D, D]),
    }
    ins = {}
    for name, shape in [("TcB", [128, NT0]), ("TsB", [128, NT0]),
                        ("TciB", [128, NT0]), ("TsiB", [128, NT0]),
                        ("TcA", [128, NT0]), ("TsA", [128, NT0]),
                        ("MSK", [128, 4 * 128]), ("padrowbf", [1, NT0]),
                        ("onesbf", [1, KMAX])]:
        ins[name] = kb.din(name, shape, BF16)
    for name, shape in [("PROTF", [128, 128]), ("BD2", [128, 2]),
                        ("BD2T", [2, 128]), ("IDF", [128, 128]),
                        ("coremask", [128, 1]),
                        ("qs_col", [128, IN_DEPTH]),
                        ("ks_col", [128, IN_DEPTH]),
                        ("ln_g", [IN_DEPTH, 128, C8]),
                        ("ln_b", [IN_DEPTH, 128, C8]),
                        ("b1p", [IN_DEPTH, 128, 44]),
                        ("b2c", [IN_DEPTH, 128, C8]),
                        ("a0_lnw", [128, C8]), ("a0_lnb", [128, C8])]:
        ins[name] = kb.din(name, shape, F32)
    x1o = kb.dout("x1T", [D, OWN])
    spill_acc = kb.dint("spill_acc", [2 * D, OWN])
    st = alloc_block0(kb)
    kb.a0w = kb.sbuf("a0w", [128, C8], F32)
    kb.a0b = kb.sbuf("a0b", [128, C8], F32)
    kb.cinH = kb.dint("cinH", [D, HALO])
    kb.coutH = kb.dint("coutH", [2 * D, HALO])
    kb.open()
    with nc.Block() as blk:
        tok = emit_prelude(kb, blk, st, ins)
        ldtok = emit_load_x(kb, blk, st["resid"], xT)

        @blk.tensor
        def _(p):
            p.wait_ge(*ldtok)

        @blk.vector
        def _(v):
            v.wait_ge(*ldtok)

        @blk.scalar
        def _(s):
            s.wait_ge(*ldtok)
        lt = emit_ln(kb, blk, st["resid"], st["resid"], NT0, tblocks(0, NT0),
                     kb.ln_scr)
        tok["resid_v"] = lt
        tok = emit_a0(kb, blk, st, waps, tok, spill_acc)
        tok = emit_load_tablesB(kb, blk, ins, tok)
        for l in range(IN_DEPTH):
            tok = emit_inner_attn(kb, blk, st, l, waps, tok)
            tok = emit_inner_ffn(kb, blk, st, l, waps, tok)
            if with_exchange and l < IN_DEPTH - 1:
                tok = emit_halo_exchange(kb, blk, st, l, tok)
        tok = emit_combine_block0(kb, blk, st, xT, spill_acc, x1o, tok)

        @blk.sync
        def _(q):
            q.wait_ge(*tok["x1_dma"])
    kb.close()
    return kb


def build_piece_a0():
    kb = KB()
    nc = kb.nc
    rin = kb.din("residT", [D, NT0])
    waps = {
        "w_qkvT": kb.din("w_qkvT", [D, 3 * 1024]),
        "w_outT": kb.din("w_outT", [D, D]),
    }
    ins = {}
    for name, shape in [("TcA", [128, NT0]), ("TsA", [128, NT0]),
                        ("MSK", [128, 4 * 128]), ("padrowbf", [1, NT0]),
                        ("onesbf", [1, KMAX])]:
        ins[name] = kb.din(name, shape, BF16)
    for name, shape in [("PROTF", [128, 128]), ("BD2", [128, 2]),
                        ("BD2T", [2, 128]), ("IDF", [128, 128]),
                        ("coremask", [128, 1]),
                        ("a0_lnw", [128, C8]), ("a0_lnb", [128, C8])]:
        ins[name] = kb.din(name, shape, F32)
    acc = kb.dout("acc", [2 * D, OWN])
    st = alloc_block0(kb)
    kb.a0w = kb.sbuf("a0w", [128, C8], F32)
    kb.a0b = kb.sbuf("a0b", [128, C8], F32)
    kb.open()
    with nc.Block() as blk:
        tok = emit_prelude(kb, blk, st, ins)
        ldtok = emit_load_x(kb, blk, st["resid"], rin)

        @blk.tensor
        def _(p):
            p.wait_ge(*ldtok)

        @blk.vector
        def _(v):
            v.wait_ge(*ldtok)

        @blk.scalar
        def _(s):
            s.wait_ge(*ldtok)
        tok = emit_a0(kb, blk, st, waps, tok, acc)

        @blk.sync
        def _(q):
            q.wait_ge(*tok["acc_dma"][0])
            q.wait_ge(*tok["acc_dma"][1])
    kb.close()
    return kb


NB1 = OWN
FT = 4096
OVL1 = 75840
RG1 = [[0, 1], [2, 3], [4, 5], [6, 7]]


def alloc_block1(kb, fused_base=None):
    st = {}
    if fused_base is None:
        st["x1"] = kb.sbuf("x1", [128, C8 * NB1], F32)
        st["xob"] = kb.sbuf("xob", [128, C8 * NB1], BF16)
        st["xpb"] = kb.sbuf("xpb", [128, C8 * NB1], BF16)
        st["cstat"] = kb.sbuf("cstat", [128, 128], F32)
        st["fcol"] = kb.sbuf("fcol", [128, 32], F32)
        st["qmcol"] = kb.sbuf("qmcol", [128, 16], F32)
        kb.gcl = kb.sbuf("gclS", [128, C8], F32)
        kb.gch = kb.sbuf("gchS", [128, C8], F32)
        kb.rtq = kb.sbuf("rtqS", [128, C8], F32)
        kb.rtkv = kb.sbuf("rtkvS", [128, C8], F32)
        kb.idf = kb.sbuf("idfS", [128, 128], F32)
        kb.idb = kb.sbuf("idbS", [128, 128], BF16)
        kb.cmask = kb.sbuf("cmaskS", [128, 1], F32)
        kb.ocb = kb.sbuf("ocbS", [128, 1], BF16)
        alloc_consts(kb)
        base = kb.mkslab("ovl1", OVL1)
    else:
        fb = fused_base
        st["x1"] = kb.sbuf_at("x1", [128, C8 * NB1], F32, fb, 0)
        st["xob"] = kb.sbuf_at("xob", [128, C8 * NB1], BF16, fb, 65536)
        st["xpb"] = kb.sbuf_at("xpb", [128, C8 * NB1], BF16, fb, 98304)
        base = fb + 131072
        so = [131072 + OVL1]

        def sm(name, shape, dtype):
            nb = int(np.prod(shape[1:])) * (2 if dtype == BF16 else 4)
            t = kb.sbuf_at(name, shape, dtype, fb, so[0])
            so[0] += (nb + 31) // 32 * 32
            return t
        st["cstat"] = sm("cstat", [128, 128], F32)
        st["fcol"] = sm("fcol", [128, 32], F32)
        st["qmcol"] = sm("qmcol", [128, 16], F32)
        kb.gcl = sm("gclS", [128, C8], F32)
        kb.gch = sm("gchS", [128, C8], F32)
        kb.rtq = sm("rtqS", [128, C8], F32)
        kb.rtkv = sm("rtkvS", [128, C8], F32)
        kb.idb = sm("idbS", [128, 128], BF16)
        kb.ocb = sm("ocbS", [128, 1], BF16)
        st["b1_consts"] = {
            "idf": sm("idfB1", [128, 128], F32),
            "cmask": sm("cmaskB1", [128, 1], F32),
            "oc": sm("ocB1", [128, 1], F32),
            "orow": sm("orowB1", [1, 128], F32),
            "c_eps": sm("cepsB1", [128, 1], F32),
            "c_zero": sm("czeroB1", [128, 1], F32),
        }
        assert fb + so[0] <= 229344, so[0]
    kb.ovl_base = base

    def at(name, shape, dtype, off):
        return kb.sbuf_at(name, shape, dtype, base, off)

    st["xnr"] = at("xnr", [128, C8 * TB], F32, 0)
    st["m1row"] = at("m1row", [1, 2 * FT], F32, 32768)
    st["ln_scr"] = {
        "pA": None, "pB": None, "bc0": None, "bc1": None,
        "sq": at("sq1", [128, 2 * TB], F32, 65536),
        "aff": at("aff1", [128, TB], F32, 69632),
        "rows": at("rows1", [1, 6 * TB], F32, 16384),
    }
    st["gtmp"] = at("gtmp", [128, C8 * NB1], BF16, 0)
    st["diff32"] = at("diff32", [128, NB1], F32, 65536)
    st["srowA"] = at("srowA", [1, NB1], F32, 65536)
    st["srowB"] = at("srowB", [1, NB1], F32, 16384)
    st["c1row"] = at("c1row", [1, FT], F32, 10016)
    st["acol"] = at("acol", [128, 2], F32, 26400)
    st["rit"] = at("rit", [128, 2 * 128], F32, 0)
    st["cells"] = at("cells", [1, 64], F32, 1024)
    st["ldq"] = at("ldq", [16, 128], F32, 1280)
    st["raccum"] = at("raccum", [128, 2], F32, 1792)
    st["kvml"] = at("kvml", [1, FT], BF16, 1824)
    st["nrow"] = at("nrow", [1, FT], F32, 0)
    st["ldn"] = at("ldn", [32, 128], F32, 16384)
    st["nsq"] = at("nsq", [128, 2 * 128], BF16, 26432)
    st["qt1"] = at("qt1", [128, C8 * NB1], BF16, 0)
    st["wvs"] = at("wvs", [128, C8 * 1024], BF16, 32768)
    st["wkb"] = at("wkb", [128, C8 * 1024], BF16, 49152)
    st["ktm"] = at("ktm", [128, 2 * 1024], BF16, 65536)
    st["wqst"] = at("wqst", [128, 2 * 1024], BF16, 65536)
    st["vtm"] = at("vtm", [128, 2 * 512], BF16, 69632)
    st["pst"] = at("pst", [128, NB1], BF16, 71680)
    st["wvstage"] = at("wvstage", [128, 2 * 512], F32, 71680)
    st["rc1"] = at("rc1", [128, 16], F32, 75776)
    st["tt1"] = at("tt1", [128, 256], F32, 53248)
    st["qlh"] = [at(f"qlh{i}", [66, NB1], BF16, i * 4096) for i in range(2)]
    st["klh"] = [at(f"klh{i}", [66, FT], BF16, 8192 * (i + 1))
                 for i in range(2)]
    st["vlh"] = at("vlh", [128, 32 * 130], BF16, 24576)
    st["olh"] = at("olh", [128, NB1], BF16, 32928)
    st["wfst"] = at("wfst", [128, C8 * 128], F32, 37024)
    st["wst"] = at("wst", [128, C8 * 128], BF16, 41120)
    st["won"] = at("won", [128, 1024], BF16, 43168)
    st["pstg"] = at("pstg", [128, 2 * 256], BF16, 45216)
    st["nm1"] = at("nm1", [128, 128], BF16, 46240)
    st["nm1t"] = at("nm1t", [128, 128], BF16, 46496)
    st["osb1"] = at("osb1", [128, 64], F32, 46752)
    st["Pq"] = at("Pq", [128, 32 * 256], BF16, 53248)
    st["prow"] = [at(f"prow{i}", [128, 512], BF16, i * 1024)
                  for i in range(2)]
    st["vres"] = kb.sbuf_at("vres", [128, 32 * 1024], BF16,
                            kb.nc.lookup_mloc(st["xob"]).addr, 0)
    if fused_base is None:
        kb.PS = [kb.psum(f"PSB{i}", [128, 512], F32) for i in range(8)]
    st["ln_scr"]["pA"] = kb.PS[4]
    st["ln_scr"]["pB"] = kb.PS[5]
    st["ln_scr"]["bc0"] = kb.PS[6]
    st["ln_scr"]["bc1"] = kb.PS[7]
    if fused_base is None:
        kb.ln_scr = st["ln_scr"]
    return st


def emit_barrier(kb, blk, toks):
    toks = [t for t in toks if t is not None]

    @blk.sync
    def _(q):
        for t in toks:
            q.wait_ge(t[0], t[1])

    @blk.vector
    def _(v):
        kb.V.do(v, lambda e: e.nop(), waits=toks)

    @blk.scalar
    def _(s):
        kb.S.do(s, lambda e: e.nop(), waits=toks)

    @blk.tensor
    def _(p):
        kb.P.do(p, lambda e: e.nop(), waits=toks, mark=True)

    @blk.gpsimd
    def _(g):
        kb.G.do(g, lambda e: e.nop(), waits=toks)
    return {"v": kb.V.prev, "s": kb.S.prev, "p": kb.P.prev, "g": kb.G.prev}


def bar_toks(bar):
    return [bar["v"], bar["s"], bar["p"], bar["g"]]


def emit_b1_prelude(kb, blk, ins):
    ctok = emit_consts(kb, blk)

    @blk.gpsimd
    def _(g):
        kb.G.do(g, lambda e: e.memset(kb.ocb[:], 1.0))
    mtok = kb.G.prev

    @blk.sync
    def _(q):
        for sb, name in [(kb.idf, "IDF"), (kb.idb, "IDB"),
                         (kb.cmask, "coremask"), (kb.gcl, "gl_col"),
                         (kb.gch, "gh_col"), (kb.rtq, "rt_q"),
                         (kb.rtkv, "rt_kv")]:
            kb.dma(q, sb[:], ins[name], slot=0)
    ldtok = (kb.dsem[0], kb.dcnt[0])
    return emit_barrier(kb, blk, [ctok, mtok, ldtok])


def emit_b1_ln_s(kb, blk, st, x1T, tok):
    V, S, P = kb.V, kb.S, kb.P
    PS = kb.PS
    ldtok = emit_load_x(kb, blk, st["x1"], x1T, nt=NB1)

    @blk.tensor
    def _(p):
        p.wait_ge(*ldtok)

    @blk.vector
    def _(v):
        v.wait_ge(*ldtok)

    @blk.scalar
    def _(s):
        s.wait_ge(*ldtok)
    rr = [None]
    srow = [None]
    for bi, (t0, t1) in enumerate(tblocks(0, NB1)):
        W = t1 - t0
        lt = emit_ln(kb, blk, st["x1"], st["xnr"], NB1, [(t0, t1)],
                     kb.ln_scr, dst_bf16=st["xob"], nt_dst=TB, dst_off=t0,
                     extra_wait=rr[0])

        @blk.tensor
        def _(p, bi=bi, W=W, lt=lt):
            for r in range(2):
                for j in range(C8):
                    P.do(p, lambda e, r=r, j=j: e.matmul(
                        PS[r][0:1, 0:W],
                        (kb.rtq if r == 0 else kb.rtkv)[:, j:j + 1],
                        st["xnr"][:, j * TB:j * TB + W],
                        start=(j == 0), stop=(j == C8 - 1)),
                        waits=[lt, srow[0]] if (r == 0 and j == 0) else [],
                        mark=(r == 1 and j == C8 - 1))
            rr[0] = P.prev
        mmt = P.prev

        @blk.vector
        def _(v, t0=t0, W=W, mmt=mmt):
            for r in range(2):
                V.do(v, lambda e, r=r: e.tensor_copy(
                    st["m1row"][0:1, r * FT + t0:r * FT + t0 + W],
                    PS[r][0:1, 0:W]), waits=[mmt] if r == 0 else [])
            srow[0] = V.prev

        @blk.sync
        def _(q, t0=t0, t1=t1, lt=lt):
            for j in range(C8):
                kb.dma(q, kb.cin1[j * 128:(j + 1) * 128, t0:t1],
                       st["xob"][:, j * NB1 + t0:j * NB1 + t1], slot=6,
                       waits=[lt] if j == 0 else [])

    @blk.sync
    def _(q):
        kb.dma(q, kb.cins[0:1, :], st["m1row"][0:1, 0:NB1], slot=6,
               waits=[kb.V.prev])
        kb.dma(q, kb.cins[1:2, :], st["m1row"][0:1, FT:FT + NB1], slot=6)
    tok["gin"] = (kb.dsem[6], kb.dcnt[6])
    return tok


def emit_b1_gather(kb, blk, st, tok):
    V = kb.V

    @blk.gpsimd
    def _(g):
        g.wait_ge(*tok["gin"])
        kb.G.do(g, lambda e: e.collective_compute(
            "AllGather", mybir.AluOpType.bypass, replica_groups=RG1,
            ins=[kb.cin1.opt()], outs=[kb.cout1.opt()]))
        kb.G.do(g, lambda e: e.collective_compute(
            "AllGather", mybir.AluOpType.bypass, replica_groups=RG1,
            ins=[kb.cins.opt()], outs=[kb.couts.opt()]))
    ctok = kb.G.prev

    @blk.sync
    def _(q):
        for j in range(C8):
            kb.dma(q, st["xpb"][:, j * NB1:(j + 1) * NB1],
                   kb.cout1[D + j * 128:D + (j + 1) * 128, :], slot=6,
                   waits=[ctok] if j == 0 else [])
            kb.dma(q, st["gtmp"][:, j * NB1:(j + 1) * NB1],
                   kb.cout1[j * 128:(j + 1) * 128, :], slot=6)
    xg = (kb.dsem[6], kb.dcnt[6])
    @blk.vector
    def _(v):
        for j in range(C8):
            V.do(v, lambda e, j=j: e.tensor_sub(
                st["diff32"][:, 0:NB1],
                st["gtmp"][:, j * NB1:(j + 1) * NB1],
                st["xpb"][:, j * NB1:(j + 1) * NB1]),
                waits=[xg] if j == 0 else [])
            V.do(v, lambda e, j=j: e.scalar_tensor_tensor(
                st["xpb"][:, j * NB1:(j + 1) * NB1],
                st["diff32"][:, 0:NB1], kb.cmask[:, 0:1],
                st["xpb"][:, j * NB1:(j + 1) * NB1],
                op0=ALU.mult, op1=ALU.add))
    xv = V.prev

    bv = [xv]
    for r in range(2):
        @blk.sync
        def _(q, r=r, bv=bv):
            kb.dma(q, st["srowA"][0:1, :], kb.couts[r:r + 1, :], slot=6,
                   waits=[bv[0]])
            kb.dma(q, st["srowB"][0:1, :], kb.couts[2 + r:3 + r, :], slot=6)
        sg = (kb.dsem[6], kb.dcnt[6])

        @blk.vector
        def _(v, r=r, sg=sg, bv=bv):
            V.do(v, lambda e: e.tensor_sub(
                st["srowA"][0:1, :], st["srowA"][0:1, :],
                st["srowB"][0:1, :]), waits=[sg])
            V.do(v, lambda e: e.scalar_tensor_tensor(
                st["m1row"][0:1, r * FT + NB1:r * FT + FT],
                st["srowA"][0:1, :], kb.cmask[0:1, 0:1],
                st["srowB"][0:1, :], op0=ALU.mult, op1=ALU.add))
            bv[0] = kb.V.prev
    tok["xnv"] = kb.V.prev
    return tok


def emit_b1_routing(kb, blk, st, tok):
    V, S, P = kb.V, kb.S, kb.P
    PS = kb.PS

    @blk.sync
    def _(q):
        kb.dma(q, kb.rscr[0:1, :], st["m1row"][0:1, :], slot=11,
               waits=[tok["xnv"]])
    t0 = (kb.dsem[11], kb.dcnt[11])

    @blk.sync
    def _(q):
        for r in range(2):
            kb.dma(q, st["rit"][:, r * 128:r * 128 + 32],
                   kb.rscr[0:1, r * FT:(r + 1) * FT].rearrange(
                       "a (p c) -> (a p) c", p=128),
                   slot=11, waits=[t0] if r == 0 else [])
    t1 = (kb.dsem[11], kb.dcnt[11])
    logk = [float(np.log(KQ)), float(np.log(KKV))]
    acell = [st["cells"][0:1, r:r + 1] for r in range(2)]
    lncell = [st["cells"][0:1, 2 + r:3 + r] for r in range(2)]
    apcell = [st["cells"][0:1, 4 + r:5 + r] for r in range(2)]

    @blk.gpsimd
    def _(g):
        kb.G.do(g, lambda e: e.memset(st["cells"][:], 0.0), waits=[t1])
    cmt = kb.G.prev

    @blk.vector
    def _(v):
        for r in range(2):
            V.do(v, lambda e, r=r: e.tensor_copy(
                st["rit"][:, r * 128 + 64:r * 128 + 96],
                st["rit"][:, r * 128:r * 128 + 32]),
                waits=[t1, cmt] if r == 0 else [])
    inittok = V.prev
    abc = [None, None]
    for it in range(N_ITERS):
        for r in range(2):
            @blk.vector
            def _(v, r=r, it=it):
                V.do(v, lambda e: e.tensor_sub(
                    st["rit"][:, r * 128 + 32:r * 128 + 64],
                    st["rit"][:, r * 128:r * 128 + 32],
                    st["rit"][:, r * 128 + 64:r * 128 + 96]),
                    waits=[inittok] if it == 0 and r == 0 else [])
            ttok = V.prev

            @blk.scalar
            def _(s, r=r, it=it, ttok=ttok):
                bias = kb.c_zero[:, 0:1] if it == 0 \
                    else st["acol"][:, r:r + 1]
                S.do(s, lambda e: e.activation(
                    st["rit"][:, r * 128 + 96:r * 128 + 128],
                    st["rit"][:, r * 128 + 32:r * 128 + 64],
                    AF.Exp, bias=bias,
                    accum_out=st["raccum"][:, r:r + 1]), waits=[ttok])
            utok = S.prev

            @blk.tensor
            def _(p, r=r, utok=utok):
                P.do(p, lambda e: e.matmul(
                    PS[6][0:1, r:r + 1], kb.oc[:, 0:1],
                    st["raccum"][:, r:r + 1], start=True, stop=True),
                    waits=[utok], mark=True)
            ttot = P.prev

            @blk.scalar
            def _(s, r=r, ttot=ttot):
                S.do(s, lambda e: e.activation(
                    lncell[r], PS[6][0:1, r:r + 1], AF.Ln,
                    bias=kb.c_zero[0:1, 0:1]), waits=[ttot])
            lnt = S.prev

            @blk.vector
            def _(v, r=r, lnt=lnt):
                V.do(v, lambda e: e.scalar_tensor_tensor(
                    acell[r], lncell[r], -1.0, apcell[r],
                    op0=ALU.mult, op1=ALU.add), waits=[lnt])
                V.do(v, lambda e: e.tensor_scalar(
                    acell[r], acell[r], float(logk[r]), None, op0=ALU.add))
                V.do(v, lambda e: e.tensor_copy(apcell[r], acell[r]))
            atok = V.prev

            @blk.tensor
            def _(p, r=r, atok=atok):
                P.do(p, lambda e: e.matmul(
                    PS[4 + r][:, 0:1], kb.orow[:, :], acell[r],
                    start=True, stop=True),
                    waits=[atok, abc[r]], mark=True)
            btok = P.prev

            @blk.vector
            def _(v, r=r, btok=btok):
                V.do(v, lambda e: e.tensor_copy(
                    st["acol"][:, r:r + 1], PS[4 + r][:, 0:1]),
                    waits=[btok])
            acopy = V.prev

            @blk.scalar
            def _(s, r=r, acopy=acopy):
                S.do(s, lambda e: e.activation(
                    st["rit"][:, r * 128 + 64:r * 128 + 96],
                    st["rit"][:, r * 128:r * 128 + 32],
                    AF.Relu, bias=st["acol"][:, r:r + 1]), waits=[acopy])
                abc[r] = kb.S.prev
    rtok = kb.S.prev
    e0 = st["cells"][0:1, 8:9]
    e1 = st["cells"][0:1, 9:10]
    sp = st["cells"][0:1, 10:11]
    so = st["cells"][0:1, 11:12]
    for r, nsel in ((0, NQ), (1, NKV)):
        row = st["m1row"][0:1, r * FT:(r + 1) * FT]

        @blk.vector
        def _(v, r=r, nsel=nsel, row=row):
            V.do(v, lambda e: e.tensor_scalar(
                row, row, apcell[r], 0.0, op0=ALU.add, op1=ALU.is_ge),
                waits=[rtok] if r == 0 else [])
            V.do(v, lambda e: e.tensor_tensor_scan(
                st["c1row"][0:1, :], row, row, 0.0,
                op0=ALU.add, op1=ALU.bypass))
            V.do(v, lambda e: e.tensor_copy(e0, st["c1row"][0:1, NB1 - 1:NB1]))
            V.do(v, lambda e: e.tensor_copy(e1, st["c1row"][0:1, FT - 1:FT]))
            V.do(v, lambda e: e.tensor_sub(sp, e1, e0))
            V.do(v, lambda e: e.tensor_mul(sp, sp, kb.cmask[0:1, 0:1]))
            V.do(v, lambda e: e.tensor_mul(so, e0, kb.cmask[0:1, 0:1]))
            V.do(v, lambda e: e.tensor_scalar(
                st["c1row"][0:1, 0:NB1], st["c1row"][0:1, 0:NB1], sp,
                float(nsel) + 0.5, op0=ALU.add, op1=ALU.is_le))
            V.do(v, lambda e: e.tensor_scalar(
                st["c1row"][0:1, NB1:FT], st["c1row"][0:1, NB1:FT], so,
                float(nsel) + 0.5, op0=ALU.subtract, op1=ALU.is_le))
            V.do(v, lambda e: e.tensor_mul(row, row, st["c1row"][0:1, :]))

    @blk.vector
    def _(v):
        V.do(v, lambda e: e.tensor_scalar(
            st["kvml"][0:1, :], st["m1row"][0:1, FT:2 * FT], -1.0, 1.0e30,
            op0=ALU.add, op1=ALU.mult))
    mtok = kb.V.prev

    @blk.sync
    def _(q):
        kb.dma(q, kb.kvmd[0:1, :], st["kvml"][0:1, :], slot=11,
               waits=[mtok])
        kb.dma(q, kb.rqm[0:1, :], st["m1row"][0:1, 0:NB1], slot=11)
    d1 = (kb.dsem[11], kb.dcnt[11])

    @blk.sync
    def _(q):
        kb.dma(q, st["ldq"][:], kb.rqm[0:1, :].rearrange(
            "a (p c) -> (a p) c", p=16), slot=11, waits=[d1])
    d2 = (kb.dsem[11], kb.dcnt[11])

    @blk.tensor
    def _(p):
        P.do(p, lambda e: e.matmul(PS[3][0:128, 0:16], st["ldq"][:],
                                   kb.idf[0:16, 0:16], start=True, stop=True),
             waits=[d2], mark=True)
    qmt = P.prev

    @blk.vector
    def _(v):
        V.do(v, lambda e: e.tensor_copy(st["qmcol"][:], PS[3][0:128, 0:16]),
             waits=[qmt])
    tok["maskv"] = kb.V.prev
    return tok


def emit_b1_norms(kb, blk, st, tok):
    V, S, P = kb.V, kb.S, kb.P
    PS = kb.PS
    warP = [None]
    mmtk = {}
    for mt in range(32):
        srcb = st["xob"] if mt < 16 else st["xpb"]
        c0 = (mt % 16) * 128

        @blk.scalar
        def _(s, mt=mt, srcb=srcb, c0=c0):
            for j in range(C8):
                war = (mmtk.get((mt, j - 2)) if j >= 2
                       else mmtk.get((mt - 1, 6 + j)))
                S.do(s, lambda e, j=j: e.activation(
                    st["nsq"][:, (j % 2) * 128:(j % 2) * 128 + 128],
                    srcb[:, j * NB1 + c0:j * NB1 + c0 + 128], AF.Square),
                    waits=[war] +
                    ([tok["maskv"]] if mt == 0 and j == 0 else []))
                sqt = kb.S.prev

                @blk.tensor
                def _(p, j=j, mt=mt, sqt=sqt):
                    kb.P.do(p, lambda e: e.matmul(
                        PS[2][0:1, 0:128], kb.ocb[:, 0:1],
                        st["nsq"][:, (j % 2) * 128:(j % 2) * 128 + 128],
                        start=(j == 0), stop=(j == C8 - 1)),
                        waits=[sqt, warP[0]] if j == 0 else [sqt],
                        mark=True)
                    mmtk[(mt, j)] = kb.P.prev

        @blk.vector
        def _(v, mt=mt):
            V.do(v, lambda e: e.tensor_copy(
                st["nrow"][0:1, mt * 128:mt * 128 + 128], PS[2][0:1, 0:128]),
                waits=[mmtk[(mt, C8 - 1)]])
            warP[0] = kb.V.prev
    rowv = kb.V.prev

    @blk.scalar
    def _(s):
        S.do(s, lambda e: e.activation(st["nrow"][0:1, :], st["nrow"][0:1, :],
                                       AF.Sqrt, bias=kb.c_zero[0:1, 0:1]),
             waits=[rowv])

    @blk.vector
    def _(v):
        V.do(v, lambda e: e.tensor_scalar(
            st["nrow"][0:1, :], st["nrow"][0:1, :], 1e-12, None,
            op0=ALU.max), waits=[kb.S.prev])
        V.do(v, lambda e: e.reciprocal(st["nrow"][0:1, :], st["nrow"][0:1, :]))
        V.do(v, lambda e: e.tensor_scalar(
            st["nrow"][0:1, :], st["nrow"][0:1, :], 32.0, None, op0=ALU.mult))
    ntok = kb.V.prev

    @blk.sync
    def _(q):
        kb.dma(q, kb.rfn[0:1, :], st["nrow"][0:1, :], slot=11, waits=[ntok])
    d1 = (kb.dsem[11], kb.dcnt[11])

    @blk.sync
    def _(q):
        kb.dma(q, st["ldn"][:], kb.rfn[0:1, :].rearrange(
            "a (p c) -> (a p) c", p=32), slot=11, waits=[d1])
    d2 = (kb.dsem[11], kb.dcnt[11])

    @blk.tensor
    def _(p):
        P.do(p, lambda e: e.matmul(PS[3][0:128, 0:32], st["ldn"][:],
                                   kb.idf[0:32, 0:32], start=True, stop=True),
             waits=[d2, tok["maskv"]], mark=True)
    fmt = P.prev

    @blk.vector
    def _(v):
        V.do(v, lambda e: e.tensor_copy(st["fcol"][:], PS[3][0:128, 0:32]),
             waits=[fmt])
    tok["fcolv"] = kb.V.prev
    return tok


def emit_b1_l1a_S(kb, blk, st, waps, tok, bar):
    V, S, P = kb.V, kb.S, kb.P
    PS = kb.PS
    wv_done = [None]
    for j in range(C8):
        dtk = [None]

        @blk.sync
        def _(q, j=j):
            for hf in range(2):
                dtk[0] = kb.dma(q, st["wvstage"][:, hf * 512:(hf + 1) * 512],
                                waps["wv1T"][j * 128:(j + 1) * 128,
                                             hf * 512:(hf + 1) * 512], slot=7,
                                waits=([wv_done[0]] + bar_toks(bar))
                                if hf == 0 else [])

        @blk.vector
        def _(v, j=j):
            V.do(v, lambda e: e.reduce_max(
                st["rc1"][:, 0:1], st["wvstage"][:, 0:512],
                axis=AX.X, negate=True), waits=[dtk[0]])
            V.do(v, lambda e: e.reduce_max(
                st["rc1"][:, 1:2], st["wvstage"][:, 512:1024],
                axis=AX.X, negate=True))
            V.do(v, lambda e: e.tensor_tensor(
                st["rc1"][:, 0:1], st["rc1"][:, 0:1], st["rc1"][:, 1:2],
                op=ALU.min))
        nmt = V.prev

        @blk.scalar
        def _(s, j=j, nmt=nmt):
            for hf in range(2):
                S.do(s, lambda e, hf=hf: e.activation(
                    st["wvstage"][:, hf * 512:(hf + 1) * 512],
                    st["wvstage"][:, hf * 512:(hf + 1) * 512], AF.Exp,
                    bias=st["rc1"][:, 0:1],
                    accum_out=st["rc1"][:, 2 + hf:3 + hf]),
                    waits=[nmt] if hf == 0 else [])
        expt = S.prev

        @blk.vector
        def _(v, j=j, expt=expt):
            V.do(v, lambda e: e.tensor_add(
                st["rc1"][:, 2:3], st["rc1"][:, 2:3], st["rc1"][:, 3:4]),
                waits=[expt])
            V.do(v, lambda e: e.reciprocal(st["rc1"][:, 2:3],
                                           st["rc1"][:, 2:3]))
            for hf in range(2):
                V.do(v, lambda e, hf=hf: e.tensor_scalar(
                    st["wvs"][:, j * 1024 + hf * 512:j * 1024 + hf * 512 + 512],
                    st["wvstage"][:, hf * 512:(hf + 1) * 512],
                    st["rc1"][:, 2:3], 1.0 / 1024.0,
                    op0=ALU.mult, op1=ALU.mult))
            wv_done[0] = V.prev
    wvs_done = kb.V.prev
    @blk.gpsimd
    def _(g):
        kb.dma(g, st["wkb"][:].rearrange("p (c m) -> p c m", c=C8),
               waps["wk1"].rearrange("(c p) m -> p c m", p=128), slot=2,
               waits=bar_toks(bar))
    wktok = (kb.dsem[2], kb.dcnt[2])
    wqv = waps["wq1"].rearrange("(c p) m -> p c m", p=128)
    evq = [None, None]
    for dd in range(C8):
        par = dd % 2
        wslot = 3 if par == 0 else 14

        @blk.gpsimd
        def _(g, dd=dd, par=par, wslot=wslot):
            kb.dma(g, st["wqst"][:, par * 1024:par * 1024 + 1024].rearrange(
                "p (c m) -> p c m", c=C8),
                wqv[:, :, dd * 128:(dd + 1) * 128], slot=wslot,
                waits=[evq[par]] + (bar_toks(bar) if dd == 0 else []))
        wqt = (kb.dsem[wslot], kb.dcnt[wslot])
        for tt in range(16):
            @blk.tensor
            def _(p, dd=dd, tt=tt, par=par, wqt=wqt):
                for c in range(C8):
                    P.do(p, lambda e, c=c: e.matmul(
                        PS[par][:, 0:128],
                        st["wqst"][:, par * 1024 + c * 128:
                                   par * 1024 + c * 128 + 128],
                        st["xob"][:, c * NB1 + tt * 128:c * NB1 + tt * 128 + 128],
                        start=(c == 0), stop=(c == C8 - 1)),
                        waits=[wqt, evq[par]] if c == 0 else [],
                        mark=(c == C8 - 1))
            mmt = P.prev

            @blk.scalar
            def _(s, dd=dd, tt=tt, par=par, mmt=mmt):
                S.do(s, lambda e: e.activation(
                    st["qt1"][:, dd * NB1 + tt * 128:dd * NB1 + tt * 128 + 128],
                    PS[par][:, 0:128], AF.Copy, bias=0.0,
                    scale=1.0 / 64.0), waits=[mmt])
                evq[par] = kb.S.prev
    qt_done = kb.S.prev
    qt_pe = kb.P.prev
    wq_dma_done = (kb.dsem[3], kb.dcnt[3])
    stpe = [None, None]
    stw = [None] * 4
    spb = [None]
    spv = [None]
    pvsum = [None]
    vcp = [None]
    for mt in range(32):
        srcb = st["xob"] if mt < 16 else st["xpb"]
        c0 = (mt % 16) * 128
        slot = mt % 2
        evk = [None]
        for dd in range(C8):
            @blk.tensor
            def _(p, dd=dd, c0=c0, srcb=srcb):
                for c in range(C8):
                    P.do(p, lambda e, c=c: e.matmul(
                        PS[2][:, 0:128],
                        st["wkb"][:, c * 1024 + dd * 128:c * 1024 + dd * 128 + 128],
                        srcb[:, c * NB1 + c0:c * NB1 + c0 + 128],
                        start=(c == 0), stop=(c == C8 - 1)),
                        waits=[wktok, evk[0]] if c == 0 else [],
                        mark=(c == C8 - 1))
            mmt = P.prev

            @blk.scalar
            def _(s, dd=dd, slot=slot, mt=mt, mmt=mmt):
                S.do(s, lambda e: e.copy(
                    st["ktm"][:, slot * 1024 + dd * 128:slot * 1024 + dd * 128 + 128],
                    PS[2][:, 0:128]),
                    waits=[mmt, stpe[slot],
                           wq_dma_done if mt < 2 else None]
                    if dd == 0 else [mmt])
                evk[0] = kb.S.prev
        ktok = kb.S.prev
        for vv in range(2):
            @blk.tensor
            def _(p, vv=vv, c0=c0, srcb=srcb):
                for c in range(C8):
                    P.do(p, lambda e, c=c: e.matmul(
                        PS[3][:, 0:512],
                        srcb[:, c * NB1 + c0:c * NB1 + c0 + 128],
                        st["wvs"][:, c * 1024 + vv * 512:c * 1024 + vv * 512 + 512],
                        start=(c == 0), stop=(c == C8 - 1)),
                        waits=[wvs_done, vcp[0]] if c == 0 else [],
                        mark=(c == C8 - 1))
            mmt = P.prev

            @blk.scalar
            def _(s, vv=vv, mmt=mmt):
                S.do(s, lambda e: e.copy(
                    st["vtm"][:, vv * 512:vv * 512 + 512],
                    PS[3][:, 0:512]),
                    waits=[mmt, spv[0]] if vv == 0 else [mmt])
                vcp[0] = kb.S.prev
        vtok = kb.S.prev

        @blk.sync
        def _(q, mt=mt, vtok=vtok):
            spv[0] = kb.dma(q, kb.spillV[mt * 128:(mt + 1) * 128, :],
                            st["vtm"][:, 0:1024], slot=9, waits=[vtok])
        nmlast = [None]
        for nb in range(4):
            @blk.tensor
            def _(p, nb=nb, slot=slot, ktok=ktok):
                for dd in range(C8):
                    P.do(p, lambda e, dd=dd: e.matmul(
                        PS[4 + nb][:, 0:512],
                        st["ktm"][:, slot * 1024 + dd * 128:
                                  slot * 1024 + dd * 128 + 128],
                        st["qt1"][:, dd * NB1 + nb * 512:dd * NB1 + nb * 512 + 512],
                        start=(dd == 0), stop=(dd == C8 - 1)),
                        waits=[ktok, qt_done, stw[nb]] if dd == 0 else [],
                        mark=(dd == C8 - 1))
                if nb == 3:
                    stpe[slot] = P.prev
            mmt = P.prev

            @blk.vector
            def _(v, nb=nb, mmt=mmt):
                V.do(v, lambda e: e.reduce_max(
                    st["rc1"][:, nb:nb + 1], PS[4 + nb][:, 0:512],
                    axis=AX.X, negate=True), waits=[mmt])
                nmlast[0] = kb.V.prev

        @blk.vector
        def _(v, mt=mt):
            V.do(v, lambda e: e.tensor_tensor(
                st["rc1"][:, 0:1], st["rc1"][:, 0:1], st["rc1"][:, 1:2],
                op=ALU.min), waits=[nmlast[0]])
            V.do(v, lambda e: e.tensor_tensor(
                st["rc1"][:, 2:3], st["rc1"][:, 2:3], st["rc1"][:, 3:4],
                op=ALU.min))
            V.do(v, lambda e: e.tensor_tensor(
                st["cstat"][:, mt:mt + 1], st["rc1"][:, 0:1],
                st["rc1"][:, 2:3], op=ALU.min))
        nmt = kb.V.prev
        for nb in range(4):
            @blk.scalar
            def _(s, nb=nb, mt=mt, nmt=nmt):
                S.do(s, lambda e: e.activation(
                    st["pst"][:, nb * 512:nb * 512 + 512],
                    PS[4 + nb][:, 0:512], AF.Exp,
                    bias=st["cstat"][:, mt:mt + 1],
                    accum_out=st["rc1"][:, 4 + nb:5 + nb]),
                    waits=[nmt, spb[0], pvsum[0]] if nb == 0 else [])
                stw[nb] = kb.S.prev
        expt = kb.S.prev

        @blk.vector
        def _(v, mt=mt, expt=expt):
            V.do(v, lambda e: e.tensor_add(
                st["rc1"][:, 4:5], st["rc1"][:, 4:5], st["rc1"][:, 5:6]),
                waits=[expt])
            V.do(v, lambda e: e.tensor_add(
                st["rc1"][:, 6:7], st["rc1"][:, 6:7], st["rc1"][:, 7:8]))
            V.do(v, lambda e: e.tensor_add(
                st["cstat"][:, 32 + mt:33 + mt], st["rc1"][:, 4:5],
                st["rc1"][:, 6:7]))
            pvsum[0] = kb.V.prev

        @blk.sync
        def _(q, mt=mt, expt=expt):
            spb[0] = kb.dma(q, kb.spillP[mt * 128:(mt + 1) * 128, :],
                            st["pst"][:, 0:NB1], slot=1, waits=[expt])
    tok["spillP"] = (kb.dsem[1], kb.dcnt[1])
    tok["spillV"] = (kb.dsem[9], kb.dcnt[9])
    return tok


def emit_b1_merge(kb, blk, st, tok, bar):
    V, S = kb.V, kb.S

    @blk.sync
    def _(q):
        kb.dma(q, kb.cing[:], st["cstat"][:, 0:64], slot=6,
               waits=bar_toks(bar))

    @blk.gpsimd
    def _(g):
        g.wait_ge(kb.dsem[6], kb.dcnt[6])
        kb.G.do(g, lambda e: e.collective_compute(
            "AllGather", mybir.AluOpType.bypass, replica_groups=RG1,
            ins=[kb.cing.opt()], outs=[kb.coutg.opt()]))
    ctok = kb.G.prev

    @blk.sync
    def _(q):
        kb.dma(q, st["tt1"][:, 0:64], kb.coutg[0:128, :], slot=6,
               waits=[ctok])
        kb.dma(q, st["tt1"][:, 64:128], kb.coutg[128:256, :], slot=6)
    gtok = (kb.dsem[6], kb.dcnt[6])

    @blk.vector
    def _(v):
        V.do(v, lambda e: e.tensor_sub(st["tt1"][:, 0:64], st["tt1"][:, 0:64],
                                       st["tt1"][:, 64:128]), waits=[gtok])
        V.do(v, lambda e: e.scalar_tensor_tensor(
            st["tt1"][:, 128:192], st["tt1"][:, 0:64], kb.cmask[:, 0:1],
            st["tt1"][:, 64:128], op0=ALU.mult, op1=ALU.add))
        for half in range(2):
            for dst in (192, 224):
                V.do(v, lambda e, half=half, dst=dst: e.tensor_copy(
                    st["tt1"][:, dst + half * 16:dst + half * 16 + 16],
                    st["tt1"][:, dst - 64 + (1 - half) * 16:
                              dst - 64 + (1 - half) * 16 + 16]))
        V.do(v, lambda e: e.tensor_tensor(
            st["cstat"][:, 96:128], st["cstat"][:, 0:32],
            st["tt1"][:, 192:224], op=ALU.min))
        V.do(v, lambda e: e.tensor_sub(
            st["tt1"][:, 0:32], st["cstat"][:, 96:128], st["cstat"][:, 0:32]))
        V.do(v, lambda e: e.tensor_sub(
            st["tt1"][:, 32:64], st["cstat"][:, 96:128],
            st["tt1"][:, 192:224]))
    dtok = V.prev

    @blk.scalar
    def _(s):
        S.do(s, lambda e: e.activation(st["tt1"][:, 0:32], st["tt1"][:, 0:32],
                                       AF.Exp, bias=kb.c_zero[:, 0:1]),
             waits=[dtok])
        S.do(s, lambda e: e.activation(st["tt1"][:, 32:64],
                                       st["tt1"][:, 32:64],
                                       AF.Exp, bias=kb.c_zero[:, 0:1]))
    etok = S.prev

    @blk.vector
    def _(v):
        V.do(v, lambda e: e.tensor_mul(st["tt1"][:, 64:96],
                                       st["cstat"][:, 32:64],
                                       st["tt1"][:, 0:32]), waits=[etok])
        V.do(v, lambda e: e.tensor_mul(st["tt1"][:, 96:128],
                                       st["tt1"][:, 224:256],
                                       st["tt1"][:, 32:64]))
        V.do(v, lambda e: e.tensor_add(st["tt1"][:, 64:96],
                                       st["tt1"][:, 64:96],
                                       st["tt1"][:, 96:128]))
        V.do(v, lambda e: e.reciprocal(st["tt1"][:, 64:96],
                                       st["tt1"][:, 64:96]))
        V.do(v, lambda e: e.tensor_mul(st["cstat"][:, 64:96],
                                       st["tt1"][:, 0:32],
                                       st["tt1"][:, 64:96]))
    tok["facv"] = V.prev
    return tok


def emit_b1_lh(kb, blk, st, waps, tok, bar, kind):
    V, S, P = kb.V, kb.S, kb.P
    PS = kb.PS
    heavy = kind == "heavy"
    gcol = kb.gch if heavy else kb.gcl
    wq_ap = waps["wq_h" if heavy else "wq_l"]
    wkv_ap = waps["wkv_hT" if heavy else "wkvl_r"]
    wo_ap = waps["wo_h" if heavy else "wo_l"]
    ktiles = 32 if heavy else 16
    trq = [tok.get("lh_trq")]
    cps = [tok.get("lh_cps")]
    for g_ in range(4):
        firstw = bar_toks(bar) if (g_ == 0 and not heavy) else []
        attn_pe = tok.get("lh_pe")
        bias_prev = tok.get("lh_bias")
        wo_v = tok.get("lh_wov")
        @blk.gpsimd
        def _(gq, attn_pe=attn_pe, firstw=firstw, g_=g_):
            kb.G.do(gq, lambda e: e.memset(st["vlh"][:], 1.0),
                    waits=[attn_pe] + firstw)
            if g_ == 0:
                kb.G.do(gq, lambda e: e.memset(st["nm1"][:], 0.0))
                for hh in range(2):
                    kb.G.do(gq, lambda e, hh=hh: e.memset(
                        st["qlh"][hh][64:65, :], 1.0))
                    if not heavy:
                        kb.G.do(gq, lambda e, hh=hh: e.memset(
                            st["klh"][hh][64:65, :], 0.0))
        mst = kb.G.prev
        if g_ == 0:
            @blk.sync
            def _(q, mst=mst, attn_pe=attn_pe):
                for hh in range(2):
                    kb.dma(q, st["klh"][hh][65:66, :],
                           kb.inp["onesb1"][0:1, :], slot=8,
                           waits=[mst, attn_pe] if hh == 0 else [])
                    if heavy:
                        kb.dma(q, st["klh"][hh][64:65, :], kb.kvmd[0:1, :],
                               slot=8)
            mrow = (kb.dsem[8], kb.dcnt[8])
            tok["lh_mrow"] = mrow
        else:
            mrow = tok["lh_mrow"]
        prod_toks = []
        for ss in range(3):
            isq = ss == 0

            @blk.sync
            def _(q, g_=g_, ss=ss, wo_v=wo_v, firstw=firstw):
                if isq := (ss == 0):
                    src = wq_ap.rearrange("(c p) m -> p c m", p=128)[
                        :, :, g_ * 128:(g_ + 1) * 128]
                else:
                    cb = (2 * g_ + ss - 1) * 128
                    src = wkv_ap.rearrange("(c p) m -> p c m", p=128)[
                        :, :, cb:cb + 128]
                kb.dma(q, st["wfst"][:].rearrange("p (c m) -> p c m", c=C8),
                       src, slot=10,
                       waits=[tok.get("lh_foldpe"), wo_v] + firstw)
            wqt = (kb.dsem[10], kb.dcnt[10])

            @blk.vector
            def _(v, ss=ss, isq=isq, wqt=wqt):
                for c in range(C8):
                    V.do(v, lambda e, c=c: e.tensor_scalar(
                        st["wst"][:, c * 128:c * 128 + 128],
                        st["wfst"][:, c * 128:c * 128 + 128],
                        gcol[:, c:c + 1], 0.125 if isq else 1.0,
                        op0=ALU.mult, op1=ALU.mult),
                        waits=[wqt, tok.get("lh_foldpe")] if c == 0 else [])
            foldv = kb.V.prev
            evp = [None, None]
            ntiles = 16 if isq else ktiles
            for t in range(ntiles):
                par = t % 2
                srcb = st["xob"] if t < 16 else st["xpb"]
                c0 = (t % 16) * 128

                @blk.tensor
                def _(p, t=t, par=par, srcb=srcb, c0=c0, foldv=foldv,
                      mst=mst):
                    for c in range(C8):
                        P.do(p, lambda e, c=c: e.matmul(
                            PS[par][:, 0:128],
                            srcb[:, c * NB1 + c0:c * NB1 + c0 + 128],
                            st["wst"][:, c * 128:c * 128 + 128],
                            start=(c == 0), stop=(c == C8 - 1)),
                            waits=[foldv, evp[par], mst] if c == 0 else [],
                            mark=(c == C8 - 1))
                mmt = P.prev

                @blk.vector
                def _(v, t=t, par=par, mmt=mmt, trq=trq):
                    V.do(v, lambda e: e.tensor_scalar(
                        st["pstg"][:, par * 256:par * 256 + 128],
                        PS[par][:, 0:128], st["fcol"][:, t:t + 1], None,
                        op0=ALU.mult), waits=[mmt, trq[0]])
                    evp[par] = kb.V.prev
                if not isq:
                    @blk.scalar
                    def _(s, t=t, par=par, ss=ss, mst=mst):
                        S.do(s, lambda e: e.copy(
                            st["vlh"][:, t * 130 + (ss - 1) * 65:
                                      t * 130 + (ss - 1) * 65 + 64],
                            st["pstg"][:, par * 256 + 64:par * 256 + 128]),
                            waits=[evp[par], mst])

                @blk.tensor
                def _(p, t=t, par=par, isq=isq, cps=cps, trq=trq):
                    ncols = 128 if isq else 64
                    P.do(p, lambda e: e.matmul(
                        PS[6][0:ncols, 0:128],
                        st["pstg"][:, par * 256:par * 256 + ncols],
                        kb.idb[:], start=True, stop=True),
                        waits=[evp[par], cps[0]], mark=True)
                    trq[0] = kb.P.prev

                @blk.scalar
                def _(s, t=t, isq=isq, ss=ss, trq=trq, cps=cps, mrow=mrow):
                    if isq:
                        for hh in range(2):
                            S.do(s, lambda e, hh=hh: e.copy(
                                st["qlh"][hh][0:64, t * 128:t * 128 + 128],
                                PS[6][hh * 64:hh * 64 + 64, 0:128]),
                                waits=[trq[0]] if hh == 0 else [])
                    else:
                        S.do(s, lambda e: e.copy(
                            st["klh"][ss - 1][0:64, t * 128:t * 128 + 128],
                            PS[6][0:64, 0:128]),
                            waits=[trq[0], mrow] if t == 0 else [trq[0]])
                    cps[0] = kb.S.prev
            prod_toks.append(kb.S.prev)
            tok["lh_foldpe"] = kb.P.prev
        prod_s = kb.S.prev
        @blk.gpsimd
        def _(gq, g_=g_, wo_v=wo_v):
            kb.dma(gq, st["won"][:], wo_ap[g_ * 128:(g_ + 1) * 128, :],
                   slot=14, waits=[wo_v])
        wot = (kb.dsem[14], kb.dcnt[14])
        s2w = [None, None]
        for hh in range(2):
            for qt in range(16):
                jbs = range(8) if heavy else [0]
                for ji in list(jbs):
                    par = ji % 2
                    if heavy:
                        j0, jn = ji * 512, 512
                    else:
                        j0, jn = qt * 128, 128

                    @blk.tensor
                    def _(p, hh=hh, qt=qt, par=par, j0=j0, jn=jn,
                          prod_s=prod_s, mrow=mrow, mst=mst, s2w=s2w):
                        P.do(p, lambda e: e.matmul(
                            PS[4 + par][:, 0:jn],
                            st["qlh"][hh][0:65, qt * 128:qt * 128 + 128],
                            st["klh"][hh][0:65, j0:j0 + jn],
                            start=True, stop=True),
                            waits=[prod_s, mrow, mst, s2w[par]], mark=True)
                    mmt = P.prev

                    @blk.vector
                    def _(v, ji=ji, par=par, jn=jn, mmt=mmt,
                          bias_prev=bias_prev, s2w=s2w):
                        dst = 8 if ji == 0 else 9
                        V.do(v, lambda e: e.reduce_max(
                            st["rc1"][:, dst:dst + 1], PS[4 + par][:, 0:jn],
                            axis=AX.X, negate=True),
                            waits=[mmt, bias_prev])
                        if ji > 0:
                            V.do(v, lambda e: e.tensor_tensor(
                                st["rc1"][:, 8:9], st["rc1"][:, 8:9],
                                st["rc1"][:, 9:10], op=ALU.min))
                        s2w[par] = kb.V.prev

                @blk.vector
                def _(v, hh=hh, qt=qt):
                    V.do(v, lambda e: e.tensor_copy(
                        st["nm1"][:, hh * 16 + qt:hh * 16 + qt + 1],
                        st["rc1"][:, 8:9]))
        nmv = kb.V.prev
        @blk.sync
        def _(q, nmv=nmv, attn_pe=attn_pe):
            t1 = kb.dma(q, st["nm1t"][:], st["nm1"][:], slot=7,
                        waits=[nmv, attn_pe], transpose=True)
            for hh in range(2):
                kb.dma(q, st["qlh"][hh][65:66, 0:NB1],
                       st["nm1t"][hh * 16:hh * 16 + 16, 0:128], slot=7,
                       waits=[t1] if hh == 0 else [])
        bias_tok = (kb.dsem[7], kb.dcnt[7])
        tok["lh_bias"] = bias_tok
        stw2 = [None, None]
        pvv = [None, None]
        trt = [None]
        ocp = [None]
        for hh in range(2):
            qbs = 8 if heavy else 16
            nq = 256 if heavy else 128
            for qb in range(qbs):
                q0 = qb * nq
                kts = list(range(32)) if heavy else [qb]
                expl = None
                etoks = []
                for ki, kt in enumerate(kts):
                    par = ki % 2

                    @blk.tensor
                    def _(p, hh=hh, kt=kt, par=par, q0=q0, nq=nq,
                          bias_tok=bias_tok, stw2=stw2):
                        P.do(p, lambda e: e.matmul(
                            PS[4 + par][:, 0:nq],
                            st["klh"][hh][0:66, kt * 128:kt * 128 + 128],
                            st["qlh"][hh][0:66, q0:q0 + nq],
                            start=True, stop=True),
                            waits=[bias_tok, stw2[par]], mark=True)
                    mmt = P.prev

                    @blk.scalar
                    def _(s, ki=ki, qb=qb, par=par, nq=nq, mmt=mmt,
                          pvv=pvv, stw2=stw2):
                        off = ki * 256 if heavy else (qb % 2) * 128
                        S.do(s, lambda e: e.activation(
                            st["Pq"][:, off:off + nq],
                            PS[4 + par][:, 0:nq], AF.Exp,
                            bias=kb.c_zero[:, 0:1]),
                            waits=[mmt, pvv[0], pvv[1]] if ki == 0
                            else [mmt])
                        stw2[par] = kb.S.prev
                        etoks.append(kb.S.prev)
                    expl = kb.S.prev
                for q2 in range(2 if heavy else 1):
                    pb = (qb * 2 + q2) % 2
                    qt = (q0 + q2 * 128) // 128

                    @blk.tensor
                    def _(p, hh=hh, q2=q2, pb=pb, qb=qb, kts=kts,
                          etoks=etoks, pvv=pvv):
                        for r, kt in enumerate(kts):
                            off = (r * 256 + q2 * 128) if heavy \
                                else (qb % 2) * 128
                            P.do(p, lambda e, kt=kt, off=off, r=r: e.matmul(
                                PS[2 + pb][:, 0:65],
                                st["Pq"][:, off:off + 128],
                                st["vlh"][:, kt * 130 + hh * 65:
                                          kt * 130 + hh * 65 + 65],
                                start=(r == 0), stop=(r == len(kts) - 1)),
                                waits=[etoks[r], pvv[pb]] if r == 0
                                else [etoks[r]],
                                mark=(r == len(kts) - 1))
                    pvt = P.prev

                    @blk.vector
                    def _(v, qt=qt, pb=pb, pvt=pvt, trt=trt, pvv=pvv):
                        V.do(v, lambda e: e.reciprocal(
                            st["rc1"][:, 12:13], PS[2 + pb][:, 64:65]),
                            waits=[pvt])
                        if heavy:
                            V.do(v, lambda e: e.tensor_mul(
                                st["rc1"][:, 12:13], st["rc1"][:, 12:13],
                                st["qmcol"][:, qt:qt + 1]))
                        V.do(v, lambda e: e.tensor_scalar(
                            st["osb1"][:, 0:64], PS[2 + pb][:, 0:64],
                            st["rc1"][:, 12:13], None, op0=ALU.mult),
                            waits=[trt[0]])
                        pvv[pb] = kb.V.prev

                    @blk.tensor
                    def _(p, pb=pb, pvv=pvv, ocp=ocp, trt=trt):
                        P.do(p, lambda e: e.transpose(
                            PS[6][0:64, 0:128], st["osb1"][:, 0:64],
                            kb.idf[:]), waits=[pvv[pb], ocp[0]], mark=True)
                        trt[0] = kb.P.prev

                    @blk.scalar
                    def _(s, hh=hh, qt=qt, trt=trt, ocp=ocp):
                        S.do(s, lambda e: e.copy(
                            st["olh"][hh * 64:hh * 64 + 64,
                                      qt * 128:qt * 128 + 128],
                            PS[6][0:64, 0:128]), waits=[trt[0]])
                        ocp[0] = kb.S.prev
                        cps[0] = kb.S.prev
        tok["lh_pe"] = kb.P.prev
        olh_done = kb.S.prev
        wov = [None, None]
        for dc in range(C8):
            for qb in range(4):
                par = (dc * 4 + qb) % 2

                @blk.tensor
                def _(p, dc=dc, qb=qb, par=par, wot=wot,
                      olh_done=olh_done, wov=wov):
                    P.do(p, lambda e: e.matmul(
                        PS[6 + par][:, 0:512],
                        st["won"][:, dc * 128:dc * 128 + 128],
                        st["olh"][:, qb * 512:qb * 512 + 512],
                        start=True, stop=True),
                        waits=[wot, olh_done, wov[par]]
                        if (dc == 0 and qb == 0) else [wov[par]],
                        mark=True)
                mmt = P.prev

                @blk.vector
                def _(v, dc=dc, qb=qb, par=par, mmt=mmt, wov=wov):
                    V.do(v, lambda e: e.tensor_add(
                        st["x1"][:, dc * NB1 + qb * 512:dc * NB1 + qb * 512 + 512],
                        st["x1"][:, dc * NB1 + qb * 512:dc * NB1 + qb * 512 + 512],
                        PS[6 + par][:, 0:512]), waits=[mmt])
                    wov[par] = kb.V.prev
        tok["lh_wov"] = kb.V.prev
        tok["lh_pe"] = kb.P.prev
    tok["lh_trq"] = trq[0]
    tok["lh_cps"] = cps[0]
    return tok


def emit_b1_pv(kb, blk, st, tok, bar):
    V, S, P = kb.V, kb.S, kb.P
    PS = kb.PS

    @blk.sync
    def _(q):
        for mt in range(32):
            kb.dma(q, st["vres"][:, mt * 1024:(mt + 1) * 1024],
                   kb.spillV[mt * 128:(mt + 1) * 128, :], slot=0,
                   waits=(bar_toks(bar) + [tok["spillV"]])
                   if mt == 0 else [])
    vg = (kb.dsem[0], kb.dcnt[0])
    eyev = [None, None]
    for mt in range(16):
        for dm in range(C8):
            par = (mt * C8 + dm) % 2

            @blk.tensor
            def _(p, mt=mt, dm=dm, par=par, vg=vg, eyev=eyev):
                P.do(p, lambda e: e.matmul(
                    PS[par][0:128, 0:128],
                    st["vres"][:, mt * 1024 + dm * 128:mt * 1024 + dm * 128 + 128],
                    kb.idb[:], start=True, stop=True),
                    waits=[vg, eyev[par]] if (mt == 0 and dm < 2)
                    else [eyev[par]], mark=True)
            mmt = P.prev

            @blk.vector
            def _(v, mt=mt, dm=dm, par=par, mmt=mmt, eyev=eyev):
                V.do(v, lambda e: e.tensor_sub(
                    st["x1"][:, dm * NB1 + mt * 128:dm * NB1 + mt * 128 + 128],
                    st["x1"][:, dm * NB1 + mt * 128:dm * NB1 + mt * 128 + 128],
                    PS[par][0:128, 0:128]), waits=[mmt])
                eyev[par] = kb.V.prev
    eyepe = kb.P.prev

    @blk.vector
    def _(v):
        for mt in range(32):
            V.do(v, lambda e, mt=mt: e.tensor_scalar(
                st["vres"][:, mt * 1024:(mt + 1) * 1024],
                st["vres"][:, mt * 1024:(mt + 1) * 1024],
                st["cstat"][:, 64 + mt:65 + mt], None, op0=ALU.mult),
                waits=[eyepe, vg] if mt == 0 else [])
    fscale = kb.V.prev
    pmark = {}
    addv = [None]
    for nb in range(4):
        for mt in range(32):
            slot = mt % 2

            @blk.sync
            def _(q, nb=nb, mt=mt, slot=slot, addv=addv):
                w = [pmark.get((nb, mt - 2)) if (nb, mt - 2) in pmark
                     else pmark.get((nb - 1, 30 + mt))]
                if nb == 0 and mt < 2:
                    w = [tok["spillP"], fscale]
                kb.dma(q, st["prow"][slot][:],
                       kb.spillP[mt * 128:(mt + 1) * 128,
                                 nb * 512:(nb + 1) * 512],
                       slot=12 + slot, waits=w)
            ptok = (kb.dsem[12 + slot], kb.dcnt[12 + slot])

            @blk.tensor
            def _(p, nb=nb, mt=mt, slot=slot, ptok=ptok, addv=addv):
                for dm in range(C8):
                    P.do(p, lambda e, dm=dm: e.matmul(
                        PS[dm][:, 0:512],
                        st["vres"][:, mt * 1024 + dm * 128:
                                   mt * 1024 + dm * 128 + 128],
                        st["prow"][slot][:],
                        start=(mt == 0), stop=(mt == 31)),
                        waits=[ptok, fscale, addv[0]]
                        if (dm == 0 and mt == 0) else
                        ([ptok] if dm == 0 else []),
                        mark=(dm == C8 - 1))
                pmark[(nb, mt)] = kb.P.prev

        @blk.vector
        def _(v, nb=nb, addv=addv):
            for dm in range(C8):
                V.do(v, lambda e, dm=dm: e.tensor_add(
                    st["x1"][:, dm * NB1 + nb * 512:dm * NB1 + nb * 512 + 512],
                    st["x1"][:, dm * NB1 + nb * 512:dm * NB1 + nb * 512 + 512],
                    PS[dm][:, 0:512]),
                    waits=[pmark[(nb, 31)]] if dm == 0 else [])
            addv[0] = kb.V.prev
    tok["pvv"] = kb.V.prev
    return tok


def emit_b1_out(kb, blk, st, outT, tok):
    @blk.sync
    def _(q):
        last = None
        for j in range(C8):
            last = kb.dma(q, outT[j * 128:(j + 1) * 128, :],
                          st["x1"][:, j * NB1:(j + 1) * NB1], slot=15,
                          waits=[tok["pvv"]] if j == 0 else [])
        q.wait_ge(*last)


B1_PHASES = ["ln_s", "gather", "routing", "norms", "l1aS", "merge",
             "light", "heavy", "pv", "out"]


def build_block1(stop_after=None):
    kb = KB()
    nc = kb.nc
    x1T = kb.din("x1T", [D, OWN])
    waps = {}
    for name, shape in [("wq1", [D, D]), ("wk1", [D, D]), ("wv1T", [D, D]),
                        ("wq_l", [D, 512]), ("wkvl_r", [D, D]),
                        ("wo_l", [512, D]), ("wq_h", [D, 512]),
                        ("wkv_hT", [D, D]), ("wo_h", [512, D])]:
        waps[name] = kb.din(name, shape, F32)
    ins = {}
    for name, shape, dt in [("IDF", [128, 128], F32),
                            ("coremask", [128, 1], F32),
                            ("gl_col", [128, C8], F32),
                            ("gh_col", [128, C8], F32),
                            ("rt_q", [128, C8], F32),
                            ("rt_kv", [128, C8], F32),
                            ("IDB", [128, 128], BF16),
                            ("onesb1", [1, FT], BF16)]:
        ins[name] = kb.din(name, shape, dt)
    outT = kb.dout("outT", [D, OWN])
    kb.spillP = kb.dint("spillP", [FT, NB1], BF16)
    kb.spillV = kb.dint("spillV", [FT, D], BF16)
    kb.cin1 = kb.dint("cin1", [D, NB1], BF16)
    kb.cout1 = kb.dint("cout1", [2 * D, NB1], BF16)
    kb.cins = kb.dint("cins", [2, NB1])
    kb.couts = kb.dint("couts", [4, NB1])
    kb.cing = kb.dint("cing", [128, 64])
    kb.coutg = kb.dint("coutg", [256, 64])
    kb.rscr = kb.dint("rscr", [1, 2 * FT])
    kb.rqm = kb.dint("rqm", [1, NB1])
    kb.kvmd = kb.dint("kvmd", [1, FT], BF16)
    kb.rfn = kb.dint("rfn", [1, FT])
    st = alloc_block1(kb)
    kb.open()
    nphase = B1_PHASES.index(stop_after) + 1 if stop_after else len(B1_PHASES)
    ph = B1_PHASES[:nphase]
    with nc.Block() as blk:
        bar = emit_b1_prelude(kb, blk, ins)
        tok = {}
        tok = emit_b1_ln_s(kb, blk, st, x1T, tok)
        if "gather" in ph:
            tok = emit_b1_gather(kb, blk, st, tok)
        if "routing" in ph:
            tok = emit_b1_routing(kb, blk, st, tok)
        if "norms" in ph:
            tok = emit_b1_norms(kb, blk, st, tok)
        if "l1aS" in ph:
            bar = emit_barrier(kb, blk, [tok.get("maskv"), tok.get("fcolv"),
                                         (kb.dsem[11], kb.dcnt[11]),
                                         (kb.dsem[6], kb.dcnt[6])])
            tok = emit_b1_l1a_S(kb, blk, st, waps, tok, bar)
        if "merge" in ph:
            bar = emit_barrier(kb, blk, [tok["spillP"], tok["spillV"]])
            tok = emit_b1_merge(kb, blk, st, tok, bar)
        if "light" in ph:
            bar = emit_barrier(kb, blk, [tok["facv"]])
            tok = emit_b1_lh(kb, blk, st, waps, tok, bar, "light")
        if "heavy" in ph:
            tok = emit_b1_lh(kb, blk, st, waps, tok, bar, "heavy")
        if "pv" in ph:
            bar = emit_barrier(kb, blk, [tok["lh_wov"], tok["lh_pe"],
                                         tok.get("lh_bias"),
                                         (kb.dsem[8], kb.dcnt[8]),
                                         (kb.dsem[10], kb.dcnt[10]),
                                         (kb.dsem[14], kb.dcnt[14])])
            tok = emit_b1_pv(kb, blk, st, tok, bar)
        if "out" in ph:
            emit_b1_out(kb, blk, st, outT, tok)
        else:
            emit_barrier(kb, blk, [(kb.dsem[i], kb.dcnt[i])
                                   for i in range(16)] +
                         [kb.V.prev, kb.S.prev, kb.G.prev])
    kb.close()
    return kb


def build_fused():
    kb = KB()
    nc = kb.nc
    xT = kb.din("xT", [D, NT0])
    waps = {
        "Wqkv": kb.din("Wqkv", [IN_DEPTH, D, 3 * 1024]),
        "Wo": kb.din("Wo", [IN_DEPTH, D, D]),
        "W1p": kb.din("W1p", [IN_DEPTH, D, NPAIR * 256]),
        "W2": kb.din("W2", [IN_DEPTH, DIM_INNER, D]),
        "w_qkvT": kb.din("w_qkvT", [D, 3 * 1024]),
        "w_outT": kb.din("w_outT", [D, D]),
    }
    ins = {}
    for name, shape in [("TcB", [128, NT0]), ("TsB", [128, NT0]),
                        ("TciB", [128, NT0]), ("TsiB", [128, NT0]),
                        ("TcA", [128, NT0]), ("TsA", [128, NT0]),
                        ("MSK", [128, 4 * 128]), ("padrowbf", [1, NT0]),
                        ("onesbf", [1, KMAX])]:
        ins[name] = kb.din(name, shape, BF16)
    for name, shape in [("PROTF", [128, 128]), ("BD2", [128, 2]),
                        ("BD2T", [2, 128]), ("IDF", [128, 128]),
                        ("coremask", [128, 1]),
                        ("qs_col", [128, IN_DEPTH]),
                        ("ks_col", [128, IN_DEPTH]),
                        ("ln_g", [IN_DEPTH, 128, C8]),
                        ("ln_b", [IN_DEPTH, 128, C8]),
                        ("b1p", [IN_DEPTH, 128, 44]),
                        ("b2c", [IN_DEPTH, 128, C8]),
                        ("a0_lnw", [128, C8]), ("a0_lnb", [128, C8])]:
        ins[name] = kb.din(name, shape, F32)
    for name, shape in [("wq1", [D, D]), ("wk1", [D, D]), ("wv1T", [D, D]),
                        ("wq_l", [D, 512]), ("wkvl_r", [D, D]),
                        ("wo_l", [512, D]), ("wq_h", [D, 512]),
                        ("wkv_hT", [D, D]), ("wo_h", [512, D]),
                        ("gl_col", [128, C8]), ("gh_col", [128, C8]),
                        ("rt_q", [128, C8]), ("rt_kv", [128, C8])]:
        waps[name] = ins[name] = kb.din(name, shape, F32)
    ins["IDB"] = kb.din("IDB", [128, 128], BF16)
    ins["onesb1"] = kb.din("onesb1", [1, FT], BF16)
    outT = kb.dout("outT", [D, OWN])
    x1o = kb.dint("x1T", [D, OWN])
    spill_acc = kb.dint("spill_acc", [2 * D, OWN])
    st = alloc_block0(kb)
    ln_scr0 = kb.ln_scr
    kb.a0w = kb.sbuf("a0w", [128, C8], F32)
    kb.a0b = kb.sbuf("a0b", [128, C8], F32)
    kb.cinH = kb.dint("cinH", [D, HALO])
    kb.coutH = kb.dint("coutH", [2 * D, HALO])
    kb.spillP = kb.dint("spillP", [FT, NB1], BF16)
    kb.spillV = kb.dint("spillV", [FT, D], BF16)
    kb.cin1 = kb.dint("cin1", [D, NB1], BF16)
    kb.cout1 = kb.dint("cout1", [2 * D, NB1], BF16)
    kb.cins = kb.dint("cins", [2, NB1])
    kb.couts = kb.dint("couts", [4, NB1])
    kb.cing = kb.dint("cing", [128, 64])
    kb.coutg = kb.dint("coutg", [256, 64])
    kb.rscr = kb.dint("rscr", [1, 2 * FT])
    kb.rqm = kb.dint("rqm", [1, NB1])
    kb.kvmd = kb.dint("kvmd", [1, FT], BF16)
    kb.rfn = kb.dint("rfn", [1, FT])
    st1 = alloc_block1(kb, fused_base=kb.nc.lookup_mloc(st["resid"]).addr)
    kb.open()
    with nc.Block() as blk:
        tok = emit_prelude(kb, blk, st, ins)
        ldtok = emit_load_x(kb, blk, st["resid"], xT)

        @blk.tensor
        def _(p):
            p.wait_ge(*ldtok)

        @blk.vector
        def _(v):
            v.wait_ge(*ldtok)

        @blk.scalar
        def _(s):
            s.wait_ge(*ldtok)
        lt = emit_ln(kb, blk, st["resid"], st["resid"], NT0, tblocks(0, NT0),
                     kb.ln_scr)
        tok["resid_v"] = lt
        tok = emit_a0(kb, blk, st, waps, tok, spill_acc)
        tok = emit_load_tablesB(kb, blk, ins, tok)
        for l in range(IN_DEPTH):
            tok = emit_inner_attn(kb, blk, st, l, waps, tok)
            tok = emit_inner_ffn(kb, blk, st, l, waps, tok)
            if l < IN_DEPTH - 1:
                tok = emit_halo_exchange(kb, blk, st, l, tok)
        tok = emit_combine_block0(kb, blk, st, xT, spill_acc, x1o, tok)
        kb.ln_scr = st1["ln_scr"]
        for nm, ap in st1["b1_consts"].items():
            setattr(kb, nm, ap)
        bar0 = emit_barrier(kb, blk, [tok["x1_dma"], kb.V.prev, kb.S.prev,
                                      kb.G.prev, kb.P.prev] +
                            [(kb.dsem[i], kb.dcnt[i]) for i in range(16)])
        bar = emit_b1_prelude(kb, blk, ins)
        tok1 = {}
        tok1 = emit_b1_ln_s(kb, blk, st1, x1o, tok1)
        tok1 = emit_b1_gather(kb, blk, st1, tok1)
        tok1 = emit_b1_routing(kb, blk, st1, tok1)
        tok1 = emit_b1_norms(kb, blk, st1, tok1)
        bar = emit_barrier(kb, blk, [tok1.get("maskv"), tok1.get("fcolv"),
                                     (kb.dsem[11], kb.dcnt[11]),
                                     (kb.dsem[6], kb.dcnt[6])])
        tok1 = emit_b1_l1a_S(kb, blk, st1, waps, tok1, bar)
        bar = emit_barrier(kb, blk, [tok1["spillP"], tok1["spillV"]])
        tok1 = emit_b1_merge(kb, blk, st1, tok1, bar)
        bar = emit_barrier(kb, blk, [tok1["facv"]])
        tok1 = emit_b1_lh(kb, blk, st1, waps, tok1, bar, "light")
        tok1 = emit_b1_lh(kb, blk, st1, waps, tok1, bar, "heavy")
        bar = emit_barrier(kb, blk, [tok1["lh_wov"], tok1["lh_pe"],
                                     tok1.get("lh_bias"),
                                     (kb.dsem[8], kb.dcnt[8]),
                                     (kb.dsem[10], kb.dcnt[10]),
                                     (kb.dsem[14], kb.dcnt[14])])
        tok1 = emit_b1_pv(kb, blk, st1, tok1, bar)
        emit_b1_out(kb, blk, st1, outT, tok1)
    kb.close()
    return kb


def kernel(**inputs):
    cores = [make_core_inputs(inputs, c) for c in range(NCORES)]
    kb = build_fused()
    maps = [{k: g[k] for k in kb.inp} for g in cores]
    res = run_bass_kernel_spmd(kb.nc, maps, core_ids=list(range(NCORES)))
    out = np.empty((B, T, D), np.float32)
    for c in range(NCORES):
        b, h = c // 2, c % 2
        out[b, h * OWN:(h + 1) * OWN] = np.asarray(res.results[c]["outT"]).T
    return out
